# version 02
# baseline (speedup 1.0000x reference)
import numpy as np
import ml_dtypes
import contextlib

import concourse.bass as bass
import concourse.mybir as mybir
from concourse.bass_utils import run_bass_kernel_spmd

F32 = mybir.dt.float32
BF16 = mybir.dt.bfloat16
AF = mybir.ActivationFunctionType
ALU = mybir.AluOpType
BF = ml_dtypes.bfloat16

NCORES = 8
BPC = 4
S = 2048
D = 1024
PD = D // 128

ENGINES = ("sync", "tensor", "vector", "scalar", "gpsimd")


class Sem:
    __slots__ = ("h", "n", "name")
    def __init__(self, h, name):
        self.h = h; self.n = 0; self.name = name

class Buf:
    __slots__ = ("w", "readers", "name")
    def __init__(self, name=""):
        self.w = None; self.readers = []; self.name = name

class Ctx:
    def __init__(self, nc):
        self.nc = nc
        self.items = {e: [] for e in ENGINES}
        self.stack = contextlib.ExitStack()
        self.esem = {e: self._mksem("sem_" + e) for e in ENGINES}
        self.dsem = {e: self._mksem("dma_" + e) for e in ENGINES}
        self.water = {e: {} for e in ENGINES}
        self.psum_banks = [Buf(f"psbank{i}") for i in range(8)]
        self.bufmap = {}
        self.banksmap = {}
        self._psum_cursor = 0
        self._names = set()
        self.nops = 0
        self.w = {}
        self._consts = {}
        self._cpool = None

    def _mksem(self, name):
        return Sem(self.stack.enter_context(self.nc.semaphore(name)), name)

    def constf(self, value):
        value = float(value)
        if value in self._consts:
            return value
        if self._cpool is None:
            self._cpool = self.sb("const_pool", [128, 64], F32)
        i = len(self._consts)
        assert i < 64
        ap = self._cpool[:, i:i + 1]
        self.op("vector", lambda e, _a=ap, _v=value: e.memset(_a, _v), writes=[self._cpool])
        self.nc.const_aps.register(F32, value, ap)
        self._consts[value] = True
        return value

    def sb(self, name, shape, dtype):
        assert name not in self._names, name
        self._names.add(name)
        t = self.nc.alloc_sbuf_tensor(name, shape, dtype)
        self.bufmap[name] = Buf(name)
        return t

    def ps(self, name, shape, dtype=F32):
        assert name not in self._names, name
        self._names.add(name)
        t = self.nc.alloc_psum_tensor(name, shape, dtype)
        nb = self.nc.num_psum_banks(shape, dtype)
        start = self._psum_cursor
        self._psum_cursor = start + nb
        assert self._psum_cursor <= 8, "psum overflow"
        self.banksmap[name] = list(range(start, start + nb))
        return t

    def _bufs_of(self, objs):
        out = []
        for o in objs:
            if o is None:
                continue
            if isinstance(o, Buf):
                out.append(o)
            elif isinstance(o, (list, tuple)):
                out.extend(self._bufs_of(o))
            else:
                nm = getattr(o, "name", None)
                if nm is None and hasattr(o, "tensor"):
                    nm = o.tensor.name
                if nm is None:
                    raise TypeError(f"cannot track {o!r}")
                if nm in self.banksmap:
                    for b in self.banksmap[nm]:
                        out.append(self.psum_banks[b])
                elif nm in self.bufmap:
                    out.append(self.bufmap[nm])
                else:
                    raise TypeError(f"untracked tensor {nm}")
        return out

    def op(self, eng, build, reads=(), writes=(), dma=False):
        rbufs = self._bufs_of(reads)
        wbufs = self._bufs_of(writes)
        need = {}
        for b in rbufs:
            if b.w is not None:
                s, v = b.w
                if need.get(s, 0) < v:
                    need[s] = v
        for b in wbufs:
            if b.w is not None:
                s, v = b.w
                if need.get(s, 0) < v:
                    need[s] = v
            for s, v in b.readers:
                if need.get(s, 0) < v:
                    need[s] = v
        sem = self.dsem[eng] if dma else self.esem[eng]
        inc = 16 if dma else 1
        if dma and sem.n > 0 and need.get(sem, 0) < sem.n:
            need[sem] = sem.n
        sem.n += inc
        val = sem.n
        wm = self.water[eng]
        waits = []
        for s, v in need.items():
            if wm.get(s, 0) >= v:
                continue
            wm[s] = v
            waits.append((s, v))
        def fn(e, _waits=tuple(waits), _build=build, _sem=sem, _inc=inc):
            for s, v in _waits:
                e.wait_ge(s.h, v)
            _build(e).then_inc(_sem.h, _inc)
        self.items[eng].append(fn)
        self.nops += 1
        for b in rbufs:
            b.readers.append((sem, val))
        for b in wbufs:
            b.w = (sem, val)
            b.readers = []
        return (sem, val)

    def fence(self):
        sems = list(self.esem.values()) + list(self.dsem.values())
        targets = [(s, s.n) for s in sems if s.n > 0]
        for e in ENGINES:
            wm = self.water[e]
            todo = [(s, v) for (s, v) in targets if wm.get(s, 0) < v]
            for s, v in todo:
                wm[s] = v
            def fn(eng_obj, _todo=tuple(todo)):
                for s, v in _todo:
                    eng_obj.wait_ge(s.h, v)
            self.items[e].append(fn)
        for b in self.psum_banks:
            b.w = None; b.readers = []

    def finalize(self):
        for e in ENGINES:
            s = self.dsem[e]
            if s.n > 0:
                self.items["sync"].append(lambda eo, _s=s, _v=s.n: eo.wait_ge(_s.h, _v))
        with self.nc.Block() as block:
            for e in ENGINES:
                def body(eng_obj, _e=e):
                    for fn in self.items[_e]:
                        fn(eng_obj)
                getattr(block, e)(body)
        self.stack.close()



def DMA(ctx, eng, out, in_, reads=(), writes=()):
    ctx.op(eng, lambda e, _o=out, _i=in_: e.dma_start(out=_o, in_=_i),
           reads=list(reads), writes=list(writes), dma=True)

def MM(ctx, out, lhsT, rhs, start, stop, reads=(), writes=(), tp=None):
    if tp is not None and tp[1] != 96:
        tp = None
    ctx.op("tensor", lambda e, _o=out, _l=lhsT, _r=rhs, _s=start, _p=stop, _t=tp:
           e.matmul(out=_o, lhsT=_l, rhs=_r, start=_s, stop=_p, tile_position=_t),
           reads=list(reads), writes=list(writes))

def TRANS(ctx, out, in_, ident, reads=(), writes=()):
    ctx.op("tensor", lambda e, _o=out, _i=in_, _d=ident: e.transpose(out=_o, in_=_i, identity=_d),
           reads=list(reads), writes=list(writes))

def COPY(ctx, eng, out, in_, reads=(), writes=()):
    ctx.op(eng, lambda e, _o=out, _i=in_: e.tensor_copy(out=_o, in_=_i),
           reads=list(reads), writes=list(writes))

def TT(ctx, eng, out, in0, in1, alu, reads=(), writes=()):
    ctx.op(eng, lambda e, _o=out, _a=in0, _b=in1, _al=alu: e.tensor_tensor(out=_o, in0=_a, in1=_b, op=_al),
           reads=list(reads), writes=list(writes))

def TS(ctx, eng, out, in0, s1, alu, reads=(), writes=()):
    ctx.op(eng, lambda e, _o=out, _a=in0, _s=s1, _al=alu: e.tensor_scalar(
           out=_o, in0=_a, scalar1=_s, scalar2=None, op0=_al),
           reads=list(reads), writes=list(writes))

def STT(ctx, eng, out, in0, scalar, in1, op0, op1, reads=(), writes=()):
    ctx.op(eng, lambda e, _o=out, _a=in0, _s=scalar, _b=in1, _0=op0, _1=op1:
           e.scalar_tensor_tensor(out=_o, in0=_a, scalar=_s, in1=_b, op0=_0, op1=_1),
           reads=list(reads), writes=list(writes))

def ACT(ctx, out, in_, func, bias=0.0, scale=1.0, accum=None, reads=(), writes=()):
    reads = list(reads)
    if isinstance(bias, float) and bias != 0.0:
        ctx.constf(bias)
        reads.append(ctx._cpool)
    ctx.op("scalar", lambda e, _o=out, _i=in_, _f=func, _b=bias, _s=scale, _a=accum:
           e.activation(out=_o, in_=_i, func=_f, bias=_b, scale=_s, accum_out=_a),
           reads=reads, writes=list(writes))

def SCAN(ctx, out, data0, data1, initial, reads=(), writes=()):
    ctx.op("vector", lambda e, _o=out, _a=data0, _b=data1, _i=initial:
           e.tensor_tensor_scan(out=_o, data0=_a, data1=_b, initial=_i,
                                op0=ALU.mult, op1=ALU.add),
           reads=list(reads), writes=list(writes))

def MEMSET(ctx, eng, ap, val, writes=()):
    ctx.op(eng, lambda e, _a=ap, _v=val: e.memset(_a, _v), writes=list(writes))

def RECIP(ctx, out, in_, reads=(), writes=()):
    ctx.op("vector", lambda e, _o=out, _i=in_: e.reciprocal(out=_o, in_=_i),
           reads=list(reads), writes=list(writes))

def MAX8(ctx, out, in_, reads=(), writes=()):
    ctx.op("vector", lambda e, _o=out, _i=in_: e.max(out=_o, in_=_i),
           reads=list(reads), writes=list(writes))

def IOTA(ctx, out, pattern, base, cm, writes=()):
    ctx.op("gpsimd", lambda e, _o=out, _p=pattern, _b=base, _c=cm:
           e.iota(_o, pattern=_p, base=_b, channel_multiplier=_c,
                  allow_small_or_imprecise_dtypes=True),
           writes=list(writes))


def load_weight_bf16(ctx, name, dram_ap, rows, cols, scale=None, scale_col=None,
                     scale_col_reads=()):
    f = ctx.sb(name + "_f", [rows, cols], F32)
    b = ctx.sb(name, [rows, cols], BF16)
    DMA(ctx, "sync", f[:], dram_ap, writes=[f])
    if scale_col is not None:
        TS(ctx, "vector", b[:], f[:], scale_col, ALU.mult,
           reads=[f] + list(scale_col_reads), writes=[b])
    elif scale is not None and scale != 1.0:
        TS(ctx, "vector", b[:], f[:], float(scale), ALU.mult, reads=[f], writes=[b])
    else:
        COPY(ctx, "vector", b[:], f[:], reads=[f], writes=[b])
    return b


def make_ident(ctx, prefix):
    iota = ctx.sb(prefix + "_iota", [128, 128], F32)
    idf = ctx.sb(prefix + "_idf", [128, 128], F32)
    idb = ctx.sb(prefix + "_idb", [128, 128], BF16)
    IOTA(ctx, iota[:], [[1, 128]], 0, -1, writes=[iota])
    TS(ctx, "vector", idf[:], iota[:], 0.0, ALU.is_equal, reads=[iota], writes=[idf])
    COPY(ctx, "vector", idb[:], idf[:], reads=[idf], writes=[idb])
    return idf, idb


class NS:
    pass



def emit_ffn0(ctx, x_in, y_out):
    TQ = 512
    NT = S // TQ
    w = ctx.w

    ones_col_b = ctx.sb("ffn_ones_col", [128, 1], BF16)
    ones_row_f = ctx.sb("ffn_ones_rowf", [1, 128], F32)
    one_f = ctx.sb("ffn_one_f", [1, 1], F32)
    MEMSET(ctx, "vector", ones_col_b[:], 1.0, writes=[ones_col_b])
    MEMSET(ctx, "vector", ones_row_f[:], 1.0, writes=[ones_row_f])
    MEMSET(ctx, "vector", one_f[:], 1.0, writes=[one_f])
    idf, idb = make_ident(ctx, "ffn_id")

    wlin_b = [load_weight_bf16(ctx, f"ffn_wlin{k}", w["wlinT"][k * 128:(k + 1) * 128, :], 128, D)
              for k in range(PD)]
    lcat_b = [load_weight_bf16(ctx, f"ffn_lcat{k}", w["lcat"][k * 128:(k + 1) * 128, :], 128, 32)
              for k in range(PD)]
    gate_b = [load_weight_bf16(ctx, f"ffn_gate{k}", w["gateT"][k * 128:(k + 1) * 128, :], 128, 8)
              for k in range(PD)]
    rcat_b = load_weight_bf16(ctx, "ffn_rcat", w["rcat"], 32, D)
    blin_f = ctx.sb("ffn_blin_f", [128, 8], F32)
    halfb = ctx.sb("ffn_halfb", [128, 8], F32)
    DMA(ctx, "sync", blin_f[:], w["blin"], writes=[blin_f])
    TS(ctx, "vector", halfb[:], blin_f[:], 0.5, ALU.mult, reads=[blin_f], writes=[halfb])

    ps_row = ctx.ps("ffn_psrow", [1, 512], F32)
    wsum_wlin = ctx.sb("ffn_wsum_wlin", [1, D], BF16)
    wsum_lcat = ctx.sb("ffn_wsum_lcat", [1, 32], BF16)
    wsum_gate = ctx.sb("ffn_wsum_gate", [1, 8], BF16)
    for (wchunks, m, dst) in ((wlin_b, D, wsum_wlin), (lcat_b, 32, wsum_lcat), (gate_b, 8, wsum_gate)):
        step = min(m, 512)
        for blk in range(0, m, step):
            for k in range(PD):
                MM(ctx, ps_row[0:1, 0:step], ones_col_b[:], wchunks[k][:, blk:blk + step],
                   start=(k == 0), stop=(k == PD - 1),
                   reads=[wchunks[k], ones_col_b], writes=[ps_row])
            TS(ctx, "vector", dst[:, blk:blk + step], ps_row[0:1, 0:step], -1.0, ALU.mult,
               reads=[ps_row], writes=[dst])

    sel_sc = ctx.sb("ffn_sel_sc", [8, 32], F32)
    sel_b = ctx.sb("ffn_sel", [8, 32], BF16)
    IOTA(ctx, sel_sc[0:8, :], [[-1, 8], [0, 4]], 0, 1, writes=[sel_sc])
    TS(ctx, "vector", sel_b[:], sel_sc[:], 0.0, ALU.is_equal, reads=[sel_sc], writes=[sel_b])

    ps_mean = ctx.ps("ffn_ps_mean", [1, TQ], F32)
    ps_sq = ctx.ps("ffn_ps_sq", [1, TQ], F32)
    ps_big = ctx.ps("ffn_ps_big", [128, TQ], F32)
    ps_aux = ctx.ps("ffn_ps_aux", [128, TQ], F32)
    ps_wrep = ctx.ps("ffn_ps_wrep", [32, TQ], F32)
    ps_tr = ctx.ps("ffn_ps_tr", [128, 128], F32)
    ps_bc = ctx.ps("ffn_ps_bc", [128, TQ], F32)

    mean_f = ctx.sb("ffn_mean_f", [1, TQ], F32)
    sq_f = ctx.sb("ffn_sq_f", [1, TQ], F32)
    var_f = ctx.sb("ffn_var_f", [1, TQ], F32)
    meanrow_b = ctx.sb("ffn_meanrow", [1, TQ], BF16)
    rstdrow = ctx.sb("ffn_rstdrow", [1, TQ], F32)
    rstdhrow = ctx.sb("ffn_rstdh", [1, TQ], F32)
    rstdB = ctx.sb("ffn_rstdB", [128, TQ], F32)
    rstd_cols = ctx.sb("ffn_rstdcols", [128, 4], F32)
    logit = ctx.sb("ffn_logit", [128, 4 * 8], F32)
    t8 = ctx.sb("ffn_t8", [128, 4 * 8], F32)
    negm1 = ctx.sb("ffn_negm1", [128, 4], F32)
    ssum = ctx.sb("ffn_ssum", [128, 4], F32)
    srec = ctx.sb("ffn_srec", [128, 4], F32)
    eprob = ctx.sb("ffn_eprob", [128, 4 * 8], F32)
    gmask = ctx.sb("ffn_gmask", [128, 4 * 8], F32)
    wts = ctx.sb("ffn_wts", [128, 4 * 8], F32)
    wtsT_b = ctx.sb("ffn_wtsT", [8, TQ], BF16)
    tall_b = ctx.sb("ffn_tall", [32, TQ], BF16)
    tw_b = ctx.sb("ffn_tw", [32, TQ], BF16)
    v1 = ctx.sb("ffn_v1", [128, TQ], F32)
    out_f = ctx.sb("ffn_outf", [128, PD * TQ], F32)
    outb = [Buf(f"ffn_outf_{k}") for k in range(PD)]

    xf = [ctx.sb(f"ffn_xf{i}", [128, PD * TQ], F32) for i in range(2)]
    xfb = [[Buf(f"ffn_xfb{i}_{k}") for k in range(PD)] for i in range(2)]
    xb = [ctx.sb(f"ffn_xb{i}", [128, PD * TQ], BF16) for i in range(2)]
    x2 = ctx.sb("ffn_x2", [128, PD * TQ], BF16)

    for b in range(BPC):
        for tci in range(NT):
            t0 = tci * TQ
            pp = (b * NT + tci) % 2
            xf_, xb_, xfb_ = xf[pp], xb[pp], xfb[pp]
            DMA(ctx, "sync",
                xf_[:].rearrange("p (k t) -> p k t", k=PD),
                x_in[b, :, t0:t0 + TQ].rearrange("(k p) t -> p k t", p=128),
                writes=list(xfb_))
            COPY(ctx, "vector", xb_[:], xf_[:], reads=xfb_, writes=[xb_])
            ACT(ctx, x2[:], xf_[:], AF.Square, reads=xfb_, writes=[x2])
            for k in range(PD):
                MM(ctx, ps_mean[0:1, :], ones_col_b[:], xb_[:, k * TQ:(k + 1) * TQ],
                   start=(k == 0), stop=(k == PD - 1), reads=[xb_, ones_col_b], writes=[ps_mean])
            for k in range(PD):
                MM(ctx, ps_sq[0:1, :], ones_col_b[:], x2[:, k * TQ:(k + 1) * TQ],
                   start=(k == 0), stop=(k == PD - 1), reads=[x2, ones_col_b], writes=[ps_sq])
            ACT(ctx, mean_f[:], ps_mean[0:1, :], AF.Copy, scale=1.0 / D,
                reads=[ps_mean], writes=[mean_f])
            COPY(ctx, "vector", meanrow_b[:], mean_f[:], reads=[mean_f], writes=[meanrow_b])
            ACT(ctx, sq_f[:], mean_f[:], AF.Square, reads=[mean_f], writes=[sq_f])
            STT(ctx, "vector", var_f[:], ps_sq[0:1, :], 1.0 / D, sq_f[:],
                ALU.mult, ALU.subtract, reads=[ps_sq, sq_f], writes=[var_f])
            ACT(ctx, sq_f[:], var_f[:], AF.Sqrt, bias=1e-5, reads=[var_f], writes=[sq_f])
            RECIP(ctx, rstdrow[:], sq_f[:], reads=[sq_f], writes=[rstdrow])
            TS(ctx, "vector", rstdhrow[:], rstdrow[:], 0.5, ALU.mult,
               reads=[rstdrow], writes=[rstdhrow])
            MM(ctx, ps_bc[:], ones_row_f[:], rstdhrow[:], start=True, stop=True,
               reads=[rstdhrow, ones_row_f], writes=[ps_bc])
            COPY(ctx, "vector", rstdB[:], ps_bc[:], reads=[ps_bc], writes=[rstdB])
            for j in range(4):
                MM(ctx, ps_tr[:, 0:1], rstdrow[:, j * 128:(j + 1) * 128], one_f[:],
                   start=True, stop=True, reads=[rstdrow, one_f], writes=[ps_tr])
                COPY(ctx, "vector", rstd_cols[:, j:j + 1], ps_tr[:, 0:1],
                     reads=[ps_tr], writes=[rstd_cols])
            for j in range(4):
                for k in range(PD):
                    MM(ctx, ps_aux[0:128, j * 8:(j + 1) * 8],
                       xb_[:, k * TQ + j * 128:k * TQ + (j + 1) * 128], gate_b[k][:],
                       start=(k == 0), stop=False, reads=[xb_, gate_b[k]], writes=[ps_aux])
                MM(ctx, ps_aux[0:128, j * 8:(j + 1) * 8],
                   meanrow_b[:, j * 128:(j + 1) * 128], wsum_gate[:],
                   start=False, stop=True, reads=[meanrow_b, wsum_gate], writes=[ps_aux])
            for j in range(4):
                sl = slice(j * 8, (j + 1) * 8)
                TS(ctx, "vector", logit[:, sl], ps_aux[0:128, sl], rstd_cols[:, j:j + 1],
                   ALU.mult, reads=[ps_aux, rstd_cols], writes=[logit])
                MAX8(ctx, t8[:, sl], logit[:, sl], reads=[logit], writes=[t8])
                TS(ctx, "vector", negm1[:, j:j + 1], t8[:, j * 8:j * 8 + 1], -1.0, ALU.mult,
                   reads=[t8], writes=[negm1])
                ACT(ctx, eprob[:, sl], logit[:, sl], AF.Exp, bias=negm1[:, j:j + 1],
                    accum=ssum[:, j:j + 1], reads=[logit, negm1], writes=[eprob, ssum])
                RECIP(ctx, srec[:, j:j + 1], ssum[:, j:j + 1], reads=[ssum], writes=[srec])
                TS(ctx, "vector", gmask[:, sl], logit[:, sl], t8[:, j * 8 + 1:j * 8 + 2],
                   ALU.is_ge, reads=[logit, t8], writes=[gmask])
                TT(ctx, "vector", wts[:, sl], eprob[:, sl], gmask[:, sl], ALU.mult,
                   reads=[eprob, gmask], writes=[wts])
                TS(ctx, "vector", wts[:, sl], wts[:, sl], srec[:, j:j + 1], ALU.mult,
                   reads=[wts, srec], writes=[wts])
                TRANS(ctx, ps_tr[0:8, 0:128], wts[:, sl], idf[:], reads=[wts, idf], writes=[ps_tr])
                COPY(ctx, "vector", wtsT_b[0:8, j * 128:(j + 1) * 128], ps_tr[0:8, 0:128],
                     reads=[ps_tr], writes=[wtsT_b])
            MM(ctx, ps_wrep[:], sel_b[:], wtsT_b[:], start=True, stop=True,
               reads=[sel_b, wtsT_b], writes=[ps_wrep])
            for k in range(PD):
                MM(ctx, ps_aux[0:32, :], lcat_b[k][:], xb_[:, k * TQ:(k + 1) * TQ],
                   start=(k == 0), stop=False, reads=[lcat_b[k], xb_], writes=[ps_aux])
            MM(ctx, ps_aux[0:32, :], wsum_lcat[:], meanrow_b[:], start=False, stop=True,
               reads=[wsum_lcat, meanrow_b], writes=[ps_aux])
            COPY(ctx, "vector", tall_b[:], ps_aux[0:32, :], reads=[ps_aux], writes=[tall_b])
            TT(ctx, "vector", tw_b[:], ps_wrep[0:32, :], tall_b[:], ALU.mult,
               reads=[ps_wrep, tall_b], writes=[tw_b])
            for m in range(PD):
                for k in range(PD):
                    MM(ctx, ps_big[:], wlin_b[k][:, m * 128:(m + 1) * 128],
                       xb_[:, k * TQ:(k + 1) * TQ], start=(k == 0), stop=False,
                       reads=[wlin_b[k], xb_], writes=[ps_big])
                MM(ctx, ps_big[:], wsum_wlin[:, m * 128:(m + 1) * 128], meanrow_b[:],
                   start=False, stop=False, reads=[wsum_wlin, meanrow_b], writes=[ps_big])
                MM(ctx, ps_big[:], rcat_b[:, m * 128:(m + 1) * 128], tw_b[:],
                   start=False, stop=True, reads=[rcat_b, tw_b], writes=[ps_big])
                TT(ctx, "vector", v1[:], ps_big[:], rstdB[:], ALU.mult,
                   reads=[ps_big, rstdB], writes=[v1])
                STT(ctx, "vector", out_f[:, m * TQ:(m + 1) * TQ], v1[:], halfb[:, m:m + 1],
                    xf_[:, m * TQ:(m + 1) * TQ], ALU.add, ALU.add,
                    reads=[v1, halfb, xfb_[m]], writes=[outb[m]])
            DMA(ctx, "gpsimd",
                y_out[b, :, t0:t0 + TQ].rearrange("(k p) t -> p k t", p=128),
                out_f[:].rearrange("p (k t) -> p k t", k=PD),
                reads=list(outb))



def _fm(x):
    return np.ascontiguousarray(np.swapaxes(x, -1, -2))


def ffn0_weight_arrays(inp):
    return {
        "wlinT": np.ascontiguousarray(inp["L0_ffn_W_lin"].T),
        "lcat": np.ascontiguousarray(np.transpose(np.asarray(inp["L0_ffn_left"]), (1, 0, 2)).reshape(D, 32)),
        "rcat": np.ascontiguousarray(np.asarray(inp["L0_ffn_right"]).reshape(32, D)),
        "gateT": np.ascontiguousarray(inp["L0_ffn_gate_w"].T),
        "blin": np.ascontiguousarray(np.asarray(inp["L0_ffn_b_lin"]).reshape(8, 128).T),
    }


def build_ffn0_program():
    nc = bass.Bass("TRN2", target_bir_lowering=False)
    ctx = Ctx(nc)
    x_in = nc.dram_tensor("x_fm", [BPC, D, S], F32, kind="ExternalInput").ap()
    y_out = nc.dram_tensor("y_fm", [BPC, D, S], F32, kind="ExternalOutput").ap()
    ctx.w = {
        "wlinT": nc.dram_tensor("wlinT", [D, D], F32, kind="ExternalInput").ap(),
        "lcat": nc.dram_tensor("lcat", [D, 32], F32, kind="ExternalInput").ap(),
        "rcat": nc.dram_tensor("rcat", [32, D], F32, kind="ExternalInput").ap(),
        "gateT": nc.dram_tensor("gateT", [D, 8], F32, kind="ExternalInput").ap(),
        "blin": nc.dram_tensor("blin", [128, 8], F32, kind="ExternalInput").ap(),
    }
    emit_ffn0(ctx, x_in, y_out)
    ctx.finalize()
    return nc


def run_ffn0(x_fm_percore, inp):
    nc = build_ffn0_program()
    warr = ffn0_weight_arrays(inp)
    in_maps = [dict(x_fm=x_fm_percore[c], **warr) for c in range(NCORES)]
    res = run_bass_kernel_spmd(nc, in_maps, core_ids=list(range(NCORES)))
    return [r["y_fm"] for r in res.results]





def convert_into(ctx, dest_ap, dest_buf, dram_ap, rows, cols, stage, scale_col=None,
                 scale=None, scale_col_reads=()):
    DMA(ctx, "sync", stage[0:rows, 0:cols], dram_ap, writes=[stage])
    src = stage[0:rows, 0:cols]
    if scale_col is not None:
        TS(ctx, "vector", dest_ap, src, scale_col, ALU.mult,
           reads=[stage] + list(scale_col_reads), writes=[dest_buf])
    elif scale is not None and scale != 1.0:
        TS(ctx, "vector", dest_ap, src, float(scale), ALU.mult, reads=[stage], writes=[dest_buf])
    else:
        COPY(ctx, "vector", dest_ap, src, reads=[stage], writes=[dest_buf])


def load_weight_staged(ctx, name, dram_ap, rows, cols, stage, scale_col=None,
                       scale=None, scale_col_reads=(), chunk=512):
    b = ctx.sb(name, [rows, cols], BF16)
    if isinstance(stage, (list, tuple)):
        stages = stage
    else:
        stages = [stage]
    i = 0
    for c0 in range(0, cols, chunk):
        cw = min(chunk, cols - c0)
        convert_into(ctx, b[:, c0:c0 + cw], b, dram_ap[:, c0:c0 + cw], rows, cw,
                     stages[i % len(stages)], scale_col=scale_col, scale=scale,
                     scale_col_reads=scale_col_reads)
        i += 1
    return b


def emit_wsum_neg(ctx, name, wchunks, m, ones_col_b, ps_row):
    dst = ctx.sb(name, [1, m], BF16)
    step = min(m, 512)
    nk = len(wchunks)
    for blk in range(0, m, step):
        for k in range(nk):
            MM(ctx, ps_row[0:1, 0:step], ones_col_b[:], wchunks[k][:, blk:blk + step],
               start=(k == 0), stop=(k == nk - 1), reads=[wchunks[k], ones_col_b],
               writes=[ps_row])
        TS(ctx, "vector", dst[:, blk:blk + step], ps_row[0:1, 0:step], -1.0, ALU.mult,
           reads=[ps_row], writes=[dst])
    return dst



def emit_mamba_A(ctx, x_in, xi_out, res_out, mrow_out, rrow_out):
    TQ = 512
    NT = S // TQ
    w = ctx.w

    ones_col_b = ctx.sb("mba_ones_col", [128, 1], BF16)
    ones_row_f = ctx.sb("mba_ones_rowf", [1, 128], F32)
    MEMSET(ctx, "vector", ones_col_b[:], 1.0, writes=[ones_col_b])
    MEMSET(ctx, "vector", ones_row_f[:], 1.0, writes=[ones_row_f])

    normw_cols = ctx.sb("mba_normw", [128, PD], F32)
    DMA(ctx, "sync", normw_cols[:], w["normw_cols"], writes=[normw_cols])

    stage = [ctx.sb(f"mba_stg{i}", [128, 2048], F32) for i in range(2)]
    wip_b = [load_weight_staged(ctx, f"mba_wip{k}", w["wipT"][k * 128:(k + 1) * 128, :],
                                128, 4096, stage,
                                scale_col=normw_cols[:, k:k + 1],
                                scale_col_reads=[normw_cols])
             for k in range(PD)]
    ps_row = ctx.ps("mba_psrow", [1, 512], F32)
    wsum_ip = emit_wsum_neg(ctx, "mba_wsum_ip", wip_b, 4096, ones_col_b, ps_row)

    ps_mean = ctx.ps("mba_ps_mean", [1, TQ], F32)
    ps_sq = ctx.ps("mba_ps_sq", [1, TQ], F32)
    ps_bc = ctx.ps("mba_ps_bc", [128, TQ], F32)
    ps_main = [ctx.ps(f"mba_ps_main{i}", [128, TQ], F32) for i in range(2)]

    mean_f = ctx.sb("mba_mean_f", [1, TQ], F32)
    sq_f = ctx.sb("mba_sq_f", [1, TQ], F32)
    var_f = ctx.sb("mba_var_f", [1, TQ], F32)
    meanrow_b = ctx.sb("mba_meanrow", [1, TQ], BF16)
    rstdrow = ctx.sb("mba_rstdrow", [1, TQ], F32)
    rstd2 = ctx.sb("mba_rstd2", [1, TQ], F32)
    qrow = ctx.sb("mba_qrow", [1, TQ], F32)
    srow = ctx.sb("mba_srow", [1, TQ], F32)
    sB = ctx.sb("mba_sB", [128, TQ], F32)

    xf = [ctx.sb(f"mba_xf{i}", [128, PD * TQ], F32) for i in range(2)]
    xfb = [[Buf(f"mba_xfb{i}_{k}") for k in range(PD)] for i in range(2)]
    xb = [ctx.sb(f"mba_xb{i}", [128, PD * TQ], BF16) for i in range(2)]
    x2 = ctx.sb("mba_x2", [128, PD * TQ], BF16)
    xi_sb = ctx.sb("mba_xi", [128, 8 * TQ], F32)
    xib = [Buf(f"mba_xib{m}") for m in range(8)]
    res_sb = ctx.sb("mba_res", [128, 16 * TQ], BF16)
    resb = [Buf(f"mba_resb{m}") for m in range(16)]

    for b in range(BPC):
        for tci in range(NT):
            t0 = tci * TQ
            pp = (b * NT + tci) % 2
            xf_, xb_, xfb_ = xf[pp], xb[pp], xfb[pp]
            DMA(ctx, "sync",
                xf_[:].rearrange("p (k t) -> p k t", k=PD),
                x_in[b, :, t0:t0 + TQ].rearrange("(k p) t -> p k t", p=128),
                writes=list(xfb_))
            COPY(ctx, "vector", xb_[:], xf_[:], reads=xfb_, writes=[xb_])
            ACT(ctx, x2[:], xf_[:], AF.Square, reads=xfb_, writes=[x2])
            for k in range(PD):
                MM(ctx, ps_mean[0:1, :], ones_col_b[:], xb_[:, k * TQ:(k + 1) * TQ],
                   start=(k == 0), stop=(k == PD - 1), reads=[xb_, ones_col_b],
                   writes=[ps_mean])
            for k in range(PD):
                MM(ctx, ps_sq[0:1, :], ones_col_b[:], x2[:, k * TQ:(k + 1) * TQ],
                   start=(k == 0), stop=(k == PD - 1), reads=[x2, ones_col_b],
                   writes=[ps_sq])
            ACT(ctx, mean_f[:], ps_mean[0:1, :], AF.Copy, scale=1.0 / D,
                reads=[ps_mean], writes=[mean_f])
            COPY(ctx, "vector", meanrow_b[:], mean_f[:], reads=[mean_f], writes=[meanrow_b])
            ACT(ctx, sq_f[:], mean_f[:], AF.Square, reads=[mean_f], writes=[sq_f])
            STT(ctx, "vector", var_f[:], ps_sq[0:1, :], 1.0 / D, sq_f[:],
                ALU.mult, ALU.subtract, reads=[ps_sq, sq_f], writes=[var_f])
            ACT(ctx, sq_f[:], var_f[:], AF.Sqrt, bias=1e-5, reads=[var_f], writes=[sq_f])
            RECIP(ctx, rstdrow[:], sq_f[:], reads=[sq_f], writes=[rstdrow])
            DMA(ctx, "gpsimd", mrow_out[b:b + 1, t0:t0 + TQ], mean_f[0:1, :], reads=[mean_f])
            DMA(ctx, "gpsimd", rrow_out[b:b + 1, t0:t0 + TQ], rstdrow[0:1, :], reads=[rstdrow])
            ACT(ctx, rstd2[:], rstdrow[:], AF.Square, reads=[rstdrow], writes=[rstd2])
            TT(ctx, "vector", qrow[:], var_f[:], rstd2[:], ALU.mult,
               reads=[var_f, rstd2], writes=[qrow])
            ACT(ctx, qrow[:], qrow[:], AF.Sqrt, bias=1e-6, reads=[qrow], writes=[qrow])
            RECIP(ctx, qrow[:], qrow[:], reads=[qrow], writes=[qrow])
            TT(ctx, "vector", srow[:], rstdrow[:], qrow[:], ALU.mult,
               reads=[rstdrow, qrow], writes=[srow])
            MM(ctx, ps_bc[:], ones_row_f[:], srow[:], start=True, stop=True,
               reads=[srow, ones_row_f], writes=[ps_bc])
            COPY(ctx, "vector", sB[:], ps_bc[:], reads=[ps_bc], writes=[sB])
            for m in range(32):
                pm = ps_main[m % 2]
                for k in range(PD):
                    MM(ctx, pm[:], wip_b[k][:, m * 128:(m + 1) * 128],
                       xb_[:, k * TQ:(k + 1) * TQ], start=(k == 0), stop=False,
                       reads=[wip_b[k], xb_], writes=[pm])
                MM(ctx, pm[:], wsum_ip[:, m * 128:(m + 1) * 128], meanrow_b[:],
                   start=False, stop=True, reads=[wsum_ip, meanrow_b], writes=[pm])
                if m < 16:
                    mh = m % 8
                    TT(ctx, "vector", xi_sb[:, mh * TQ:(mh + 1) * TQ], pm[:], sB[:],
                       ALU.mult, reads=[pm, sB], writes=[xib[mh]])
                    if m == 7 or m == 15:
                        h0 = (m // 8) * 1024
                        DMA(ctx, "gpsimd",
                            xi_out[b, h0:h0 + 1024, t0:t0 + TQ].rearrange("(k p) t -> p k t", p=128),
                            xi_sb[:].rearrange("p (k t) -> p k t", k=8),
                            reads=list(xib))
                else:
                    mm = m - 16
                    TT(ctx, "vector", res_sb[:, mm * TQ:(mm + 1) * TQ], pm[:], sB[:],
                       ALU.mult, reads=[pm, sB], writes=[resb[mm]])
            DMA(ctx, "gpsimd",
                res_out[b, :, t0:t0 + TQ].rearrange("(k p) t -> p k t", p=128),
                res_sb[:].rearrange("p (k t) -> p k t", k=16), reads=list(resb))



def emit_mamba_B(ctx, xi_in, xc_out, dt_out, B_out):
    TQ = 512
    NT = S // TQ
    w = ctx.w

    conv_cols = ctx.sb("mbb_convw", [128, 16 * 4], F32)
    convb_cols = ctx.sb("mbb_convb", [128, 16], F32)
    dtb_cols = ctx.sb("mbb_dtb", [128, 16], F32)
    DMA(ctx, "sync", conv_cols[:], w["conv_cols"], writes=[conv_cols])
    DMA(ctx, "sync", convb_cols[:], w["convb_cols"], writes=[convb_cols])
    DMA(ctx, "sync", dtb_cols[:], w["dtb_cols"], writes=[dtb_cols])

    stage = [ctx.sb(f"mbb_stg{i}", [128, 2048], F32) for i in range(2)]
    dtw_b = [load_weight_staged(ctx, f"mbb_dtw{k}", w["dtwT"][k * 128:(k + 1) * 128, :],
                                128, 2048, stage[k % 2]) for k in range(16)]
    xpw_b = [load_weight_staged(ctx, f"mbb_xpw{k}", w["xpwT"][k * 128:(k + 1) * 128, :],
                                128, 64, stage[k % 2]) for k in range(16)]

    xi_sb = [ctx.sb(f"mbb_xi{i}", [128, 16 * (TQ + 3)], F32) for i in range(2)]
    xib = [[Buf(f"mbb_xib{i}_{k}") for k in range(16)] for i in range(2)]
    acc = ctx.sb("mbb_acc", [128, TQ], F32)
    xc_b = ctx.sb("mbb_xc", [128, 16 * TQ], BF16)
    xcb = [Buf(f"mbb_xcb{k}") for k in range(16)]
    ef = ctx.sb("mbb_ef", [128, TQ], F32)
    dt_sb = ctx.sb("mbb_dt", [128, 16 * TQ], BF16)
    dtb_ = [Buf(f"mbb_dtb{k}") for k in range(16)]
    B_sb = ctx.sb("mbb_B", [64, TQ], BF16)

    ps_main = [ctx.ps(f"mbb_ps{i}", [128, TQ], F32) for i in range(2)]
    ps_B = ctx.ps("mbb_psB", [64, TQ], F32)

    W = TQ + 3
    for b in range(BPC):
        for tci in range(NT):
            t0 = tci * TQ
            pp = (b * NT + tci) % 2
            xi_, xib_ = xi_sb[pp], xib[pp]
            v3 = xi_[:].rearrange("p (k w) -> p k w", k=16)
            if tci == 0:
                for k in range(16):
                    MEMSET(ctx, "vector", xi_[:, k * W:k * W + 3], 0.0, writes=[xib_[k]])
                DMA(ctx, "sync", v3[:, :, 3:W],
                    xi_in[b, :, t0:t0 + TQ].rearrange("(k p) t -> p k t", p=128),
                    reads=list(xib_), writes=list(xib_))
            else:
                DMA(ctx, "sync", v3[:, :, 0:W],
                    xi_in[b, :, t0 - 3:t0 + TQ].rearrange("(k p) t -> p k t", p=128),
                    reads=list(xib_), writes=list(xib_))
            for k in range(16):
                base = k * W
                TS(ctx, "vector", acc[:], xi_[:, base:base + TQ], conv_cols[:, 4 * k:4 * k + 1],
                   ALU.mult, reads=[xib_[k], conv_cols], writes=[acc])
                for j in (1, 2, 3):
                    STT(ctx, "vector", acc[:], xi_[:, base + j:base + j + TQ],
                        conv_cols[:, 4 * k + j:4 * k + j + 1], acc[:], ALU.mult, ALU.add,
                        reads=[xib_[k], conv_cols, acc], writes=[acc])
                ACT(ctx, xc_b[:, k * TQ:(k + 1) * TQ], acc[:], AF.Silu,
                    bias=convb_cols[:, k:k + 1], reads=[acc, convb_cols], writes=[xcb[k]])
            DMA(ctx, "gpsimd",
                xc_out[b, :, t0:t0 + TQ].rearrange("(k p) t -> p k t", p=128),
                xc_b[:].rearrange("p (k t) -> p k t", k=16), reads=list(xcb))
            for m in range(16):
                pm = ps_main[m % 2]
                for k in range(16):
                    MM(ctx, pm[:], dtw_b[k][:, m * 128:(m + 1) * 128],
                       xc_b[:, k * TQ:(k + 1) * TQ], start=(k == 0), stop=(k == 15),
                       reads=[dtw_b[k], xcb[k]], writes=[pm])
                ACT(ctx, ef[:], pm[:], AF.Exp, bias=dtb_cols[:, m:m + 1],
                    reads=[pm, dtb_cols], writes=[ef])
                ACT(ctx, dt_sb[:, m * TQ:(m + 1) * TQ], ef[:], AF.Ln, bias=1.0,
                    reads=[ef], writes=[dtb_[m]])
            DMA(ctx, "gpsimd",
                dt_out[b, :, t0:t0 + TQ].rearrange("(k p) t -> p k t", p=128),
                dt_sb[:].rearrange("p (k t) -> p k t", k=16), reads=list(dtb_))
            for k in range(16):
                MM(ctx, ps_B[:], xpw_b[k][:], xc_b[:, k * TQ:(k + 1) * TQ],
                   start=(k == 0), stop=(k == 15), reads=[xpw_b[k], xcb[k]], writes=[ps_B])
            COPY(ctx, "vector", B_sb[:], ps_B[:], reads=[ps_B], writes=[B_sb])
            DMA(ctx, "gpsimd", B_out[b, :, t0:t0 + TQ], B_sb[:], reads=[B_sb])



def emit_mamba_C(ctx, xc_in, dt_in, B_in, y_out):
    TC = 1024
    w = ctx.w

    A_sb = ctx.sb("mbc_A", [128, 1024], F32)
    A_stage = ctx.sb("mbc_Astg", [128, 1024], F32)
    DMA(ctx, "sync", A_stage[:], w["A_rs"], writes=[A_stage])
    ACT(ctx, A_sb[:], A_stage[:], AF.Exp, reads=[A_stage], writes=[A_sb])
    TS(ctx, "vector", A_sb[:], A_sb[:], -1.0, ALU.mult, reads=[A_sb], writes=[A_sb])
    D_cols = ctx.sb("mbc_D", [128, 16], F32)
    DMA(ctx, "sync", D_cols[:], w["D_cols"], writes=[D_cols])
    identb = make_ident(ctx, "mbc_id")[1]

    Bb = ctx.sb("mbc_Bb", [128, 32 * TC], BF16)
    h_state = ctx.sb("mbc_hstate", [128, 1024], F32)
    dt_t = [ctx.sb(f"mbc_dt{i}", [128, TC], BF16) for i in range(2)]
    xc_t = [ctx.sb(f"mbc_xc{i}", [128, TC], BF16) for i in range(2)]
    a_t = [ctx.sb(f"mbc_a{i}", [128, TC], F32) for i in range(2)]
    u_t = [ctx.sb(f"mbc_u{i}", [128, TC], BF16) for i in range(2)]
    h_t = [ctx.sb(f"mbc_h{i}", [128, TC], BF16) for i in range(2)]
    y_acc = ctx.sb("mbc_yacc", [128, 16 * TC], BF16)
    yab = [Buf(f"mbc_yab{k}") for k in range(16)]
    t2 = ctx.sb("mbc_t2", [128, TC], F32)
    yout_sb = ctx.sb("mbc_yout", [128, 16 * TC], BF16)
    youtb = [Buf(f"mbc_youtb{k}") for k in range(16)]

    ps_y = [ctx.ps(f"mbc_psy{i}", [128, TC], F32) for i in range(2)]

    for b in range(BPC):
        MEMSET(ctx, "vector", h_state[:], 0.0, writes=[h_state])
        for tci in range(2):
            t0 = tci * TC
            for half in range(2):
                DMA(ctx, "sync",
                    Bb[:].rearrange("p (n t) -> p n t", n=32),
                    bass.AP(B_in.tensor, (b * 64 + half * 32) * S + t0,
                            [[0, 128], [S, 32], [1, TC]]),
                    writes=[Bb])
                for db in range(16):
                    pp = db % 2
                    dt_, xc_, a_, u_, h_ = dt_t[pp], xc_t[pp], a_t[pp], u_t[pp], h_t[pp]
                    py = ps_y[pp]
                    DMA(ctx, "sync", dt_[:], dt_in[b, db * 128:(db + 1) * 128, t0:t0 + TC],
                        writes=[dt_])
                    DMA(ctx, "sync", xc_[:], xc_in[b, db * 128:(db + 1) * 128, t0:t0 + TC],
                        writes=[xc_])
                    for ni in range(32):
                        n = half * 32 + ni
                        col = db * 64 + n
                        ACT(ctx, a_[:], dt_[:], AF.Exp, scale=A_sb[:, col:col + 1],
                            reads=[dt_, A_sb], writes=[a_])
                        TT(ctx, "vector", u_[:], xc_[:], Bb[:, ni * TC:(ni + 1) * TC],
                           ALU.mult, reads=[xc_, Bb], writes=[u_])
                        SCAN(ctx, h_[:], a_[:], u_[:], h_state[:, col:col + 1],
                             reads=[a_, u_, h_state], writes=[h_])
                        COPY(ctx, "vector", h_state[:, col:col + 1], h_[:, TC - 1:TC],
                             reads=[h_], writes=[h_state])
                        MM(ctx, py[:, 0:512], identb[:], h_[:, 0:512],
                           start=(ni == 0), stop=(ni == 31), reads=[identb, h_], writes=[py])
                        MM(ctx, py[:, 512:1024], identb[:], h_[:, 512:1024],
                           start=(ni == 0), stop=(ni == 31), reads=[identb, h_], writes=[py])
                    if half == 0:
                        COPY(ctx, "vector", y_acc[:, db * TC:(db + 1) * TC], py[:],
                             reads=[py], writes=[yab[db]])
                    else:
                        STT(ctx, "vector", t2[:], xc_[:], D_cols[:, db:db + 1], py[:],
                            ALU.mult, ALU.add, reads=[xc_, D_cols, py], writes=[t2])
                        TT(ctx, "vector", yout_sb[:, db * TC:(db + 1) * TC], t2[:],
                           y_acc[:, db * TC:(db + 1) * TC], ALU.add,
                           reads=[t2, yab[db]], writes=[youtb[db]])
            DMA(ctx, "gpsimd",
                y_out[b, :, t0:t0 + TC].rearrange("(k p) t -> p k t", p=128),
                yout_sb[:].rearrange("p (k t) -> p k t", k=16), reads=list(youtb))



def emit_mamba_D(ctx, y_in, res_in, x_in, x_out, mrow_in, rrow_in):
    TQ = 512
    NT = S // TQ
    w = ctx.w
    ones_row_f = ctx.sb("mbd_ones_rowf", [1, 128], F32)
    MEMSET(ctx, "vector", ones_row_f[:], 1.0, writes=[ones_row_f])
    mrow_sb = ctx.sb("mbd_mrow", [1, S], F32)
    rrow_sb = ctx.sb("mbd_rrow", [1, S], F32)
    rp1 = ctx.sb("mbd_rp1", [1, TQ], F32)
    m2 = ctx.sb("mbd_m2", [1, TQ], F32)
    rp1B = ctx.sb("mbd_rp1B", [128, TQ], F32)
    m2B = ctx.sb("mbd_m2B", [128, TQ], F32)
    v1 = ctx.sb("mbd_v1", [128, TQ], F32)
    ps_bc = ctx.ps("mbd_ps_bc", [128, TQ], F32)

    stage = [ctx.sb(f"mbd_stg{i}", [128, 1024], F32) for i in range(2)]
    wop_b = [load_weight_staged(ctx, f"mbd_wop{k}", w["wopT"][k * 128:(k + 1) * 128, :],
                                128, 1024, stage[k % 2]) for k in range(16)]

    y_sb = [ctx.sb(f"mbd_y{i}", [128, 16 * TQ], BF16) for i in range(2)]
    res_sb = [ctx.sb(f"mbd_res{i}", [128, 16 * TQ], BF16) for i in range(2)]
    xf = [ctx.sb(f"mbd_xf{i}", [128, PD * TQ], F32) for i in range(2)]
    g_sb = ctx.sb("mbd_g", [128, 16 * TQ], BF16)
    yg_sb = ctx.sb("mbd_yg", [128, 16 * TQ], BF16)
    out_f = ctx.sb("mbd_out", [128, 4 * TQ], F32)
    outb = [Buf(f"mbd_outb{k}") for k in range(4)]
    ps_main = [ctx.ps(f"mbd_ps{i}", [128, TQ], F32) for i in range(2)]

    for b in range(BPC):
        DMA(ctx, "sync", mrow_sb[0:1, :], mrow_in[b:b + 1, :], writes=[mrow_sb])
        DMA(ctx, "sync", rrow_sb[0:1, :], rrow_in[b:b + 1, :], writes=[rrow_sb])
        for tci in range(NT):
            t0 = tci * TQ
            pp = (b * NT + tci) % 2
            y_, res_, xf_ = y_sb[pp], res_sb[pp], xf[pp]
            TS(ctx, "vector", rp1[:], rrow_sb[0:1, t0:t0 + TQ], 1.0, ALU.add,
               reads=[rrow_sb], writes=[rp1])
            TT(ctx, "vector", m2[:], mrow_sb[0:1, t0:t0 + TQ], rrow_sb[0:1, t0:t0 + TQ],
               ALU.mult, reads=[mrow_sb, rrow_sb], writes=[m2])
            MM(ctx, ps_bc[:], ones_row_f[:], rp1[:], start=True, stop=True,
               reads=[rp1, ones_row_f], writes=[ps_bc])
            COPY(ctx, "vector", rp1B[:], ps_bc[:], reads=[ps_bc], writes=[rp1B])
            MM(ctx, ps_bc[:], ones_row_f[:], m2[:], start=True, stop=True,
               reads=[m2, ones_row_f], writes=[ps_bc])
            COPY(ctx, "vector", m2B[:], ps_bc[:], reads=[ps_bc], writes=[m2B])
            DMA(ctx, "sync", y_[:].rearrange("p (k t) -> p k t", k=16),
                y_in[b, :, t0:t0 + TQ].rearrange("(k p) t -> p k t", p=128), writes=[y_])
            DMA(ctx, "sync", res_[:].rearrange("p (k t) -> p k t", k=16),
                res_in[b, :, t0:t0 + TQ].rearrange("(k p) t -> p k t", p=128), writes=[res_])
            DMA(ctx, "sync", xf_[:].rearrange("p (k t) -> p k t", k=PD),
                x_in[b, :, t0:t0 + TQ].rearrange("(k p) t -> p k t", p=128), writes=[xf_])
            ACT(ctx, g_sb[:], res_[:], AF.Silu, reads=[res_], writes=[g_sb])
            TT(ctx, "vector", yg_sb[:], y_[:], g_sb[:], ALU.mult,
               reads=[y_, g_sb], writes=[yg_sb])
            for m in range(PD):
                pm = ps_main[m % 2]
                for k in range(16):
                    MM(ctx, pm[:], wop_b[k][:, m * 128:(m + 1) * 128],
                       yg_sb[:, k * TQ:(k + 1) * TQ], start=(k == 0), stop=(k == 15),
                       reads=[wop_b[k], yg_sb], writes=[pm])
                TT(ctx, "vector", v1[:], xf_[:, m * TQ:(m + 1) * TQ], rp1B[:], ALU.mult,
                   reads=[xf_, rp1B], writes=[v1])
                TT(ctx, "vector", v1[:], v1[:], m2B[:], ALU.subtract,
                   reads=[v1, m2B], writes=[v1])
                TT(ctx, "vector", out_f[:, (m % 4) * TQ:(m % 4 + 1) * TQ], v1[:], pm[:],
                   ALU.add, reads=[v1, pm], writes=[outb[m % 4]])
                if m % 4 == 3:
                    DMA(ctx, "gpsimd",
                        x_out[b, (m - 3) * 128:(m + 1) * 128, t0:t0 + TQ].rearrange("(k p) t -> p k t", p=128),
                        out_f[:].rearrange("p (k t) -> p k t", k=4), reads=list(outb))



def mamba_weight_arrays(inp):
    ipw = np.asarray(inp["L0_attn_in_proj_w"])
    return {
        "wipT": np.ascontiguousarray(ipw.T),
        "normw_cols": np.ascontiguousarray(np.asarray(inp["L0_attn_norm_w"]).reshape(PD, 128).T),
        "dtwT": np.ascontiguousarray(np.asarray(inp["L0_attn_dt_w"]).T),
        "xpwT": np.ascontiguousarray(np.asarray(inp["L0_attn_x_proj_w"])[64:128].T),
        "conv_cols": np.ascontiguousarray(
            np.transpose(np.asarray(inp["L0_attn_conv_w"])[:, 0, :].reshape(16, 128, 4), (1, 0, 2)).reshape(128, 64)),
        "convb_cols": np.ascontiguousarray(np.asarray(inp["L0_attn_conv_b"]).reshape(16, 128).T),
        "dtb_cols": np.ascontiguousarray(np.asarray(inp["L0_attn_dt_b"]).reshape(16, 128).T),
        "A_rs": np.ascontiguousarray(
            np.transpose(np.asarray(inp["L0_attn_A_log"]).reshape(16, 128, 64), (1, 0, 2)).reshape(128, 1024)),
        "D_cols": np.ascontiguousarray(np.asarray(inp["L0_attn_D_param"]).reshape(16, 128).T),
        "wopT": np.ascontiguousarray(np.asarray(inp["L0_attn_out_proj_w"]).T),
    }


def build_mba_program():
    nc = bass.Bass("TRN2", target_bir_lowering=False)
    ctx = Ctx(nc)
    x_in = nc.dram_tensor("x_fm", [BPC, D, S], F32, kind="ExternalInput").ap()
    xi_out = nc.dram_tensor("xi_fm", [BPC, 2048, S], F32, kind="ExternalOutput").ap()
    res_out = nc.dram_tensor("res_fm", [BPC, 2048, S], BF16, kind="ExternalOutput").ap()
    mrow_out = nc.dram_tensor("mrow", [BPC, S], F32, kind="ExternalOutput").ap()
    rrow_out = nc.dram_tensor("rrow", [BPC, S], F32, kind="ExternalOutput").ap()
    ctx.bufmap["mrow"] = Buf("mrow"); ctx.bufmap["rrow"] = Buf("rrow")
    ctx.w = {
        "wipT": nc.dram_tensor("wipT", [D, 4096], F32, kind="ExternalInput").ap(),
        "normw_cols": nc.dram_tensor("normw_cols", [128, PD], F32, kind="ExternalInput").ap(),
    }
    emit_mamba_A(ctx, x_in, xi_out, res_out, mrow_out, rrow_out)
    ctx.finalize()
    return nc


def build_mbb_program():
    nc = bass.Bass("TRN2", target_bir_lowering=False)
    ctx = Ctx(nc)
    xi_in = nc.dram_tensor("xi_fm", [BPC, 2048, S], F32, kind="ExternalInput").ap()
    xc_out = nc.dram_tensor("xc_fm", [BPC, 2048, S], BF16, kind="ExternalOutput").ap()
    dt_out = nc.dram_tensor("dt_fm", [BPC, 2048, S], BF16, kind="ExternalOutput").ap()
    B_out = nc.dram_tensor("B_fm", [BPC, 64, S], BF16, kind="ExternalOutput").ap()
    ctx.w = {
        "dtwT": nc.dram_tensor("dtwT", [2048, 2048], F32, kind="ExternalInput").ap(),
        "xpwT": nc.dram_tensor("xpwT", [2048, 64], F32, kind="ExternalInput").ap(),
        "conv_cols": nc.dram_tensor("conv_cols", [128, 64], F32, kind="ExternalInput").ap(),
        "convb_cols": nc.dram_tensor("convb_cols", [128, 16], F32, kind="ExternalInput").ap(),
        "dtb_cols": nc.dram_tensor("dtb_cols", [128, 16], F32, kind="ExternalInput").ap(),
    }
    emit_mamba_B(ctx, xi_in, xc_out, dt_out, B_out)
    ctx.finalize()
    return nc


def build_mbc_program():
    nc = bass.Bass("TRN2", target_bir_lowering=False)
    ctx = Ctx(nc)
    xc_in = nc.dram_tensor("xc_fm", [BPC, 2048, S], BF16, kind="ExternalInput").ap()
    dt_in = nc.dram_tensor("dt_fm", [BPC, 2048, S], BF16, kind="ExternalInput").ap()
    B_in = nc.dram_tensor("B_fm", [BPC, 64, S], BF16, kind="ExternalInput").ap()
    y_out = nc.dram_tensor("y_fm", [BPC, 2048, S], BF16, kind="ExternalOutput").ap()
    ctx.w = {
        "A_rs": nc.dram_tensor("A_rs", [128, 1024], F32, kind="ExternalInput").ap(),
        "D_cols": nc.dram_tensor("D_cols", [128, 16], F32, kind="ExternalInput").ap(),
    }
    emit_mamba_C(ctx, xc_in, dt_in, B_in, y_out)
    ctx.finalize()
    return nc


def build_mbd_program():
    nc = bass.Bass("TRN2", target_bir_lowering=False)
    ctx = Ctx(nc)
    y_in = nc.dram_tensor("y_fm", [BPC, 2048, S], BF16, kind="ExternalInput").ap()
    res_in = nc.dram_tensor("res_fm", [BPC, 2048, S], BF16, kind="ExternalInput").ap()
    x_in = nc.dram_tensor("x_fm", [BPC, D, S], F32, kind="ExternalInput").ap()
    x_out = nc.dram_tensor("xo_fm", [BPC, D, S], F32, kind="ExternalOutput").ap()
    mrow_in = nc.dram_tensor("mrow", [BPC, S], F32, kind="ExternalInput").ap()
    rrow_in = nc.dram_tensor("rrow", [BPC, S], F32, kind="ExternalInput").ap()
    ctx.bufmap["mrow"] = Buf("mrow"); ctx.bufmap["rrow"] = Buf("rrow")
    ctx.w = {
        "wopT": nc.dram_tensor("wopT", [2048, D], F32, kind="ExternalInput").ap(),
    }
    emit_mamba_D(ctx, y_in, res_in, x_in, x_out, mrow_in, rrow_in)
    ctx.finalize()
    return nc


def run_mamba(x_fm_percore, inp, debug=False):
    warr = mamba_weight_arrays(inp)
    nc = build_mba_program()
    in_maps = [dict(x_fm=x_fm_percore[c], wipT=warr["wipT"], normw_cols=warr["normw_cols"])
               for c in range(NCORES)]
    resA = run_bass_kernel_spmd(nc, in_maps, core_ids=list(range(NCORES))).results
    nc = build_mbb_program()
    in_maps = [dict(xi_fm=resA[c]["xi_fm"], dtwT=warr["dtwT"], xpwT=warr["xpwT"],
                    conv_cols=warr["conv_cols"], convb_cols=warr["convb_cols"],
                    dtb_cols=warr["dtb_cols"]) for c in range(NCORES)]
    resB = run_bass_kernel_spmd(nc, in_maps, core_ids=list(range(NCORES))).results
    nc = build_mbc_program()
    in_maps = [dict(xc_fm=resB[c]["xc_fm"], dt_fm=resB[c]["dt_fm"], B_fm=resB[c]["B_fm"],
                    A_rs=warr["A_rs"], D_cols=warr["D_cols"]) for c in range(NCORES)]
    resC = run_bass_kernel_spmd(nc, in_maps, core_ids=list(range(NCORES))).results
    nc = build_mbd_program()
    in_maps = [dict(y_fm=resC[c]["y_fm"], res_fm=resA[c]["res_fm"], x_fm=x_fm_percore[c],
                    mrow=resA[c]["mrow"], rrow=resA[c]["rrow"],
                    wopT=warr["wopT"]) for c in range(NCORES)]
    resD = run_bass_kernel_spmd(nc, in_maps, core_ids=list(range(NCORES))).results
    out = [resD[c]["xo_fm"] for c in range(NCORES)]
    if debug:
        return out, dict(A=resA, B=resB, C=resC)
    return out



def emit_mla(ctx, x_in, xt1_out, dbg=None):
    TQ = 512
    NT = S // TQ
    w = ctx.w

    ones_col_b = ctx.sb("mla_ones_col", [128, 1], BF16)
    ones_row_f = ctx.sb("mla_ones_rowf", [1, 128], F32)
    MEMSET(ctx, "vector", ones_col_b[:], 1.0, writes=[ones_col_b])
    MEMSET(ctx, "vector", ones_row_f[:], 1.0, writes=[ones_row_f])

    lnw_cols = ctx.sb("mla_lnw", [128, PD], F32)
    DMA(ctx, "sync", lnw_cols[:], w["lnw1_cols"], writes=[lnw_cols])
    stage0 = ctx.sb("mla_stg0", [128, 512], F32)
    stage = [stage0, stage0]
    wkvd_b = [load_weight_staged(ctx, f"mla_wkvd{k}", w["wkvd"][k * 128:(k + 1) * 128, :],
                                 128, 256, stage[k % 2], scale_col=lnw_cols[:, k:k + 1],
                                 scale_col_reads=[lnw_cols]) for k in range(PD)]
    wqd_b = [load_weight_staged(ctx, f"mla_wqd{k}", w["wqd"][k * 128:(k + 1) * 128, :],
                                128, 256, stage[k % 2], scale_col=lnw_cols[:, k:k + 1],
                                scale_col_reads=[lnw_cols]) for k in range(PD)]
    w16 = ctx.sb("mla_w16", [128, 16384], BF16)
    negln = ctx.sb("mla_negln", [128, PD], F32)
    TS(ctx, "vector", negln[:], lnw_cols[:], -1.0, ALU.mult, reads=[lnw_cols], writes=[negln])
    wropek_b = [w16[:, k * 512:(k + 1) * 512] for k in range(PD)]
    wropekr_b = [w16[:, 4096 + k * 512:4096 + (k + 1) * 512] for k in range(PD)]
    wo_b = [w16[:, k * 1024:(k + 1) * 1024] for k in range(PD)]
    w16buf = ctx.bufmap["mla_w16"]

    def load_ropek(ctx=ctx):
        for k in range(PD):
            convert_into(ctx, wropek_b[k], w16buf, w["wropek"][k * 128:(k + 1) * 128, :],
                         128, 512, stage[0], scale_col=lnw_cols[:, k:k + 1],
                         scale_col_reads=[lnw_cols])
            DMA(ctx, "sync", stage[0][:, 0:512], w["wropek_perm"][k * 128:(k + 1) * 128, :],
                writes=[stage[0]])
            v_in = stage[0][:].rearrange("p (h c) -> p h c", c=32)
            v_out2 = wropekr_b[k].rearrange("p (h c) -> p h c", c=32)
            TS(ctx, "vector", v_out2[:, :, 0:16], v_in[:, 0:16, 0:16] if False else v_in[:, :, 0:16],
               negln[:, k:k + 1], ALU.mult, reads=[stage[0], negln], writes=[w16buf])
            TS(ctx, "vector", v_out2[:, :, 16:32], v_in[:, :, 16:32], lnw_cols[:, k:k + 1],
               ALU.mult, reads=[stage[0], lnw_cols], writes=[w16buf])

    def load_wo(ctx=ctx):
        for k in range(PD):
            for cc in range(2):
                convert_into(ctx, wo_b[k][:, cc * 512:(cc + 1) * 512], w16buf,
                             w["wo"][k * 128:(k + 1) * 128, cc * 512:(cc + 1) * 512],
                             128, 512, stage[0])
    wku_b = [load_weight_staged(ctx, f"mla_wku{k}", w["wku"][k * 128:(k + 1) * 128, :],
                                128, 512, stage[k % 2]) for k in range(2)]
    wqu_b = [load_weight_staged(ctx, f"mla_wqu{k}", w["wqu"][k * 128:(k + 1) * 128, :],
                                128, 512, stage[k % 2], scale=0.125) for k in range(2)]
    wvu_b = [load_weight_staged(ctx, f"mla_wvu{k}", w["wvu"][k * 128:(k + 1) * 128, :],
                                128, 1024, stage[k % 2]) for k in range(2)]
    wropeq_b = [load_weight_staged(ctx, f"mla_wrq{k}", w["wropeq"][k * 128:(k + 1) * 128, :],
                                   128, 512, stage[k % 2], scale=0.125) for k in range(2)]
    wropeqr_b = []
    for k in range(2):
        t = ctx.sb(f"mla_wrqr{k}", [128, 512], BF16)
        DMA(ctx, "sync", stage[k % 2][:, 0:512], w["wropeq_perm"][k * 128:(k + 1) * 128, :],
            writes=[stage[k % 2]])
        v_in = stage[k % 2][:].rearrange("p (h c) -> p h c", c=32)
        v_out = t[:].rearrange("p (h c) -> p h c", c=32)
        TS(ctx, "vector", v_out[:, :, 0:16], v_in[:, :, 0:16], -0.125, ALU.mult,
           reads=[stage[k % 2]], writes=[t])
        TS(ctx, "vector", v_out[:, :, 16:32], v_in[:, :, 16:32], 0.125, ALU.mult,
           reads=[stage[k % 2]], writes=[t])
        wropeqr_b.append(t)


    cosT = ctx.sb("mla_cosT", [32, S], BF16)
    sinT = ctx.sb("mla_sinT", [32, S], BF16)
    for (dst, src) in ((cosT, "cosT"), (sinT, "sinT")):
        for cc in range(4):
            DMA(ctx, "sync", stage0[0:32, 0:512], w[src][:, cc * 512:(cc + 1) * 512],
                writes=[stage0])
            COPY(ctx, "vector", dst[:, cc * 512:(cc + 1) * 512], stage0[0:32, 0:512],
                 reads=[stage0], writes=[dst])
    idf_mla, idb_mla = make_ident(ctx, "mla_id")
    ctx.identb = idb_mla
    ident64 = ctx.sb("mla_ident64", [128, 64], BF16)
    COPY(ctx, "vector", ident64[0:64, :], idb_mla[0:64, 0:64], reads=[idb_mla], writes=[ident64])
    COPY(ctx, "vector", ident64[64:128, :], idb_mla[0:64, 0:64], reads=[idb_mla], writes=[ident64])
    ctx.ident64 = ident64
    masks = ctx.sb("mla_masks", [128, 4 * TQ], BF16)
    mask_stage = ctx.sb("mla_mask_stage", [128, TQ], F32)
    for i in range(4):
        DMA(ctx, "sync", mask_stage[:], w["maskneg"][i], writes=[mask_stage])
        COPY(ctx, "vector", masks[:, i * TQ:(i + 1) * TQ], mask_stage[:],
             reads=[mask_stage], writes=[masks])

    ps_mean = ctx.ps("mla_ps_mean", [1, TQ], F32)
    ps_sq = ctx.ps("mla_ps_sq", [1, TQ], F32)
    wsum_kvd = emit_wsum_neg(ctx, "mla_ws_kvd", wkvd_b, 256, ones_col_b, ps_mean)
    wsum_qd = emit_wsum_neg(ctx, "mla_ws_qd", wqd_b, 256, ones_col_b, ps_mean)
    wsum_rk = ctx.sb("mla_ws_rk", [1, 512], BF16)
    wsum_rkr = ctx.sb("mla_ws_rkr", [1, 512], BF16)

    def calc_rope_wsums(ctx=ctx):
        for (wchunks, dst) in ((wropek_b, wsum_rk), (wropekr_b, wsum_rkr)):
            for k in range(PD):
                MM(ctx, ps_mean[0:1, 0:512], ones_col_b[:], wchunks[k],
                   start=(k == 0), stop=(k == PD - 1), reads=[w16buf, ones_col_b],
                   writes=[ps_mean])
            TS(ctx, "vector", dst[:], ps_mean[0:1, 0:512], -1.0, ALU.mult,
               reads=[ps_mean], writes=[dst])
    ps_util = ctx.ps("mla_ps_util", [128, TQ], F32)
    ps_main = ctx.ps("mla_ps_main", [128, TQ], F32)
    ps_rot = ctx.ps("mla_ps_rot", [64, TQ], F32)
    ps_tr = ctx.ps("mla_ps_tr", [128, 128], BF16)
    ps_St = ctx.ps("mla_ps_St", [128, TQ], F32)
    ps_o = ctx.ps("mla_ps_o", [64, TQ], F32)

    mean_f = ctx.sb("mla_mean_f", [1, TQ], F32)
    sq_f = ctx.sb("mla_sq_f", [1, TQ], F32)
    var_f = ctx.sb("mla_var_f", [1, TQ], F32)
    rstd2 = ctx.sb("mla_rstd2", [1, TQ], F32)
    meansave_b = ctx.sb("mla_meansave_b", [1, S], BF16)
    rstdsave = ctx.sb("mla_rstdsave", [1, S], BF16)
    s1row = ctx.sb("mla_s1row", [1, 512], F32)
    sB1 = ctx.sb("mla_sB1", [128, TQ], F32)
    csK = ctx.sb("mla_csK", [32, 512], BF16)
    ssK = ctx.sb("mla_ssK", [32, 512], BF16)
    zrecB = ctx.sb("mla_zrecB", [64, TQ], F32)

    xf = [ctx.sb("mla_xf0", [128, 4 * TQ], F32)]
    xfb = [[Buf(f"mla_xfb0_{k}") for k in range(4)]]
    xb = ctx.sb("mla_xb", [128, PD * TQ], BF16)
    x2s = ctx.sb("mla_x2s", [128, TQ], BF16)
    kvlat = ctx.sb("mla_kvlat", [128, 2 * TQ], BF16)
    kvb = [Buf(f"mla_kvb{i}") for i in range(2)]
    qlat = ctx.sb("mla_qlat", [128, 2 * TQ], BF16)
    qlb = [Buf(f"mla_qlb{i}") for i in range(2)]
    qlat_hbm = ctx.nc.dram_tensor("mla_qlat_hbm", [BPC, 256, S], BF16, kind="Internal").ap()
    qlhb = Buf("mla_qlat_hbm")
    ctx.bufmap["mla_qlat_hbm"] = qlhb
    kk = ctx.sb("mla_kk", [128, 8 * S], BF16)
    kkb = [Buf(f"mla_kkb{g}") for g in range(8)]
    vT = ctx.sb("mla_vT", [128, 256 * 64], BF16)
    vTb = Buf("mla_vTb")
    vtmp = ctx.sb("mla_vtmp", [128, TQ], BF16)
    ta = ctx.sb("mla_ta", [32, TQ], F32)
    tb = ctx.sb("mla_tb", [32, TQ], F32)
    qq = ctx.sb("mla_qq", [128, TQ], BF16)
    qqb = [Buf("mla_qqb")]
    probs = [ctx.sb(f"mla_probs{i}", [128, TQ], BF16) for i in range(2)]
    stmask = ctx.sb("mla_stmask", [128, TQ], F32)
    o_qc = ctx.sb("mla_oqc", [128, 8 * TQ], BF16)
    oqb = [Buf(f"mla_oqb{g}") for g in range(8)]
    v1t = ctx.sb("mla_v1t", [128, TQ], F32)
    out_f = ctx.sb("mla_outf", [128, TQ], F32)
    outb = [Buf("mla_outb0")]

    for b in range(BPC):
        load_ropek()
        calc_rope_wsums()
        for tci in range(NT):
            t0 = tci * TQ
            xf_, xfb_ = xf[0], xfb[0]
            for hh in range(2):
                DMA(ctx, "sync", xf_[:].rearrange("p (k t) -> p k t", k=4),
                    x_in[b, hh * 512:(hh + 1) * 512, t0:t0 + TQ].rearrange("(k p) t -> p k t", p=128),
                    writes=list(xfb_))
                for k in range(4):
                    kk4 = hh * 4 + k
                    COPY(ctx, "vector", xb[:, kk4 * TQ:(kk4 + 1) * TQ],
                         xf_[:, k * TQ:(k + 1) * TQ], reads=[xfb_[k]], writes=[xb])
                    ACT(ctx, x2s[:], xf_[:, k * TQ:(k + 1) * TQ], AF.Square,
                        reads=[xfb_[k]], writes=[x2s])
                    MM(ctx, ps_sq[0:1, :], ones_col_b[:], x2s[:],
                       start=(kk4 == 0), stop=(kk4 == PD - 1), reads=[x2s, ones_col_b],
                       writes=[ps_sq])
            for k in range(PD):
                MM(ctx, ps_mean[0:1, :], ones_col_b[:], xb[:, k * TQ:(k + 1) * TQ],
                   start=(k == 0), stop=(k == PD - 1), reads=[xb, ones_col_b], writes=[ps_mean])
            ACT(ctx, mean_f[:], ps_mean[0:1, :], AF.Copy, scale=1.0 / D,
                reads=[ps_mean], writes=[mean_f])
            COPY(ctx, "vector", meansave_b[:, t0:t0 + TQ], mean_f[:], reads=[mean_f], writes=[meansave_b])
            ACT(ctx, sq_f[:], mean_f[:], AF.Square, reads=[mean_f], writes=[sq_f])
            STT(ctx, "vector", var_f[:], ps_sq[0:1, :], 1.0 / D, sq_f[:],
                ALU.mult, ALU.subtract, reads=[ps_sq, sq_f], writes=[var_f])
            ACT(ctx, sq_f[:], var_f[:], AF.Sqrt, bias=1e-5, reads=[var_f], writes=[sq_f])
            RECIP(ctx, mean_f[:], sq_f[:], reads=[sq_f], writes=[mean_f])
            COPY(ctx, "vector", rstdsave[:, t0:t0 + TQ], mean_f[:], reads=[mean_f], writes=[rstdsave])
            ACT(ctx, rstd2[:], mean_f[:], AF.Square, reads=[mean_f], writes=[rstd2])
            TT(ctx, "vector", rstd2[:], var_f[:], rstd2[:], ALU.mult, reads=[var_f, rstd2], writes=[rstd2])
            ACT(ctx, rstd2[:], rstd2[:], AF.Sqrt, reads=[rstd2], writes=[rstd2])
            TS(ctx, "vector", rstd2[:], rstd2[:], 1e-5, ALU.add, reads=[rstd2], writes=[rstd2])
            RECIP(ctx, rstd2[:], rstd2[:], reads=[rstd2], writes=[rstd2])
            TT(ctx, "vector", s1row[:], mean_f[:], rstd2[:],
               ALU.mult, reads=[mean_f, rstd2], writes=[s1row])
            MM(ctx, ps_util[:], ones_row_f[:], s1row[:], start=True, stop=True,
               reads=[s1row, ones_row_f], writes=[ps_util])
            COPY(ctx, "vector", sB1[:], ps_util[:], reads=[ps_util], writes=[sB1])
            TT(ctx, "vector", csK[:], cosT[:, t0:t0 + TQ], sB1[0:32, :],
               ALU.mult, reads=[cosT, sB1], writes=[csK])
            TT(ctx, "vector", ssK[:], sinT[:, t0:t0 + TQ], sB1[0:32, :],
               ALU.mult, reads=[sinT, sB1], writes=[ssK])
            for m in range(2):
                for k in range(PD):
                    MM(ctx, ps_main[:], wkvd_b[k][:, m * 128:(m + 1) * 128],
                       xb[:, k * TQ:(k + 1) * TQ], start=(k == 0), stop=False,
                       reads=[wkvd_b[k], xb], writes=[ps_main])
                MM(ctx, ps_main[:], wsum_kvd[:, m * 128:(m + 1) * 128],
                   meansave_b[:, t0:t0 + TQ], start=False, stop=True,
                   reads=[wsum_kvd, meansave_b], writes=[ps_main])
                TT(ctx, "vector", kvlat[:, m * TQ:(m + 1) * TQ], ps_main[:], sB1[:],
                   ALU.mult, reads=[ps_main, sB1], writes=[kvb[m]])
                for k in range(PD):
                    MM(ctx, ps_main[:], wqd_b[k][:, m * 128:(m + 1) * 128],
                       xb[:, k * TQ:(k + 1) * TQ], start=(k == 0), stop=False,
                       reads=[wqd_b[k], xb], writes=[ps_main])
                MM(ctx, ps_main[:], wsum_qd[:, m * 128:(m + 1) * 128],
                   meansave_b[:, t0:t0 + TQ], start=False, stop=True,
                   reads=[wsum_qd, meansave_b], writes=[ps_main])
                TT(ctx, "vector", qlat[:, m * TQ:(m + 1) * TQ], ps_main[:], sB1[:],
                   ALU.mult, reads=[ps_main, sB1], writes=[qlb[m]])
                DMA(ctx, "gpsimd", qlat_hbm[b, m * 128:(m + 1) * 128, t0:t0 + TQ],
                    qlat[:, m * TQ:(m + 1) * TQ], reads=[qlb[m]])
            for g in range(8):
                h0, h1 = 2 * g, 2 * g + 1
                for (base, hh) in ((0, h0), (64, h1)):
                    for kc in range(2):
                        MM(ctx, ps_main[base:base + 32, :],
                           wku_b[kc][:, hh * 32:(hh + 1) * 32],
                           kvlat[:, kc * TQ:(kc + 1) * TQ], start=(kc == 0),
                           stop=False, reads=[wku_b[kc], kvb[kc]], writes=[ps_main],
                           tp=(0, base))
                for (base, hh) in ((32, h0), (96, h1)):
                    for k in range(PD):
                        MM(ctx, ps_main[base:base + 32, :],
                           wropek_b[k][:, hh * 32:(hh + 1) * 32],
                           xb[:, k * TQ:(k + 1) * TQ], start=(k == 0), stop=False,
                           reads=[w16buf, xb], writes=[ps_main], tp=(0, base))
                    MM(ctx, ps_main[base:base + 32, :],
                       wsum_rk[:, hh * 32:(hh + 1) * 32], meansave_b[:, t0:t0 + TQ],
                       start=False, stop=(base == 96), reads=[wsum_rk, meansave_b],
                       writes=[ps_main], tp=(0, base))
                for (base, hh) in ((0, h0), (32, h1)):
                    for k in range(PD):
                        MM(ctx, ps_rot[base:base + 32, :],
                           wropekr_b[k][:, hh * 32:(hh + 1) * 32],
                           xb[:, k * TQ:(k + 1) * TQ], start=(k == 0),
                           stop=False, reads=[w16buf, xb], writes=[ps_rot], tp=(0, base))
                    MM(ctx, ps_rot[base:base + 32, :],
                       wsum_rkr[:, hh * 32:(hh + 1) * 32], meansave_b[:, t0:t0 + TQ],
                       start=False, stop=(base == 32), reads=[wsum_rkr, meansave_b],
                       writes=[ps_rot], tp=(0, base))
                col = g * S + t0
                COPY(ctx, "vector", kk[0:32, col:col + TQ], ps_main[0:32, :],
                     reads=[ps_main], writes=[kkb[g]])
                COPY(ctx, "vector", kk[64:96, col:col + TQ], ps_main[64:96, :],
                     reads=[ps_main], writes=[kkb[g]])
                for (pbase, rbase, kbase) in ((32, 0, 32), (96, 32, 96)):
                    TT(ctx, "vector", ta[:], ps_main[pbase:pbase + 32, :], csK[:],
                       ALU.mult, reads=[ps_main, csK], writes=[ta])
                    TT(ctx, "vector", tb[:], ps_rot[rbase:rbase + 32, :], ssK[:],
                       ALU.mult, reads=[ps_rot, ssK], writes=[tb])
                    TT(ctx, "vector", kk[kbase:kbase + 32, col:col + TQ], ta[:], tb[:],
                       ALU.add, reads=[ta, tb], writes=[kkb[g]])
                for kc in range(2):
                    MM(ctx, ps_main[:], wvu_b[kc][:, g * 128:(g + 1) * 128],
                       kvlat[:, kc * TQ:(kc + 1) * TQ], start=(kc == 0), stop=(kc == 1),
                       reads=[wvu_b[kc], kvb[kc]], writes=[ps_main])
                COPY(ctx, "vector", vtmp[:], ps_main[:], reads=[ps_main], writes=[vtmp])
                for j in range(4):
                    kb = tci * 4 + j
                    for (vb, hh) in ((0, h0), (64, h1)):
                        TRANS(ctx, ps_tr[:, 0:64], vtmp[vb:vb + 64, j * 128:(j + 1) * 128],
                              ctx.ident64[vb:vb + 64, :], reads=[vtmp, ctx.ident64], writes=[ps_tr])
                        COPY(ctx, "vector", vT[:, (kb * 16 + hh) * 64:(kb * 16 + hh) * 64 + 64],
                             ps_tr[:, 0:64], reads=[ps_tr], writes=[vTb])
        if dbg is not None:
            for g in range(8):
                DMA(ctx, "gpsimd", dbg["kk"][b, g * 128:(g + 1) * 128, :],
                    kk[:, g * S:(g + 1) * S], reads=[kkb[g]])
            DMA(ctx, "gpsimd", dbg["vT"][b], vT[:], reads=[vTb])
        load_wo()
        for qci in range(NT):
            q0 = qci * TQ
            for g in range(8):
                h0, h1 = 2 * g, 2 * g + 1
                for (base, hh) in ((0, h0), (64, h1)):
                    for kc in range(2):
                        MM(ctx, ps_main[base:base + 32, :],
                           wqu_b[kc][:, hh * 32:(hh + 1) * 32],
                           qlat[:, kc * TQ:(kc + 1) * TQ],
                           start=(kc == 0), stop=False,
                           reads=[wqu_b[kc], qlb[kc]], writes=[ps_main], tp=(0, base))
                for (base, hh) in ((32, h0), (96, h1)):
                    for kc in range(2):
                        MM(ctx, ps_main[base:base + 32, :],
                           wropeq_b[kc][:, hh * 32:(hh + 1) * 32],
                           qlat[:, kc * TQ:(kc + 1) * TQ],
                           start=(kc == 0), stop=(base == 96 and kc == 1),
                           reads=[wropeq_b[kc], qlb[kc]], writes=[ps_main], tp=(0, base))
                for (base, hh) in ((0, h0), (32, h1)):
                    for kc in range(2):
                        MM(ctx, ps_rot[base:base + 32, :],
                           wropeqr_b[kc][:, hh * 32:(hh + 1) * 32],
                           qlat[:, kc * TQ:(kc + 1) * TQ],
                           start=(kc == 0), stop=(base == 32 and kc == 1),
                           reads=[wropeqr_b[kc], qlb[kc]], writes=[ps_rot], tp=(0, base))
                COPY(ctx, "vector", qq[0:32, :], ps_main[0:32, :],
                     reads=[ps_main], writes=[qqb[0]])
                COPY(ctx, "vector", qq[64:96, :], ps_main[64:96, :],
                     reads=[ps_main], writes=[qqb[0]])
                for (pbase, rbase, kbase) in ((32, 0, 32), (96, 32, 96)):
                    TT(ctx, "vector", ta[:], ps_main[pbase:pbase + 32, :], cosT[:, q0:q0 + TQ],
                       ALU.mult, reads=[ps_main, cosT], writes=[ta])
                    TT(ctx, "vector", tb[:], ps_rot[rbase:rbase + 32, :], sinT[:, q0:q0 + TQ],
                       ALU.mult, reads=[ps_rot, sinT], writes=[tb])
                    TT(ctx, "vector", qq[kbase:kbase + 32, :], ta[:], tb[:],
                       ALU.add, reads=[ta, tb], writes=[qqb[0]])
                for h in (2 * g, 2 * g + 1):
                    vb = (h % 2) * 64
                    nkb = qci * 4 + 4
                    for kb in range(nkb):
                        MM(ctx, ps_St[:], kk[vb:vb + 64, g * S + kb * 128:g * S + (kb + 1) * 128],
                           qq[vb:vb + 64, :], start=True, stop=True,
                           reads=[kkb[g], qqb[0]], writes=[ps_St])
                        pslot = probs[kb % 2]
                        if kb >= qci * 4:
                            ii = kb - qci * 4
                            TT(ctx, "vector", stmask[:], ps_St[:], masks[:, ii * TQ:(ii + 1) * TQ],
                               ALU.add, reads=[ps_St, masks], writes=[stmask])
                            ACT(ctx, pslot[:], stmask[:], AF.Exp, reads=[stmask], writes=[pslot])
                        else:
                            ACT(ctx, pslot[:], ps_St[:], AF.Exp, reads=[ps_St], writes=[pslot])
                        MM(ctx, ps_util[0:1, :], ones_col_b[:], pslot[:],
                           start=(kb == 0), stop=(kb == nkb - 1), reads=[pslot, ones_col_b],
                           writes=[ps_util])
                        MM(ctx, ps_o[:], vT[:, (kb * 16 + h) * 64:(kb * 16 + h) * 64 + 64],
                           pslot[:], start=(kb == 0), stop=(kb == nkb - 1),
                           reads=[vTb, pslot], writes=[ps_o])
                    COPY(ctx, "vector", mean_f[:], ps_util[0:1, :], reads=[ps_util], writes=[mean_f])
                    RECIP(ctx, var_f[:], mean_f[:], reads=[mean_f], writes=[var_f])
                    MM(ctx, ps_util[0:64, :], ones_row_f[0:1, 0:64], var_f[:], start=True, stop=True,
                       reads=[var_f, ones_row_f], writes=[ps_util])
                    COPY(ctx, "vector", zrecB[:], ps_util[0:64, :], reads=[ps_util], writes=[zrecB])
                    TT(ctx, "vector", o_qc[vb:vb + 64, g * TQ:(g + 1) * TQ], ps_o[:], zrecB[:],
                       ALU.mult, reads=[ps_o, zrecB], writes=[oqb[g]])
            xf_, xfb_ = xf[0], xfb[0]
            COPY(ctx, "vector", var_f[:], rstdsave[:, q0:q0 + TQ], reads=[rstdsave], writes=[var_f])
            MM(ctx, ps_util[:], ones_row_f[:], var_f[:], start=True, stop=True,
               reads=[var_f, ones_row_f], writes=[ps_util])
            COPY(ctx, "vector", sB1[:], ps_util[:], reads=[ps_util], writes=[sB1])
            TT(ctx, "vector", sq_f[:], meansave_b[:, q0:q0 + TQ], rstdsave[:, q0:q0 + TQ],
               ALU.mult, reads=[meansave_b, rstdsave], writes=[sq_f])
            MM(ctx, ps_util[:], ones_row_f[:], sq_f[:], start=True, stop=True,
               reads=[sq_f, ones_row_f], writes=[ps_util])
            COPY(ctx, "vector", stmask[:], ps_util[:], reads=[ps_util], writes=[stmask])
            for m in range(PD):
                if m % 4 == 0:
                    DMA(ctx, "sync", xf_[:].rearrange("p (k t) -> p k t", k=4),
                        x_in[b, m * 128:(m + 4) * 128, q0:q0 + TQ].rearrange("(k p) t -> p k t", p=128),
                        writes=list(xfb_))
                for k in range(PD):
                    MM(ctx, ps_main[:], wo_b[k][:, m * 128:(m + 1) * 128],
                       o_qc[:, k * TQ:(k + 1) * TQ], start=(k == 0), stop=(k == PD - 1),
                       reads=[w16buf, oqb[k]], writes=[ps_main])
                TT(ctx, "vector", v1t[:], xf_[:, (m % 4) * TQ:(m % 4 + 1) * TQ], sB1[:], ALU.mult,
                   reads=[xfb_[m % 4], sB1], writes=[v1t])
                TT(ctx, "vector", v1t[:], v1t[:], stmask[:], ALU.subtract,
                   reads=[v1t, stmask], writes=[v1t])
                TT(ctx, "vector", out_f[:], v1t[:], ps_main[:],
                   ALU.add, reads=[v1t, ps_main], writes=[outb[0]])
                DMA(ctx, "gpsimd",
                    xt1_out[b, m * 128:(m + 1) * 128, q0:q0 + TQ],
                    out_f[:], reads=[outb[0]])


def mla_weight_arrays(inp):
    HH = 32
    inv_freq = 1.0 / (10000.0 ** (np.arange(0, HH, 2, dtype=np.float64) / HH))
    t = np.arange(S, dtype=np.float64)
    freqs = np.outer(t, inv_freq)
    emb = np.concatenate([freqs, freqs], axis=1)
    cosT = np.ascontiguousarray(np.cos(emb).T.astype(np.float32))
    sinT = np.ascontiguousarray(np.sin(emb).T.astype(np.float32))
    kkk = np.arange(128)[:, None]
    qqq = np.arange(512)[None, :]
    maskneg = np.stack([np.where(128 * i + kkk <= qqq, 0.0, -1e9).astype(np.float32)
                        for i in range(4)])
    perm = np.concatenate([np.arange(16, 32), np.arange(0, 16)])
    full_perm = np.concatenate([h * 32 + perm for h in range(16)])
    return {
        "lnw1_cols": np.ascontiguousarray(np.asarray(inp["L1_ffn_w_ln1"]).reshape(PD, 128).T),
        "wkvd": np.ascontiguousarray(inp["L1_ffn_W_kvd"]),
        "wqd": np.ascontiguousarray(inp["L1_ffn_W_qd"]),
        "wku": np.ascontiguousarray(inp["L1_ffn_W_ku"]),
        "wqu": np.ascontiguousarray(inp["L1_ffn_W_qu"]),
        "wvu": np.ascontiguousarray(inp["L1_ffn_W_vu"]),
        "wropeq": np.ascontiguousarray(inp["L1_ffn_W_ropeq"]),
        "wropeq_perm": np.ascontiguousarray(np.asarray(inp["L1_ffn_W_ropeq"])[:, full_perm[:512]]),
        "wropek": np.ascontiguousarray(inp["L1_ffn_W_ropek"]),
        "wropek_perm": np.ascontiguousarray(np.asarray(inp["L1_ffn_W_ropek"])[:, full_perm[:512]]),
        "wo": np.ascontiguousarray(inp["L1_ffn_W_o"]),
        "cosT": cosT, "sinT": sinT, "maskneg": maskneg,
    }


def build_mla_program(dbg=False):
    nc = bass.Bass("TRN2", target_bir_lowering=False)
    ctx = Ctx(nc)
    x_in = nc.dram_tensor("x_fm", [BPC, D, S], F32, kind="ExternalInput").ap()
    xt1_out = nc.dram_tensor("xt1_fm", [BPC, D, S], F32, kind="ExternalOutput").ap()
    names = {"wkvd": [D, 256], "wqd": [D, 256], "wku": [256, 512], "wqu": [256, 512],
             "wvu": [256, 1024], "wropeq": [256, 512], "wropeq_perm": [256, 512],
             "wropek": [D, 512], "wropek_perm": [D, 512], "wo": [D, D],
             "lnw1_cols": [128, PD], "cosT": [32, S], "sinT": [32, S]}
    ctx.w = {k: nc.dram_tensor(k, v, F32, kind="ExternalInput").ap() for k, v in names.items()}
    ctx.w["maskneg"] = nc.dram_tensor("maskneg", [4, 128, 512], F32, kind="ExternalInput").ap()
    dbgd = None
    if dbg:
        dbgd = {"kk": nc.dram_tensor("kk_dbg", [BPC, 1024, S], BF16, kind="ExternalOutput").ap(),
                "vT": nc.dram_tensor("vT_dbg", [BPC, 128, 256 * 64], BF16, kind="ExternalOutput").ap()}
        for k in dbgd:
            ctx.bufmap[dbgd[k].tensor.name] = Buf(k)
    emit_mla(ctx, x_in, xt1_out, dbg=dbgd)
    ctx.finalize()
    return nc


def run_mla(x_fm_percore, inp, dbg=False):
    warr = mla_weight_arrays(inp)
    nc = build_mla_program(dbg=dbg)
    in_maps = [dict(x_fm=x_fm_percore[c], **warr) for c in range(NCORES)]
    res = run_bass_kernel_spmd(nc, in_maps, core_ids=list(range(NCORES)))
    if dbg:
        return [r["xt1_fm"] for r in res.results], res.results
    return [r["xt1_fm"] for r in res.results]



def emit_moe(ctx, xt1_in, x_in, x_out):
    TQ = 512
    NT = S // TQ
    w = ctx.w
    nc = ctx.nc

    ones_col_b = ctx.sb("moe_ones_col", [128, 1], BF16)
    ones_row_b = ctx.sb("moe_ones_rowb", [1, 128], BF16)
    ones_row_f = ctx.sb("moe_ones_rowf", [1, 128], F32)
    MEMSET(ctx, "vector", ones_col_b[:], 1.0, writes=[ones_col_b])
    MEMSET(ctx, "vector", ones_row_b[:], 1.0, writes=[ones_row_b])
    MEMSET(ctx, "vector", ones_row_f[:], 1.0, writes=[ones_row_f])
    idf, idb = make_ident(ctx, "moe_id")

    lnw_cols = ctx.sb("moe_lnw", [128, PD], F32)
    DMA(ctx, "sync", lnw_cols[:], w["lnw2_cols"], writes=[lnw_cols])
    stage0 = ctx.sb("moe_stg0", [128, 1024], F32)
    stg_buf = ctx.bufmap["moe_stg0"]
    cvt = ctx.sb("moe_cvt", [128, 1024], BF16)
    cvt_buf = ctx.bufmap["moe_cvt"]

    wgu_bf = nc.dram_tensor("moe_wgu_bf", [8, 2, PD, 128, 1024], BF16, kind="Internal").ap()
    wgu_buf = Buf("moe_wgu_bf"); ctx.bufmap["moe_wgu_bf"] = wgu_buf
    wd_bf = nc.dram_tensor("moe_wd_bf", [PD, 64, 128, 128], BF16, kind="Internal").ap()
    wd_buf = Buf("moe_wd_bf"); ctx.bufmap["moe_wd_bf"] = wd_buf
    hmid_hbm = nc.dram_tensor("moe_hmid", [BPC, NT, 64, 128, TQ], BF16, kind="Internal").ap()
    hmid_buf = Buf("moe_hmid"); ctx.bufmap["moe_hmid"] = hmid_buf

    for e in range(8):
        for gi in range(2):
            src = w["wg_r"] if gi == 0 else w["wu_r"]
            for kc in range(PD):
                if e < 7:
                    sap = src[e, kc * 128:(kc + 1) * 128, :]
                else:
                    sap = (w["wg_s"] if gi == 0 else w["wu_s"])[kc * 128:(kc + 1) * 128, :]
                DMA(ctx, "sync", stage0[:], sap, writes=[stg_buf])
                TS(ctx, "vector", cvt[:], stage0[:], lnw_cols[:, kc:kc + 1], ALU.mult,
                   reads=[stg_buf, lnw_cols], writes=[cvt_buf])
                DMA(ctx, "gpsimd", wgu_bf[e, gi, kc], cvt[:], reads=[cvt_buf], writes=[wgu_buf])
    for e in range(8):
        for kc in range(PD):
            if e < 7:
                sap = w["wd_r"][e, kc * 128:(kc + 1) * 128, :]
            else:
                sap = w["wd_s"][kc * 128:(kc + 1) * 128, :]
            DMA(ctx, "sync", stage0[:], sap, writes=[stg_buf])
            COPY(ctx, "vector", cvt[:], stage0[:], reads=[stg_buf], writes=[cvt_buf])
            DMA(ctx, "gpsimd", wd_bf[:, e * PD + kc].rearrange("m p c -> p m c"),
                cvt[:].rearrange("p (m c) -> p m c", m=PD), reads=[cvt_buf], writes=[wd_buf])

    wr_b = [load_weight_staged(ctx, f"moe_wr{k}", w["wrouter"][k * 128:(k + 1) * 128, :],
                               128, 7, stage0, scale_col=lnw_cols[:, k:k + 1],
                               scale_col_reads=[lnw_cols]) for k in range(PD)]
    biasB = ctx.sb("moe_biasB", [128, 7], F32)
    DMA(ctx, "sync", biasB[:], bass.AP(w["rbias"].tensor, 0, [[0, 128], [1, 7]]), writes=[biasB])

    ps_sq = ctx.ps("moe_ps_sq", [1, TQ], F32)
    ps_util = ctx.ps("moe_ps_util", [128, TQ], F32)
    ps_g = ctx.ps("moe_ps_g", [128, TQ], F32)
    ps_u = ctx.ps("moe_ps_u", [128, TQ], F32)
    ps_w = ctx.ps("moe_ps_w", [128, TQ], F32)
    ps_tr = ctx.ps("moe_ps_tr", [128, 128], F32)
    ps_comb = [ctx.ps(f"moe_ps_comb{i}", [128, TQ], F32) for i in range(2)]

    xf = ctx.sb("moe_xf", [128, 4 * TQ], F32)
    xfb = [Buf(f"moe_xfb{k}") for k in range(4)]
    x2s = ctx.sb("moe_x2s", [128, TQ], BF16)
    sqrow = ctx.sb("moe_sqrow", [1, TQ], F32)
    s2row = ctx.sb("moe_s2row", [1, TQ], F32)
    s2B = ctx.sb("moe_s2B", [128, TQ], F32)
    xn_b = ctx.sb("moe_xn", [128, PD * S], BF16)
    xnb = [Buf(f"moe_xnb{i}") for i in range(NT)]
    logit = ctx.sb("moe_logit", [128, 4 * 8], F32)
    p8 = ctx.sb("moe_p8", [128, 4 * 8], F32)
    t8 = ctx.sb("moe_t8", [128, 4 * 8], F32)
    denr = ctx.sb("moe_denr", [128, 4], F32)
    gmask = ctx.sb("moe_gmask", [128, 4 * 8], F32)
    wts = ctx.sb("moe_wts", [128, 4 * 8], F32)
    rstd_cols = ctx.sb("moe_rstdcols", [128, 4], F32)
    wts_all = ctx.sb("moe_wts_all", [128, NT * 32], F32)
    wts_all_buf = ctx.bufmap["moe_wts_all"]
    wrow_t = ctx.sb("moe_wrow_t", [1, TQ], BF16)
    wrow_buf = ctx.bufmap["moe_wrow_t"]
    wgu_sb = ctx.sb("moe_wgu_sb", [128, 2 * PD * 1024], BF16)
    wgu_sb_buf = ctx.bufmap["moe_wgu_sb"]
    g_sb = ctx.sb("moe_g", [128, TQ], BF16)
    hm_sb = ctx.sb("moe_hm", [128, TQ], BF16)
    hmw_sb = ctx.sb("moe_hmw", [128, PD * TQ], BF16)
    hmwb = [Buf(f"moe_hmwb{m}") for m in range(PD)]
    hmid_sb = ctx.sb("moe_hmid_sb", [128, 64 * TQ], BF16)
    hmid_sb_buf = ctx.bufmap["moe_hmid_sb"]
    wd_sb = [ctx.sb("moe_wd_sb0", [128, 64 * 128], BF16)]
    xb2 = ctx.sb("moe_xb2", [128, PD * TQ], F32)
    xb2b = [Buf(f"moe_xb2b{m}") for m in range(PD)]
    out_t = ctx.sb("moe_out", [128, TQ], F32)
    outb = Buf("moe_outb")

    for b in range(BPC):
        for tci in range(NT):
            t0 = tci * TQ
            for hh in range(2):
                DMA(ctx, "sync", xf[:].rearrange("p (k t) -> p k t", k=4),
                    xt1_in[b, hh * 512:(hh + 1) * 512, t0:t0 + TQ].rearrange("(k p) t -> p k t", p=128),
                    writes=list(xfb))
                for k in range(4):
                    kk4 = hh * 4 + k
                    ACT(ctx, x2s[:], xf[:, k * TQ:(k + 1) * TQ], AF.Square,
                        reads=[xfb[k]], writes=[x2s])
                    MM(ctx, ps_sq[0:1, :], ones_col_b[:], x2s[:],
                       start=(kk4 == 0), stop=(kk4 == PD - 1), reads=[x2s, ones_col_b],
                       writes=[ps_sq])
            ACT(ctx, sqrow[:], ps_sq[0:1, :], AF.Sqrt, scale=1.0 / D, reads=[ps_sq], writes=[sqrow])
            TS(ctx, "vector", sqrow[:], sqrow[:], 1e-5, ALU.add, reads=[sqrow], writes=[sqrow])
            RECIP(ctx, s2row[:], sqrow[:], reads=[sqrow], writes=[s2row])
            MM(ctx, ps_util[:], ones_row_f[:], s2row[:], start=True, stop=True,
               reads=[s2row, ones_row_f], writes=[ps_util])
            COPY(ctx, "vector", s2B[:], ps_util[:], reads=[ps_util], writes=[s2B])
            for hh in range(2):
                DMA(ctx, "sync", xf[:].rearrange("p (k t) -> p k t", k=4),
                    xt1_in[b, hh * 512:(hh + 1) * 512, t0:t0 + TQ].rearrange("(k p) t -> p k t", p=128),
                    writes=list(xfb))
                for k in range(4):
                    kk4 = hh * 4 + k
                    TT(ctx, "vector", xn_b[:, kk4 * S + t0:kk4 * S + t0 + TQ],
                       xf[:, k * TQ:(k + 1) * TQ], s2B[:], ALU.mult,
                       reads=[xfb[k], s2B], writes=[xnb[tci]])
            for j in range(4):
                sl = slice(j * 8, j * 8 + 7)
                for k in range(PD):
                    MM(ctx, ps_util[0:128, j * 8:j * 8 + 7],
                       xn_b[:, k * S + t0 + j * 128:k * S + t0 + (j + 1) * 128], wr_b[k][:],
                       start=(k == 0), stop=(k == PD - 1), reads=[xnb[tci], wr_b[k]],
                       writes=[ps_util])
                TT(ctx, "vector", logit[:, sl], ps_util[0:128, j * 8:j * 8 + 7], biasB[0:128, 0:7],
                   ALU.add, reads=[ps_util, biasB], writes=[logit])
                ACT(ctx, p8[:, sl], logit[:, sl], AF.Sigmoid, reads=[logit], writes=[p8])
                ctx.op("vector", lambda e, _j=j: e.memset(p8[:, _j * 8 + 7:_j * 8 + 8], -1.0),
                       writes=[p8])
                MAX8(ctx, t8[:, j * 8:(j + 1) * 8], p8[:, j * 8:(j + 1) * 8],
                     reads=[p8], writes=[t8])
                TT(ctx, "vector", denr[:, j:j + 1], t8[:, j * 8:j * 8 + 1],
                   t8[:, j * 8 + 1:j * 8 + 2], ALU.add, reads=[t8], writes=[denr])
                RECIP(ctx, denr[:, j:j + 1], denr[:, j:j + 1], reads=[denr], writes=[denr])
                TS(ctx, "vector", gmask[:, sl], p8[:, sl], t8[:, j * 8 + 1:j * 8 + 2],
                   ALU.is_ge, reads=[p8, t8], writes=[gmask])
                TT(ctx, "vector", wts[:, sl], p8[:, sl], gmask[:, sl], ALU.mult,
                   reads=[p8, gmask], writes=[wts])
                TS(ctx, "vector", wts_all[:, tci * 32 + j * 8:tci * 32 + j * 8 + 7],
                   wts[:, sl], denr[:, j:j + 1], ALU.mult,
                   reads=[wts, denr], writes=[wts_all_buf])
        for e in range(8):
            DMA(ctx, "sync", wgu_sb[:].rearrange("p (i c) -> p i c", i=2 * PD),
                wgu_bf[e].rearrange("g k p c -> p (g k) c"), reads=[wgu_buf],
                writes=[wgu_sb_buf])
            for tci in range(NT):
                t0 = tci * TQ
                if e < 7:
                    for j in range(4):
                        TRANS(ctx, ps_tr[0:1, 0:128],
                              wts_all[:, tci * 32 + j * 8 + e:tci * 32 + j * 8 + e + 1], idf[:],
                              reads=[wts_all_buf, idf], writes=[ps_tr])
                        COPY(ctx, "vector", wrow_t[0:1, j * 128:(j + 1) * 128],
                             ps_tr[0:1, 0:128], reads=[ps_tr], writes=[wrow_buf])
                    MM(ctx, ps_w[:], ones_row_b[:], wrow_t[:],
                       start=True, stop=True, reads=[ones_row_b, wrow_buf], writes=[ps_w])
                for m in range(PD):
                    for kc in range(PD):
                        MM(ctx, ps_g[:], wgu_sb[:, kc * 1024 + m * 128:kc * 1024 + (m + 1) * 128],
                           xn_b[:, kc * S + t0:kc * S + t0 + TQ], start=(kc == 0), stop=(kc == PD - 1),
                           reads=[wgu_sb_buf, xnb[tci]], writes=[ps_g])
                    for kc in range(PD):
                        MM(ctx, ps_u[:], wgu_sb[:, (PD + kc) * 1024 + m * 128:(PD + kc) * 1024 + (m + 1) * 128],
                           xn_b[:, kc * S + t0:kc * S + t0 + TQ], start=(kc == 0), stop=(kc == PD - 1),
                           reads=[wgu_sb_buf, xnb[tci]], writes=[ps_u])
                    ACT(ctx, g_sb[:], ps_g[:], AF.Silu, reads=[ps_g], writes=[g_sb])
                    TT(ctx, "vector", hm_sb[:], g_sb[:], ps_u[:], ALU.mult,
                       reads=[g_sb, ps_u], writes=[hm_sb])
                    if e < 7:
                        TT(ctx, "vector", hmw_sb[:, m * TQ:(m + 1) * TQ], hm_sb[:], ps_w[:],
                           ALU.mult, reads=[hm_sb, ps_w], writes=[hmwb[m]])
                    else:
                        COPY(ctx, "vector", hmw_sb[:, m * TQ:(m + 1) * TQ], hm_sb[:],
                             reads=[hm_sb], writes=[hmwb[m]])
                DMA(ctx, "gpsimd", hmid_hbm[b, tci, e * PD:(e + 1) * PD].rearrange("m p t -> p m t"),
                    hmw_sb[:].rearrange("p (m t) -> p m t", m=PD), reads=list(hmwb),
                    writes=[hmid_buf])
        for tci in range(NT):
            t0 = tci * TQ
            DMA(ctx, "sync", hmid_sb[:].rearrange("p (i t) -> p i t", i=64),
                hmid_hbm[b, tci].rearrange("i p t -> p i t"), reads=[hmid_buf],
                writes=[hmid_sb_buf])
            for hh in range(2):
                DMA(ctx, "sync", xf[:].rearrange("p (k t) -> p k t", k=4),
                    xt1_in[b, hh * 512:(hh + 1) * 512, t0:t0 + TQ].rearrange("(k p) t -> p k t", p=128),
                    writes=list(xfb))
                for k in range(4):
                    TS(ctx, "vector", xb2[:, (hh * 4 + k) * TQ:(hh * 4 + k + 1) * TQ],
                       xf[:, k * TQ:(k + 1) * TQ], 0.5, ALU.mult,
                       reads=[xfb[k]], writes=[xb2b[hh * 4 + k]])
                DMA(ctx, "sync", xf[:].rearrange("p (k t) -> p k t", k=4),
                    x_in[b, hh * 512:(hh + 1) * 512, t0:t0 + TQ].rearrange("(k p) t -> p k t", p=128),
                    writes=list(xfb))
                for k in range(4):
                    m = hh * 4 + k
                    TT(ctx, "vector", xb2[:, m * TQ:(m + 1) * TQ], xb2[:, m * TQ:(m + 1) * TQ],
                       xf[:, k * TQ:(k + 1) * TQ], ALU.add, reads=[xfb[k], xb2b[m]],
                       writes=[xb2b[m]])
            for m in range(PD):
                wd_ = wd_sb[0]
                DMA(ctx, "sync", wd_[:].rearrange("p (i c) -> p i c", i=64),
                    wd_bf[m].rearrange("i p c -> p i c"), reads=[wd_buf], writes=[wd_])
                pc = ps_comb[m % 2]
                for kc in range(64):
                    MM(ctx, pc[:], wd_[:, kc * 128:(kc + 1) * 128],
                       hmid_sb[:, kc * TQ:(kc + 1) * TQ], start=(kc == 0), stop=(kc == 63),
                       reads=[wd_, hmid_sb_buf], writes=[pc])
                STT(ctx, "vector", out_t[:], pc[:], 0.5, xb2[:, m * TQ:(m + 1) * TQ],
                    ALU.mult, ALU.add, reads=[pc, xb2b[m]], writes=[outb])
                DMA(ctx, "gpsimd", x_out[b, m * 128:(m + 1) * 128, t0:t0 + TQ], out_t[:],
                    reads=[outb])


def moe_weight_arrays(inp):
    return {
        "lnw2_cols": np.ascontiguousarray(np.asarray(inp["L1_ffn_w_ln2"]).reshape(PD, 128).T),
        "wg_r": np.ascontiguousarray(inp["L1_ffn_Wg_r"]),
        "wu_r": np.ascontiguousarray(inp["L1_ffn_Wu_r"]),
        "wd_r": np.ascontiguousarray(inp["L1_ffn_Wd_r"]),
        "wg_s": np.ascontiguousarray(inp["L1_ffn_Wg_s"]),
        "wu_s": np.ascontiguousarray(inp["L1_ffn_Wu_s"]),
        "wd_s": np.ascontiguousarray(inp["L1_ffn_Wd_s"]),
        "wrouter": np.ascontiguousarray(inp["L1_ffn_W_router"]),
        "rbias": np.ascontiguousarray(np.asarray(inp["L1_ffn_routing_bias"]).reshape(1, 7)),
    }


def build_moe_program():
    nc = bass.Bass("TRN2", target_bir_lowering=False)
    ctx = Ctx(nc)
    xt1_in = nc.dram_tensor("xt1_fm", [BPC, D, S], F32, kind="ExternalInput").ap()
    x_in = nc.dram_tensor("x_fm", [BPC, D, S], F32, kind="ExternalInput").ap()
    x_out = nc.dram_tensor("xo_fm", [BPC, D, S], F32, kind="ExternalOutput").ap()
    names = {"lnw2_cols": [128, PD], "wg_r": [7, D, D], "wu_r": [7, D, D], "wd_r": [7, D, D],
             "wg_s": [D, D], "wu_s": [D, D], "wd_s": [D, D], "wrouter": [D, 7], "rbias": [1, 7]}
    ctx.w = {k: nc.dram_tensor(k, v, F32, kind="ExternalInput").ap() for k, v in names.items()}
    emit_moe(ctx, xt1_in, x_in, x_out)
    ctx.finalize()
    return nc


def run_moe(xt1_percore, x_percore, inp):
    warr = moe_weight_arrays(inp)
    nc = build_moe_program()
    in_maps = [dict(xt1_fm=xt1_percore[c], x_fm=x_percore[c], **warr) for c in range(NCORES)]
    res = run_bass_kernel_spmd(nc, in_maps, core_ids=list(range(NCORES)))
    return [r["xo_fm"] for r in res.results]



def emit_qatt(ctx, x_in, x_out):
    TQ = 512
    NT = S // TQ
    w = ctx.w
    RT2 = 0.7071067811865476

    ones_col_b = ctx.sb("qa_ones_col", [128, 1], BF16)
    ones_row_f = ctx.sb("qa_ones_rowf", [1, 128], F32)
    MEMSET(ctx, "vector", ones_col_b[:], 1.0, writes=[ones_col_b])
    MEMSET(ctx, "vector", ones_row_f[:], 1.0, writes=[ones_row_f])
    idf, idb = make_ident(ctx, "qa_id")
    ident64 = ctx.sb("qa_ident64", [128, 64], BF16)
    COPY(ctx, "vector", ident64[0:64, :], idb[0:64, 0:64], reads=[idb], writes=[ident64])
    COPY(ctx, "vector", ident64[64:128, :], idb[0:64, 0:64], reads=[idb], writes=[ident64])

    stage0 = ctx.sb("qa_stg0", [128, 512], F32)
    stg_buf = ctx.bufmap["qa_stg0"]
    wbuf = ctx.sb("qa_wbuf", [128, PD * 1024], BF16)
    wbuf_b = ctx.bufmap["qa_wbuf"]

    def load_w(which, scale=None):
        for k in range(PD):
            for cc in range(2):
                convert_into(ctx, wbuf[:, k * 1024 + cc * 512:k * 1024 + (cc + 1) * 512],
                             wbuf_b, w[which][k * 128:(k + 1) * 128, cc * 512:(cc + 1) * 512],
                             128, 512, stage0, scale=scale)

    wsum = ctx.sb("qa_wsum", [1, 1024], BF16)
    wsum_b = ctx.bufmap["qa_wsum"]
    ps_mean = ctx.ps("qa_ps_mean", [1, TQ], F32)
    ps_sq = ctx.ps("qa_ps_sq", [1, TQ], F32)
    ps_util = ctx.ps("qa_ps_util", [128, TQ], F32)
    ps_main = ctx.ps("qa_ps_main", [128, TQ], F32)
    ps_St = ctx.ps("qa_ps_St", [128, TQ], F32)
    ps_o = ctx.ps("qa_ps_o", [64, TQ], F32)
    ps_tr = ctx.ps("qa_ps_tr", [128, 128], BF16)

    def calc_wsum(ctx=ctx):
        for blk in range(0, 1024, 512):
            for k in range(PD):
                MM(ctx, ps_mean[0:1, 0:512], ones_col_b[:],
                   wbuf[:, k * 1024 + blk:k * 1024 + blk + 512],
                   start=(k == 0), stop=(k == PD - 1), reads=[wbuf_b, ones_col_b],
                   writes=[ps_mean])
            TS(ctx, "vector", wsum[:, blk:blk + 512], ps_mean[0:1, 0:512], -1.0, ALU.mult,
               reads=[ps_mean], writes=[wsum_b])

    masks01 = ctx.sb("qa_masks", [128, 4 * TQ], BF16)
    mstage = ctx.sb("qa_mstage", [128, TQ], F32)
    for i in range(4):
        DMA(ctx, "sync", mstage[:], w["mask01"][i], writes=[mstage])
        COPY(ctx, "vector", masks01[:, i * TQ:(i + 1) * TQ], mstage[:],
             reads=[mstage], writes=[masks01])

    mean_f = ctx.sb("qa_mean_f", [1, TQ], F32)
    sq_f = ctx.sb("qa_sq_f", [1, TQ], F32)
    var_f = ctx.sb("qa_var_f", [1, TQ], F32)
    meansave_b = ctx.sb("qa_meansave", [1, S], BF16)
    rstdsave = ctx.sb("qa_rstdsave", [1, S], F32)
    rB = ctx.sb("qa_rB", [128, TQ], F32)
    xf = ctx.sb("qa_xf", [128, 4 * TQ], F32)
    xfb = [Buf(f"qa_xfb{k}") for k in range(4)]
    xb = ctx.sb("qa_xb", [128, PD * TQ], BF16)
    x2s = ctx.sb("qa_x2s", [128, TQ], BF16)
    qq = ctx.sb("qa_qq", [128, PD * S], BF16)
    qqb = [Buf(f"qa_qqb{g}") for g in range(PD)]
    kk = ctx.sb("qa_kk", [128, PD * S], BF16)
    kkb = [Buf(f"qa_kkb{g}") for g in range(PD)]
    vT = ctx.sb("qa_vT", [128, 256 * 64], BF16)
    vTb = Buf("qa_vTb")
    vtmp = ctx.sb("qa_vtmp", [128, TQ], BF16)
    attn_f = ctx.sb("qa_attnf", [128, TQ], F32)
    probs = [ctx.sb(f"qa_probs{i}", [128, TQ], BF16) for i in range(2)]
    zrecB = ctx.sb("qa_zrecB", [64, TQ], F32)
    o_qc = ctx.sb("qa_oqc", [128, PD * TQ], BF16)
    oqb = [Buf(f"qa_oqb{g}") for g in range(PD)]
    out_t = ctx.sb("qa_out", [128, TQ], F32)
    outb = Buf("qa_outb")

    for b in range(BPC):
        for (wi, (wname, wscale, dest, destb)) in enumerate(
                (("wq", 0.125, qq, qqb), ("wk", None, kk, kkb), ("wv", None, None, None))):
            load_w(wname, scale=wscale)
            calc_wsum()
            for tci in range(NT):
                t0 = tci * TQ
                for hh in range(2):
                    DMA(ctx, "sync", xf[:].rearrange("p (k t) -> p k t", k=4),
                        x_in[b, hh * 512:(hh + 1) * 512, t0:t0 + TQ].rearrange("(k p) t -> p k t", p=128),
                        writes=list(xfb))
                    for k in range(4):
                        kk4 = hh * 4 + k
                        COPY(ctx, "vector", xb[:, kk4 * TQ:(kk4 + 1) * TQ],
                             xf[:, k * TQ:(k + 1) * TQ], reads=[xfb[k]], writes=[xb])
                        if wi == 0:
                            ACT(ctx, x2s[:], xf[:, k * TQ:(k + 1) * TQ], AF.Square,
                                reads=[xfb[k]], writes=[x2s])
                            MM(ctx, ps_sq[0:1, :], ones_col_b[:], x2s[:],
                               start=(kk4 == 0), stop=(kk4 == PD - 1),
                               reads=[x2s, ones_col_b], writes=[ps_sq])
                if wi == 0:
                    for k in range(PD):
                        MM(ctx, ps_mean[0:1, :], ones_col_b[:], xb[:, k * TQ:(k + 1) * TQ],
                           start=(k == 0), stop=(k == PD - 1), reads=[xb, ones_col_b],
                           writes=[ps_mean])
                    ACT(ctx, mean_f[:], ps_mean[0:1, :], AF.Copy, scale=1.0 / D,
                        reads=[ps_mean], writes=[mean_f])
                    COPY(ctx, "vector", meansave_b[:, t0:t0 + TQ], mean_f[:],
                         reads=[mean_f], writes=[meansave_b])
                    ACT(ctx, sq_f[:], mean_f[:], AF.Square, reads=[mean_f], writes=[sq_f])
                    STT(ctx, "vector", var_f[:], ps_sq[0:1, :], 1.0 / D, sq_f[:],
                        ALU.mult, ALU.subtract, reads=[ps_sq, sq_f], writes=[var_f])
                    ACT(ctx, sq_f[:], var_f[:], AF.Sqrt, bias=1e-5, reads=[var_f], writes=[sq_f])
                    RECIP(ctx, var_f[:], sq_f[:], reads=[sq_f], writes=[var_f])
                    COPY(ctx, "vector", rstdsave[:, t0:t0 + TQ], var_f[:],
                         reads=[var_f], writes=[rstdsave])
                else:
                    COPY(ctx, "vector", var_f[:], rstdsave[:, t0:t0 + TQ],
                         reads=[rstdsave], writes=[var_f])
                MM(ctx, ps_util[:], ones_row_f[:], var_f[:], start=True, stop=True,
                   reads=[var_f, ones_row_f], writes=[ps_util])
                COPY(ctx, "vector", rB[:], ps_util[:], reads=[ps_util], writes=[rB])
                for g in range(PD):
                    for k in range(PD):
                        MM(ctx, ps_main[:], wbuf[:, k * 1024 + g * 128:k * 1024 + (g + 1) * 128],
                           xb[:, k * TQ:(k + 1) * TQ], start=(k == 0), stop=False,
                           reads=[wbuf_b, xb], writes=[ps_main])
                    MM(ctx, ps_main[:], wsum[:, g * 128:(g + 1) * 128],
                       meansave_b[:, t0:t0 + TQ], start=False, stop=True,
                       reads=[wsum_b, meansave_b], writes=[ps_main])
                    if wi < 2:
                        TT(ctx, "vector", dest[:, g * S + t0:g * S + t0 + TQ], ps_main[:], rB[:],
                           ALU.mult, reads=[ps_main, rB], writes=[destb[g]])
                    else:
                        TT(ctx, "vector", vtmp[:], ps_main[:], rB[:], ALU.mult,
                           reads=[ps_main, rB], writes=[vtmp])
                        for j in range(4):
                            kb = tci * 4 + j
                            for (vb, hh2) in ((0, 2 * g), (64, 2 * g + 1)):
                                TRANS(ctx, ps_tr[:, 0:64], vtmp[vb:vb + 64, j * 128:(j + 1) * 128],
                                      ident64[vb:vb + 64, :], reads=[vtmp, ident64], writes=[ps_tr])
                                COPY(ctx, "vector", vT[:, (kb * 16 + hh2) * 64:(kb * 16 + hh2) * 64 + 64],
                                     ps_tr[:, 0:64], reads=[ps_tr], writes=[vTb])
        load_w("wo")
        for qci in range(NT):
            q0 = qci * TQ
            for h in range(16):
                g = h // 2
                vb = (h % 2) * 64
                nkb = qci * 4 + 4
                for kb in range(nkb):
                    MM(ctx, ps_St[:], kk[vb:vb + 64, g * S + kb * 128:g * S + (kb + 1) * 128],
                       qq[vb:vb + 64, g * S + q0:g * S + q0 + TQ], start=True, stop=True,
                       reads=[kkb[g], qqb[g]], writes=[ps_St])
                    pslot = probs[kb % 2]
                    ACT(ctx, attn_f[:], ps_St[:], AF.Square, bias=RT2, scale=RT2,
                        reads=[ps_St], writes=[attn_f])
                    if kb >= qci * 4:
                        ii = kb - qci * 4
                        STT(ctx, "vector", pslot[:], attn_f[:], 0.5,
                            masks01[:, ii * TQ:(ii + 1) * TQ], ALU.add, ALU.mult,
                            reads=[attn_f, masks01], writes=[pslot])
                    else:
                        ctx.op("scalar", lambda e, _o=pslot[:], _i=attn_f[:]:
                               e.add(out=_o, in_=_i, add=0.5), reads=[attn_f, ctx._cpool],
                               writes=[pslot]) if False else None
                        TS(ctx, "vector", pslot[:], attn_f[:], 0.5, ALU.add,
                           reads=[attn_f], writes=[pslot])
                    MM(ctx, ps_util[0:1, :], ones_col_b[:], pslot[:],
                       start=(kb == 0), stop=(kb == nkb - 1), reads=[pslot, ones_col_b],
                       writes=[ps_util])
                    MM(ctx, ps_o[:], vT[:, (kb * 16 + h) * 64:(kb * 16 + h) * 64 + 64],
                       pslot[:], start=(kb == 0), stop=(kb == nkb - 1),
                       reads=[vTb, pslot], writes=[ps_o])
                ACT(ctx, mean_f[:], ps_util[0:1, :], AF.Copy, bias=1e-6,
                    reads=[ps_util], writes=[mean_f])
                RECIP(ctx, var_f[:], mean_f[:], reads=[mean_f], writes=[var_f])
                MM(ctx, ps_util[0:64, :], ones_row_f[0:1, 0:64], var_f[:], start=True, stop=True,
                   reads=[var_f, ones_row_f], writes=[ps_util])
                COPY(ctx, "vector", zrecB[:], ps_util[0:64, :], reads=[ps_util], writes=[zrecB])
                TT(ctx, "vector", o_qc[vb:vb + 64, g * TQ:(g + 1) * TQ], ps_o[:], zrecB[:],
                   ALU.mult, reads=[ps_o, zrecB], writes=[oqb[g]])
            for m in range(PD):
                if m % 4 == 0:
                    DMA(ctx, "sync", xf[:].rearrange("p (k t) -> p k t", k=4),
                        x_in[b, m * 128:(m + 4) * 128, q0:q0 + TQ].rearrange("(k p) t -> p k t", p=128),
                        writes=list(xfb))
                for k in range(PD):
                    MM(ctx, ps_main[:], wbuf[:, k * 1024 + m * 128:k * 1024 + (m + 1) * 128],
                       o_qc[:, k * TQ:(k + 1) * TQ], start=(k == 0), stop=(k == PD - 1),
                       reads=[wbuf_b, oqb[k]], writes=[ps_main])
                TT(ctx, "vector", out_t[:], ps_main[:], xf[:, (m % 4) * TQ:(m % 4 + 1) * TQ],
                   ALU.add, reads=[ps_main, xfb[m % 4]], writes=[outb])
                DMA(ctx, "gpsimd", x_out[b, m * 128:(m + 1) * 128, q0:q0 + TQ], out_t[:],
                    reads=[outb])


def qatt_weight_arrays(inp):
    kkk = np.arange(128)[:, None]
    qqq = np.arange(512)[None, :]
    mask01 = np.stack([np.where(128 * i + kkk <= qqq, 1.0, 0.0).astype(np.float32)
                       for i in range(4)])
    return {
        "wq": np.ascontiguousarray(np.asarray(inp["L1_attn_Wq"]).T),
        "wk": np.ascontiguousarray(np.asarray(inp["L1_attn_Wk"]).T),
        "wv": np.ascontiguousarray(np.asarray(inp["L1_attn_Wv"]).T),
        "wo": np.ascontiguousarray(np.asarray(inp["L1_attn_Wo"]).T),
        "mask01": mask01,
    }


def build_qatt_program():
    nc = bass.Bass("TRN2", target_bir_lowering=False)
    ctx = Ctx(nc)
    x_in = nc.dram_tensor("x_fm", [BPC, D, S], F32, kind="ExternalInput").ap()
    x_out = nc.dram_tensor("xo_fm", [BPC, D, S], F32, kind="ExternalOutput").ap()
    names = {"wq": [D, D], "wk": [D, D], "wv": [D, D], "wo": [D, D]}
    ctx.w = {k: nc.dram_tensor(k, v, F32, kind="ExternalInput").ap() for k, v in names.items()}
    ctx.w["mask01"] = nc.dram_tensor("mask01", [4, 128, 512], F32, kind="ExternalInput").ap()
    emit_qatt(ctx, x_in, x_out)
    ctx.finalize()
    return nc


def run_qatt(x_fm_percore, inp):
    warr = qatt_weight_arrays(inp)
    nc = build_qatt_program()
    in_maps = [dict(x_fm=x_fm_percore[c], **warr) for c in range(NCORES)]
    res = run_bass_kernel_spmd(nc, in_maps, core_ids=list(range(NCORES)))
    return [r["xo_fm"] for r in res.results]



_PROGRAMS = {}

def _get_prog(name):
    if name not in _PROGRAMS:
        _PROGRAMS[name] = {
            "ffn0": build_ffn0_program, "mba": build_mba_program,
            "mbb": build_mbb_program, "mbc": build_mbc_program,
            "mbd": build_mbd_program, "mla": build_mla_program,
            "moe": build_moe_program, "qatt": build_qatt_program,
        }[name]()
    return _PROGRAMS[name]


def _launch(name, in_maps):
    nc = _get_prog(name)
    return run_bass_kernel_spmd(nc, in_maps, core_ids=list(range(NCORES))).results


def kernel(**inputs):
    inp = {k: np.asarray(v) for k, v in inputs.items()}
    x = inp["L0_attn_x"]
    x_fm = _fm(x).reshape(NCORES, BPC, D, S)
    xs = [np.ascontiguousarray(x_fm[c]) for c in range(NCORES)]

    w_ffn = ffn0_weight_arrays(inp)
    w_mb = mamba_weight_arrays(inp)
    w_mla = mla_weight_arrays(inp)
    w_moe = moe_weight_arrays(inp)
    w_qa = qatt_weight_arrays(inp)

    r = _launch("ffn0", [dict(x_fm=xs[c], **w_ffn) for c in range(NCORES)])
    x1 = [r[c]["y_fm"] for c in range(NCORES)]
    r = _launch("mba", [dict(x_fm=x1[c], wipT=w_mb["wipT"], normw_cols=w_mb["normw_cols"])
                        for c in range(NCORES)])
    xi = [r[c]["xi_fm"] for c in range(NCORES)]
    res_ = [r[c]["res_fm"] for c in range(NCORES)]
    mrow = [r[c]["mrow"] for c in range(NCORES)]
    rrow = [r[c]["rrow"] for c in range(NCORES)]
    r = _launch("mbb", [dict(xi_fm=xi[c], dtwT=w_mb["dtwT"], xpwT=w_mb["xpwT"],
                             conv_cols=w_mb["conv_cols"], convb_cols=w_mb["convb_cols"],
                             dtb_cols=w_mb["dtb_cols"]) for c in range(NCORES)])
    xc = [r[c]["xc_fm"] for c in range(NCORES)]
    dt = [r[c]["dt_fm"] for c in range(NCORES)]
    Bs = [r[c]["B_fm"] for c in range(NCORES)]
    r = _launch("mbc", [dict(xc_fm=xc[c], dt_fm=dt[c], B_fm=Bs[c],
                             A_rs=w_mb["A_rs"], D_cols=w_mb["D_cols"]) for c in range(NCORES)])
    ys = [r[c]["y_fm"] for c in range(NCORES)]
    r = _launch("mbd", [dict(y_fm=ys[c], res_fm=res_[c], x_fm=x1[c], wopT=w_mb["wopT"],
                             mrow=mrow[c], rrow=rrow[c]) for c in range(NCORES)])
    x2 = [r[c]["xo_fm"] for c in range(NCORES)]
    r = _launch("ffn0", [dict(x_fm=x2[c], **w_ffn) for c in range(NCORES)])
    x3 = [r[c]["y_fm"] for c in range(NCORES)]
    r = _launch("mla", [dict(x_fm=x3[c], **w_mla) for c in range(NCORES)])
    xt1 = [r[c]["xt1_fm"] for c in range(NCORES)]
    r = _launch("moe", [dict(xt1_fm=xt1[c], x_fm=x3[c], **w_moe) for c in range(NCORES)])
    x4 = [r[c]["xo_fm"] for c in range(NCORES)]
    r = _launch("qatt", [dict(x_fm=x4[c], **w_qa) for c in range(NCORES)])
    x5 = [r[c]["xo_fm"] for c in range(NCORES)]
    r = _launch("mla", [dict(x_fm=x5[c], **w_mla) for c in range(NCORES)])
    xt1b = [r[c]["xt1_fm"] for c in range(NCORES)]
    r = _launch("moe", [dict(xt1_fm=xt1b[c], x_fm=x5[c], **w_moe) for c in range(NCORES)])
    out = [r[c]["xo_fm"] for c in range(NCORES)]

    y = np.stack(out).reshape(32, D, S)
    return np.ascontiguousarray(np.swapaxes(y, -1, -2)).astype(np.float32)



# version 04 (final)
# speedup vs baseline: 1.0018x; 1.0018x over previous
import numpy as np
import ml_dtypes
import contextlib

import concourse.bass as bass
import concourse.mybir as mybir
from concourse.bass_utils import run_bass_kernel_spmd

F32 = mybir.dt.float32
BF16 = mybir.dt.bfloat16
AF = mybir.ActivationFunctionType
ALU = mybir.AluOpType
BF = ml_dtypes.bfloat16

NCORES = 8
BPC = 4
S = 2048
D = 1024
PD = D // 128

ENGINES = ("sync", "tensor", "vector", "scalar", "gpsimd")


class Sem:
    __slots__ = ("h", "n", "name")
    def __init__(self, h, name):
        self.h = h; self.n = 0; self.name = name

class Buf:
    __slots__ = ("w", "readers", "name")
    def __init__(self, name=""):
        self.w = None; self.readers = []; self.name = name

class Ctx:
    def __init__(self, nc):
        self.nc = nc
        self.items = {e: [] for e in ENGINES}
        self.stack = contextlib.ExitStack()
        self.esem_pool = {e: [self._mksem(f"sem_{e}{i}") for i in range(6)] for e in ENGINES}
        self.dsem_pool = {e: [self._mksem(f"dma_{e}{i}") for i in range(6)] for e in ENGINES}
        self._stage_idx = 0
        self.esem = {e: self.esem_pool[e][0] for e in ENGINES}
        self.dsem = {e: self.dsem_pool[e][0] for e in ENGINES}
        self.water = {e: {} for e in ENGINES}
        self.psum_banks = [Buf(f"psbank{i}") for i in range(8)]
        self.bufmap = {}
        self.banksmap = {}
        self._psum_cursor = 0
        self._names = set()
        self.nops = 0
        self.w = {}
        self._consts = {}
        self._cpool = None
        self.prefix = ""
        self._stage_stack = None
        self._bar = None
        self._done = None

    def _mksem(self, name):
        return Sem(self.stack.enter_context(self.nc.semaphore(name)), name)

    def begin_stage(self, prefix):
        self.prefix = prefix
        self._stage_stack = contextlib.ExitStack()
        self._psum_cursor = 0
        self._stage_idx += 1
        i = self._stage_idx % 6
        self.esem = {e: self.esem_pool[e][i] for e in ENGINES}
        self.dsem = {e: self.dsem_pool[e][i] for e in ENGINES}

    CLEAR_SEMS = False

    def end_stage(self):
        if self._bar is None:
            self._bar = self._mksem("bar_sem")
            self._done = self._mksem("done_sem")
        sems = list(self.esem.values()) + list(self.dsem.values())
        targets = [(t, t.n) for t in sems if t.n > 0]
        self._bar.n += len(ENGINES)
        self._done.n += 1
        barv = self._bar.n
        donev = self._done.n
        for e in ENGINES:
            wm = self.water[e]
            todo = [(t, v) for (t, v) in targets if wm.get(t, 0) < v]
            def fn(eo, _todo=tuple(todo), _bar=self._bar):
                for t, v in _todo:
                    eo.wait_ge(t.h, v)
                eo.sem_inc(_bar.h, 1)
            self.items[e].append(fn)
        do_clear = self.CLEAR_SEMS
        def fn_clear(eo, _sems=tuple(sems), _bar=self._bar, _bv=barv, _done=self._done,
                     _dc=do_clear):
            eo.wait_ge(_bar.h, _bv)
            if _dc:
                for t in _sems:
                    eo.sem_clear(t.h)
            eo.sem_inc(_done.h, 1)
        self.items["sync"].append(fn_clear)
        for e in ENGINES:
            if e == "sync":
                continue
            def fn_wait(eo, _done=self._done, _dv=donev):
                eo.wait_ge(_done.h, _dv)
            self.items[e].append(fn_wait)
        if self.CLEAR_SEMS:
            for t in sems:
                t.n = 0
        for e in ENGINES:
            self.water[e] = {}
        for b in self.psum_banks:
            b.w = None; b.readers = []
        for b in self.bufmap.values():
            b.w = None; b.readers = []
        self._stage_stack.close()
        self.prefix = ""

    def constf(self, value):
        value = float(value)
        if value in self._consts:
            return value
        if self._cpool is None:
            self._cpool = self.nc.alloc_sbuf_tensor("const_pool", [128, 64], F32)
            self.bufmap["const_pool"] = Buf("const_pool")
        i = len(self._consts)
        assert i < 64
        ap = self._cpool[:, i:i + 1]
        self.op("vector", lambda e, _a=ap, _v=value: e.memset(_a, _v), writes=[self._cpool])
        self.nc.const_aps.register(F32, value, ap)
        self._consts[value] = True
        return value

    def sb(self, name, shape, dtype):
        name = self.prefix + name
        assert name not in self._names, name
        self._names.add(name)
        if self._stage_stack is not None:
            t = self._stage_stack.enter_context(self.nc.sbuf_tensor(name, shape, dtype))
        else:
            t = self.nc.alloc_sbuf_tensor(name, shape, dtype)
        self.bufmap[t.name] = Buf(name)
        return t

    def ps(self, name, shape, dtype=F32):
        name = self.prefix + name
        assert name not in self._names, name
        self._names.add(name)
        if self._stage_stack is not None:
            t = self._stage_stack.enter_context(self.nc.psum_tensor(name, shape, dtype))
        else:
            t = self.nc.alloc_psum_tensor(name, shape, dtype)
        nb = self.nc.num_psum_banks(shape, dtype)
        start = self._psum_cursor
        self._psum_cursor = start + nb
        assert self._psum_cursor <= 8, "psum overflow"
        self.banksmap[t.name] = list(range(start, start + nb))
        return t

    def mkbuf(self, name):
        name = self.prefix + name
        b = Buf(name)
        self.bufmap["@" + name] = b
        return b

    def dram_i(self, name, shape, dtype):
        name = self.prefix + name
        t = self.nc.dram_tensor(name, shape, dtype, kind="Internal")
        self.bufmap[name] = Buf(name)
        return t.ap()

    def _bufs_of(self, objs):
        out = []
        for o in objs:
            if o is None:
                continue
            if isinstance(o, Buf):
                out.append(o)
            elif isinstance(o, (list, tuple)):
                out.extend(self._bufs_of(o))
            else:
                nm = getattr(o, "name", None)
                if nm is None and hasattr(o, "tensor"):
                    nm = o.tensor.name
                if nm is None:
                    raise TypeError(f"cannot track {o!r}")
                if nm in self.banksmap:
                    for b in self.banksmap[nm]:
                        out.append(self.psum_banks[b])
                elif nm in self.bufmap:
                    out.append(self.bufmap[nm])
                else:
                    raise TypeError(f"untracked tensor {nm}")
        return out

    def op(self, eng, build, reads=(), writes=(), dma=False):
        rbufs = self._bufs_of(reads)
        wbufs = self._bufs_of(writes)
        need = {}
        for b in rbufs:
            if b.w is not None:
                s, v = b.w
                if need.get(s, 0) < v:
                    need[s] = v
        for b in wbufs:
            if b.w is not None:
                s, v = b.w
                if need.get(s, 0) < v:
                    need[s] = v
            for s, v in b.readers:
                if need.get(s, 0) < v:
                    need[s] = v
        sem = self.dsem[eng] if dma else self.esem[eng]
        inc = 16 if dma else 1
        if dma and sem.n > 0 and need.get(sem, 0) < sem.n:
            need[sem] = sem.n
        sem.n += inc
        val = sem.n
        wm = self.water[eng]
        waits = []
        for s, v in need.items():
            if wm.get(s, 0) >= v:
                continue
            wm[s] = v
            waits.append((s, v))
        def fn(e, _waits=tuple(waits), _build=build, _sem=sem, _inc=inc):
            for s, v in _waits:
                e.wait_ge(s.h, v)
            _build(e).then_inc(_sem.h, _inc)
        self.items[eng].append(fn)
        self.nops += 1
        for b in rbufs:
            b.readers.append((sem, val))
        for b in wbufs:
            b.w = (sem, val)
            b.readers = []
        return (sem, val)

    def fence(self):
        sems = list(self.esem.values()) + list(self.dsem.values())
        targets = [(s, s.n) for s in sems if s.n > 0]
        for e in ENGINES:
            wm = self.water[e]
            todo = [(s, v) for (s, v) in targets if wm.get(s, 0) < v]
            for s, v in todo:
                wm[s] = v
            def fn(eng_obj, _todo=tuple(todo)):
                for s, v in _todo:
                    eng_obj.wait_ge(s.h, v)
            self.items[e].append(fn)
        for b in self.psum_banks:
            b.w = None; b.readers = []

    def finalize(self):
        for e in ENGINES:
            s = self.dsem[e]
            if s.n > 0:
                self.items["sync"].append(lambda eo, _s=s, _v=s.n: eo.wait_ge(_s.h, _v))
        with self.nc.Block() as block:
            for e in ENGINES:
                def body(eng_obj, _e=e):
                    for fn in self.items[_e]:
                        fn(eng_obj)
                getattr(block, e)(body)
        self.stack.close()



def DMA(ctx, eng, out, in_, reads=(), writes=()):
    ctx.op(eng, lambda e, _o=out, _i=in_: e.dma_start(out=_o, in_=_i),
           reads=list(reads), writes=list(writes), dma=True)

def MM(ctx, out, lhsT, rhs, start, stop, reads=(), writes=(), tp=None):
    if tp is not None and tp[1] != 96:
        tp = None
    ctx.op("tensor", lambda e, _o=out, _l=lhsT, _r=rhs, _s=start, _p=stop, _t=tp:
           e.matmul(out=_o, lhsT=_l, rhs=_r, start=_s, stop=_p, tile_position=_t),
           reads=list(reads), writes=list(writes))

def TRANS(ctx, out, in_, ident, reads=(), writes=()):
    ctx.op("tensor", lambda e, _o=out, _i=in_, _d=ident: e.transpose(out=_o, in_=_i, identity=_d),
           reads=list(reads), writes=list(writes))

def COPY(ctx, eng, out, in_, reads=(), writes=()):
    ctx.op(eng, lambda e, _o=out, _i=in_: e.tensor_copy(out=_o, in_=_i),
           reads=list(reads), writes=list(writes))

def TT(ctx, eng, out, in0, in1, alu, reads=(), writes=()):
    ctx.op(eng, lambda e, _o=out, _a=in0, _b=in1, _al=alu: e.tensor_tensor(out=_o, in0=_a, in1=_b, op=_al),
           reads=list(reads), writes=list(writes))

def TS(ctx, eng, out, in0, s1, alu, reads=(), writes=()):
    ctx.op(eng, lambda e, _o=out, _a=in0, _s=s1, _al=alu: e.tensor_scalar(
           out=_o, in0=_a, scalar1=_s, scalar2=None, op0=_al),
           reads=list(reads), writes=list(writes))

def STT(ctx, eng, out, in0, scalar, in1, op0, op1, reads=(), writes=()):
    ctx.op(eng, lambda e, _o=out, _a=in0, _s=scalar, _b=in1, _0=op0, _1=op1:
           e.scalar_tensor_tensor(out=_o, in0=_a, scalar=_s, in1=_b, op0=_0, op1=_1),
           reads=list(reads), writes=list(writes))

def ACT(ctx, out, in_, func, bias=0.0, scale=1.0, accum=None, reads=(), writes=()):
    reads = list(reads)
    if isinstance(bias, float) and bias != 0.0:
        ctx.constf(bias)
        reads.append(ctx._cpool)
    ctx.op("scalar", lambda e, _o=out, _i=in_, _f=func, _b=bias, _s=scale, _a=accum:
           e.activation(out=_o, in_=_i, func=_f, bias=_b, scale=_s, accum_out=_a),
           reads=reads, writes=list(writes))

def SCAN(ctx, out, data0, data1, initial, reads=(), writes=()):
    ctx.op("vector", lambda e, _o=out, _a=data0, _b=data1, _i=initial:
           e.tensor_tensor_scan(out=_o, data0=_a, data1=_b, initial=_i,
                                op0=ALU.mult, op1=ALU.add),
           reads=list(reads), writes=list(writes))

def MEMSET(ctx, eng, ap, val, writes=()):
    ctx.op(eng, lambda e, _a=ap, _v=val: e.memset(_a, _v), writes=list(writes))

def RECIP(ctx, out, in_, reads=(), writes=()):
    ctx.op("vector", lambda e, _o=out, _i=in_: e.reciprocal(out=_o, in_=_i),
           reads=list(reads), writes=list(writes))

def MAX8(ctx, out, in_, reads=(), writes=()):
    ctx.op("vector", lambda e, _o=out, _i=in_: e.max(out=_o, in_=_i),
           reads=list(reads), writes=list(writes))

def IOTA(ctx, out, pattern, base, cm, writes=()):
    ctx.op("gpsimd", lambda e, _o=out, _p=pattern, _b=base, _c=cm:
           e.iota(_o, pattern=_p, base=_b, channel_multiplier=_c,
                  allow_small_or_imprecise_dtypes=True),
           writes=list(writes))


def load_weight_bf16(ctx, name, dram_ap, rows, cols, scale=None, scale_col=None,
                     scale_col_reads=()):
    f = ctx.sb(name + "_f", [rows, cols], F32)
    b = ctx.sb(name, [rows, cols], BF16)
    DMA(ctx, "sync", f[:], dram_ap, writes=[f])
    if scale_col is not None:
        TS(ctx, "vector", b[:], f[:], scale_col, ALU.mult,
           reads=[f] + list(scale_col_reads), writes=[b])
    elif scale is not None and scale != 1.0:
        TS(ctx, "vector", b[:], f[:], float(scale), ALU.mult, reads=[f], writes=[b])
    else:
        COPY(ctx, "vector", b[:], f[:], reads=[f], writes=[b])
    return b


def make_ident(ctx, prefix):
    iota = ctx.sb(prefix + "_iota", [128, 128], F32)
    idf = ctx.sb(prefix + "_idf", [128, 128], F32)
    idb = ctx.sb(prefix + "_idb", [128, 128], BF16)
    IOTA(ctx, iota[:], [[1, 128]], 0, -1, writes=[iota])
    TS(ctx, "vector", idf[:], iota[:], 0.0, ALU.is_equal, reads=[iota], writes=[idf])
    COPY(ctx, "vector", idb[:], idf[:], reads=[idf], writes=[idb])
    return idf, idb


class NS:
    pass



def emit_ffn0(ctx, x_in, y_out):
    TQ = 512
    NT = S // TQ
    w = ctx.w

    ones_col_b = ctx.sb("ffn_ones_col", [128, 1], BF16)
    ones_row_f = ctx.sb("ffn_ones_rowf", [1, 128], F32)
    one_f = ctx.sb("ffn_one_f", [1, 1], F32)
    MEMSET(ctx, "vector", ones_col_b[:], 1.0, writes=[ones_col_b])
    MEMSET(ctx, "vector", ones_row_f[:], 1.0, writes=[ones_row_f])
    MEMSET(ctx, "vector", one_f[:], 1.0, writes=[one_f])
    idf, idb = make_ident(ctx, "ffn_id")

    wlin_b = [load_weight_bf16(ctx, f"ffn_wlin{k}", w["wlinT"][k * 128:(k + 1) * 128, :], 128, D)
              for k in range(PD)]
    lcat_b = [load_weight_bf16(ctx, f"ffn_lcat{k}", w["lcat"][k * 128:(k + 1) * 128, :], 128, 32)
              for k in range(PD)]
    gate_b = [load_weight_bf16(ctx, f"ffn_gate{k}", w["gateT"][k * 128:(k + 1) * 128, :], 128, 8)
              for k in range(PD)]
    rcat_b = load_weight_bf16(ctx, "ffn_rcat", w["rcat"], 32, D)
    blin_f = ctx.sb("ffn_blin_f", [128, 8], F32)
    halfb = ctx.sb("ffn_halfb", [128, 8], F32)
    DMA(ctx, "sync", blin_f[:], w["blin"], writes=[blin_f])
    TS(ctx, "vector", halfb[:], blin_f[:], 0.5, ALU.mult, reads=[blin_f], writes=[halfb])

    ps_row = ctx.ps("ffn_psrow", [1, 512], F32)
    wsum_wlin = ctx.sb("ffn_wsum_wlin", [1, D], BF16)
    wsum_lcat = ctx.sb("ffn_wsum_lcat", [1, 32], BF16)
    wsum_gate = ctx.sb("ffn_wsum_gate", [1, 8], BF16)
    for (wchunks, m, dst) in ((wlin_b, D, wsum_wlin), (lcat_b, 32, wsum_lcat), (gate_b, 8, wsum_gate)):
        step = min(m, 512)
        for blk in range(0, m, step):
            for k in range(PD):
                MM(ctx, ps_row[0:1, 0:step], ones_col_b[:], wchunks[k][:, blk:blk + step],
                   start=(k == 0), stop=(k == PD - 1),
                   reads=[wchunks[k], ones_col_b], writes=[ps_row])
            TS(ctx, "vector", dst[:, blk:blk + step], ps_row[0:1, 0:step], -1.0, ALU.mult,
               reads=[ps_row], writes=[dst])

    sel_sc = ctx.sb("ffn_sel_sc", [8, 32], F32)
    sel_b = ctx.sb("ffn_sel", [8, 32], BF16)
    IOTA(ctx, sel_sc[0:8, :], [[-1, 8], [0, 4]], 0, 1, writes=[sel_sc])
    TS(ctx, "vector", sel_b[:], sel_sc[:], 0.0, ALU.is_equal, reads=[sel_sc], writes=[sel_b])

    ps_mean = ctx.ps("ffn_ps_mean", [1, TQ], F32)
    ps_sq = ctx.ps("ffn_ps_sq", [1, TQ], F32)
    ps_big = ctx.ps("ffn_ps_big", [128, TQ], F32)
    ps_aux = ctx.ps("ffn_ps_aux", [128, TQ], F32)
    ps_wrep = ctx.ps("ffn_ps_wrep", [32, TQ], F32)
    ps_tr = ctx.ps("ffn_ps_tr", [128, 128], F32)
    ps_bc = ctx.ps("ffn_ps_bc", [128, TQ], F32)

    mean_f = ctx.sb("ffn_mean_f", [1, TQ], F32)
    sq_f = ctx.sb("ffn_sq_f", [1, TQ], F32)
    var_f = ctx.sb("ffn_var_f", [1, TQ], F32)
    meanrow_b = ctx.sb("ffn_meanrow", [1, TQ], BF16)
    rstdrow = ctx.sb("ffn_rstdrow", [1, TQ], F32)
    rstdhrow = ctx.sb("ffn_rstdh", [1, TQ], F32)
    rstdB = ctx.sb("ffn_rstdB", [128, TQ], F32)
    rstd_cols = ctx.sb("ffn_rstdcols", [128, 4], F32)
    logit = ctx.sb("ffn_logit", [128, 4 * 8], F32)
    t8 = ctx.sb("ffn_t8", [128, 4 * 8], F32)
    negm1 = ctx.sb("ffn_negm1", [128, 4], F32)
    ssum = ctx.sb("ffn_ssum", [128, 4], F32)
    srec = ctx.sb("ffn_srec", [128, 4], F32)
    eprob = ctx.sb("ffn_eprob", [128, 4 * 8], F32)
    gmask = ctx.sb("ffn_gmask", [128, 4 * 8], F32)
    wts = ctx.sb("ffn_wts", [128, 4 * 8], F32)
    wtsT_b = ctx.sb("ffn_wtsT", [8, TQ], BF16)
    tall_b = ctx.sb("ffn_tall", [32, TQ], BF16)
    tw_b = ctx.sb("ffn_tw", [32, TQ], BF16)
    v1 = ctx.sb("ffn_v1", [128, TQ], F32)
    out_f = ctx.sb("ffn_outf", [128, PD * TQ], F32)
    outb = [ctx.mkbuf(f"ffn_outf_{k}") for k in range(PD)]

    xf = [ctx.sb(f"ffn_xf{i}", [128, PD * TQ], F32) for i in range(2)]
    xfb = [[ctx.mkbuf(f"ffn_xfb{i}_{k}") for k in range(PD)] for i in range(2)]
    xb = [ctx.sb(f"ffn_xb{i}", [128, PD * TQ], BF16) for i in range(2)]
    x2 = ctx.sb("ffn_x2", [128, PD * TQ], BF16)

    for b in range(BPC):
        for tci in range(NT):
            t0 = tci * TQ
            pp = (b * NT + tci) % 2
            xf_, xb_, xfb_ = xf[pp], xb[pp], xfb[pp]
            DMA(ctx, "sync",
                xf_[:].rearrange("p (k t) -> p k t", k=PD),
                x_in[b, :, t0:t0 + TQ].rearrange("(k p) t -> p k t", p=128),
                writes=list(xfb_))
            COPY(ctx, "vector", xb_[:], xf_[:], reads=xfb_, writes=[xb_])
            ACT(ctx, x2[:], xf_[:], AF.Square, reads=xfb_, writes=[x2])
            for k in range(PD):
                MM(ctx, ps_mean[0:1, :], ones_col_b[:], xb_[:, k * TQ:(k + 1) * TQ],
                   start=(k == 0), stop=(k == PD - 1), reads=[xb_, ones_col_b], writes=[ps_mean])
            for k in range(PD):
                MM(ctx, ps_sq[0:1, :], ones_col_b[:], x2[:, k * TQ:(k + 1) * TQ],
                   start=(k == 0), stop=(k == PD - 1), reads=[x2, ones_col_b], writes=[ps_sq])
            ACT(ctx, mean_f[:], ps_mean[0:1, :], AF.Copy, scale=1.0 / D,
                reads=[ps_mean], writes=[mean_f])
            COPY(ctx, "vector", meanrow_b[:], mean_f[:], reads=[mean_f], writes=[meanrow_b])
            ACT(ctx, sq_f[:], mean_f[:], AF.Square, reads=[mean_f], writes=[sq_f])
            STT(ctx, "vector", var_f[:], ps_sq[0:1, :], 1.0 / D, sq_f[:],
                ALU.mult, ALU.subtract, reads=[ps_sq, sq_f], writes=[var_f])
            ACT(ctx, sq_f[:], var_f[:], AF.Sqrt, bias=1e-5, reads=[var_f], writes=[sq_f])
            RECIP(ctx, rstdrow[:], sq_f[:], reads=[sq_f], writes=[rstdrow])
            TS(ctx, "vector", rstdhrow[:], rstdrow[:], 0.5, ALU.mult,
               reads=[rstdrow], writes=[rstdhrow])
            MM(ctx, ps_bc[:], ones_row_f[:], rstdhrow[:], start=True, stop=True,
               reads=[rstdhrow, ones_row_f], writes=[ps_bc])
            COPY(ctx, "vector", rstdB[:], ps_bc[:], reads=[ps_bc], writes=[rstdB])
            for j in range(4):
                MM(ctx, ps_tr[:, 0:1], rstdrow[:, j * 128:(j + 1) * 128], one_f[:],
                   start=True, stop=True, reads=[rstdrow, one_f], writes=[ps_tr])
                COPY(ctx, "vector", rstd_cols[:, j:j + 1], ps_tr[:, 0:1],
                     reads=[ps_tr], writes=[rstd_cols])
            for j in range(4):
                for k in range(PD):
                    MM(ctx, ps_aux[0:128, j * 8:(j + 1) * 8],
                       xb_[:, k * TQ + j * 128:k * TQ + (j + 1) * 128], gate_b[k][:],
                       start=(k == 0), stop=False, reads=[xb_, gate_b[k]], writes=[ps_aux])
                MM(ctx, ps_aux[0:128, j * 8:(j + 1) * 8],
                   meanrow_b[:, j * 128:(j + 1) * 128], wsum_gate[:],
                   start=False, stop=True, reads=[meanrow_b, wsum_gate], writes=[ps_aux])
            for j in range(4):
                sl = slice(j * 8, (j + 1) * 8)
                TS(ctx, "vector", logit[:, sl], ps_aux[0:128, sl], rstd_cols[:, j:j + 1],
                   ALU.mult, reads=[ps_aux, rstd_cols], writes=[logit])
                MAX8(ctx, t8[:, sl], logit[:, sl], reads=[logit], writes=[t8])
                TS(ctx, "vector", negm1[:, j:j + 1], t8[:, j * 8:j * 8 + 1], -1.0, ALU.mult,
                   reads=[t8], writes=[negm1])
                ACT(ctx, eprob[:, sl], logit[:, sl], AF.Exp, bias=negm1[:, j:j + 1],
                    accum=ssum[:, j:j + 1], reads=[logit, negm1], writes=[eprob, ssum])
                RECIP(ctx, srec[:, j:j + 1], ssum[:, j:j + 1], reads=[ssum], writes=[srec])
                TS(ctx, "vector", gmask[:, sl], logit[:, sl], t8[:, j * 8 + 1:j * 8 + 2],
                   ALU.is_ge, reads=[logit, t8], writes=[gmask])
                TT(ctx, "vector", wts[:, sl], eprob[:, sl], gmask[:, sl], ALU.mult,
                   reads=[eprob, gmask], writes=[wts])
                TS(ctx, "vector", wts[:, sl], wts[:, sl], srec[:, j:j + 1], ALU.mult,
                   reads=[wts, srec], writes=[wts])
                TRANS(ctx, ps_tr[0:8, 0:128], wts[:, sl], idf[:], reads=[wts, idf], writes=[ps_tr])
                COPY(ctx, "vector", wtsT_b[0:8, j * 128:(j + 1) * 128], ps_tr[0:8, 0:128],
                     reads=[ps_tr], writes=[wtsT_b])
            MM(ctx, ps_wrep[:], sel_b[:], wtsT_b[:], start=True, stop=True,
               reads=[sel_b, wtsT_b], writes=[ps_wrep])
            for k in range(PD):
                MM(ctx, ps_aux[0:32, :], lcat_b[k][:], xb_[:, k * TQ:(k + 1) * TQ],
                   start=(k == 0), stop=False, reads=[lcat_b[k], xb_], writes=[ps_aux])
            MM(ctx, ps_aux[0:32, :], wsum_lcat[:], meanrow_b[:], start=False, stop=True,
               reads=[wsum_lcat, meanrow_b], writes=[ps_aux])
            COPY(ctx, "vector", tall_b[:], ps_aux[0:32, :], reads=[ps_aux], writes=[tall_b])
            TT(ctx, "vector", tw_b[:], ps_wrep[0:32, :], tall_b[:], ALU.mult,
               reads=[ps_wrep, tall_b], writes=[tw_b])
            for m in range(PD):
                for k in range(PD):
                    MM(ctx, ps_big[:], wlin_b[k][:, m * 128:(m + 1) * 128],
                       xb_[:, k * TQ:(k + 1) * TQ], start=(k == 0), stop=False,
                       reads=[wlin_b[k], xb_], writes=[ps_big])
                MM(ctx, ps_big[:], wsum_wlin[:, m * 128:(m + 1) * 128], meanrow_b[:],
                   start=False, stop=False, reads=[wsum_wlin, meanrow_b], writes=[ps_big])
                MM(ctx, ps_big[:], rcat_b[:, m * 128:(m + 1) * 128], tw_b[:],
                   start=False, stop=True, reads=[rcat_b, tw_b], writes=[ps_big])
                TT(ctx, "vector", v1[:], ps_big[:], rstdB[:], ALU.mult,
                   reads=[ps_big, rstdB], writes=[v1])
                STT(ctx, "vector", out_f[:, m * TQ:(m + 1) * TQ], v1[:], halfb[:, m:m + 1],
                    xf_[:, m * TQ:(m + 1) * TQ], ALU.add, ALU.add,
                    reads=[v1, halfb, xfb_[m]], writes=[outb[m]])
            DMA(ctx, "gpsimd",
                y_out[b, :, t0:t0 + TQ].rearrange("(k p) t -> p k t", p=128),
                out_f[:].rearrange("p (k t) -> p k t", k=PD),
                reads=list(outb))



def _fm(x):
    return np.ascontiguousarray(np.swapaxes(x, -1, -2))


def ffn0_weight_arrays(inp):
    return {
        "wlinT": np.ascontiguousarray(inp["L0_ffn_W_lin"].T),
        "lcat": np.ascontiguousarray(np.transpose(np.asarray(inp["L0_ffn_left"]), (1, 0, 2)).reshape(D, 32)),
        "rcat": np.ascontiguousarray(np.asarray(inp["L0_ffn_right"]).reshape(32, D)),
        "gateT": np.ascontiguousarray(inp["L0_ffn_gate_w"].T),
        "blin": np.ascontiguousarray(np.asarray(inp["L0_ffn_b_lin"]).reshape(8, 128).T),
    }


def build_ffn0_program():
    nc = bass.Bass("TRN2", target_bir_lowering=False)
    ctx = Ctx(nc)
    x_in = nc.dram_tensor("x_fm", [BPC, D, S], F32, kind="ExternalInput").ap()
    y_out = nc.dram_tensor("y_fm", [BPC, D, S], F32, kind="ExternalOutput").ap()
    ctx.w = {
        "wlinT": nc.dram_tensor("wlinT", [D, D], F32, kind="ExternalInput").ap(),
        "lcat": nc.dram_tensor("lcat", [D, 32], F32, kind="ExternalInput").ap(),
        "rcat": nc.dram_tensor("rcat", [32, D], F32, kind="ExternalInput").ap(),
        "gateT": nc.dram_tensor("gateT", [D, 8], F32, kind="ExternalInput").ap(),
        "blin": nc.dram_tensor("blin", [128, 8], F32, kind="ExternalInput").ap(),
    }
    emit_ffn0(ctx, x_in, y_out)
    ctx.finalize()
    return nc


def run_ffn0(x_fm_percore, inp):
    nc = build_ffn0_program()
    warr = ffn0_weight_arrays(inp)
    in_maps = [dict(x_fm=x_fm_percore[c], **warr) for c in range(NCORES)]
    res = run_bass_kernel_spmd(nc, in_maps, core_ids=list(range(NCORES)))
    return [r["y_fm"] for r in res.results]





def convert_into(ctx, dest_ap, dest_buf, dram_ap, rows, cols, stage, scale_col=None,
                 scale=None, scale_col_reads=()):
    DMA(ctx, "sync", stage[0:rows, 0:cols], dram_ap, writes=[stage])
    src = stage[0:rows, 0:cols]
    if scale_col is not None:
        TS(ctx, "vector", dest_ap, src, scale_col, ALU.mult,
           reads=[stage] + list(scale_col_reads), writes=[dest_buf])
    elif scale is not None and scale != 1.0:
        TS(ctx, "vector", dest_ap, src, float(scale), ALU.mult, reads=[stage], writes=[dest_buf])
    else:
        COPY(ctx, "vector", dest_ap, src, reads=[stage], writes=[dest_buf])


def load_weight_staged(ctx, name, dram_ap, rows, cols, stage, scale_col=None,
                       scale=None, scale_col_reads=(), chunk=512):
    b = ctx.sb(name, [rows, cols], BF16)
    if isinstance(stage, (list, tuple)):
        stages = stage
    else:
        stages = [stage]
    i = 0
    for c0 in range(0, cols, chunk):
        cw = min(chunk, cols - c0)
        convert_into(ctx, b[:, c0:c0 + cw], b, dram_ap[:, c0:c0 + cw], rows, cw,
                     stages[i % len(stages)], scale_col=scale_col, scale=scale,
                     scale_col_reads=scale_col_reads)
        i += 1
    return b


def emit_wsum_neg(ctx, name, wchunks, m, ones_col_b, ps_row):
    dst = ctx.sb(name, [1, m], BF16)
    step = min(m, 512)
    nk = len(wchunks)
    for blk in range(0, m, step):
        for k in range(nk):
            MM(ctx, ps_row[0:1, 0:step], ones_col_b[:], wchunks[k][:, blk:blk + step],
               start=(k == 0), stop=(k == nk - 1), reads=[wchunks[k], ones_col_b],
               writes=[ps_row])
        TS(ctx, "vector", dst[:, blk:blk + step], ps_row[0:1, 0:step], -1.0, ALU.mult,
           reads=[ps_row], writes=[dst])
    return dst



def emit_mamba_A(ctx, x_in, xi_out, res_out, mrow_out, rrow_out):
    TQ = 512
    NT = S // TQ
    w = ctx.w

    ones_col_b = ctx.sb("mba_ones_col", [128, 1], BF16)
    ones_row_f = ctx.sb("mba_ones_rowf", [1, 128], F32)
    MEMSET(ctx, "vector", ones_col_b[:], 1.0, writes=[ones_col_b])
    MEMSET(ctx, "vector", ones_row_f[:], 1.0, writes=[ones_row_f])

    normw_cols = ctx.sb("mba_normw", [128, PD], F32)
    DMA(ctx, "sync", normw_cols[:], w["normw_cols"], writes=[normw_cols])

    stage = [ctx.sb(f"mba_stg{i}", [128, 2048], F32) for i in range(2)]
    wip_b = [load_weight_staged(ctx, f"mba_wip{k}", w["wipT"][k * 128:(k + 1) * 128, :],
                                128, 4096, stage,
                                scale_col=normw_cols[:, k:k + 1],
                                scale_col_reads=[normw_cols])
             for k in range(PD)]
    ps_row = ctx.ps("mba_psrow", [1, 512], F32)
    wsum_ip = emit_wsum_neg(ctx, "mba_wsum_ip", wip_b, 4096, ones_col_b, ps_row)

    ps_mean = ctx.ps("mba_ps_mean", [1, TQ], F32)
    ps_sq = ctx.ps("mba_ps_sq", [1, TQ], F32)
    ps_bc = ctx.ps("mba_ps_bc", [128, TQ], F32)
    ps_main = [ctx.ps(f"mba_ps_main{i}", [128, TQ], F32) for i in range(2)]

    mean_f = ctx.sb("mba_mean_f", [1, TQ], F32)
    sq_f = ctx.sb("mba_sq_f", [1, TQ], F32)
    var_f = ctx.sb("mba_var_f", [1, TQ], F32)
    meanrow_b = ctx.sb("mba_meanrow", [1, TQ], BF16)
    rstdrow = ctx.sb("mba_rstdrow", [1, TQ], F32)
    rstd2 = ctx.sb("mba_rstd2", [1, TQ], F32)
    qrow = ctx.sb("mba_qrow", [1, TQ], F32)
    srow = ctx.sb("mba_srow", [1, TQ], F32)
    sB = ctx.sb("mba_sB", [128, TQ], F32)

    xf = [ctx.sb(f"mba_xf{i}", [128, PD * TQ], F32) for i in range(2)]
    xfb = [[ctx.mkbuf(f"mba_xfb{i}_{k}") for k in range(PD)] for i in range(2)]
    xb = [ctx.sb(f"mba_xb{i}", [128, PD * TQ], BF16) for i in range(2)]
    x2 = ctx.sb("mba_x2", [128, PD * TQ], BF16)
    xi_sb = ctx.sb("mba_xi", [128, 8 * TQ], F32)
    xib = [ctx.mkbuf(f"mba_xib{m}") for m in range(8)]
    res_sb = ctx.sb("mba_res", [128, 16 * TQ], BF16)
    resb = [ctx.mkbuf(f"mba_resb{m}") for m in range(16)]

    for b in range(BPC):
        for tci in range(NT):
            t0 = tci * TQ
            pp = (b * NT + tci) % 2
            xf_, xb_, xfb_ = xf[pp], xb[pp], xfb[pp]
            DMA(ctx, "sync",
                xf_[:].rearrange("p (k t) -> p k t", k=PD),
                x_in[b, :, t0:t0 + TQ].rearrange("(k p) t -> p k t", p=128),
                writes=list(xfb_))
            COPY(ctx, "vector", xb_[:], xf_[:], reads=xfb_, writes=[xb_])
            ACT(ctx, x2[:], xf_[:], AF.Square, reads=xfb_, writes=[x2])
            for k in range(PD):
                MM(ctx, ps_mean[0:1, :], ones_col_b[:], xb_[:, k * TQ:(k + 1) * TQ],
                   start=(k == 0), stop=(k == PD - 1), reads=[xb_, ones_col_b],
                   writes=[ps_mean])
            for k in range(PD):
                MM(ctx, ps_sq[0:1, :], ones_col_b[:], x2[:, k * TQ:(k + 1) * TQ],
                   start=(k == 0), stop=(k == PD - 1), reads=[x2, ones_col_b],
                   writes=[ps_sq])
            ACT(ctx, mean_f[:], ps_mean[0:1, :], AF.Copy, scale=1.0 / D,
                reads=[ps_mean], writes=[mean_f])
            COPY(ctx, "vector", meanrow_b[:], mean_f[:], reads=[mean_f], writes=[meanrow_b])
            ACT(ctx, sq_f[:], mean_f[:], AF.Square, reads=[mean_f], writes=[sq_f])
            STT(ctx, "vector", var_f[:], ps_sq[0:1, :], 1.0 / D, sq_f[:],
                ALU.mult, ALU.subtract, reads=[ps_sq, sq_f], writes=[var_f])
            ACT(ctx, sq_f[:], var_f[:], AF.Sqrt, bias=1e-5, reads=[var_f], writes=[sq_f])
            RECIP(ctx, rstdrow[:], sq_f[:], reads=[sq_f], writes=[rstdrow])
            DMA(ctx, "gpsimd", mrow_out[b:b + 1, t0:t0 + TQ], mean_f[0:1, :], reads=[mean_f])
            DMA(ctx, "gpsimd", rrow_out[b:b + 1, t0:t0 + TQ], rstdrow[0:1, :], reads=[rstdrow])
            ACT(ctx, rstd2[:], rstdrow[:], AF.Square, reads=[rstdrow], writes=[rstd2])
            TT(ctx, "vector", qrow[:], var_f[:], rstd2[:], ALU.mult,
               reads=[var_f, rstd2], writes=[qrow])
            ACT(ctx, qrow[:], qrow[:], AF.Sqrt, bias=1e-6, reads=[qrow], writes=[qrow])
            RECIP(ctx, qrow[:], qrow[:], reads=[qrow], writes=[qrow])
            TT(ctx, "vector", srow[:], rstdrow[:], qrow[:], ALU.mult,
               reads=[rstdrow, qrow], writes=[srow])
            MM(ctx, ps_bc[:], ones_row_f[:], srow[:], start=True, stop=True,
               reads=[srow, ones_row_f], writes=[ps_bc])
            COPY(ctx, "vector", sB[:], ps_bc[:], reads=[ps_bc], writes=[sB])
            for m in range(32):
                pm = ps_main[m % 2]
                for k in range(PD):
                    MM(ctx, pm[:], wip_b[k][:, m * 128:(m + 1) * 128],
                       xb_[:, k * TQ:(k + 1) * TQ], start=(k == 0), stop=False,
                       reads=[wip_b[k], xb_], writes=[pm])
                MM(ctx, pm[:], wsum_ip[:, m * 128:(m + 1) * 128], meanrow_b[:],
                   start=False, stop=True, reads=[wsum_ip, meanrow_b], writes=[pm])
                if m < 16:
                    mh = m % 8
                    TT(ctx, "vector", xi_sb[:, mh * TQ:(mh + 1) * TQ], pm[:], sB[:],
                       ALU.mult, reads=[pm, sB], writes=[xib[mh]])
                    if m == 7 or m == 15:
                        h0 = (m // 8) * 1024
                        DMA(ctx, "gpsimd",
                            xi_out[b, h0:h0 + 1024, t0:t0 + TQ].rearrange("(k p) t -> p k t", p=128),
                            xi_sb[:].rearrange("p (k t) -> p k t", k=8),
                            reads=list(xib))
                else:
                    mm = m - 16
                    TT(ctx, "vector", res_sb[:, mm * TQ:(mm + 1) * TQ], pm[:], sB[:],
                       ALU.mult, reads=[pm, sB], writes=[resb[mm]])
            DMA(ctx, "gpsimd",
                res_out[b, :, t0:t0 + TQ].rearrange("(k p) t -> p k t", p=128),
                res_sb[:].rearrange("p (k t) -> p k t", k=16), reads=list(resb))



def emit_mamba_B(ctx, xi_in, xc_out, dt_out, B_out):
    TQ = 512
    NT = S // TQ
    w = ctx.w

    conv_cols = ctx.sb("mbb_convw", [128, 16 * 4], F32)
    convb_cols = ctx.sb("mbb_convb", [128, 16], F32)
    dtb_cols = ctx.sb("mbb_dtb", [128, 16], F32)
    DMA(ctx, "sync", conv_cols[:], w["conv_cols"], writes=[conv_cols])
    DMA(ctx, "sync", convb_cols[:], w["convb_cols"], writes=[convb_cols])
    DMA(ctx, "sync", dtb_cols[:], w["dtb_cols"], writes=[dtb_cols])

    stage = [ctx.sb(f"mbb_stg{i}", [128, 2048], F32) for i in range(2)]
    dtw_b = [load_weight_staged(ctx, f"mbb_dtw{k}", w["dtwT"][k * 128:(k + 1) * 128, :],
                                128, 2048, stage[k % 2]) for k in range(16)]
    xpw_b = [load_weight_staged(ctx, f"mbb_xpw{k}", w["xpwT"][k * 128:(k + 1) * 128, :],
                                128, 64, stage[k % 2]) for k in range(16)]

    xi_sb = [ctx.sb(f"mbb_xi{i}", [128, 16 * (TQ + 3)], F32) for i in range(2)]
    xib = [[ctx.mkbuf(f"mbb_xib{i}_{k}") for k in range(16)] for i in range(2)]
    acc = ctx.sb("mbb_acc", [128, TQ], F32)
    xc_b = ctx.sb("mbb_xc", [128, 16 * TQ], BF16)
    xcb = [ctx.mkbuf(f"mbb_xcb{k}") for k in range(16)]
    ef = ctx.sb("mbb_ef", [128, TQ], F32)
    dt_sb = ctx.sb("mbb_dt", [128, 16 * TQ], BF16)
    dtb_ = [ctx.mkbuf(f"mbb_dtb{k}") for k in range(16)]
    B_sb = ctx.sb("mbb_B", [64, TQ], BF16)

    ps_main = [ctx.ps(f"mbb_ps{i}", [128, TQ], F32) for i in range(2)]
    ps_B = ctx.ps("mbb_psB", [64, TQ], F32)

    W = TQ + 3
    for b in range(BPC):
        for tci in range(NT):
            t0 = tci * TQ
            pp = (b * NT + tci) % 2
            xi_, xib_ = xi_sb[pp], xib[pp]
            v3 = xi_[:].rearrange("p (k w) -> p k w", k=16)
            if tci == 0:
                for k in range(16):
                    MEMSET(ctx, "vector", xi_[:, k * W:k * W + 3], 0.0, writes=[xib_[k]])
                DMA(ctx, "sync", v3[:, :, 3:W],
                    xi_in[b, :, t0:t0 + TQ].rearrange("(k p) t -> p k t", p=128),
                    reads=list(xib_), writes=list(xib_))
            else:
                DMA(ctx, "sync", v3[:, :, 0:W],
                    xi_in[b, :, t0 - 3:t0 + TQ].rearrange("(k p) t -> p k t", p=128),
                    reads=list(xib_), writes=list(xib_))
            for k in range(16):
                base = k * W
                TS(ctx, "vector", acc[:], xi_[:, base:base + TQ], conv_cols[:, 4 * k:4 * k + 1],
                   ALU.mult, reads=[xib_[k], conv_cols], writes=[acc])
                for j in (1, 2, 3):
                    STT(ctx, "vector", acc[:], xi_[:, base + j:base + j + TQ],
                        conv_cols[:, 4 * k + j:4 * k + j + 1], acc[:], ALU.mult, ALU.add,
                        reads=[xib_[k], conv_cols, acc], writes=[acc])
                ACT(ctx, xc_b[:, k * TQ:(k + 1) * TQ], acc[:], AF.Silu,
                    bias=convb_cols[:, k:k + 1], reads=[acc, convb_cols], writes=[xcb[k]])
            DMA(ctx, "gpsimd",
                xc_out[b, :, t0:t0 + TQ].rearrange("(k p) t -> p k t", p=128),
                xc_b[:].rearrange("p (k t) -> p k t", k=16), reads=list(xcb))
            for m in range(16):
                pm = ps_main[m % 2]
                for k in range(16):
                    MM(ctx, pm[:], dtw_b[k][:, m * 128:(m + 1) * 128],
                       xc_b[:, k * TQ:(k + 1) * TQ], start=(k == 0), stop=(k == 15),
                       reads=[dtw_b[k], xcb[k]], writes=[pm])
                ACT(ctx, ef[:], pm[:], AF.Exp, bias=dtb_cols[:, m:m + 1],
                    reads=[pm, dtb_cols], writes=[ef])
                ACT(ctx, dt_sb[:, m * TQ:(m + 1) * TQ], ef[:], AF.Ln, bias=1.0,
                    reads=[ef], writes=[dtb_[m]])
            DMA(ctx, "gpsimd",
                dt_out[b, :, t0:t0 + TQ].rearrange("(k p) t -> p k t", p=128),
                dt_sb[:].rearrange("p (k t) -> p k t", k=16), reads=list(dtb_))
            for k in range(16):
                MM(ctx, ps_B[:], xpw_b[k][:], xc_b[:, k * TQ:(k + 1) * TQ],
                   start=(k == 0), stop=(k == 15), reads=[xpw_b[k], xcb[k]], writes=[ps_B])
            COPY(ctx, "vector", B_sb[:], ps_B[:], reads=[ps_B], writes=[B_sb])
            DMA(ctx, "gpsimd", B_out[b, :, t0:t0 + TQ], B_sb[:], reads=[B_sb])



def emit_mamba_C(ctx, xc_in, dt_in, B_in, y_out):
    TC = 1024
    w = ctx.w

    A_sb = ctx.sb("mbc_A", [128, 1024], F32)
    A_stage = ctx.sb("mbc_Astg", [128, 1024], F32)
    DMA(ctx, "sync", A_stage[:], w["A_rs"], writes=[A_stage])
    ACT(ctx, A_sb[:], A_stage[:], AF.Exp, reads=[A_stage], writes=[A_sb])
    TS(ctx, "vector", A_sb[:], A_sb[:], -1.0, ALU.mult, reads=[A_sb], writes=[A_sb])
    D_cols = ctx.sb("mbc_D", [128, 16], F32)
    DMA(ctx, "sync", D_cols[:], w["D_cols"], writes=[D_cols])
    identb = make_ident(ctx, "mbc_id")[1]

    Bb = ctx.sb("mbc_Bb", [128, 32 * TC], BF16)
    h_state = ctx.sb("mbc_hstate", [128, 1024], F32)
    dt_t = [ctx.sb(f"mbc_dt{i}", [128, TC], BF16) for i in range(2)]
    xc_t = [ctx.sb(f"mbc_xc{i}", [128, TC], BF16) for i in range(2)]
    a_t = [ctx.sb(f"mbc_a{i}", [128, TC], F32) for i in range(2)]
    u_t = [ctx.sb(f"mbc_u{i}", [128, TC], BF16) for i in range(2)]
    h_t = [ctx.sb(f"mbc_h{i}", [128, TC], BF16) for i in range(2)]
    y_acc = ctx.sb("mbc_yacc", [128, 16 * TC], BF16)
    yab = [ctx.mkbuf(f"mbc_yab{k}") for k in range(16)]
    t2 = ctx.sb("mbc_t2", [128, TC], F32)
    yout_sb = ctx.sb("mbc_yout", [128, 16 * TC], BF16)
    youtb = [ctx.mkbuf(f"mbc_youtb{k}") for k in range(16)]

    ps_y = [ctx.ps(f"mbc_psy{i}", [128, TC], F32) for i in range(2)]

    for b in range(BPC):
        MEMSET(ctx, "vector", h_state[:], 0.0, writes=[h_state])
        for tci in range(2):
            t0 = tci * TC
            for half in range(2):
                DMA(ctx, "sync",
                    Bb[:].rearrange("p (n t) -> p n t", n=32),
                    bass.AP(B_in.tensor, (b * 64 + half * 32) * S + t0,
                            [[0, 128], [S, 32], [1, TC]]),
                    writes=[Bb])
                for db in range(16):
                    pp = db % 2
                    dt_, xc_, a_, u_, h_ = dt_t[pp], xc_t[pp], a_t[pp], u_t[pp], h_t[pp]
                    py = ps_y[pp]
                    DMA(ctx, "sync", dt_[:], dt_in[b, db * 128:(db + 1) * 128, t0:t0 + TC],
                        writes=[dt_])
                    DMA(ctx, "sync", xc_[:], xc_in[b, db * 128:(db + 1) * 128, t0:t0 + TC],
                        writes=[xc_])
                    for ni in range(32):
                        n = half * 32 + ni
                        col = db * 64 + n
                        ACT(ctx, a_[:], dt_[:], AF.Exp, scale=A_sb[:, col:col + 1],
                            reads=[dt_, A_sb], writes=[a_])
                        TT(ctx, "vector", u_[:], xc_[:], Bb[:, ni * TC:(ni + 1) * TC],
                           ALU.mult, reads=[xc_, Bb], writes=[u_])
                        SCAN(ctx, h_[:], a_[:], u_[:], h_state[:, col:col + 1],
                             reads=[a_, u_, h_state], writes=[h_])
                        COPY(ctx, "vector", h_state[:, col:col + 1], h_[:, TC - 1:TC],
                             reads=[h_], writes=[h_state])
                        MM(ctx, py[:, 0:512], identb[:], h_[:, 0:512],
                           start=(ni == 0), stop=(ni == 31), reads=[identb, h_], writes=[py])
                        MM(ctx, py[:, 512:1024], identb[:], h_[:, 512:1024],
                           start=(ni == 0), stop=(ni == 31), reads=[identb, h_], writes=[py])
                    if half == 0:
                        COPY(ctx, "vector", y_acc[:, db * TC:(db + 1) * TC], py[:],
                             reads=[py], writes=[yab[db]])
                    else:
                        STT(ctx, "vector", t2[:], xc_[:], D_cols[:, db:db + 1], py[:],
                            ALU.mult, ALU.add, reads=[xc_, D_cols, py], writes=[t2])
                        TT(ctx, "vector", yout_sb[:, db * TC:(db + 1) * TC], t2[:],
                           y_acc[:, db * TC:(db + 1) * TC], ALU.add,
                           reads=[t2, yab[db]], writes=[youtb[db]])
            DMA(ctx, "gpsimd",
                y_out[b, :, t0:t0 + TC].rearrange("(k p) t -> p k t", p=128),
                yout_sb[:].rearrange("p (k t) -> p k t", k=16), reads=list(youtb))



def emit_mamba_D(ctx, y_in, res_in, x_in, x_out, mrow_in, rrow_in):
    TQ = 512
    NT = S // TQ
    w = ctx.w
    ones_row_f = ctx.sb("mbd_ones_rowf", [1, 128], F32)
    MEMSET(ctx, "vector", ones_row_f[:], 1.0, writes=[ones_row_f])
    mrow_sb = ctx.sb("mbd_mrow", [1, S], F32)
    rrow_sb = ctx.sb("mbd_rrow", [1, S], F32)
    rp1 = ctx.sb("mbd_rp1", [1, TQ], F32)
    m2 = ctx.sb("mbd_m2", [1, TQ], F32)
    rp1B = ctx.sb("mbd_rp1B", [128, TQ], F32)
    m2B = ctx.sb("mbd_m2B", [128, TQ], F32)
    v1 = ctx.sb("mbd_v1", [128, TQ], F32)
    ps_bc = ctx.ps("mbd_ps_bc", [128, TQ], F32)

    stage = [ctx.sb(f"mbd_stg{i}", [128, 1024], F32) for i in range(2)]
    wop_b = [load_weight_staged(ctx, f"mbd_wop{k}", w["wopT"][k * 128:(k + 1) * 128, :],
                                128, 1024, stage[k % 2]) for k in range(16)]

    y_sb = [ctx.sb(f"mbd_y{i}", [128, 16 * TQ], BF16) for i in range(2)]
    res_sb = [ctx.sb(f"mbd_res{i}", [128, 16 * TQ], BF16) for i in range(2)]
    xf = [ctx.sb(f"mbd_xf{i}", [128, PD * TQ], F32) for i in range(2)]
    g_sb = ctx.sb("mbd_g", [128, 16 * TQ], BF16)
    yg_sb = ctx.sb("mbd_yg", [128, 16 * TQ], BF16)
    out_f = ctx.sb("mbd_out", [128, 4 * TQ], F32)
    outb = [ctx.mkbuf(f"mbd_outb{k}") for k in range(4)]
    ps_main = [ctx.ps(f"mbd_ps{i}", [128, TQ], F32) for i in range(2)]

    for b in range(BPC):
        DMA(ctx, "sync", mrow_sb[0:1, :], mrow_in[b:b + 1, :], writes=[mrow_sb])
        DMA(ctx, "sync", rrow_sb[0:1, :], rrow_in[b:b + 1, :], writes=[rrow_sb])
        for tci in range(NT):
            t0 = tci * TQ
            pp = (b * NT + tci) % 2
            y_, res_, xf_ = y_sb[pp], res_sb[pp], xf[pp]
            TS(ctx, "vector", rp1[:], rrow_sb[0:1, t0:t0 + TQ], 1.0, ALU.add,
               reads=[rrow_sb], writes=[rp1])
            TT(ctx, "vector", m2[:], mrow_sb[0:1, t0:t0 + TQ], rrow_sb[0:1, t0:t0 + TQ],
               ALU.mult, reads=[mrow_sb, rrow_sb], writes=[m2])
            MM(ctx, ps_bc[:], ones_row_f[:], rp1[:], start=True, stop=True,
               reads=[rp1, ones_row_f], writes=[ps_bc])
            COPY(ctx, "vector", rp1B[:], ps_bc[:], reads=[ps_bc], writes=[rp1B])
            MM(ctx, ps_bc[:], ones_row_f[:], m2[:], start=True, stop=True,
               reads=[m2, ones_row_f], writes=[ps_bc])
            COPY(ctx, "vector", m2B[:], ps_bc[:], reads=[ps_bc], writes=[m2B])
            DMA(ctx, "sync", y_[:].rearrange("p (k t) -> p k t", k=16),
                y_in[b, :, t0:t0 + TQ].rearrange("(k p) t -> p k t", p=128), writes=[y_])
            DMA(ctx, "sync", res_[:].rearrange("p (k t) -> p k t", k=16),
                res_in[b, :, t0:t0 + TQ].rearrange("(k p) t -> p k t", p=128), writes=[res_])
            DMA(ctx, "sync", xf_[:].rearrange("p (k t) -> p k t", k=PD),
                x_in[b, :, t0:t0 + TQ].rearrange("(k p) t -> p k t", p=128), writes=[xf_])
            ACT(ctx, g_sb[:], res_[:], AF.Silu, reads=[res_], writes=[g_sb])
            TT(ctx, "vector", yg_sb[:], y_[:], g_sb[:], ALU.mult,
               reads=[y_, g_sb], writes=[yg_sb])
            for m in range(PD):
                pm = ps_main[m % 2]
                for k in range(16):
                    MM(ctx, pm[:], wop_b[k][:, m * 128:(m + 1) * 128],
                       yg_sb[:, k * TQ:(k + 1) * TQ], start=(k == 0), stop=(k == 15),
                       reads=[wop_b[k], yg_sb], writes=[pm])
                TT(ctx, "vector", v1[:], xf_[:, m * TQ:(m + 1) * TQ], rp1B[:], ALU.mult,
                   reads=[xf_, rp1B], writes=[v1])
                TT(ctx, "vector", v1[:], v1[:], m2B[:], ALU.subtract,
                   reads=[v1, m2B], writes=[v1])
                TT(ctx, "vector", out_f[:, (m % 4) * TQ:(m % 4 + 1) * TQ], v1[:], pm[:],
                   ALU.add, reads=[v1, pm], writes=[outb[m % 4]])
                if m % 4 == 3:
                    DMA(ctx, "gpsimd",
                        x_out[b, (m - 3) * 128:(m + 1) * 128, t0:t0 + TQ].rearrange("(k p) t -> p k t", p=128),
                        out_f[:].rearrange("p (k t) -> p k t", k=4), reads=list(outb))



def mamba_weight_arrays(inp):
    ipw = np.asarray(inp["L0_attn_in_proj_w"])
    return {
        "wipT": np.ascontiguousarray(ipw.T),
        "normw_cols": np.ascontiguousarray(np.asarray(inp["L0_attn_norm_w"]).reshape(PD, 128).T),
        "dtwT": np.ascontiguousarray(np.asarray(inp["L0_attn_dt_w"]).T),
        "xpwT": np.ascontiguousarray(np.asarray(inp["L0_attn_x_proj_w"])[64:128].T),
        "conv_cols": np.ascontiguousarray(
            np.transpose(np.asarray(inp["L0_attn_conv_w"])[:, 0, :].reshape(16, 128, 4), (1, 0, 2)).reshape(128, 64)),
        "convb_cols": np.ascontiguousarray(np.asarray(inp["L0_attn_conv_b"]).reshape(16, 128).T),
        "dtb_cols": np.ascontiguousarray(np.asarray(inp["L0_attn_dt_b"]).reshape(16, 128).T),
        "A_rs": np.ascontiguousarray(
            np.transpose(np.asarray(inp["L0_attn_A_log"]).reshape(16, 128, 64), (1, 0, 2)).reshape(128, 1024)),
        "D_cols": np.ascontiguousarray(np.asarray(inp["L0_attn_D_param"]).reshape(16, 128).T),
        "wopT": np.ascontiguousarray(np.asarray(inp["L0_attn_out_proj_w"]).T),
    }


def build_mba_program():
    nc = bass.Bass("TRN2", target_bir_lowering=False)
    ctx = Ctx(nc)
    x_in = nc.dram_tensor("x_fm", [BPC, D, S], F32, kind="ExternalInput").ap()
    xi_out = nc.dram_tensor("xi_fm", [BPC, 2048, S], F32, kind="ExternalOutput").ap()
    res_out = nc.dram_tensor("res_fm", [BPC, 2048, S], BF16, kind="ExternalOutput").ap()
    mrow_out = nc.dram_tensor("mrow", [BPC, S], F32, kind="ExternalOutput").ap()
    rrow_out = nc.dram_tensor("rrow", [BPC, S], F32, kind="ExternalOutput").ap()
    ctx.bufmap["mrow"] = ctx.mkbuf("mrow"); ctx.bufmap["rrow"] = ctx.mkbuf("rrow")
    ctx.w = {
        "wipT": nc.dram_tensor("wipT", [D, 4096], F32, kind="ExternalInput").ap(),
        "normw_cols": nc.dram_tensor("normw_cols", [128, PD], F32, kind="ExternalInput").ap(),
    }
    emit_mamba_A(ctx, x_in, xi_out, res_out, mrow_out, rrow_out)
    ctx.finalize()
    return nc


def build_mbb_program():
    nc = bass.Bass("TRN2", target_bir_lowering=False)
    ctx = Ctx(nc)
    xi_in = nc.dram_tensor("xi_fm", [BPC, 2048, S], F32, kind="ExternalInput").ap()
    xc_out = nc.dram_tensor("xc_fm", [BPC, 2048, S], BF16, kind="ExternalOutput").ap()
    dt_out = nc.dram_tensor("dt_fm", [BPC, 2048, S], BF16, kind="ExternalOutput").ap()
    B_out = nc.dram_tensor("B_fm", [BPC, 64, S], BF16, kind="ExternalOutput").ap()
    ctx.w = {
        "dtwT": nc.dram_tensor("dtwT", [2048, 2048], F32, kind="ExternalInput").ap(),
        "xpwT": nc.dram_tensor("xpwT", [2048, 64], F32, kind="ExternalInput").ap(),
        "conv_cols": nc.dram_tensor("conv_cols", [128, 64], F32, kind="ExternalInput").ap(),
        "convb_cols": nc.dram_tensor("convb_cols", [128, 16], F32, kind="ExternalInput").ap(),
        "dtb_cols": nc.dram_tensor("dtb_cols", [128, 16], F32, kind="ExternalInput").ap(),
    }
    emit_mamba_B(ctx, xi_in, xc_out, dt_out, B_out)
    ctx.finalize()
    return nc


def build_mbc_program():
    nc = bass.Bass("TRN2", target_bir_lowering=False)
    ctx = Ctx(nc)
    xc_in = nc.dram_tensor("xc_fm", [BPC, 2048, S], BF16, kind="ExternalInput").ap()
    dt_in = nc.dram_tensor("dt_fm", [BPC, 2048, S], BF16, kind="ExternalInput").ap()
    B_in = nc.dram_tensor("B_fm", [BPC, 64, S], BF16, kind="ExternalInput").ap()
    y_out = nc.dram_tensor("y_fm", [BPC, 2048, S], BF16, kind="ExternalOutput").ap()
    ctx.w = {
        "A_rs": nc.dram_tensor("A_rs", [128, 1024], F32, kind="ExternalInput").ap(),
        "D_cols": nc.dram_tensor("D_cols", [128, 16], F32, kind="ExternalInput").ap(),
    }
    emit_mamba_C(ctx, xc_in, dt_in, B_in, y_out)
    ctx.finalize()
    return nc


def build_mbd_program():
    nc = bass.Bass("TRN2", target_bir_lowering=False)
    ctx = Ctx(nc)
    y_in = nc.dram_tensor("y_fm", [BPC, 2048, S], BF16, kind="ExternalInput").ap()
    res_in = nc.dram_tensor("res_fm", [BPC, 2048, S], BF16, kind="ExternalInput").ap()
    x_in = nc.dram_tensor("x_fm", [BPC, D, S], F32, kind="ExternalInput").ap()
    x_out = nc.dram_tensor("xo_fm", [BPC, D, S], F32, kind="ExternalOutput").ap()
    mrow_in = nc.dram_tensor("mrow", [BPC, S], F32, kind="ExternalInput").ap()
    rrow_in = nc.dram_tensor("rrow", [BPC, S], F32, kind="ExternalInput").ap()
    ctx.bufmap["mrow"] = ctx.mkbuf("mrow"); ctx.bufmap["rrow"] = ctx.mkbuf("rrow")
    ctx.w = {
        "wopT": nc.dram_tensor("wopT", [2048, D], F32, kind="ExternalInput").ap(),
    }
    emit_mamba_D(ctx, y_in, res_in, x_in, x_out, mrow_in, rrow_in)
    ctx.finalize()
    return nc


def run_mamba(x_fm_percore, inp, debug=False):
    warr = mamba_weight_arrays(inp)
    nc = build_mba_program()
    in_maps = [dict(x_fm=x_fm_percore[c], wipT=warr["wipT"], normw_cols=warr["normw_cols"])
               for c in range(NCORES)]
    resA = run_bass_kernel_spmd(nc, in_maps, core_ids=list(range(NCORES))).results
    nc = build_mbb_program()
    in_maps = [dict(xi_fm=resA[c]["xi_fm"], dtwT=warr["dtwT"], xpwT=warr["xpwT"],
                    conv_cols=warr["conv_cols"], convb_cols=warr["convb_cols"],
                    dtb_cols=warr["dtb_cols"]) for c in range(NCORES)]
    resB = run_bass_kernel_spmd(nc, in_maps, core_ids=list(range(NCORES))).results
    nc = build_mbc_program()
    in_maps = [dict(xc_fm=resB[c]["xc_fm"], dt_fm=resB[c]["dt_fm"], B_fm=resB[c]["B_fm"],
                    A_rs=warr["A_rs"], D_cols=warr["D_cols"]) for c in range(NCORES)]
    resC = run_bass_kernel_spmd(nc, in_maps, core_ids=list(range(NCORES))).results
    nc = build_mbd_program()
    in_maps = [dict(y_fm=resC[c]["y_fm"], res_fm=resA[c]["res_fm"], x_fm=x_fm_percore[c],
                    mrow=resA[c]["mrow"], rrow=resA[c]["rrow"],
                    wopT=warr["wopT"]) for c in range(NCORES)]
    resD = run_bass_kernel_spmd(nc, in_maps, core_ids=list(range(NCORES))).results
    out = [resD[c]["xo_fm"] for c in range(NCORES)]
    if debug:
        return out, dict(A=resA, B=resB, C=resC)
    return out



def emit_mla(ctx, x_in, xt1_out, dbg=None):
    TQ = 512
    NT = S // TQ
    w = ctx.w

    ones_col_b = ctx.sb("mla_ones_col", [128, 1], BF16)
    ones_row_f = ctx.sb("mla_ones_rowf", [1, 128], F32)
    MEMSET(ctx, "vector", ones_col_b[:], 1.0, writes=[ones_col_b])
    MEMSET(ctx, "vector", ones_row_f[:], 1.0, writes=[ones_row_f])

    lnw_cols = ctx.sb("mla_lnw", [128, PD], F32)
    DMA(ctx, "sync", lnw_cols[:], w["lnw1_cols"], writes=[lnw_cols])
    stage0 = ctx.sb("mla_stg0", [128, 512], F32)
    stage = [stage0, stage0]
    wkvd_b = [load_weight_staged(ctx, f"mla_wkvd{k}", w["wkvd"][k * 128:(k + 1) * 128, :],
                                 128, 256, stage[k % 2], scale_col=lnw_cols[:, k:k + 1],
                                 scale_col_reads=[lnw_cols]) for k in range(PD)]
    wqd_b = [load_weight_staged(ctx, f"mla_wqd{k}", w["wqd"][k * 128:(k + 1) * 128, :],
                                128, 256, stage[k % 2], scale_col=lnw_cols[:, k:k + 1],
                                scale_col_reads=[lnw_cols]) for k in range(PD)]
    w16 = ctx.sb("mla_w16", [128, 16384], BF16)
    negln = ctx.sb("mla_negln", [128, PD], F32)
    TS(ctx, "vector", negln[:], lnw_cols[:], -1.0, ALU.mult, reads=[lnw_cols], writes=[negln])
    wropek_b = [w16[:, k * 512:(k + 1) * 512] for k in range(PD)]
    wropekr_b = [w16[:, 4096 + k * 512:4096 + (k + 1) * 512] for k in range(PD)]
    wo_b = [w16[:, k * 1024:(k + 1) * 1024] for k in range(PD)]
    w16buf = ctx.bufmap[ctx.prefix + "mla_w16"]

    def load_ropek(ctx=ctx):
        for k in range(PD):
            convert_into(ctx, wropek_b[k], w16buf, w["wropek"][k * 128:(k + 1) * 128, :],
                         128, 512, stage[0], scale_col=lnw_cols[:, k:k + 1],
                         scale_col_reads=[lnw_cols])
            DMA(ctx, "sync", stage[0][:, 0:512], w["wropek_perm"][k * 128:(k + 1) * 128, :],
                writes=[stage[0]])
            v_in = stage[0][:].rearrange("p (h c) -> p h c", c=32)
            v_out2 = wropekr_b[k].rearrange("p (h c) -> p h c", c=32)
            TS(ctx, "vector", v_out2[:, :, 0:16], v_in[:, 0:16, 0:16] if False else v_in[:, :, 0:16],
               negln[:, k:k + 1], ALU.mult, reads=[stage[0], negln], writes=[w16buf])
            TS(ctx, "vector", v_out2[:, :, 16:32], v_in[:, :, 16:32], lnw_cols[:, k:k + 1],
               ALU.mult, reads=[stage[0], lnw_cols], writes=[w16buf])

    def load_wo(ctx=ctx):
        for k in range(PD):
            for cc in range(2):
                convert_into(ctx, wo_b[k][:, cc * 512:(cc + 1) * 512], w16buf,
                             w["wo"][k * 128:(k + 1) * 128, cc * 512:(cc + 1) * 512],
                             128, 512, stage[0])
    wku_b = [load_weight_staged(ctx, f"mla_wku{k}", w["wku"][k * 128:(k + 1) * 128, :],
                                128, 512, stage[k % 2]) for k in range(2)]
    wqu_b = [load_weight_staged(ctx, f"mla_wqu{k}", w["wqu"][k * 128:(k + 1) * 128, :],
                                128, 512, stage[k % 2], scale=0.125) for k in range(2)]
    wvu_b = [load_weight_staged(ctx, f"mla_wvu{k}", w["wvu"][k * 128:(k + 1) * 128, :],
                                128, 1024, stage[k % 2]) for k in range(2)]
    wropeq_b = [load_weight_staged(ctx, f"mla_wrq{k}", w["wropeq"][k * 128:(k + 1) * 128, :],
                                   128, 512, stage[k % 2], scale=0.125) for k in range(2)]
    wropeqr_b = []
    for k in range(2):
        t = ctx.sb(f"mla_wrqr{k}", [128, 512], BF16)
        DMA(ctx, "sync", stage[k % 2][:, 0:512], w["wropeq_perm"][k * 128:(k + 1) * 128, :],
            writes=[stage[k % 2]])
        v_in = stage[k % 2][:].rearrange("p (h c) -> p h c", c=32)
        v_out = t[:].rearrange("p (h c) -> p h c", c=32)
        TS(ctx, "vector", v_out[:, :, 0:16], v_in[:, :, 0:16], -0.125, ALU.mult,
           reads=[stage[k % 2]], writes=[t])
        TS(ctx, "vector", v_out[:, :, 16:32], v_in[:, :, 16:32], 0.125, ALU.mult,
           reads=[stage[k % 2]], writes=[t])
        wropeqr_b.append(t)


    cosT = ctx.sb("mla_cosT", [32, S], BF16)
    sinT = ctx.sb("mla_sinT", [32, S], BF16)
    for (dst, src) in ((cosT, "cosT"), (sinT, "sinT")):
        for cc in range(4):
            DMA(ctx, "sync", stage0[0:32, 0:512], w[src][:, cc * 512:(cc + 1) * 512],
                writes=[stage0])
            COPY(ctx, "vector", dst[:, cc * 512:(cc + 1) * 512], stage0[0:32, 0:512],
                 reads=[stage0], writes=[dst])
    idf_mla, idb_mla = make_ident(ctx, "mla_id")
    ctx.identb = idb_mla
    ident64 = ctx.sb("mla_ident64", [128, 64], BF16)
    COPY(ctx, "vector", ident64[0:64, :], idb_mla[0:64, 0:64], reads=[idb_mla], writes=[ident64])
    COPY(ctx, "vector", ident64[64:128, :], idb_mla[0:64, 0:64], reads=[idb_mla], writes=[ident64])
    ctx.ident64 = ident64
    masks = ctx.sb("mla_masks", [128, 4 * TQ], BF16)
    mask_stage = ctx.sb("mla_mask_stage", [128, TQ], F32)
    for i in range(4):
        DMA(ctx, "sync", mask_stage[:], w["maskneg"][i], writes=[mask_stage])
        COPY(ctx, "vector", masks[:, i * TQ:(i + 1) * TQ], mask_stage[:],
             reads=[mask_stage], writes=[masks])

    ps_mean = ctx.ps("mla_ps_mean", [1, TQ], F32)
    ps_sq = ctx.ps("mla_ps_sq", [1, TQ], F32)
    wsum_kvd = emit_wsum_neg(ctx, "mla_ws_kvd", wkvd_b, 256, ones_col_b, ps_mean)
    wsum_qd = emit_wsum_neg(ctx, "mla_ws_qd", wqd_b, 256, ones_col_b, ps_mean)
    wsum_rk = ctx.sb("mla_ws_rk", [1, 512], BF16)
    wsum_rkr = ctx.sb("mla_ws_rkr", [1, 512], BF16)

    def calc_rope_wsums(ctx=ctx):
        for (wchunks, dst) in ((wropek_b, wsum_rk), (wropekr_b, wsum_rkr)):
            for k in range(PD):
                MM(ctx, ps_mean[0:1, 0:512], ones_col_b[:], wchunks[k],
                   start=(k == 0), stop=(k == PD - 1), reads=[w16buf, ones_col_b],
                   writes=[ps_mean])
            TS(ctx, "vector", dst[:], ps_mean[0:1, 0:512], -1.0, ALU.mult,
               reads=[ps_mean], writes=[dst])
    ps_util = ctx.ps("mla_ps_util", [128, TQ], F32)
    ps_main = ctx.ps("mla_ps_main", [128, TQ], F32)
    ps_rot = ctx.ps("mla_ps_rot", [64, TQ], F32)
    ps_tr = ctx.ps("mla_ps_tr", [128, 128], BF16)
    ps_St = ctx.ps("mla_ps_St", [128, TQ], F32)
    ps_o = ctx.ps("mla_ps_o", [64, TQ], F32)

    mean_f = ctx.sb("mla_mean_f", [1, TQ], F32)
    sq_f = ctx.sb("mla_sq_f", [1, TQ], F32)
    var_f = ctx.sb("mla_var_f", [1, TQ], F32)
    rstd2 = ctx.sb("mla_rstd2", [1, TQ], F32)
    meansave_b = ctx.sb("mla_meansave_b", [1, S], BF16)
    rstdsave = ctx.sb("mla_rstdsave", [1, S], BF16)
    s1row = ctx.sb("mla_s1row", [1, 512], F32)
    sB1 = ctx.sb("mla_sB1", [128, TQ], F32)
    csK = ctx.sb("mla_csK", [32, 512], BF16)
    ssK = ctx.sb("mla_ssK", [32, 512], BF16)
    zrecB = ctx.sb("mla_zrecB", [64, TQ], F32)

    xf = [ctx.sb("mla_xf0", [128, 4 * TQ], F32)]
    xfb = [[ctx.mkbuf(f"mla_xfb0_{k}") for k in range(4)]]
    xb = ctx.sb("mla_xb", [128, PD * TQ], BF16)
    x2s = ctx.sb("mla_x2s", [128, TQ], BF16)
    kvlat = ctx.sb("mla_kvlat", [128, 2 * TQ], BF16)
    kvb = [ctx.mkbuf(f"mla_kvb{i}") for i in range(2)]
    qlat = ctx.sb("mla_qlat", [128, 2 * TQ], BF16)
    qlb = [ctx.mkbuf(f"mla_qlb{i}") for i in range(2)]
    qlat_hbm = ctx.dram_i("mla_qlat_hbm", [BPC, 256, S], BF16)
    qlhb = ctx.bufmap[ctx.prefix + "mla_qlat_hbm"]
    kk = ctx.sb("mla_kk", [128, 8 * S], BF16)
    kkb = [ctx.mkbuf(f"mla_kkb{g}") for g in range(8)]
    vT = ctx.sb("mla_vT", [128, 256 * 64], BF16)
    vTb = ctx.mkbuf("mla_vTb")
    vtmp = ctx.sb("mla_vtmp", [128, TQ], BF16)
    ta = ctx.sb("mla_ta", [32, TQ], F32)
    tb = ctx.sb("mla_tb", [32, TQ], F32)
    qq = ctx.sb("mla_qq", [128, TQ], BF16)
    qqb = [ctx.mkbuf("mla_qqb")]
    probs = [ctx.sb(f"mla_probs{i}", [128, TQ], BF16) for i in range(2)]
    stmask = ctx.sb("mla_stmask", [128, TQ], F32)
    o_qc = ctx.sb("mla_oqc", [128, 8 * TQ], BF16)
    oqb = [ctx.mkbuf(f"mla_oqb{g}") for g in range(8)]
    v1t = ctx.sb("mla_v1t", [128, TQ], F32)
    out_f = ctx.sb("mla_outf", [128, TQ], F32)
    outb = [ctx.mkbuf("mla_outb0")]

    for b in range(BPC):
        load_ropek()
        calc_rope_wsums()
        for tci in range(NT):
            t0 = tci * TQ
            xf_, xfb_ = xf[0], xfb[0]
            for hh in range(2):
                DMA(ctx, "sync", xf_[:].rearrange("p (k t) -> p k t", k=4),
                    x_in[b, hh * 512:(hh + 1) * 512, t0:t0 + TQ].rearrange("(k p) t -> p k t", p=128),
                    writes=list(xfb_))
                for k in range(4):
                    kk4 = hh * 4 + k
                    COPY(ctx, "vector", xb[:, kk4 * TQ:(kk4 + 1) * TQ],
                         xf_[:, k * TQ:(k + 1) * TQ], reads=[xfb_[k]], writes=[xb])
                    ACT(ctx, x2s[:], xf_[:, k * TQ:(k + 1) * TQ], AF.Square,
                        reads=[xfb_[k]], writes=[x2s])
                    MM(ctx, ps_sq[0:1, :], ones_col_b[:], x2s[:],
                       start=(kk4 == 0), stop=(kk4 == PD - 1), reads=[x2s, ones_col_b],
                       writes=[ps_sq])
            for k in range(PD):
                MM(ctx, ps_mean[0:1, :], ones_col_b[:], xb[:, k * TQ:(k + 1) * TQ],
                   start=(k == 0), stop=(k == PD - 1), reads=[xb, ones_col_b], writes=[ps_mean])
            ACT(ctx, mean_f[:], ps_mean[0:1, :], AF.Copy, scale=1.0 / D,
                reads=[ps_mean], writes=[mean_f])
            COPY(ctx, "vector", meansave_b[:, t0:t0 + TQ], mean_f[:], reads=[mean_f], writes=[meansave_b])
            ACT(ctx, sq_f[:], mean_f[:], AF.Square, reads=[mean_f], writes=[sq_f])
            STT(ctx, "vector", var_f[:], ps_sq[0:1, :], 1.0 / D, sq_f[:],
                ALU.mult, ALU.subtract, reads=[ps_sq, sq_f], writes=[var_f])
            ACT(ctx, sq_f[:], var_f[:], AF.Sqrt, bias=1e-5, reads=[var_f], writes=[sq_f])
            RECIP(ctx, mean_f[:], sq_f[:], reads=[sq_f], writes=[mean_f])
            COPY(ctx, "vector", rstdsave[:, t0:t0 + TQ], mean_f[:], reads=[mean_f], writes=[rstdsave])
            ACT(ctx, rstd2[:], mean_f[:], AF.Square, reads=[mean_f], writes=[rstd2])
            TT(ctx, "vector", rstd2[:], var_f[:], rstd2[:], ALU.mult, reads=[var_f, rstd2], writes=[rstd2])
            ACT(ctx, rstd2[:], rstd2[:], AF.Sqrt, reads=[rstd2], writes=[rstd2])
            TS(ctx, "vector", rstd2[:], rstd2[:], 1e-5, ALU.add, reads=[rstd2], writes=[rstd2])
            RECIP(ctx, rstd2[:], rstd2[:], reads=[rstd2], writes=[rstd2])
            TT(ctx, "vector", s1row[:], mean_f[:], rstd2[:],
               ALU.mult, reads=[mean_f, rstd2], writes=[s1row])
            MM(ctx, ps_util[:], ones_row_f[:], s1row[:], start=True, stop=True,
               reads=[s1row, ones_row_f], writes=[ps_util])
            COPY(ctx, "vector", sB1[:], ps_util[:], reads=[ps_util], writes=[sB1])
            TT(ctx, "vector", csK[:], cosT[:, t0:t0 + TQ], sB1[0:32, :],
               ALU.mult, reads=[cosT, sB1], writes=[csK])
            TT(ctx, "vector", ssK[:], sinT[:, t0:t0 + TQ], sB1[0:32, :],
               ALU.mult, reads=[sinT, sB1], writes=[ssK])
            for m in range(2):
                for k in range(PD):
                    MM(ctx, ps_main[:], wkvd_b[k][:, m * 128:(m + 1) * 128],
                       xb[:, k * TQ:(k + 1) * TQ], start=(k == 0), stop=False,
                       reads=[wkvd_b[k], xb], writes=[ps_main])
                MM(ctx, ps_main[:], wsum_kvd[:, m * 128:(m + 1) * 128],
                   meansave_b[:, t0:t0 + TQ], start=False, stop=True,
                   reads=[wsum_kvd, meansave_b], writes=[ps_main])
                TT(ctx, "vector", kvlat[:, m * TQ:(m + 1) * TQ], ps_main[:], sB1[:],
                   ALU.mult, reads=[ps_main, sB1], writes=[kvb[m]])
                for k in range(PD):
                    MM(ctx, ps_main[:], wqd_b[k][:, m * 128:(m + 1) * 128],
                       xb[:, k * TQ:(k + 1) * TQ], start=(k == 0), stop=False,
                       reads=[wqd_b[k], xb], writes=[ps_main])
                MM(ctx, ps_main[:], wsum_qd[:, m * 128:(m + 1) * 128],
                   meansave_b[:, t0:t0 + TQ], start=False, stop=True,
                   reads=[wsum_qd, meansave_b], writes=[ps_main])
                TT(ctx, "vector", qlat[:, m * TQ:(m + 1) * TQ], ps_main[:], sB1[:],
                   ALU.mult, reads=[ps_main, sB1], writes=[qlb[m]])
                DMA(ctx, "gpsimd", qlat_hbm[b, m * 128:(m + 1) * 128, t0:t0 + TQ],
                    qlat[:, m * TQ:(m + 1) * TQ], reads=[qlb[m]])
            for g in range(8):
                h0, h1 = 2 * g, 2 * g + 1
                for (base, hh) in ((0, h0), (64, h1)):
                    for kc in range(2):
                        MM(ctx, ps_main[base:base + 32, :],
                           wku_b[kc][:, hh * 32:(hh + 1) * 32],
                           kvlat[:, kc * TQ:(kc + 1) * TQ], start=(kc == 0),
                           stop=False, reads=[wku_b[kc], kvb[kc]], writes=[ps_main],
                           tp=(0, base))
                for (base, hh) in ((32, h0), (96, h1)):
                    for k in range(PD):
                        MM(ctx, ps_main[base:base + 32, :],
                           wropek_b[k][:, hh * 32:(hh + 1) * 32],
                           xb[:, k * TQ:(k + 1) * TQ], start=(k == 0), stop=False,
                           reads=[w16buf, xb], writes=[ps_main], tp=(0, base))
                    MM(ctx, ps_main[base:base + 32, :],
                       wsum_rk[:, hh * 32:(hh + 1) * 32], meansave_b[:, t0:t0 + TQ],
                       start=False, stop=(base == 96), reads=[wsum_rk, meansave_b],
                       writes=[ps_main], tp=(0, base))
                for (base, hh) in ((0, h0), (32, h1)):
                    for k in range(PD):
                        MM(ctx, ps_rot[base:base + 32, :],
                           wropekr_b[k][:, hh * 32:(hh + 1) * 32],
                           xb[:, k * TQ:(k + 1) * TQ], start=(k == 0),
                           stop=False, reads=[w16buf, xb], writes=[ps_rot], tp=(0, base))
                    MM(ctx, ps_rot[base:base + 32, :],
                       wsum_rkr[:, hh * 32:(hh + 1) * 32], meansave_b[:, t0:t0 + TQ],
                       start=False, stop=(base == 32), reads=[wsum_rkr, meansave_b],
                       writes=[ps_rot], tp=(0, base))
                col = g * S + t0
                COPY(ctx, "vector", kk[0:32, col:col + TQ], ps_main[0:32, :],
                     reads=[ps_main], writes=[kkb[g]])
                COPY(ctx, "vector", kk[64:96, col:col + TQ], ps_main[64:96, :],
                     reads=[ps_main], writes=[kkb[g]])
                for (pbase, rbase, kbase) in ((32, 0, 32), (96, 32, 96)):
                    TT(ctx, "vector", ta[:], ps_main[pbase:pbase + 32, :], csK[:],
                       ALU.mult, reads=[ps_main, csK], writes=[ta])
                    TT(ctx, "vector", tb[:], ps_rot[rbase:rbase + 32, :], ssK[:],
                       ALU.mult, reads=[ps_rot, ssK], writes=[tb])
                    TT(ctx, "vector", kk[kbase:kbase + 32, col:col + TQ], ta[:], tb[:],
                       ALU.add, reads=[ta, tb], writes=[kkb[g]])
                for kc in range(2):
                    MM(ctx, ps_main[:], wvu_b[kc][:, g * 128:(g + 1) * 128],
                       kvlat[:, kc * TQ:(kc + 1) * TQ], start=(kc == 0), stop=(kc == 1),
                       reads=[wvu_b[kc], kvb[kc]], writes=[ps_main])
                COPY(ctx, "vector", vtmp[:], ps_main[:], reads=[ps_main], writes=[vtmp])
                for j in range(4):
                    kb = tci * 4 + j
                    for (vb, hh) in ((0, h0), (64, h1)):
                        TRANS(ctx, ps_tr[:, 0:64], vtmp[vb:vb + 64, j * 128:(j + 1) * 128],
                              ctx.ident64[vb:vb + 64, :], reads=[vtmp, ctx.ident64], writes=[ps_tr])
                        COPY(ctx, "vector", vT[:, (kb * 16 + hh) * 64:(kb * 16 + hh) * 64 + 64],
                             ps_tr[:, 0:64], reads=[ps_tr], writes=[vTb])
        if dbg is not None:
            for g in range(8):
                DMA(ctx, "gpsimd", dbg["kk"][b, g * 128:(g + 1) * 128, :],
                    kk[:, g * S:(g + 1) * S], reads=[kkb[g]])
            DMA(ctx, "gpsimd", dbg["vT"][b], vT[:], reads=[vTb])
        load_wo()
        for qci in range(NT):
            q0 = qci * TQ
            for g in range(8):
                h0, h1 = 2 * g, 2 * g + 1
                for (base, hh) in ((0, h0), (64, h1)):
                    for kc in range(2):
                        MM(ctx, ps_main[base:base + 32, :],
                           wqu_b[kc][:, hh * 32:(hh + 1) * 32],
                           qlat[:, kc * TQ:(kc + 1) * TQ],
                           start=(kc == 0), stop=False,
                           reads=[wqu_b[kc], qlb[kc]], writes=[ps_main], tp=(0, base))
                for (base, hh) in ((32, h0), (96, h1)):
                    for kc in range(2):
                        MM(ctx, ps_main[base:base + 32, :],
                           wropeq_b[kc][:, hh * 32:(hh + 1) * 32],
                           qlat[:, kc * TQ:(kc + 1) * TQ],
                           start=(kc == 0), stop=(base == 96 and kc == 1),
                           reads=[wropeq_b[kc], qlb[kc]], writes=[ps_main], tp=(0, base))
                for (base, hh) in ((0, h0), (32, h1)):
                    for kc in range(2):
                        MM(ctx, ps_rot[base:base + 32, :],
                           wropeqr_b[kc][:, hh * 32:(hh + 1) * 32],
                           qlat[:, kc * TQ:(kc + 1) * TQ],
                           start=(kc == 0), stop=(base == 32 and kc == 1),
                           reads=[wropeqr_b[kc], qlb[kc]], writes=[ps_rot], tp=(0, base))
                COPY(ctx, "vector", qq[0:32, :], ps_main[0:32, :],
                     reads=[ps_main], writes=[qqb[0]])
                COPY(ctx, "vector", qq[64:96, :], ps_main[64:96, :],
                     reads=[ps_main], writes=[qqb[0]])
                for (pbase, rbase, kbase) in ((32, 0, 32), (96, 32, 96)):
                    TT(ctx, "vector", ta[:], ps_main[pbase:pbase + 32, :], cosT[:, q0:q0 + TQ],
                       ALU.mult, reads=[ps_main, cosT], writes=[ta])
                    TT(ctx, "vector", tb[:], ps_rot[rbase:rbase + 32, :], sinT[:, q0:q0 + TQ],
                       ALU.mult, reads=[ps_rot, sinT], writes=[tb])
                    TT(ctx, "vector", qq[kbase:kbase + 32, :], ta[:], tb[:],
                       ALU.add, reads=[ta, tb], writes=[qqb[0]])
                for h in (2 * g, 2 * g + 1):
                    vb = (h % 2) * 64
                    nkb = qci * 4 + 4
                    for kb in range(nkb):
                        MM(ctx, ps_St[:], kk[vb:vb + 64, g * S + kb * 128:g * S + (kb + 1) * 128],
                           qq[vb:vb + 64, :], start=True, stop=True,
                           reads=[kkb[g], qqb[0]], writes=[ps_St])
                        pslot = probs[kb % 2]
                        if kb >= qci * 4:
                            ii = kb - qci * 4
                            TT(ctx, "vector", stmask[:], ps_St[:], masks[:, ii * TQ:(ii + 1) * TQ],
                               ALU.add, reads=[ps_St, masks], writes=[stmask])
                            ACT(ctx, pslot[:], stmask[:], AF.Exp, reads=[stmask], writes=[pslot])
                        else:
                            ACT(ctx, pslot[:], ps_St[:], AF.Exp, reads=[ps_St], writes=[pslot])
                        MM(ctx, ps_util[0:1, :], ones_col_b[:], pslot[:],
                           start=(kb == 0), stop=(kb == nkb - 1), reads=[pslot, ones_col_b],
                           writes=[ps_util])
                        MM(ctx, ps_o[:], vT[:, (kb * 16 + h) * 64:(kb * 16 + h) * 64 + 64],
                           pslot[:], start=(kb == 0), stop=(kb == nkb - 1),
                           reads=[vTb, pslot], writes=[ps_o])
                    COPY(ctx, "vector", mean_f[:], ps_util[0:1, :], reads=[ps_util], writes=[mean_f])
                    RECIP(ctx, var_f[:], mean_f[:], reads=[mean_f], writes=[var_f])
                    MM(ctx, ps_util[0:64, :], ones_row_f[0:1, 0:64], var_f[:], start=True, stop=True,
                       reads=[var_f, ones_row_f], writes=[ps_util])
                    COPY(ctx, "vector", zrecB[:], ps_util[0:64, :], reads=[ps_util], writes=[zrecB])
                    TT(ctx, "vector", o_qc[vb:vb + 64, g * TQ:(g + 1) * TQ], ps_o[:], zrecB[:],
                       ALU.mult, reads=[ps_o, zrecB], writes=[oqb[g]])
            xf_, xfb_ = xf[0], xfb[0]
            COPY(ctx, "vector", var_f[:], rstdsave[:, q0:q0 + TQ], reads=[rstdsave], writes=[var_f])
            MM(ctx, ps_util[:], ones_row_f[:], var_f[:], start=True, stop=True,
               reads=[var_f, ones_row_f], writes=[ps_util])
            COPY(ctx, "vector", sB1[:], ps_util[:], reads=[ps_util], writes=[sB1])
            TT(ctx, "vector", sq_f[:], meansave_b[:, q0:q0 + TQ], rstdsave[:, q0:q0 + TQ],
               ALU.mult, reads=[meansave_b, rstdsave], writes=[sq_f])
            MM(ctx, ps_util[:], ones_row_f[:], sq_f[:], start=True, stop=True,
               reads=[sq_f, ones_row_f], writes=[ps_util])
            COPY(ctx, "vector", stmask[:], ps_util[:], reads=[ps_util], writes=[stmask])
            for m in range(PD):
                if m % 4 == 0:
                    DMA(ctx, "sync", xf_[:].rearrange("p (k t) -> p k t", k=4),
                        x_in[b, m * 128:(m + 4) * 128, q0:q0 + TQ].rearrange("(k p) t -> p k t", p=128),
                        writes=list(xfb_))
                for k in range(PD):
                    MM(ctx, ps_main[:], wo_b[k][:, m * 128:(m + 1) * 128],
                       o_qc[:, k * TQ:(k + 1) * TQ], start=(k == 0), stop=(k == PD - 1),
                       reads=[w16buf, oqb[k]], writes=[ps_main])
                TT(ctx, "vector", v1t[:], xf_[:, (m % 4) * TQ:(m % 4 + 1) * TQ], sB1[:], ALU.mult,
                   reads=[xfb_[m % 4], sB1], writes=[v1t])
                TT(ctx, "vector", v1t[:], v1t[:], stmask[:], ALU.subtract,
                   reads=[v1t, stmask], writes=[v1t])
                TT(ctx, "vector", out_f[:], v1t[:], ps_main[:],
                   ALU.add, reads=[v1t, ps_main], writes=[outb[0]])
                DMA(ctx, "gpsimd",
                    xt1_out[b, m * 128:(m + 1) * 128, q0:q0 + TQ],
                    out_f[:], reads=[outb[0]])


def mla_weight_arrays(inp):
    HH = 32
    inv_freq = 1.0 / (10000.0 ** (np.arange(0, HH, 2, dtype=np.float64) / HH))
    t = np.arange(S, dtype=np.float64)
    freqs = np.outer(t, inv_freq)
    emb = np.concatenate([freqs, freqs], axis=1)
    cosT = np.ascontiguousarray(np.cos(emb).T.astype(np.float32))
    sinT = np.ascontiguousarray(np.sin(emb).T.astype(np.float32))
    kkk = np.arange(128)[:, None]
    qqq = np.arange(512)[None, :]
    maskneg = np.stack([np.where(128 * i + kkk <= qqq, 0.0, -1e9).astype(np.float32)
                        for i in range(4)])
    perm = np.concatenate([np.arange(16, 32), np.arange(0, 16)])
    full_perm = np.concatenate([h * 32 + perm for h in range(16)])
    return {
        "lnw1_cols": np.ascontiguousarray(np.asarray(inp["L1_ffn_w_ln1"]).reshape(PD, 128).T),
        "wkvd": np.ascontiguousarray(inp["L1_ffn_W_kvd"]),
        "wqd": np.ascontiguousarray(inp["L1_ffn_W_qd"]),
        "wku": np.ascontiguousarray(inp["L1_ffn_W_ku"]),
        "wqu": np.ascontiguousarray(inp["L1_ffn_W_qu"]),
        "wvu": np.ascontiguousarray(inp["L1_ffn_W_vu"]),
        "wropeq": np.ascontiguousarray(inp["L1_ffn_W_ropeq"]),
        "wropeq_perm": np.ascontiguousarray(np.asarray(inp["L1_ffn_W_ropeq"])[:, full_perm[:512]]),
        "wropek": np.ascontiguousarray(inp["L1_ffn_W_ropek"]),
        "wropek_perm": np.ascontiguousarray(np.asarray(inp["L1_ffn_W_ropek"])[:, full_perm[:512]]),
        "wo": np.ascontiguousarray(inp["L1_ffn_W_o"]),
        "cosT": cosT, "sinT": sinT, "maskneg": maskneg,
    }


def build_mla_program(dbg=False):
    nc = bass.Bass("TRN2", target_bir_lowering=False)
    ctx = Ctx(nc)
    x_in = nc.dram_tensor("x_fm", [BPC, D, S], F32, kind="ExternalInput").ap()
    xt1_out = nc.dram_tensor("xt1_fm", [BPC, D, S], F32, kind="ExternalOutput").ap()
    names = {"wkvd": [D, 256], "wqd": [D, 256], "wku": [256, 512], "wqu": [256, 512],
             "wvu": [256, 1024], "wropeq": [256, 512], "wropeq_perm": [256, 512],
             "wropek": [D, 512], "wropek_perm": [D, 512], "wo": [D, D],
             "lnw1_cols": [128, PD], "cosT": [32, S], "sinT": [32, S]}
    ctx.w = {k: nc.dram_tensor(k, v, F32, kind="ExternalInput").ap() for k, v in names.items()}
    ctx.w["maskneg"] = nc.dram_tensor("maskneg", [4, 128, 512], F32, kind="ExternalInput").ap()
    dbgd = None
    if dbg:
        dbgd = {"kk": nc.dram_tensor("kk_dbg", [BPC, 1024, S], BF16, kind="ExternalOutput").ap(),
                "vT": nc.dram_tensor("vT_dbg", [BPC, 128, 256 * 64], BF16, kind="ExternalOutput").ap()}
        for k in dbgd:
            ctx.bufmap[dbgd[k].tensor.name] = Buf(k)
    emit_mla(ctx, x_in, xt1_out, dbg=dbgd)
    ctx.finalize()
    return nc


def run_mla(x_fm_percore, inp, dbg=False):
    warr = mla_weight_arrays(inp)
    nc = build_mla_program(dbg=dbg)
    in_maps = [dict(x_fm=x_fm_percore[c], **warr) for c in range(NCORES)]
    res = run_bass_kernel_spmd(nc, in_maps, core_ids=list(range(NCORES)))
    if dbg:
        return [r["xt1_fm"] for r in res.results], res.results
    return [r["xt1_fm"] for r in res.results]



def emit_moe(ctx, xt1_in, x_in, x_out):
    TQ = 512
    NT = S // TQ
    w = ctx.w
    nc = ctx.nc

    ones_col_b = ctx.sb("moe_ones_col", [128, 1], BF16)
    ones_row_b = ctx.sb("moe_ones_rowb", [1, 128], BF16)
    ones_row_f = ctx.sb("moe_ones_rowf", [1, 128], F32)
    MEMSET(ctx, "vector", ones_col_b[:], 1.0, writes=[ones_col_b])
    MEMSET(ctx, "vector", ones_row_b[:], 1.0, writes=[ones_row_b])
    MEMSET(ctx, "vector", ones_row_f[:], 1.0, writes=[ones_row_f])
    idf, idb = make_ident(ctx, "moe_id")

    lnw_cols = ctx.sb("moe_lnw", [128, PD], F32)
    DMA(ctx, "sync", lnw_cols[:], w["lnw2_cols"], writes=[lnw_cols])
    stage0 = ctx.sb("moe_stg0", [128, 1024], F32)
    stg_buf = ctx.bufmap[ctx.prefix + "moe_stg0"]
    cvt = ctx.sb("moe_cvt", [128, 1024], BF16)
    cvt_buf = ctx.bufmap[ctx.prefix + "moe_cvt"]

    wgu_bf = ctx.dram_i("moe_wgu_bf", [8, 2, PD, 128, 1024], BF16)
    wgu_buf = ctx.bufmap[ctx.prefix + "moe_wgu_bf"]
    wd_bf = ctx.dram_i("moe_wd_bf", [PD, 64, 128, 128], BF16)
    wd_buf = ctx.bufmap[ctx.prefix + "moe_wd_bf"]
    hmid_hbm = ctx.dram_i("moe_hmid", [BPC, NT, 64, 128, TQ], BF16)
    hmid_buf = ctx.bufmap[ctx.prefix + "moe_hmid"]

    for e in range(8):
        for gi in range(2):
            src = w["wg_r"] if gi == 0 else w["wu_r"]
            for kc in range(PD):
                if e < 7:
                    sap = src[e, kc * 128:(kc + 1) * 128, :]
                else:
                    sap = (w["wg_s"] if gi == 0 else w["wu_s"])[kc * 128:(kc + 1) * 128, :]
                DMA(ctx, "sync", stage0[:], sap, writes=[stg_buf])
                TS(ctx, "vector", cvt[:], stage0[:], lnw_cols[:, kc:kc + 1], ALU.mult,
                   reads=[stg_buf, lnw_cols], writes=[cvt_buf])
                DMA(ctx, "gpsimd", wgu_bf[e, gi, kc], cvt[:], reads=[cvt_buf], writes=[wgu_buf])
    for e in range(8):
        for kc in range(PD):
            if e < 7:
                sap = w["wd_r"][e, kc * 128:(kc + 1) * 128, :]
            else:
                sap = w["wd_s"][kc * 128:(kc + 1) * 128, :]
            DMA(ctx, "sync", stage0[:], sap, writes=[stg_buf])
            COPY(ctx, "vector", cvt[:], stage0[:], reads=[stg_buf], writes=[cvt_buf])
            DMA(ctx, "gpsimd", wd_bf[:, e * PD + kc].rearrange("m p c -> p m c"),
                cvt[:].rearrange("p (m c) -> p m c", m=PD), reads=[cvt_buf], writes=[wd_buf])

    wr_b = [load_weight_staged(ctx, f"moe_wr{k}", w["wrouter"][k * 128:(k + 1) * 128, :],
                               128, 7, stage0, scale_col=lnw_cols[:, k:k + 1],
                               scale_col_reads=[lnw_cols]) for k in range(PD)]
    biasB = ctx.sb("moe_biasB", [128, 7], F32)
    DMA(ctx, "sync", biasB[:], bass.AP(w["rbias"].tensor, 0, [[0, 128], [1, 7]]), writes=[biasB])

    ps_sq = ctx.ps("moe_ps_sq", [1, TQ], F32)
    ps_util = ctx.ps("moe_ps_util", [128, TQ], F32)
    ps_g = ctx.ps("moe_ps_g", [128, TQ], F32)
    ps_u = ctx.ps("moe_ps_u", [128, TQ], F32)
    ps_w = ctx.ps("moe_ps_w", [128, TQ], F32)
    ps_tr = ctx.ps("moe_ps_tr", [128, 128], F32)
    ps_comb = [ctx.ps(f"moe_ps_comb{i}", [128, TQ], F32) for i in range(2)]

    xf = ctx.sb("moe_xf", [128, 4 * TQ], F32)
    xfb = [ctx.mkbuf(f"moe_xfb{k}") for k in range(4)]
    x2s = ctx.sb("moe_x2s", [128, TQ], BF16)
    sqrow = ctx.sb("moe_sqrow", [1, TQ], F32)
    s2row = ctx.sb("moe_s2row", [1, TQ], F32)
    s2B = ctx.sb("moe_s2B", [128, TQ], F32)
    xn_b = ctx.sb("moe_xn", [128, PD * S], BF16)
    xnb = [ctx.mkbuf(f"moe_xnb{i}") for i in range(NT)]
    logit = ctx.sb("moe_logit", [128, 4 * 8], F32)
    p8 = ctx.sb("moe_p8", [128, 4 * 8], F32)
    t8 = ctx.sb("moe_t8", [128, 4 * 8], F32)
    denr = ctx.sb("moe_denr", [128, 4], F32)
    gmask = ctx.sb("moe_gmask", [128, 4 * 8], F32)
    wts = ctx.sb("moe_wts", [128, 4 * 8], F32)
    rstd_cols = ctx.sb("moe_rstdcols", [128, 4], F32)
    wts_all = ctx.sb("moe_wts_all", [128, NT * 32], F32)
    wts_all_buf = ctx.bufmap[ctx.prefix + "moe_wts_all"]
    wrow_t = ctx.sb("moe_wrow_t", [1, TQ], BF16)
    wrow_buf = ctx.bufmap[ctx.prefix + "moe_wrow_t"]
    wgu_sb = ctx.sb("moe_wgu_sb", [128, 2 * PD * 1024], BF16)
    wgu_sb_buf = ctx.bufmap[ctx.prefix + "moe_wgu_sb"]
    g_sb = ctx.sb("moe_g", [128, TQ], BF16)
    hm_sb = ctx.sb("moe_hm", [128, TQ], BF16)
    hmw_sb = ctx.sb("moe_hmw", [128, PD * TQ], BF16)
    hmwb = [ctx.mkbuf(f"moe_hmwb{m}") for m in range(PD)]
    hmid_sb = ctx.sb("moe_hmid_sb", [128, 64 * TQ], BF16)
    hmid_sb_buf = ctx.bufmap[ctx.prefix + "moe_hmid_sb"]
    wd_sb = [ctx.sb("moe_wd_sb0", [128, 64 * 128], BF16)]
    xb2 = ctx.sb("moe_xb2", [128, PD * TQ], F32)
    xb2b = [ctx.mkbuf(f"moe_xb2b{m}") for m in range(PD)]
    out_t = ctx.sb("moe_out", [128, TQ], F32)
    outb = ctx.mkbuf("moe_outb")

    for b in range(BPC):
        for tci in range(NT):
            t0 = tci * TQ
            for hh in range(2):
                DMA(ctx, "sync", xf[:].rearrange("p (k t) -> p k t", k=4),
                    xt1_in[b, hh * 512:(hh + 1) * 512, t0:t0 + TQ].rearrange("(k p) t -> p k t", p=128),
                    writes=list(xfb))
                for k in range(4):
                    kk4 = hh * 4 + k
                    ACT(ctx, x2s[:], xf[:, k * TQ:(k + 1) * TQ], AF.Square,
                        reads=[xfb[k]], writes=[x2s])
                    MM(ctx, ps_sq[0:1, :], ones_col_b[:], x2s[:],
                       start=(kk4 == 0), stop=(kk4 == PD - 1), reads=[x2s, ones_col_b],
                       writes=[ps_sq])
            ACT(ctx, sqrow[:], ps_sq[0:1, :], AF.Sqrt, scale=1.0 / D, reads=[ps_sq], writes=[sqrow])
            TS(ctx, "vector", sqrow[:], sqrow[:], 1e-5, ALU.add, reads=[sqrow], writes=[sqrow])
            RECIP(ctx, s2row[:], sqrow[:], reads=[sqrow], writes=[s2row])
            MM(ctx, ps_util[:], ones_row_f[:], s2row[:], start=True, stop=True,
               reads=[s2row, ones_row_f], writes=[ps_util])
            COPY(ctx, "vector", s2B[:], ps_util[:], reads=[ps_util], writes=[s2B])
            for hh in range(2):
                DMA(ctx, "sync", xf[:].rearrange("p (k t) -> p k t", k=4),
                    xt1_in[b, hh * 512:(hh + 1) * 512, t0:t0 + TQ].rearrange("(k p) t -> p k t", p=128),
                    writes=list(xfb))
                for k in range(4):
                    kk4 = hh * 4 + k
                    TT(ctx, "vector", xn_b[:, kk4 * S + t0:kk4 * S + t0 + TQ],
                       xf[:, k * TQ:(k + 1) * TQ], s2B[:], ALU.mult,
                       reads=[xfb[k], s2B], writes=[xnb[tci]])
            for j in range(4):
                sl = slice(j * 8, j * 8 + 7)
                for k in range(PD):
                    MM(ctx, ps_util[0:128, j * 8:j * 8 + 7],
                       xn_b[:, k * S + t0 + j * 128:k * S + t0 + (j + 1) * 128], wr_b[k][:],
                       start=(k == 0), stop=(k == PD - 1), reads=[xnb[tci], wr_b[k]],
                       writes=[ps_util])
                TT(ctx, "vector", logit[:, sl], ps_util[0:128, j * 8:j * 8 + 7], biasB[0:128, 0:7],
                   ALU.add, reads=[ps_util, biasB], writes=[logit])
                ACT(ctx, p8[:, sl], logit[:, sl], AF.Sigmoid, reads=[logit], writes=[p8])
                ctx.op("vector", lambda e, _j=j: e.memset(p8[:, _j * 8 + 7:_j * 8 + 8], -1.0),
                       writes=[p8])
                MAX8(ctx, t8[:, j * 8:(j + 1) * 8], p8[:, j * 8:(j + 1) * 8],
                     reads=[p8], writes=[t8])
                TT(ctx, "vector", denr[:, j:j + 1], t8[:, j * 8:j * 8 + 1],
                   t8[:, j * 8 + 1:j * 8 + 2], ALU.add, reads=[t8], writes=[denr])
                RECIP(ctx, denr[:, j:j + 1], denr[:, j:j + 1], reads=[denr], writes=[denr])
                TS(ctx, "vector", gmask[:, sl], p8[:, sl], t8[:, j * 8 + 1:j * 8 + 2],
                   ALU.is_ge, reads=[p8, t8], writes=[gmask])
                TT(ctx, "vector", wts[:, sl], p8[:, sl], gmask[:, sl], ALU.mult,
                   reads=[p8, gmask], writes=[wts])
                TS(ctx, "vector", wts_all[:, tci * 32 + j * 8:tci * 32 + j * 8 + 7],
                   wts[:, sl], denr[:, j:j + 1], ALU.mult,
                   reads=[wts, denr], writes=[wts_all_buf])
        for e in range(8):
            DMA(ctx, "sync", wgu_sb[:].rearrange("p (i c) -> p i c", i=2 * PD),
                wgu_bf[e].rearrange("g k p c -> p (g k) c"), reads=[wgu_buf],
                writes=[wgu_sb_buf])
            for tci in range(NT):
                t0 = tci * TQ
                if e < 7:
                    for j in range(4):
                        TRANS(ctx, ps_tr[0:1, 0:128],
                              wts_all[:, tci * 32 + j * 8 + e:tci * 32 + j * 8 + e + 1], idf[:],
                              reads=[wts_all_buf, idf], writes=[ps_tr])
                        COPY(ctx, "vector", wrow_t[0:1, j * 128:(j + 1) * 128],
                             ps_tr[0:1, 0:128], reads=[ps_tr], writes=[wrow_buf])
                    MM(ctx, ps_w[:], ones_row_b[:], wrow_t[:],
                       start=True, stop=True, reads=[ones_row_b, wrow_buf], writes=[ps_w])
                for m in range(PD):
                    for kc in range(PD):
                        MM(ctx, ps_g[:], wgu_sb[:, kc * 1024 + m * 128:kc * 1024 + (m + 1) * 128],
                           xn_b[:, kc * S + t0:kc * S + t0 + TQ], start=(kc == 0), stop=(kc == PD - 1),
                           reads=[wgu_sb_buf, xnb[tci]], writes=[ps_g])
                    for kc in range(PD):
                        MM(ctx, ps_u[:], wgu_sb[:, (PD + kc) * 1024 + m * 128:(PD + kc) * 1024 + (m + 1) * 128],
                           xn_b[:, kc * S + t0:kc * S + t0 + TQ], start=(kc == 0), stop=(kc == PD - 1),
                           reads=[wgu_sb_buf, xnb[tci]], writes=[ps_u])
                    ACT(ctx, g_sb[:], ps_g[:], AF.Silu, reads=[ps_g], writes=[g_sb])
                    TT(ctx, "vector", hm_sb[:], g_sb[:], ps_u[:], ALU.mult,
                       reads=[g_sb, ps_u], writes=[hm_sb])
                    if e < 7:
                        TT(ctx, "vector", hmw_sb[:, m * TQ:(m + 1) * TQ], hm_sb[:], ps_w[:],
                           ALU.mult, reads=[hm_sb, ps_w], writes=[hmwb[m]])
                    else:
                        COPY(ctx, "vector", hmw_sb[:, m * TQ:(m + 1) * TQ], hm_sb[:],
                             reads=[hm_sb], writes=[hmwb[m]])
                DMA(ctx, "gpsimd", hmid_hbm[b, tci, e * PD:(e + 1) * PD].rearrange("m p t -> p m t"),
                    hmw_sb[:].rearrange("p (m t) -> p m t", m=PD), reads=list(hmwb),
                    writes=[hmid_buf])
        for tci in range(NT):
            t0 = tci * TQ
            DMA(ctx, "sync", hmid_sb[:].rearrange("p (i t) -> p i t", i=64),
                hmid_hbm[b, tci].rearrange("i p t -> p i t"), reads=[hmid_buf],
                writes=[hmid_sb_buf])
            for hh in range(2):
                DMA(ctx, "sync", xf[:].rearrange("p (k t) -> p k t", k=4),
                    xt1_in[b, hh * 512:(hh + 1) * 512, t0:t0 + TQ].rearrange("(k p) t -> p k t", p=128),
                    writes=list(xfb))
                for k in range(4):
                    TS(ctx, "vector", xb2[:, (hh * 4 + k) * TQ:(hh * 4 + k + 1) * TQ],
                       xf[:, k * TQ:(k + 1) * TQ], 0.5, ALU.mult,
                       reads=[xfb[k]], writes=[xb2b[hh * 4 + k]])
                DMA(ctx, "sync", xf[:].rearrange("p (k t) -> p k t", k=4),
                    x_in[b, hh * 512:(hh + 1) * 512, t0:t0 + TQ].rearrange("(k p) t -> p k t", p=128),
                    writes=list(xfb))
                for k in range(4):
                    m = hh * 4 + k
                    TT(ctx, "vector", xb2[:, m * TQ:(m + 1) * TQ], xb2[:, m * TQ:(m + 1) * TQ],
                       xf[:, k * TQ:(k + 1) * TQ], ALU.add, reads=[xfb[k], xb2b[m]],
                       writes=[xb2b[m]])
            for m in range(PD):
                wd_ = wd_sb[0]
                DMA(ctx, "sync", wd_[:].rearrange("p (i c) -> p i c", i=64),
                    wd_bf[m].rearrange("i p c -> p i c"), reads=[wd_buf], writes=[wd_])
                pc = ps_comb[m % 2]
                for kc in range(64):
                    MM(ctx, pc[:], wd_[:, kc * 128:(kc + 1) * 128],
                       hmid_sb[:, kc * TQ:(kc + 1) * TQ], start=(kc == 0), stop=(kc == 63),
                       reads=[wd_, hmid_sb_buf], writes=[pc])
                STT(ctx, "vector", out_t[:], pc[:], 0.5, xb2[:, m * TQ:(m + 1) * TQ],
                    ALU.mult, ALU.add, reads=[pc, xb2b[m]], writes=[outb])
                DMA(ctx, "gpsimd", x_out[b, m * 128:(m + 1) * 128, t0:t0 + TQ], out_t[:],
                    reads=[outb])


def moe_weight_arrays(inp):
    return {
        "lnw2_cols": np.ascontiguousarray(np.asarray(inp["L1_ffn_w_ln2"]).reshape(PD, 128).T),
        "wg_r": np.ascontiguousarray(inp["L1_ffn_Wg_r"]),
        "wu_r": np.ascontiguousarray(inp["L1_ffn_Wu_r"]),
        "wd_r": np.ascontiguousarray(inp["L1_ffn_Wd_r"]),
        "wg_s": np.ascontiguousarray(inp["L1_ffn_Wg_s"]),
        "wu_s": np.ascontiguousarray(inp["L1_ffn_Wu_s"]),
        "wd_s": np.ascontiguousarray(inp["L1_ffn_Wd_s"]),
        "wrouter": np.ascontiguousarray(inp["L1_ffn_W_router"]),
        "rbias": np.ascontiguousarray(np.asarray(inp["L1_ffn_routing_bias"]).reshape(1, 7)),
    }


def build_moe_program():
    nc = bass.Bass("TRN2", target_bir_lowering=False)
    ctx = Ctx(nc)
    xt1_in = nc.dram_tensor("xt1_fm", [BPC, D, S], F32, kind="ExternalInput").ap()
    x_in = nc.dram_tensor("x_fm", [BPC, D, S], F32, kind="ExternalInput").ap()
    x_out = nc.dram_tensor("xo_fm", [BPC, D, S], F32, kind="ExternalOutput").ap()
    names = {"lnw2_cols": [128, PD], "wg_r": [7, D, D], "wu_r": [7, D, D], "wd_r": [7, D, D],
             "wg_s": [D, D], "wu_s": [D, D], "wd_s": [D, D], "wrouter": [D, 7], "rbias": [1, 7]}
    ctx.w = {k: nc.dram_tensor(k, v, F32, kind="ExternalInput").ap() for k, v in names.items()}
    emit_moe(ctx, xt1_in, x_in, x_out)
    ctx.finalize()
    return nc


def run_moe(xt1_percore, x_percore, inp):
    warr = moe_weight_arrays(inp)
    nc = build_moe_program()
    in_maps = [dict(xt1_fm=xt1_percore[c], x_fm=x_percore[c], **warr) for c in range(NCORES)]
    res = run_bass_kernel_spmd(nc, in_maps, core_ids=list(range(NCORES)))
    return [r["xo_fm"] for r in res.results]



def emit_qatt(ctx, x_in, x_out):
    TQ = 512
    NT = S // TQ
    w = ctx.w
    RT2 = 0.7071067811865476

    ones_col_b = ctx.sb("qa_ones_col", [128, 1], BF16)
    ones_row_f = ctx.sb("qa_ones_rowf", [1, 128], F32)
    MEMSET(ctx, "vector", ones_col_b[:], 1.0, writes=[ones_col_b])
    MEMSET(ctx, "vector", ones_row_f[:], 1.0, writes=[ones_row_f])
    idf, idb = make_ident(ctx, "qa_id")
    ident64 = ctx.sb("qa_ident64", [128, 64], BF16)
    COPY(ctx, "vector", ident64[0:64, :], idb[0:64, 0:64], reads=[idb], writes=[ident64])
    COPY(ctx, "vector", ident64[64:128, :], idb[0:64, 0:64], reads=[idb], writes=[ident64])

    stage0 = ctx.sb("qa_stg0", [128, 512], F32)
    stg_buf = ctx.bufmap[ctx.prefix + "qa_stg0"]
    wbuf = ctx.sb("qa_wbuf", [128, PD * 1024], BF16)
    wbuf_b = ctx.bufmap[ctx.prefix + "qa_wbuf"]

    def load_w(which, scale=None):
        for k in range(PD):
            for cc in range(2):
                convert_into(ctx, wbuf[:, k * 1024 + cc * 512:k * 1024 + (cc + 1) * 512],
                             wbuf_b, w[which][k * 128:(k + 1) * 128, cc * 512:(cc + 1) * 512],
                             128, 512, stage0, scale=scale)

    wsum = ctx.sb("qa_wsum", [1, 1024], BF16)
    wsum_b = ctx.bufmap[ctx.prefix + "qa_wsum"]
    ps_mean = ctx.ps("qa_ps_mean", [1, TQ], F32)
    ps_sq = ctx.ps("qa_ps_sq", [1, TQ], F32)
    ps_util = ctx.ps("qa_ps_util", [128, TQ], F32)
    ps_main = ctx.ps("qa_ps_main", [128, TQ], F32)
    ps_St = ctx.ps("qa_ps_St", [128, TQ], F32)
    ps_o = ctx.ps("qa_ps_o", [64, TQ], F32)
    ps_tr = ctx.ps("qa_ps_tr", [128, 128], BF16)

    def calc_wsum(ctx=ctx):
        for blk in range(0, 1024, 512):
            for k in range(PD):
                MM(ctx, ps_mean[0:1, 0:512], ones_col_b[:],
                   wbuf[:, k * 1024 + blk:k * 1024 + blk + 512],
                   start=(k == 0), stop=(k == PD - 1), reads=[wbuf_b, ones_col_b],
                   writes=[ps_mean])
            TS(ctx, "vector", wsum[:, blk:blk + 512], ps_mean[0:1, 0:512], -1.0, ALU.mult,
               reads=[ps_mean], writes=[wsum_b])

    masks01 = ctx.sb("qa_masks", [128, 4 * TQ], BF16)
    mstage = ctx.sb("qa_mstage", [128, TQ], F32)
    for i in range(4):
        DMA(ctx, "sync", mstage[:], w["mask01"][i], writes=[mstage])
        COPY(ctx, "vector", masks01[:, i * TQ:(i + 1) * TQ], mstage[:],
             reads=[mstage], writes=[masks01])

    mean_f = ctx.sb("qa_mean_f", [1, TQ], F32)
    sq_f = ctx.sb("qa_sq_f", [1, TQ], F32)
    var_f = ctx.sb("qa_var_f", [1, TQ], F32)
    meansave_b = ctx.sb("qa_meansave", [1, S], BF16)
    rstdsave = ctx.sb("qa_rstdsave", [1, S], F32)
    rB = ctx.sb("qa_rB", [128, TQ], F32)
    xf = ctx.sb("qa_xf", [128, 4 * TQ], F32)
    xfb = [ctx.mkbuf(f"qa_xfb{k}") for k in range(4)]
    xb = ctx.sb("qa_xb", [128, PD * TQ], BF16)
    x2s = ctx.sb("qa_x2s", [128, TQ], BF16)
    qq = ctx.sb("qa_qq", [128, PD * S], BF16)
    qqb = [ctx.mkbuf(f"qa_qqb{g}") for g in range(PD)]
    kk = ctx.sb("qa_kk", [128, PD * S], BF16)
    kkb = [ctx.mkbuf(f"qa_kkb{g}") for g in range(PD)]
    vT = ctx.sb("qa_vT", [128, 256 * 64], BF16)
    vTb = ctx.mkbuf("qa_vTb")
    vtmp = ctx.sb("qa_vtmp", [128, TQ], BF16)
    attn_f = ctx.sb("qa_attnf", [128, TQ], F32)
    probs = [ctx.sb(f"qa_probs{i}", [128, TQ], BF16) for i in range(2)]
    zrecB = ctx.sb("qa_zrecB", [64, TQ], F32)
    o_qc = ctx.sb("qa_oqc", [128, PD * TQ], BF16)
    oqb = [ctx.mkbuf(f"qa_oqb{g}") for g in range(PD)]
    out_t = ctx.sb("qa_out", [128, TQ], F32)
    outb = ctx.mkbuf("qa_outb")

    for b in range(BPC):
        for (wi, (wname, wscale, dest, destb)) in enumerate(
                (("wq", 0.125, qq, qqb), ("wk", None, kk, kkb), ("wv", None, None, None))):
            load_w(wname, scale=wscale)
            calc_wsum()
            for tci in range(NT):
                t0 = tci * TQ
                for hh in range(2):
                    DMA(ctx, "sync", xf[:].rearrange("p (k t) -> p k t", k=4),
                        x_in[b, hh * 512:(hh + 1) * 512, t0:t0 + TQ].rearrange("(k p) t -> p k t", p=128),
                        writes=list(xfb))
                    for k in range(4):
                        kk4 = hh * 4 + k
                        COPY(ctx, "vector", xb[:, kk4 * TQ:(kk4 + 1) * TQ],
                             xf[:, k * TQ:(k + 1) * TQ], reads=[xfb[k]], writes=[xb])
                        if wi == 0:
                            ACT(ctx, x2s[:], xf[:, k * TQ:(k + 1) * TQ], AF.Square,
                                reads=[xfb[k]], writes=[x2s])
                            MM(ctx, ps_sq[0:1, :], ones_col_b[:], x2s[:],
                               start=(kk4 == 0), stop=(kk4 == PD - 1),
                               reads=[x2s, ones_col_b], writes=[ps_sq])
                if wi == 0:
                    for k in range(PD):
                        MM(ctx, ps_mean[0:1, :], ones_col_b[:], xb[:, k * TQ:(k + 1) * TQ],
                           start=(k == 0), stop=(k == PD - 1), reads=[xb, ones_col_b],
                           writes=[ps_mean])
                    ACT(ctx, mean_f[:], ps_mean[0:1, :], AF.Copy, scale=1.0 / D,
                        reads=[ps_mean], writes=[mean_f])
                    COPY(ctx, "vector", meansave_b[:, t0:t0 + TQ], mean_f[:],
                         reads=[mean_f], writes=[meansave_b])
                    ACT(ctx, sq_f[:], mean_f[:], AF.Square, reads=[mean_f], writes=[sq_f])
                    STT(ctx, "vector", var_f[:], ps_sq[0:1, :], 1.0 / D, sq_f[:],
                        ALU.mult, ALU.subtract, reads=[ps_sq, sq_f], writes=[var_f])
                    ACT(ctx, sq_f[:], var_f[:], AF.Sqrt, bias=1e-5, reads=[var_f], writes=[sq_f])
                    RECIP(ctx, var_f[:], sq_f[:], reads=[sq_f], writes=[var_f])
                    COPY(ctx, "vector", rstdsave[:, t0:t0 + TQ], var_f[:],
                         reads=[var_f], writes=[rstdsave])
                else:
                    COPY(ctx, "vector", var_f[:], rstdsave[:, t0:t0 + TQ],
                         reads=[rstdsave], writes=[var_f])
                MM(ctx, ps_util[:], ones_row_f[:], var_f[:], start=True, stop=True,
                   reads=[var_f, ones_row_f], writes=[ps_util])
                COPY(ctx, "vector", rB[:], ps_util[:], reads=[ps_util], writes=[rB])
                for g in range(PD):
                    for k in range(PD):
                        MM(ctx, ps_main[:], wbuf[:, k * 1024 + g * 128:k * 1024 + (g + 1) * 128],
                           xb[:, k * TQ:(k + 1) * TQ], start=(k == 0), stop=False,
                           reads=[wbuf_b, xb], writes=[ps_main])
                    MM(ctx, ps_main[:], wsum[:, g * 128:(g + 1) * 128],
                       meansave_b[:, t0:t0 + TQ], start=False, stop=True,
                       reads=[wsum_b, meansave_b], writes=[ps_main])
                    if wi < 2:
                        TT(ctx, "vector", dest[:, g * S + t0:g * S + t0 + TQ], ps_main[:], rB[:],
                           ALU.mult, reads=[ps_main, rB], writes=[destb[g]])
                    else:
                        TT(ctx, "vector", vtmp[:], ps_main[:], rB[:], ALU.mult,
                           reads=[ps_main, rB], writes=[vtmp])
                        for j in range(4):
                            kb = tci * 4 + j
                            for (vb, hh2) in ((0, 2 * g), (64, 2 * g + 1)):
                                TRANS(ctx, ps_tr[:, 0:64], vtmp[vb:vb + 64, j * 128:(j + 1) * 128],
                                      ident64[vb:vb + 64, :], reads=[vtmp, ident64], writes=[ps_tr])
                                COPY(ctx, "vector", vT[:, (kb * 16 + hh2) * 64:(kb * 16 + hh2) * 64 + 64],
                                     ps_tr[:, 0:64], reads=[ps_tr], writes=[vTb])
        load_w("wo")
        for qci in range(NT):
            q0 = qci * TQ
            for h in range(16):
                g = h // 2
                vb = (h % 2) * 64
                nkb = qci * 4 + 4
                for kb in range(nkb):
                    MM(ctx, ps_St[:], kk[vb:vb + 64, g * S + kb * 128:g * S + (kb + 1) * 128],
                       qq[vb:vb + 64, g * S + q0:g * S + q0 + TQ], start=True, stop=True,
                       reads=[kkb[g], qqb[g]], writes=[ps_St])
                    pslot = probs[kb % 2]
                    ACT(ctx, attn_f[:], ps_St[:], AF.Square, bias=RT2, scale=RT2,
                        reads=[ps_St], writes=[attn_f])
                    if kb >= qci * 4:
                        ii = kb - qci * 4
                        STT(ctx, "vector", pslot[:], attn_f[:], 0.5,
                            masks01[:, ii * TQ:(ii + 1) * TQ], ALU.add, ALU.mult,
                            reads=[attn_f, masks01], writes=[pslot])
                    else:
                        ctx.op("scalar", lambda e, _o=pslot[:], _i=attn_f[:]:
                               e.add(out=_o, in_=_i, add=0.5), reads=[attn_f, ctx._cpool],
                               writes=[pslot]) if False else None
                        TS(ctx, "vector", pslot[:], attn_f[:], 0.5, ALU.add,
                           reads=[attn_f], writes=[pslot])
                    MM(ctx, ps_util[0:1, :], ones_col_b[:], pslot[:],
                       start=(kb == 0), stop=(kb == nkb - 1), reads=[pslot, ones_col_b],
                       writes=[ps_util])
                    MM(ctx, ps_o[:], vT[:, (kb * 16 + h) * 64:(kb * 16 + h) * 64 + 64],
                       pslot[:], start=(kb == 0), stop=(kb == nkb - 1),
                       reads=[vTb, pslot], writes=[ps_o])
                ACT(ctx, mean_f[:], ps_util[0:1, :], AF.Copy, bias=1e-6,
                    reads=[ps_util], writes=[mean_f])
                RECIP(ctx, var_f[:], mean_f[:], reads=[mean_f], writes=[var_f])
                MM(ctx, ps_util[0:64, :], ones_row_f[0:1, 0:64], var_f[:], start=True, stop=True,
                   reads=[var_f, ones_row_f], writes=[ps_util])
                COPY(ctx, "vector", zrecB[:], ps_util[0:64, :], reads=[ps_util], writes=[zrecB])
                TT(ctx, "vector", o_qc[vb:vb + 64, g * TQ:(g + 1) * TQ], ps_o[:], zrecB[:],
                   ALU.mult, reads=[ps_o, zrecB], writes=[oqb[g]])
            for m in range(PD):
                if m % 4 == 0:
                    DMA(ctx, "sync", xf[:].rearrange("p (k t) -> p k t", k=4),
                        x_in[b, m * 128:(m + 4) * 128, q0:q0 + TQ].rearrange("(k p) t -> p k t", p=128),
                        writes=list(xfb))
                for k in range(PD):
                    MM(ctx, ps_main[:], wbuf[:, k * 1024 + m * 128:k * 1024 + (m + 1) * 128],
                       o_qc[:, k * TQ:(k + 1) * TQ], start=(k == 0), stop=(k == PD - 1),
                       reads=[wbuf_b, oqb[k]], writes=[ps_main])
                TT(ctx, "vector", out_t[:], ps_main[:], xf[:, (m % 4) * TQ:(m % 4 + 1) * TQ],
                   ALU.add, reads=[ps_main, xfb[m % 4]], writes=[outb])
                DMA(ctx, "gpsimd", x_out[b, m * 128:(m + 1) * 128, q0:q0 + TQ], out_t[:],
                    reads=[outb])


def qatt_weight_arrays(inp):
    kkk = np.arange(128)[:, None]
    qqq = np.arange(512)[None, :]
    mask01 = np.stack([np.where(128 * i + kkk <= qqq, 1.0, 0.0).astype(np.float32)
                       for i in range(4)])
    return {
        "wq": np.ascontiguousarray(np.asarray(inp["L1_attn_Wq"]).T),
        "wk": np.ascontiguousarray(np.asarray(inp["L1_attn_Wk"]).T),
        "wv": np.ascontiguousarray(np.asarray(inp["L1_attn_Wv"]).T),
        "wo": np.ascontiguousarray(np.asarray(inp["L1_attn_Wo"]).T),
        "mask01": mask01,
    }


def build_qatt_program():
    nc = bass.Bass("TRN2", target_bir_lowering=False)
    ctx = Ctx(nc)
    x_in = nc.dram_tensor("x_fm", [BPC, D, S], F32, kind="ExternalInput").ap()
    x_out = nc.dram_tensor("xo_fm", [BPC, D, S], F32, kind="ExternalOutput").ap()
    names = {"wq": [D, D], "wk": [D, D], "wv": [D, D], "wo": [D, D]}
    ctx.w = {k: nc.dram_tensor(k, v, F32, kind="ExternalInput").ap() for k, v in names.items()}
    ctx.w["mask01"] = nc.dram_tensor("mask01", [4, 128, 512], F32, kind="ExternalInput").ap()
    emit_qatt(ctx, x_in, x_out)
    ctx.finalize()
    return nc


def run_qatt(x_fm_percore, inp):
    warr = qatt_weight_arrays(inp)
    nc = build_qatt_program()
    in_maps = [dict(x_fm=x_fm_percore[c], **warr) for c in range(NCORES)]
    res = run_bass_kernel_spmd(nc, in_maps, core_ids=list(range(NCORES)))
    return [r["xo_fm"] for r in res.results]



_PROGRAMS = {}

def _get_prog(name):
    if name not in _PROGRAMS:
        _PROGRAMS[name] = {
            "ffn0": build_ffn0_program, "mba": build_mba_program,
            "mbb": build_mbb_program, "mbc": build_mbc_program,
            "mbd": build_mbd_program, "mla": build_mla_program,
            "moe": build_moe_program, "qatt": build_qatt_program,
        }[name]()
    return _PROGRAMS[name]


def _launch(name, in_maps):
    nc = _get_prog(name)
    return run_bass_kernel_spmd(nc, in_maps, core_ids=list(range(NCORES))).results


def kernel_unfused(**inputs):
    inp = {k: np.asarray(v) for k, v in inputs.items()}
    x = inp["L0_attn_x"]
    x_fm = _fm(x).reshape(NCORES, BPC, D, S)
    xs = [np.ascontiguousarray(x_fm[c]) for c in range(NCORES)]

    w_ffn = ffn0_weight_arrays(inp)
    w_mb = mamba_weight_arrays(inp)
    w_mla = mla_weight_arrays(inp)
    w_moe = moe_weight_arrays(inp)
    w_qa = qatt_weight_arrays(inp)

    r = _launch("ffn0", [dict(x_fm=xs[c], **w_ffn) for c in range(NCORES)])
    x1 = [r[c]["y_fm"] for c in range(NCORES)]
    r = _launch("mba", [dict(x_fm=x1[c], wipT=w_mb["wipT"], normw_cols=w_mb["normw_cols"])
                        for c in range(NCORES)])
    xi = [r[c]["xi_fm"] for c in range(NCORES)]
    res_ = [r[c]["res_fm"] for c in range(NCORES)]
    mrow = [r[c]["mrow"] for c in range(NCORES)]
    rrow = [r[c]["rrow"] for c in range(NCORES)]
    r = _launch("mbb", [dict(xi_fm=xi[c], dtwT=w_mb["dtwT"], xpwT=w_mb["xpwT"],
                             conv_cols=w_mb["conv_cols"], convb_cols=w_mb["convb_cols"],
                             dtb_cols=w_mb["dtb_cols"]) for c in range(NCORES)])
    xc = [r[c]["xc_fm"] for c in range(NCORES)]
    dt = [r[c]["dt_fm"] for c in range(NCORES)]
    Bs = [r[c]["B_fm"] for c in range(NCORES)]
    r = _launch("mbc", [dict(xc_fm=xc[c], dt_fm=dt[c], B_fm=Bs[c],
                             A_rs=w_mb["A_rs"], D_cols=w_mb["D_cols"]) for c in range(NCORES)])
    ys = [r[c]["y_fm"] for c in range(NCORES)]
    r = _launch("mbd", [dict(y_fm=ys[c], res_fm=res_[c], x_fm=x1[c], wopT=w_mb["wopT"],
                             mrow=mrow[c], rrow=rrow[c]) for c in range(NCORES)])
    x2 = [r[c]["xo_fm"] for c in range(NCORES)]
    r = _launch("ffn0", [dict(x_fm=x2[c], **w_ffn) for c in range(NCORES)])
    x3 = [r[c]["y_fm"] for c in range(NCORES)]
    r = _launch("mla", [dict(x_fm=x3[c], **w_mla) for c in range(NCORES)])
    xt1 = [r[c]["xt1_fm"] for c in range(NCORES)]
    r = _launch("moe", [dict(xt1_fm=xt1[c], x_fm=x3[c], **w_moe) for c in range(NCORES)])
    x4 = [r[c]["xo_fm"] for c in range(NCORES)]
    r = _launch("qatt", [dict(x_fm=x4[c], **w_qa) for c in range(NCORES)])
    x5 = [r[c]["xo_fm"] for c in range(NCORES)]
    r = _launch("mla", [dict(x_fm=x5[c], **w_mla) for c in range(NCORES)])
    xt1b = [r[c]["xt1_fm"] for c in range(NCORES)]
    r = _launch("moe", [dict(xt1_fm=xt1b[c], x_fm=x5[c], **w_moe) for c in range(NCORES)])
    out = [r[c]["xo_fm"] for c in range(NCORES)]

    y = np.stack(out).reshape(32, D, S)
    return np.ascontiguousarray(np.swapaxes(y, -1, -2)).astype(np.float32)



def build_fused_program():
    nc = bass.Bass("TRN2", target_bir_lowering=False)
    ctx = Ctx(nc)
    x_in = nc.dram_tensor("x_fm", [BPC, D, S], F32, kind="ExternalInput").ap()
    out_fm = nc.dram_tensor("out_fm", [BPC, D, S], F32, kind="ExternalOutput").ap()
    wnames = {
        "wlinT": [D, D], "lcat": [D, 32], "rcat": [32, D], "gateT": [D, 8], "blin": [128, 8],
        "wipT": [D, 4096], "normw_cols": [128, PD], "dtwT": [2048, 2048], "xpwT": [2048, 64],
        "conv_cols": [128, 64], "convb_cols": [128, 16], "dtb_cols": [128, 16],
        "A_rs": [128, 1024], "D_cols": [128, 16], "wopT": [2048, D],
        "wkvd": [D, 256], "wqd": [D, 256], "wku": [256, 512], "wqu": [256, 512],
        "wvu": [256, 1024], "wropeq": [256, 512], "wropeq_perm": [256, 512],
        "wropek": [D, 512], "wropek_perm": [D, 512], "wo": [D, D],
        "lnw1_cols": [128, PD], "cosT": [32, S], "sinT": [32, S],
        "lnw2_cols": [128, PD], "wg_r": [7, D, D], "wu_r": [7, D, D], "wd_r": [7, D, D],
        "wg_s": [D, D], "wu_s": [D, D], "wd_s": [D, D], "wrouter": [D, 7], "rbias": [1, 7],
        "wq": [D, D], "wk": [D, D], "wv": [D, D], "wo_qa": [D, D],
    }
    ctx.w = {k: nc.dram_tensor(k, v, F32, kind="ExternalInput").ap() for k, v in wnames.items()}
    ctx.w["maskneg"] = nc.dram_tensor("maskneg", [4, 128, 512], F32, kind="ExternalInput").ap()
    ctx.w["mask01"] = nc.dram_tensor("mask01", [4, 128, 512], F32, kind="ExternalInput").ap()
    for v in (1e-5, 1e-6, 1.0, 0.5, 0.7071067811865476):
        ctx.constf(v)

    x1 = nc.dram_tensor("fz_x1", [BPC, D, S], F32, kind="Internal").ap()
    x2 = nc.dram_tensor("fz_x2", [BPC, D, S], F32, kind="Internal").ap()
    x3 = nc.dram_tensor("fz_x3", [BPC, D, S], F32, kind="Internal").ap()
    x4 = nc.dram_tensor("fz_x4", [BPC, D, S], F32, kind="Internal").ap()
    x5 = nc.dram_tensor("fz_x5", [BPC, D, S], F32, kind="Internal").ap()
    xt1 = nc.dram_tensor("fz_xt1", [BPC, D, S], F32, kind="Internal").ap()
    xi = nc.dram_tensor("fz_xi", [BPC, 2048, S], F32, kind="Internal").ap()
    res = nc.dram_tensor("fz_res", [BPC, 2048, S], BF16, kind="Internal").ap()
    xc = nc.dram_tensor("fz_xc", [BPC, 2048, S], BF16, kind="Internal").ap()
    dtt = nc.dram_tensor("fz_dt", [BPC, 2048, S], BF16, kind="Internal").ap()
    Bss = nc.dram_tensor("fz_B", [BPC, 64, S], BF16, kind="Internal").ap()
    yy = nc.dram_tensor("fz_y", [BPC, 2048, S], BF16, kind="Internal").ap()
    mrow = nc.dram_tensor("fz_mrow", [BPC, S], F32, kind="Internal").ap()
    rrow = nc.dram_tensor("fz_rrow", [BPC, S], F32, kind="Internal").ap()
    for nm in ("fz_mrow", "fz_rrow"):
        ctx.bufmap[nm] = Buf(nm)
    ctx.bufmap["mrow"] = ctx.bufmap["fz_mrow"]
    ctx.bufmap["rrow"] = ctx.bufmap["fz_rrow"]

    base_w = ctx.w

    ctx.begin_stage("s0_"); emit_ffn0(ctx, x_in, x1); ctx.end_stage()
    ctx.begin_stage("s1_"); emit_mamba_A(ctx, x1, xi, res, mrow, rrow); ctx.end_stage()
    ctx.begin_stage("s2_"); emit_mamba_B(ctx, xi, xc, dtt, Bss); ctx.end_stage()
    ctx.begin_stage("s3_"); emit_mamba_C(ctx, xc, dtt, Bss, yy); ctx.end_stage()
    ctx.begin_stage("s4_"); emit_mamba_D(ctx, yy, res, x1, x2, mrow, rrow); ctx.end_stage()
    ctx.begin_stage("s5_"); emit_ffn0(ctx, x2, x3); ctx.end_stage()
    ctx.begin_stage("s6_"); emit_mla(ctx, x3, xt1); ctx.end_stage()
    ctx.begin_stage("s7_"); emit_moe(ctx, xt1, x3, x4); ctx.end_stage()
    ctx.w = dict(base_w); ctx.w["wo"] = base_w["wo_qa"]
    ctx.begin_stage("s8_"); emit_qatt(ctx, x4, x5); ctx.end_stage()
    ctx.w = base_w
    ctx.begin_stage("s9_"); emit_mla(ctx, x5, xt1); ctx.end_stage()
    ctx.begin_stage("s10_"); emit_moe(ctx, xt1, x5, out_fm); ctx.end_stage()
    ctx.finalize()
    return nc


def kernel(**inputs):
    inp = {k: np.asarray(v) for k, v in inputs.items()}
    x = inp["L0_attn_x"]
    x_fm = _fm(x).reshape(NCORES, BPC, D, S)
    xs = [np.ascontiguousarray(x_fm[c]) for c in range(NCORES)]
    warr = {}
    warr.update(ffn0_weight_arrays(inp))
    warr.update(mamba_weight_arrays(inp))
    warr.update(mla_weight_arrays(inp))
    warr.update(moe_weight_arrays(inp))
    qa = qatt_weight_arrays(inp)
    warr["wq"] = qa["wq"]; warr["wk"] = qa["wk"]; warr["wv"] = qa["wv"]
    warr["wo_qa"] = qa["wo"]; warr["mask01"] = qa["mask01"]
    if "fused" not in _PROGRAMS:
        _PROGRAMS["fused"] = build_fused_program()
    nc = _PROGRAMS["fused"]
    in_maps = [dict(x_fm=xs[c], **warr) for c in range(NCORES)]
    res = run_bass_kernel_spmd(nc, in_maps, core_ids=list(range(NCORES)))
    out = [res.results[c]["out_fm"] for c in range(NCORES)]
    y = np.stack(out).reshape(32, D, S)
    return np.ascontiguousarray(np.swapaxes(y, -1, -2)).astype(np.float32)



# version 05
# speedup vs baseline: 1.0358x; 1.0339x over previous
import numpy as np
import ml_dtypes
import contextlib

import concourse.bass as bass
import concourse.mybir as mybir
from concourse.bass_utils import run_bass_kernel_spmd

F32 = mybir.dt.float32
BF16 = mybir.dt.bfloat16
AF = mybir.ActivationFunctionType
ALU = mybir.AluOpType
BF = ml_dtypes.bfloat16

NCORES = 8
BPC = 4
S = 2048
D = 1024
PD = D // 128

ENGINES = ("sync", "tensor", "vector", "scalar", "gpsimd")


class Sem:
    __slots__ = ("h", "n", "name")
    def __init__(self, h, name):
        self.h = h; self.n = 0; self.name = name

class Buf:
    __slots__ = ("w", "readers", "name")
    def __init__(self, name=""):
        self.w = None; self.readers = []; self.name = name

class Ctx:
    def __init__(self, nc):
        self.nc = nc
        self.items = {e: [] for e in ENGINES}
        self.stack = contextlib.ExitStack()
        self.esem_pool = {e: [self._mksem(f"sem_{e}{i}") for i in range(6)] for e in ENGINES}
        self.dsem_pool = {e: [self._mksem(f"dma_{e}{i}") for i in range(6)] for e in ENGINES}
        self._stage_idx = 0
        self.esem = {e: self.esem_pool[e][0] for e in ENGINES}
        self.dsem = {e: self.dsem_pool[e][0] for e in ENGINES}
        self.water = {e: {} for e in ENGINES}
        self.psum_banks = [Buf(f"psbank{i}") for i in range(8)]
        self.bufmap = {}
        self.banksmap = {}
        self._psum_cursor = 0
        self._names = set()
        self.nops = 0
        self.w = {}
        self._consts = {}
        self._cpool = None
        self.prefix = ""
        self._stage_stack = None
        self._bar = None
        self._done = None

    def _mksem(self, name):
        return Sem(self.stack.enter_context(self.nc.semaphore(name)), name)

    def begin_stage(self, prefix):
        self.prefix = prefix
        self._stage_stack = contextlib.ExitStack()
        self._psum_cursor = 0
        self._stage_idx += 1
        i = self._stage_idx % 6
        self.esem = {e: self.esem_pool[e][i] for e in ENGINES}
        self.dsem = {e: self.dsem_pool[e][i] for e in ENGINES}

    CLEAR_SEMS = False

    def end_stage(self):
        if self._bar is None:
            self._bar = self._mksem("bar_sem")
            self._done = self._mksem("done_sem")
        sems = list(self.esem.values()) + list(self.dsem.values())
        targets = [(t, t.n) for t in sems if t.n > 0]
        self._bar.n += len(ENGINES)
        self._done.n += 1
        barv = self._bar.n
        donev = self._done.n
        for e in ENGINES:
            wm = self.water[e]
            todo = [(t, v) for (t, v) in targets if wm.get(t, 0) < v]
            def fn(eo, _todo=tuple(todo), _bar=self._bar):
                for t, v in _todo:
                    eo.wait_ge(t.h, v)
                eo.sem_inc(_bar.h, 1)
            self.items[e].append(fn)
        do_clear = self.CLEAR_SEMS
        def fn_clear(eo, _sems=tuple(sems), _bar=self._bar, _bv=barv, _done=self._done,
                     _dc=do_clear):
            eo.wait_ge(_bar.h, _bv)
            if _dc:
                for t in _sems:
                    eo.sem_clear(t.h)
            eo.sem_inc(_done.h, 1)
        self.items["sync"].append(fn_clear)
        for e in ENGINES:
            if e == "sync":
                continue
            def fn_wait(eo, _done=self._done, _dv=donev):
                eo.wait_ge(_done.h, _dv)
            self.items[e].append(fn_wait)
        if self.CLEAR_SEMS:
            for t in sems:
                t.n = 0
        for e in ENGINES:
            self.water[e] = {}
        for b in self.psum_banks:
            b.w = None; b.readers = []
        for b in self.bufmap.values():
            b.w = None; b.readers = []
        self._stage_stack.close()
        self.prefix = ""

    def constf(self, value):
        value = float(value)
        if value in self._consts:
            return value
        if self._cpool is None:
            self._cpool = self.nc.alloc_sbuf_tensor("const_pool", [128, 64], F32)
            self.bufmap["const_pool"] = Buf("const_pool")
        i = len(self._consts)
        assert i < 64
        ap = self._cpool[:, i:i + 1]
        self.op("vector", lambda e, _a=ap, _v=value: e.memset(_a, _v), writes=[self._cpool])
        self.nc.const_aps.register(F32, value, ap)
        self._consts[value] = True
        return value

    def sb(self, name, shape, dtype):
        name = self.prefix + name
        assert name not in self._names, name
        self._names.add(name)
        if self._stage_stack is not None:
            t = self._stage_stack.enter_context(self.nc.sbuf_tensor(name, shape, dtype))
        else:
            t = self.nc.alloc_sbuf_tensor(name, shape, dtype)
        self.bufmap[t.name] = Buf(name)
        return t

    def ps(self, name, shape, dtype=F32):
        name = self.prefix + name
        assert name not in self._names, name
        self._names.add(name)
        if self._stage_stack is not None:
            t = self._stage_stack.enter_context(self.nc.psum_tensor(name, shape, dtype))
        else:
            t = self.nc.alloc_psum_tensor(name, shape, dtype)
        nb = self.nc.num_psum_banks(shape, dtype)
        start = self._psum_cursor
        self._psum_cursor = start + nb
        assert self._psum_cursor <= 8, "psum overflow"
        self.banksmap[t.name] = list(range(start, start + nb))
        return t

    def mkbuf(self, name):
        name = self.prefix + name
        b = Buf(name)
        self.bufmap["@" + name] = b
        return b

    def dram_i(self, name, shape, dtype):
        name = self.prefix + name
        t = self.nc.dram_tensor(name, shape, dtype, kind="Internal")
        self.bufmap[name] = Buf(name)
        return t.ap()

    def _bufs_of(self, objs):
        out = []
        for o in objs:
            if o is None:
                continue
            if isinstance(o, Buf):
                out.append(o)
            elif isinstance(o, (list, tuple)):
                out.extend(self._bufs_of(o))
            else:
                nm = getattr(o, "name", None)
                if nm is None and hasattr(o, "tensor"):
                    nm = o.tensor.name
                if nm is None:
                    raise TypeError(f"cannot track {o!r}")
                if nm in self.banksmap:
                    for b in self.banksmap[nm]:
                        out.append(self.psum_banks[b])
                elif nm in self.bufmap:
                    out.append(self.bufmap[nm])
                else:
                    raise TypeError(f"untracked tensor {nm}")
        return out

    def op(self, eng, build, reads=(), writes=(), dma=False):
        rbufs = self._bufs_of(reads)
        wbufs = self._bufs_of(writes)
        need = {}
        for b in rbufs:
            if b.w is not None:
                s, v = b.w
                if need.get(s, 0) < v:
                    need[s] = v
        for b in wbufs:
            if b.w is not None:
                s, v = b.w
                if need.get(s, 0) < v:
                    need[s] = v
            for s, v in b.readers:
                if need.get(s, 0) < v:
                    need[s] = v
        sem = self.dsem[eng] if dma else self.esem[eng]
        inc = 16 if dma else 1
        if dma and sem.n > 0 and need.get(sem, 0) < sem.n:
            need[sem] = sem.n
        sem.n += inc
        val = sem.n
        wm = self.water[eng]
        waits = []
        for s, v in need.items():
            if wm.get(s, 0) >= v:
                continue
            wm[s] = v
            waits.append((s, v))
        def fn(e, _waits=tuple(waits), _build=build, _sem=sem, _inc=inc):
            for s, v in _waits:
                e.wait_ge(s.h, v)
            _build(e).then_inc(_sem.h, _inc)
        self.items[eng].append(fn)
        self.nops += 1
        for b in rbufs:
            b.readers.append((sem, val))
        for b in wbufs:
            b.w = (sem, val)
            b.readers = []
        return (sem, val)

    def fence(self):
        sems = list(self.esem.values()) + list(self.dsem.values())
        targets = [(s, s.n) for s in sems if s.n > 0]
        for e in ENGINES:
            wm = self.water[e]
            todo = [(s, v) for (s, v) in targets if wm.get(s, 0) < v]
            for s, v in todo:
                wm[s] = v
            def fn(eng_obj, _todo=tuple(todo)):
                for s, v in _todo:
                    eng_obj.wait_ge(s.h, v)
            self.items[e].append(fn)
        for b in self.psum_banks:
            b.w = None; b.readers = []

    def finalize(self):
        for e in ENGINES:
            s = self.dsem[e]
            if s.n > 0:
                self.items["sync"].append(lambda eo, _s=s, _v=s.n: eo.wait_ge(_s.h, _v))
        with self.nc.Block() as block:
            for e in ENGINES:
                def body(eng_obj, _e=e):
                    for fn in self.items[_e]:
                        fn(eng_obj)
                getattr(block, e)(body)
        self.stack.close()



def DMA(ctx, eng, out, in_, reads=(), writes=()):
    ctx.op(eng, lambda e, _o=out, _i=in_: e.dma_start(out=_o, in_=_i),
           reads=list(reads), writes=list(writes), dma=True)

def MM(ctx, out, lhsT, rhs, start, stop, reads=(), writes=(), tp=None):
    if tp is not None and tp[1] != 96:
        tp = None
    ctx.op("tensor", lambda e, _o=out, _l=lhsT, _r=rhs, _s=start, _p=stop, _t=tp:
           e.matmul(out=_o, lhsT=_l, rhs=_r, start=_s, stop=_p, tile_position=_t),
           reads=list(reads), writes=list(writes))

def TRANS(ctx, out, in_, ident, reads=(), writes=()):
    ctx.op("tensor", lambda e, _o=out, _i=in_, _d=ident: e.transpose(out=_o, in_=_i, identity=_d),
           reads=list(reads), writes=list(writes))

def COPY(ctx, eng, out, in_, reads=(), writes=()):
    ctx.op(eng, lambda e, _o=out, _i=in_: e.tensor_copy(out=_o, in_=_i),
           reads=list(reads), writes=list(writes))

def TT(ctx, eng, out, in0, in1, alu, reads=(), writes=()):
    ctx.op(eng, lambda e, _o=out, _a=in0, _b=in1, _al=alu: e.tensor_tensor(out=_o, in0=_a, in1=_b, op=_al),
           reads=list(reads), writes=list(writes))

def TS(ctx, eng, out, in0, s1, alu, reads=(), writes=()):
    ctx.op(eng, lambda e, _o=out, _a=in0, _s=s1, _al=alu: e.tensor_scalar(
           out=_o, in0=_a, scalar1=_s, scalar2=None, op0=_al),
           reads=list(reads), writes=list(writes))

def STT(ctx, eng, out, in0, scalar, in1, op0, op1, reads=(), writes=()):
    ctx.op(eng, lambda e, _o=out, _a=in0, _s=scalar, _b=in1, _0=op0, _1=op1:
           e.scalar_tensor_tensor(out=_o, in0=_a, scalar=_s, in1=_b, op0=_0, op1=_1),
           reads=list(reads), writes=list(writes))

def ACT(ctx, out, in_, func, bias=0.0, scale=1.0, accum=None, reads=(), writes=()):
    reads = list(reads)
    if isinstance(bias, float) and bias != 0.0:
        ctx.constf(bias)
        reads.append(ctx._cpool)
    ctx.op("scalar", lambda e, _o=out, _i=in_, _f=func, _b=bias, _s=scale, _a=accum:
           e.activation(out=_o, in_=_i, func=_f, bias=_b, scale=_s, accum_out=_a),
           reads=reads, writes=list(writes))

def SCAN(ctx, out, data0, data1, initial, reads=(), writes=()):
    ctx.op("vector", lambda e, _o=out, _a=data0, _b=data1, _i=initial:
           e.tensor_tensor_scan(out=_o, data0=_a, data1=_b, initial=_i,
                                op0=ALU.mult, op1=ALU.add),
           reads=list(reads), writes=list(writes))

def MEMSET(ctx, eng, ap, val, writes=()):
    ctx.op(eng, lambda e, _a=ap, _v=val: e.memset(_a, _v), writes=list(writes))

def RECIP(ctx, out, in_, reads=(), writes=()):
    ctx.op("vector", lambda e, _o=out, _i=in_: e.reciprocal(out=_o, in_=_i),
           reads=list(reads), writes=list(writes))

def MAX8(ctx, out, in_, reads=(), writes=()):
    ctx.op("vector", lambda e, _o=out, _i=in_: e.max(out=_o, in_=_i),
           reads=list(reads), writes=list(writes))

def IOTA(ctx, out, pattern, base, cm, writes=()):
    ctx.op("gpsimd", lambda e, _o=out, _p=pattern, _b=base, _c=cm:
           e.iota(_o, pattern=_p, base=_b, channel_multiplier=_c,
                  allow_small_or_imprecise_dtypes=True),
           writes=list(writes))


def load_weight_bf16(ctx, name, dram_ap, rows, cols, scale=None, scale_col=None,
                     scale_col_reads=()):
    f = ctx.sb(name + "_f", [rows, cols], F32)
    b = ctx.sb(name, [rows, cols], BF16)
    DMA(ctx, "sync", f[:], dram_ap, writes=[f])
    if scale_col is not None:
        TS(ctx, "vector", b[:], f[:], scale_col, ALU.mult,
           reads=[f] + list(scale_col_reads), writes=[b])
    elif scale is not None and scale != 1.0:
        TS(ctx, "vector", b[:], f[:], float(scale), ALU.mult, reads=[f], writes=[b])
    else:
        COPY(ctx, "vector", b[:], f[:], reads=[f], writes=[b])
    return b


def make_ident(ctx, prefix):
    iota = ctx.sb(prefix + "_iota", [128, 128], F32)
    idf = ctx.sb(prefix + "_idf", [128, 128], F32)
    idb = ctx.sb(prefix + "_idb", [128, 128], BF16)
    IOTA(ctx, iota[:], [[1, 128]], 0, -1, writes=[iota])
    TS(ctx, "vector", idf[:], iota[:], 0.0, ALU.is_equal, reads=[iota], writes=[idf])
    COPY(ctx, "vector", idb[:], idf[:], reads=[idf], writes=[idb])
    return idf, idb


class NS:
    pass



def emit_ffn0(ctx, x_in, y_out):
    TQ = 512
    NT = S // TQ
    w = ctx.w

    ones_col_b = ctx.sb("ffn_ones_col", [128, 1], BF16)
    ones_row_f = ctx.sb("ffn_ones_rowf", [1, 128], F32)
    one_f = ctx.sb("ffn_one_f", [1, 1], F32)
    MEMSET(ctx, "vector", ones_col_b[:], 1.0, writes=[ones_col_b])
    MEMSET(ctx, "vector", ones_row_f[:], 1.0, writes=[ones_row_f])
    MEMSET(ctx, "vector", one_f[:], 1.0, writes=[one_f])
    idf, idb = make_ident(ctx, "ffn_id")

    wlin_b = [load_weight_bf16(ctx, f"ffn_wlin{k}", w["wlinT"][k * 128:(k + 1) * 128, :], 128, D)
              for k in range(PD)]
    lcat_b = [load_weight_bf16(ctx, f"ffn_lcat{k}", w["lcat"][k * 128:(k + 1) * 128, :], 128, 32)
              for k in range(PD)]
    gate_b = [load_weight_bf16(ctx, f"ffn_gate{k}", w["gateT"][k * 128:(k + 1) * 128, :], 128, 8)
              for k in range(PD)]
    rcat_b = load_weight_bf16(ctx, "ffn_rcat", w["rcat"], 32, D)
    blin_f = ctx.sb("ffn_blin_f", [128, 8], F32)
    halfb = ctx.sb("ffn_halfb", [128, 8], F32)
    DMA(ctx, "sync", blin_f[:], w["blin"], writes=[blin_f])
    TS(ctx, "vector", halfb[:], blin_f[:], 0.5, ALU.mult, reads=[blin_f], writes=[halfb])

    ps_row = ctx.ps("ffn_psrow", [1, 512], F32)
    wsum_wlin = ctx.sb("ffn_wsum_wlin", [1, D], BF16)
    wsum_lcat = ctx.sb("ffn_wsum_lcat", [1, 32], BF16)
    wsum_gate = ctx.sb("ffn_wsum_gate", [1, 8], BF16)
    for (wchunks, m, dst) in ((wlin_b, D, wsum_wlin), (lcat_b, 32, wsum_lcat), (gate_b, 8, wsum_gate)):
        step = min(m, 512)
        for blk in range(0, m, step):
            for k in range(PD):
                MM(ctx, ps_row[0:1, 0:step], ones_col_b[:], wchunks[k][:, blk:blk + step],
                   start=(k == 0), stop=(k == PD - 1),
                   reads=[wchunks[k], ones_col_b], writes=[ps_row])
            TS(ctx, "vector", dst[:, blk:blk + step], ps_row[0:1, 0:step], -1.0, ALU.mult,
               reads=[ps_row], writes=[dst])

    sel_sc = ctx.sb("ffn_sel_sc", [8, 32], F32)
    sel_b = ctx.sb("ffn_sel", [8, 32], BF16)
    IOTA(ctx, sel_sc[0:8, :], [[-1, 8], [0, 4]], 0, 1, writes=[sel_sc])
    TS(ctx, "vector", sel_b[:], sel_sc[:], 0.0, ALU.is_equal, reads=[sel_sc], writes=[sel_b])

    ps_mean = ctx.ps("ffn_ps_mean", [1, TQ], F32)
    ps_sq = ctx.ps("ffn_ps_sq", [1, TQ], F32)
    ps_big = ctx.ps("ffn_ps_big", [128, TQ], F32)
    ps_aux = ctx.ps("ffn_ps_aux", [128, TQ], F32)
    ps_wrep = ctx.ps("ffn_ps_wrep", [32, TQ], F32)
    ps_tr = ctx.ps("ffn_ps_tr", [128, 128], F32)
    ps_bc = ctx.ps("ffn_ps_bc", [128, TQ], F32)

    mean_f = ctx.sb("ffn_mean_f", [1, TQ], F32)
    sq_f = ctx.sb("ffn_sq_f", [1, TQ], F32)
    var_f = ctx.sb("ffn_var_f", [1, TQ], F32)
    meanrow_b = ctx.sb("ffn_meanrow", [1, TQ], BF16)
    rstdrow = ctx.sb("ffn_rstdrow", [1, TQ], F32)
    rstdhrow = ctx.sb("ffn_rstdh", [1, TQ], F32)
    rstdB = ctx.sb("ffn_rstdB", [128, TQ], F32)
    rstd_cols = ctx.sb("ffn_rstdcols", [128, 4], F32)
    logit = ctx.sb("ffn_logit", [128, 4 * 8], F32)
    t8 = ctx.sb("ffn_t8", [128, 4 * 8], F32)
    negm1 = ctx.sb("ffn_negm1", [128, 4], F32)
    ssum = ctx.sb("ffn_ssum", [128, 4], F32)
    srec = ctx.sb("ffn_srec", [128, 4], F32)
    eprob = ctx.sb("ffn_eprob", [128, 4 * 8], F32)
    gmask = ctx.sb("ffn_gmask", [128, 4 * 8], F32)
    wts = ctx.sb("ffn_wts", [128, 4 * 8], F32)
    wtsT_b = ctx.sb("ffn_wtsT", [8, TQ], BF16)
    tall_b = ctx.sb("ffn_tall", [32, TQ], BF16)
    tw_b = ctx.sb("ffn_tw", [32, TQ], BF16)
    v1 = ctx.sb("ffn_v1", [128, TQ], F32)
    out_f = ctx.sb("ffn_outf", [128, PD * TQ], F32)
    outb = [ctx.mkbuf(f"ffn_outf_{k}") for k in range(PD)]

    xf = [ctx.sb(f"ffn_xf{i}", [128, PD * TQ], F32) for i in range(2)]
    xfb = [[ctx.mkbuf(f"ffn_xfb{i}_{k}") for k in range(PD)] for i in range(2)]
    xb = [ctx.sb(f"ffn_xb{i}", [128, PD * TQ], BF16) for i in range(2)]
    x2 = ctx.sb("ffn_x2", [128, PD * TQ], BF16)

    for b in range(BPC):
        for tci in range(NT):
            t0 = tci * TQ
            pp = (b * NT + tci) % 2
            xf_, xb_, xfb_ = xf[pp], xb[pp], xfb[pp]
            DMA(ctx, "sync",
                xf_[:].rearrange("p (k t) -> p k t", k=PD),
                x_in[b, :, t0:t0 + TQ].rearrange("(k p) t -> p k t", p=128),
                writes=list(xfb_))
            COPY(ctx, "vector", xb_[:], xf_[:], reads=xfb_, writes=[xb_])
            ACT(ctx, x2[:], xf_[:], AF.Square, reads=xfb_, writes=[x2])
            for k in range(PD):
                MM(ctx, ps_mean[0:1, :], ones_col_b[:], xb_[:, k * TQ:(k + 1) * TQ],
                   start=(k == 0), stop=(k == PD - 1), reads=[xb_, ones_col_b], writes=[ps_mean])
            for k in range(PD):
                MM(ctx, ps_sq[0:1, :], ones_col_b[:], x2[:, k * TQ:(k + 1) * TQ],
                   start=(k == 0), stop=(k == PD - 1), reads=[x2, ones_col_b], writes=[ps_sq])
            ACT(ctx, mean_f[:], ps_mean[0:1, :], AF.Copy, scale=1.0 / D,
                reads=[ps_mean], writes=[mean_f])
            COPY(ctx, "vector", meanrow_b[:], mean_f[:], reads=[mean_f], writes=[meanrow_b])
            ACT(ctx, sq_f[:], mean_f[:], AF.Square, reads=[mean_f], writes=[sq_f])
            STT(ctx, "vector", var_f[:], ps_sq[0:1, :], 1.0 / D, sq_f[:],
                ALU.mult, ALU.subtract, reads=[ps_sq, sq_f], writes=[var_f])
            ACT(ctx, sq_f[:], var_f[:], AF.Sqrt, bias=1e-5, reads=[var_f], writes=[sq_f])
            RECIP(ctx, rstdrow[:], sq_f[:], reads=[sq_f], writes=[rstdrow])
            TS(ctx, "vector", rstdhrow[:], rstdrow[:], 0.5, ALU.mult,
               reads=[rstdrow], writes=[rstdhrow])
            MM(ctx, ps_bc[:], ones_row_f[:], rstdhrow[:], start=True, stop=True,
               reads=[rstdhrow, ones_row_f], writes=[ps_bc])
            COPY(ctx, "vector", rstdB[:], ps_bc[:], reads=[ps_bc], writes=[rstdB])
            for j in range(4):
                MM(ctx, ps_tr[:, 0:1], rstdrow[:, j * 128:(j + 1) * 128], one_f[:],
                   start=True, stop=True, reads=[rstdrow, one_f], writes=[ps_tr])
                COPY(ctx, "vector", rstd_cols[:, j:j + 1], ps_tr[:, 0:1],
                     reads=[ps_tr], writes=[rstd_cols])
            for j in range(4):
                for k in range(PD):
                    MM(ctx, ps_aux[0:128, j * 8:(j + 1) * 8],
                       xb_[:, k * TQ + j * 128:k * TQ + (j + 1) * 128], gate_b[k][:],
                       start=(k == 0), stop=False, reads=[xb_, gate_b[k]], writes=[ps_aux])
                MM(ctx, ps_aux[0:128, j * 8:(j + 1) * 8],
                   meanrow_b[:, j * 128:(j + 1) * 128], wsum_gate[:],
                   start=False, stop=True, reads=[meanrow_b, wsum_gate], writes=[ps_aux])
            for j in range(4):
                sl = slice(j * 8, (j + 1) * 8)
                TS(ctx, "vector", logit[:, sl], ps_aux[0:128, sl], rstd_cols[:, j:j + 1],
                   ALU.mult, reads=[ps_aux, rstd_cols], writes=[logit])
                MAX8(ctx, t8[:, sl], logit[:, sl], reads=[logit], writes=[t8])
                TS(ctx, "vector", negm1[:, j:j + 1], t8[:, j * 8:j * 8 + 1], -1.0, ALU.mult,
                   reads=[t8], writes=[negm1])
                ACT(ctx, eprob[:, sl], logit[:, sl], AF.Exp, bias=negm1[:, j:j + 1],
                    accum=ssum[:, j:j + 1], reads=[logit, negm1], writes=[eprob, ssum])
                RECIP(ctx, srec[:, j:j + 1], ssum[:, j:j + 1], reads=[ssum], writes=[srec])
                TS(ctx, "vector", gmask[:, sl], logit[:, sl], t8[:, j * 8 + 1:j * 8 + 2],
                   ALU.is_ge, reads=[logit, t8], writes=[gmask])
                TT(ctx, "vector", wts[:, sl], eprob[:, sl], gmask[:, sl], ALU.mult,
                   reads=[eprob, gmask], writes=[wts])
                TS(ctx, "vector", wts[:, sl], wts[:, sl], srec[:, j:j + 1], ALU.mult,
                   reads=[wts, srec], writes=[wts])
                TRANS(ctx, ps_tr[0:8, 0:128], wts[:, sl], idf[:], reads=[wts, idf], writes=[ps_tr])
                COPY(ctx, "vector", wtsT_b[0:8, j * 128:(j + 1) * 128], ps_tr[0:8, 0:128],
                     reads=[ps_tr], writes=[wtsT_b])
            MM(ctx, ps_wrep[:], sel_b[:], wtsT_b[:], start=True, stop=True,
               reads=[sel_b, wtsT_b], writes=[ps_wrep])
            for k in range(PD):
                MM(ctx, ps_aux[0:32, :], lcat_b[k][:], xb_[:, k * TQ:(k + 1) * TQ],
                   start=(k == 0), stop=False, reads=[lcat_b[k], xb_], writes=[ps_aux])
            MM(ctx, ps_aux[0:32, :], wsum_lcat[:], meanrow_b[:], start=False, stop=True,
               reads=[wsum_lcat, meanrow_b], writes=[ps_aux])
            COPY(ctx, "vector", tall_b[:], ps_aux[0:32, :], reads=[ps_aux], writes=[tall_b])
            TT(ctx, "vector", tw_b[:], ps_wrep[0:32, :], tall_b[:], ALU.mult,
               reads=[ps_wrep, tall_b], writes=[tw_b])
            for m in range(PD):
                for k in range(PD):
                    MM(ctx, ps_big[:], wlin_b[k][:, m * 128:(m + 1) * 128],
                       xb_[:, k * TQ:(k + 1) * TQ], start=(k == 0), stop=False,
                       reads=[wlin_b[k], xb_], writes=[ps_big])
                MM(ctx, ps_big[:], wsum_wlin[:, m * 128:(m + 1) * 128], meanrow_b[:],
                   start=False, stop=False, reads=[wsum_wlin, meanrow_b], writes=[ps_big])
                MM(ctx, ps_big[:], rcat_b[:, m * 128:(m + 1) * 128], tw_b[:],
                   start=False, stop=True, reads=[rcat_b, tw_b], writes=[ps_big])
                TT(ctx, "vector", v1[:], ps_big[:], rstdB[:], ALU.mult,
                   reads=[ps_big, rstdB], writes=[v1])
                STT(ctx, "vector", out_f[:, m * TQ:(m + 1) * TQ], v1[:], halfb[:, m:m + 1],
                    xf_[:, m * TQ:(m + 1) * TQ], ALU.add, ALU.add,
                    reads=[v1, halfb, xfb_[m]], writes=[outb[m]])
            DMA(ctx, "gpsimd",
                y_out[b, :, t0:t0 + TQ].rearrange("(k p) t -> p k t", p=128),
                out_f[:].rearrange("p (k t) -> p k t", k=PD),
                reads=list(outb))



def _fm(x):
    return np.ascontiguousarray(np.swapaxes(x, -1, -2))


def ffn0_weight_arrays(inp):
    return {
        "wlinT": np.ascontiguousarray(inp["L0_ffn_W_lin"].T),
        "lcat": np.ascontiguousarray(np.transpose(np.asarray(inp["L0_ffn_left"]), (1, 0, 2)).reshape(D, 32)),
        "rcat": np.ascontiguousarray(np.asarray(inp["L0_ffn_right"]).reshape(32, D)),
        "gateT": np.ascontiguousarray(inp["L0_ffn_gate_w"].T),
        "blin": np.ascontiguousarray(np.asarray(inp["L0_ffn_b_lin"]).reshape(8, 128).T),
    }


def build_ffn0_program():
    nc = bass.Bass("TRN2", target_bir_lowering=False)
    ctx = Ctx(nc)
    x_in = nc.dram_tensor("x_fm", [BPC, D, S], F32, kind="ExternalInput").ap()
    y_out = nc.dram_tensor("y_fm", [BPC, D, S], F32, kind="ExternalOutput").ap()
    ctx.w = {
        "wlinT": nc.dram_tensor("wlinT", [D, D], F32, kind="ExternalInput").ap(),
        "lcat": nc.dram_tensor("lcat", [D, 32], F32, kind="ExternalInput").ap(),
        "rcat": nc.dram_tensor("rcat", [32, D], F32, kind="ExternalInput").ap(),
        "gateT": nc.dram_tensor("gateT", [D, 8], F32, kind="ExternalInput").ap(),
        "blin": nc.dram_tensor("blin", [128, 8], F32, kind="ExternalInput").ap(),
    }
    emit_ffn0(ctx, x_in, y_out)
    ctx.finalize()
    return nc


def run_ffn0(x_fm_percore, inp):
    nc = build_ffn0_program()
    warr = ffn0_weight_arrays(inp)
    in_maps = [dict(x_fm=x_fm_percore[c], **warr) for c in range(NCORES)]
    res = run_bass_kernel_spmd(nc, in_maps, core_ids=list(range(NCORES)))
    return [r["y_fm"] for r in res.results]





def convert_into(ctx, dest_ap, dest_buf, dram_ap, rows, cols, stage, scale_col=None,
                 scale=None, scale_col_reads=()):
    DMA(ctx, "sync", stage[0:rows, 0:cols], dram_ap, writes=[stage])
    src = stage[0:rows, 0:cols]
    if scale_col is not None:
        TS(ctx, "vector", dest_ap, src, scale_col, ALU.mult,
           reads=[stage] + list(scale_col_reads), writes=[dest_buf])
    elif scale is not None and scale != 1.0:
        TS(ctx, "vector", dest_ap, src, float(scale), ALU.mult, reads=[stage], writes=[dest_buf])
    else:
        COPY(ctx, "vector", dest_ap, src, reads=[stage], writes=[dest_buf])


def load_weight_staged(ctx, name, dram_ap, rows, cols, stage, scale_col=None,
                       scale=None, scale_col_reads=(), chunk=512):
    b = ctx.sb(name, [rows, cols], BF16)
    if isinstance(stage, (list, tuple)):
        stages = stage
    else:
        stages = [stage]
    i = 0
    for c0 in range(0, cols, chunk):
        cw = min(chunk, cols - c0)
        convert_into(ctx, b[:, c0:c0 + cw], b, dram_ap[:, c0:c0 + cw], rows, cw,
                     stages[i % len(stages)], scale_col=scale_col, scale=scale,
                     scale_col_reads=scale_col_reads)
        i += 1
    return b


def emit_wsum_neg(ctx, name, wchunks, m, ones_col_b, ps_row):
    dst = ctx.sb(name, [1, m], BF16)
    step = min(m, 512)
    nk = len(wchunks)
    for blk in range(0, m, step):
        for k in range(nk):
            MM(ctx, ps_row[0:1, 0:step], ones_col_b[:], wchunks[k][:, blk:blk + step],
               start=(k == 0), stop=(k == nk - 1), reads=[wchunks[k], ones_col_b],
               writes=[ps_row])
        TS(ctx, "vector", dst[:, blk:blk + step], ps_row[0:1, 0:step], -1.0, ALU.mult,
           reads=[ps_row], writes=[dst])
    return dst



def emit_mamba_A(ctx, x_in, xi_out, res_out, mrow_out, rrow_out):
    TQ = 512
    NT = S // TQ
    w = ctx.w

    ones_col_b = ctx.sb("mba_ones_col", [128, 1], BF16)
    ones_row_f = ctx.sb("mba_ones_rowf", [1, 128], F32)
    MEMSET(ctx, "vector", ones_col_b[:], 1.0, writes=[ones_col_b])
    MEMSET(ctx, "vector", ones_row_f[:], 1.0, writes=[ones_row_f])

    normw_cols = ctx.sb("mba_normw", [128, PD], F32)
    DMA(ctx, "sync", normw_cols[:], w["normw_cols"], writes=[normw_cols])

    stage = [ctx.sb(f"mba_stg{i}", [128, 2048], F32) for i in range(2)]
    wip_b = [load_weight_staged(ctx, f"mba_wip{k}", w["wipT"][k * 128:(k + 1) * 128, :],
                                128, 4096, stage,
                                scale_col=normw_cols[:, k:k + 1],
                                scale_col_reads=[normw_cols])
             for k in range(PD)]
    ps_row = ctx.ps("mba_psrow", [1, 512], F32)
    wsum_ip = emit_wsum_neg(ctx, "mba_wsum_ip", wip_b, 4096, ones_col_b, ps_row)

    ps_mean = ctx.ps("mba_ps_mean", [1, TQ], F32)
    ps_sq = ctx.ps("mba_ps_sq", [1, TQ], F32)
    ps_bc = ctx.ps("mba_ps_bc", [128, TQ], F32)
    ps_main = [ctx.ps(f"mba_ps_main{i}", [128, TQ], F32) for i in range(2)]

    mean_f = ctx.sb("mba_mean_f", [1, TQ], F32)
    sq_f = ctx.sb("mba_sq_f", [1, TQ], F32)
    var_f = ctx.sb("mba_var_f", [1, TQ], F32)
    meanrow_b = ctx.sb("mba_meanrow", [1, TQ], BF16)
    rstdrow = ctx.sb("mba_rstdrow", [1, TQ], F32)
    rstd2 = ctx.sb("mba_rstd2", [1, TQ], F32)
    qrow = ctx.sb("mba_qrow", [1, TQ], F32)
    srow = ctx.sb("mba_srow", [1, TQ], F32)
    sB = ctx.sb("mba_sB", [128, TQ], F32)

    xf = [ctx.sb(f"mba_xf{i}", [128, PD * TQ], F32) for i in range(2)]
    xfb = [[ctx.mkbuf(f"mba_xfb{i}_{k}") for k in range(PD)] for i in range(2)]
    xb = [ctx.sb(f"mba_xb{i}", [128, PD * TQ], BF16) for i in range(2)]
    x2 = ctx.sb("mba_x2", [128, PD * TQ], BF16)
    xi_sb = ctx.sb("mba_xi", [128, 8 * TQ], F32)
    xib = [ctx.mkbuf(f"mba_xib{m}") for m in range(8)]
    res_sb = ctx.sb("mba_res", [128, 16 * TQ], BF16)
    resb = [ctx.mkbuf(f"mba_resb{m}") for m in range(16)]

    for b in range(BPC):
        for tci in range(NT):
            t0 = tci * TQ
            pp = (b * NT + tci) % 2
            xf_, xb_, xfb_ = xf[pp], xb[pp], xfb[pp]
            DMA(ctx, "sync",
                xf_[:].rearrange("p (k t) -> p k t", k=PD),
                x_in[b, :, t0:t0 + TQ].rearrange("(k p) t -> p k t", p=128),
                writes=list(xfb_))
            COPY(ctx, "vector", xb_[:], xf_[:], reads=xfb_, writes=[xb_])
            ACT(ctx, x2[:], xf_[:], AF.Square, reads=xfb_, writes=[x2])
            for k in range(PD):
                MM(ctx, ps_mean[0:1, :], ones_col_b[:], xb_[:, k * TQ:(k + 1) * TQ],
                   start=(k == 0), stop=(k == PD - 1), reads=[xb_, ones_col_b],
                   writes=[ps_mean])
            for k in range(PD):
                MM(ctx, ps_sq[0:1, :], ones_col_b[:], x2[:, k * TQ:(k + 1) * TQ],
                   start=(k == 0), stop=(k == PD - 1), reads=[x2, ones_col_b],
                   writes=[ps_sq])
            ACT(ctx, mean_f[:], ps_mean[0:1, :], AF.Copy, scale=1.0 / D,
                reads=[ps_mean], writes=[mean_f])
            COPY(ctx, "vector", meanrow_b[:], mean_f[:], reads=[mean_f], writes=[meanrow_b])
            ACT(ctx, sq_f[:], mean_f[:], AF.Square, reads=[mean_f], writes=[sq_f])
            STT(ctx, "vector", var_f[:], ps_sq[0:1, :], 1.0 / D, sq_f[:],
                ALU.mult, ALU.subtract, reads=[ps_sq, sq_f], writes=[var_f])
            ACT(ctx, sq_f[:], var_f[:], AF.Sqrt, bias=1e-5, reads=[var_f], writes=[sq_f])
            RECIP(ctx, rstdrow[:], sq_f[:], reads=[sq_f], writes=[rstdrow])
            DMA(ctx, "gpsimd", mrow_out[b:b + 1, t0:t0 + TQ], mean_f[0:1, :], reads=[mean_f])
            DMA(ctx, "gpsimd", rrow_out[b:b + 1, t0:t0 + TQ], rstdrow[0:1, :], reads=[rstdrow])
            ACT(ctx, rstd2[:], rstdrow[:], AF.Square, reads=[rstdrow], writes=[rstd2])
            TT(ctx, "vector", qrow[:], var_f[:], rstd2[:], ALU.mult,
               reads=[var_f, rstd2], writes=[qrow])
            ACT(ctx, qrow[:], qrow[:], AF.Sqrt, bias=1e-6, reads=[qrow], writes=[qrow])
            RECIP(ctx, qrow[:], qrow[:], reads=[qrow], writes=[qrow])
            TT(ctx, "vector", srow[:], rstdrow[:], qrow[:], ALU.mult,
               reads=[rstdrow, qrow], writes=[srow])
            MM(ctx, ps_bc[:], ones_row_f[:], srow[:], start=True, stop=True,
               reads=[srow, ones_row_f], writes=[ps_bc])
            COPY(ctx, "vector", sB[:], ps_bc[:], reads=[ps_bc], writes=[sB])
            for m in range(32):
                pm = ps_main[m % 2]
                for k in range(PD):
                    MM(ctx, pm[:], wip_b[k][:, m * 128:(m + 1) * 128],
                       xb_[:, k * TQ:(k + 1) * TQ], start=(k == 0), stop=False,
                       reads=[wip_b[k], xb_], writes=[pm])
                MM(ctx, pm[:], wsum_ip[:, m * 128:(m + 1) * 128], meanrow_b[:],
                   start=False, stop=True, reads=[wsum_ip, meanrow_b], writes=[pm])
                if m < 16:
                    mh = m % 8
                    TT(ctx, "vector", xi_sb[:, mh * TQ:(mh + 1) * TQ], pm[:], sB[:],
                       ALU.mult, reads=[pm, sB], writes=[xib[mh]])
                    if m == 7 or m == 15:
                        h0 = (m // 8) * 1024
                        DMA(ctx, "gpsimd",
                            xi_out[b, h0:h0 + 1024, t0:t0 + TQ].rearrange("(k p) t -> p k t", p=128),
                            xi_sb[:].rearrange("p (k t) -> p k t", k=8),
                            reads=list(xib))
                else:
                    mm = m - 16
                    TT(ctx, "vector", res_sb[:, mm * TQ:(mm + 1) * TQ], pm[:], sB[:],
                       ALU.mult, reads=[pm, sB], writes=[resb[mm]])
            DMA(ctx, "gpsimd",
                res_out[b, :, t0:t0 + TQ].rearrange("(k p) t -> p k t", p=128),
                res_sb[:].rearrange("p (k t) -> p k t", k=16), reads=list(resb))



def emit_mamba_B(ctx, xi_in, xc_out, dt_out, B_out):
    TQ = 512
    NT = S // TQ
    w = ctx.w

    conv_cols = ctx.sb("mbb_convw", [128, 16 * 4], F32)
    convb_cols = ctx.sb("mbb_convb", [128, 16], F32)
    dtb_cols = ctx.sb("mbb_dtb", [128, 16], F32)
    DMA(ctx, "sync", conv_cols[:], w["conv_cols"], writes=[conv_cols])
    DMA(ctx, "sync", convb_cols[:], w["convb_cols"], writes=[convb_cols])
    DMA(ctx, "sync", dtb_cols[:], w["dtb_cols"], writes=[dtb_cols])

    stage = [ctx.sb(f"mbb_stg{i}", [128, 2048], F32) for i in range(2)]
    dtw_b = [load_weight_staged(ctx, f"mbb_dtw{k}", w["dtwT"][k * 128:(k + 1) * 128, :],
                                128, 2048, stage[k % 2]) for k in range(16)]
    xpw_b = [load_weight_staged(ctx, f"mbb_xpw{k}", w["xpwT"][k * 128:(k + 1) * 128, :],
                                128, 64, stage[k % 2]) for k in range(16)]

    xi_sb = [ctx.sb(f"mbb_xi{i}", [128, 16 * (TQ + 3)], F32) for i in range(2)]
    xib = [[ctx.mkbuf(f"mbb_xib{i}_{k}") for k in range(16)] for i in range(2)]
    acc = ctx.sb("mbb_acc", [128, TQ], F32)
    xc_b = ctx.sb("mbb_xc", [128, 16 * TQ], BF16)
    xcb = [ctx.mkbuf(f"mbb_xcb{k}") for k in range(16)]
    ef = ctx.sb("mbb_ef", [128, TQ], F32)
    dt_sb = ctx.sb("mbb_dt", [128, 16 * TQ], BF16)
    dtb_ = [ctx.mkbuf(f"mbb_dtb{k}") for k in range(16)]
    B_sb = ctx.sb("mbb_B", [64, TQ], BF16)

    ps_main = [ctx.ps(f"mbb_ps{i}", [128, TQ], F32) for i in range(2)]
    ps_B = ctx.ps("mbb_psB", [64, TQ], F32)

    W = TQ + 3
    for b in range(BPC):
        for tci in range(NT):
            t0 = tci * TQ
            pp = (b * NT + tci) % 2
            xi_, xib_ = xi_sb[pp], xib[pp]
            v3 = xi_[:].rearrange("p (k w) -> p k w", k=16)
            if tci == 0:
                for k in range(16):
                    MEMSET(ctx, "vector", xi_[:, k * W:k * W + 3], 0.0, writes=[xib_[k]])
                DMA(ctx, "sync", v3[:, :, 3:W],
                    xi_in[b, :, t0:t0 + TQ].rearrange("(k p) t -> p k t", p=128),
                    reads=list(xib_), writes=list(xib_))
            else:
                DMA(ctx, "sync", v3[:, :, 0:W],
                    xi_in[b, :, t0 - 3:t0 + TQ].rearrange("(k p) t -> p k t", p=128),
                    reads=list(xib_), writes=list(xib_))
            for k in range(16):
                base = k * W
                TS(ctx, "vector", acc[:], xi_[:, base:base + TQ], conv_cols[:, 4 * k:4 * k + 1],
                   ALU.mult, reads=[xib_[k], conv_cols], writes=[acc])
                for j in (1, 2, 3):
                    STT(ctx, "vector", acc[:], xi_[:, base + j:base + j + TQ],
                        conv_cols[:, 4 * k + j:4 * k + j + 1], acc[:], ALU.mult, ALU.add,
                        reads=[xib_[k], conv_cols, acc], writes=[acc])
                ACT(ctx, xc_b[:, k * TQ:(k + 1) * TQ], acc[:], AF.Silu,
                    bias=convb_cols[:, k:k + 1], reads=[acc, convb_cols], writes=[xcb[k]])
            DMA(ctx, "gpsimd",
                xc_out[b, :, t0:t0 + TQ].rearrange("(k p) t -> p k t", p=128),
                xc_b[:].rearrange("p (k t) -> p k t", k=16), reads=list(xcb))
            for m in range(16):
                pm = ps_main[m % 2]
                for k in range(16):
                    MM(ctx, pm[:], dtw_b[k][:, m * 128:(m + 1) * 128],
                       xc_b[:, k * TQ:(k + 1) * TQ], start=(k == 0), stop=(k == 15),
                       reads=[dtw_b[k], xcb[k]], writes=[pm])
                ACT(ctx, ef[:], pm[:], AF.Exp, bias=dtb_cols[:, m:m + 1],
                    reads=[pm, dtb_cols], writes=[ef])
                ACT(ctx, dt_sb[:, m * TQ:(m + 1) * TQ], ef[:], AF.Ln, bias=1.0,
                    reads=[ef], writes=[dtb_[m]])
            DMA(ctx, "gpsimd",
                dt_out[b, :, t0:t0 + TQ].rearrange("(k p) t -> p k t", p=128),
                dt_sb[:].rearrange("p (k t) -> p k t", k=16), reads=list(dtb_))
            for k in range(16):
                MM(ctx, ps_B[:], xpw_b[k][:], xc_b[:, k * TQ:(k + 1) * TQ],
                   start=(k == 0), stop=(k == 15), reads=[xpw_b[k], xcb[k]], writes=[ps_B])
            COPY(ctx, "vector", B_sb[:], ps_B[:], reads=[ps_B], writes=[B_sb])
            DMA(ctx, "gpsimd", B_out[b, :, t0:t0 + TQ], B_sb[:], reads=[B_sb])



def emit_mamba_C(ctx, xc_in, dt_in, B_in, y_out):
    TC = 1024
    w = ctx.w

    A_sb = ctx.sb("mbc_A", [128, 1024], F32)
    A_stage = ctx.sb("mbc_Astg", [128, 1024], F32)
    DMA(ctx, "sync", A_stage[:], w["A_rs"], writes=[A_stage])
    ACT(ctx, A_sb[:], A_stage[:], AF.Exp, reads=[A_stage], writes=[A_sb])
    TS(ctx, "vector", A_sb[:], A_sb[:], -1.0, ALU.mult, reads=[A_sb], writes=[A_sb])
    D_cols = ctx.sb("mbc_D", [128, 16], F32)
    DMA(ctx, "sync", D_cols[:], w["D_cols"], writes=[D_cols])
    identb = make_ident(ctx, "mbc_id")[1]

    Bb = ctx.sb("mbc_Bb", [128, 32 * TC], BF16)
    h_state = ctx.sb("mbc_hstate", [128, 1024], F32)
    dt_t = [ctx.sb(f"mbc_dt{i}", [128, TC], BF16) for i in range(2)]
    xc_t = [ctx.sb(f"mbc_xc{i}", [128, TC], BF16) for i in range(2)]
    a_t = [ctx.sb(f"mbc_a{i}", [128, TC], F32) for i in range(2)]
    u_t = [ctx.sb(f"mbc_u{i}", [128, TC], BF16) for i in range(2)]
    h_t = [ctx.sb(f"mbc_h{i}", [128, TC], BF16) for i in range(2)]
    y_acc = ctx.sb("mbc_yacc", [128, 16 * TC], BF16)
    yab = [ctx.mkbuf(f"mbc_yab{k}") for k in range(16)]
    t2 = ctx.sb("mbc_t2", [128, TC], F32)
    yout_sb = ctx.sb("mbc_yout", [128, 16 * TC], BF16)
    youtb = [ctx.mkbuf(f"mbc_youtb{k}") for k in range(16)]

    ps_y = [ctx.ps(f"mbc_psy{i}", [128, TC], F32) for i in range(2)]

    for b in range(BPC):
        MEMSET(ctx, "vector", h_state[:], 0.0, writes=[h_state])
        for tci in range(2):
            t0 = tci * TC
            for half in range(2):
                DMA(ctx, "sync",
                    Bb[:].rearrange("p (n t) -> p n t", n=32),
                    bass.AP(B_in.tensor, (b * 64 + half * 32) * S + t0,
                            [[0, 128], [S, 32], [1, TC]]),
                    writes=[Bb])
                for db in range(16):
                    pp = db % 2
                    dt_, xc_ = dt_t[pp], xc_t[pp]
                    py = ps_y[pp]
                    DMA(ctx, "sync", dt_[:], dt_in[b, db * 128:(db + 1) * 128, t0:t0 + TC],
                        writes=[dt_])
                    DMA(ctx, "sync", xc_[:], xc_in[b, db * 128:(db + 1) * 128, t0:t0 + TC],
                        writes=[xc_])
                    for ni in range(32):
                        n = half * 32 + ni
                        col = db * 64 + n
                        qq = ni % 2
                        a_, u_, h_ = a_t[qq], u_t[qq], h_t[qq]
                        ACT(ctx, a_[:], dt_[:], AF.Exp, scale=A_sb[:, col:col + 1],
                            reads=[dt_, A_sb], writes=[a_])
                        TT(ctx, "vector", u_[:], xc_[:], Bb[:, ni * TC:(ni + 1) * TC],
                           ALU.mult, reads=[xc_, Bb], writes=[u_])
                        SCAN(ctx, h_[:], a_[:], u_[:], h_state[:, col:col + 1],
                             reads=[a_, u_, h_state], writes=[h_])
                        COPY(ctx, "vector", h_state[:, col:col + 1], h_[:, TC - 1:TC],
                             reads=[h_], writes=[h_state])
                        MM(ctx, py[:, 0:512], identb[:], h_[:, 0:512],
                           start=(ni == 0), stop=(ni == 31), reads=[identb, h_], writes=[py])
                        MM(ctx, py[:, 512:1024], identb[:], h_[:, 512:1024],
                           start=(ni == 0), stop=(ni == 31), reads=[identb, h_], writes=[py])
                    if half == 0:
                        COPY(ctx, "vector", y_acc[:, db * TC:(db + 1) * TC], py[:],
                             reads=[py], writes=[yab[db]])
                    else:
                        STT(ctx, "vector", t2[:], xc_[:], D_cols[:, db:db + 1], py[:],
                            ALU.mult, ALU.add, reads=[xc_, D_cols, py], writes=[t2])
                        TT(ctx, "vector", yout_sb[:, db * TC:(db + 1) * TC], t2[:],
                           y_acc[:, db * TC:(db + 1) * TC], ALU.add,
                           reads=[t2, yab[db]], writes=[youtb[db]])
            DMA(ctx, "gpsimd",
                y_out[b, :, t0:t0 + TC].rearrange("(k p) t -> p k t", p=128),
                yout_sb[:].rearrange("p (k t) -> p k t", k=16), reads=list(youtb))



def emit_mamba_D(ctx, y_in, res_in, x_in, x_out, mrow_in, rrow_in):
    TQ = 512
    NT = S // TQ
    w = ctx.w
    ones_row_f = ctx.sb("mbd_ones_rowf", [1, 128], F32)
    MEMSET(ctx, "vector", ones_row_f[:], 1.0, writes=[ones_row_f])
    mrow_sb = ctx.sb("mbd_mrow", [1, S], F32)
    rrow_sb = ctx.sb("mbd_rrow", [1, S], F32)
    rp1 = ctx.sb("mbd_rp1", [1, TQ], F32)
    m2 = ctx.sb("mbd_m2", [1, TQ], F32)
    rp1B = ctx.sb("mbd_rp1B", [128, TQ], F32)
    m2B = ctx.sb("mbd_m2B", [128, TQ], F32)
    v1 = ctx.sb("mbd_v1", [128, TQ], F32)
    ps_bc = ctx.ps("mbd_ps_bc", [128, TQ], F32)

    stage = [ctx.sb(f"mbd_stg{i}", [128, 1024], F32) for i in range(2)]
    wop_b = [load_weight_staged(ctx, f"mbd_wop{k}", w["wopT"][k * 128:(k + 1) * 128, :],
                                128, 1024, stage[k % 2]) for k in range(16)]

    y_sb = [ctx.sb(f"mbd_y{i}", [128, 16 * TQ], BF16) for i in range(2)]
    res_sb = [ctx.sb(f"mbd_res{i}", [128, 16 * TQ], BF16) for i in range(2)]
    xf = [ctx.sb(f"mbd_xf{i}", [128, PD * TQ], F32) for i in range(2)]
    g_sb = ctx.sb("mbd_g", [128, 16 * TQ], BF16)
    yg_sb = ctx.sb("mbd_yg", [128, 16 * TQ], BF16)
    out_f = ctx.sb("mbd_out", [128, 4 * TQ], F32)
    outb = [ctx.mkbuf(f"mbd_outb{k}") for k in range(4)]
    ps_main = [ctx.ps(f"mbd_ps{i}", [128, TQ], F32) for i in range(2)]

    for b in range(BPC):
        DMA(ctx, "sync", mrow_sb[0:1, :], mrow_in[b:b + 1, :], writes=[mrow_sb])
        DMA(ctx, "sync", rrow_sb[0:1, :], rrow_in[b:b + 1, :], writes=[rrow_sb])
        for tci in range(NT):
            t0 = tci * TQ
            pp = (b * NT + tci) % 2
            y_, res_, xf_ = y_sb[pp], res_sb[pp], xf[pp]
            TS(ctx, "vector", rp1[:], rrow_sb[0:1, t0:t0 + TQ], 1.0, ALU.add,
               reads=[rrow_sb], writes=[rp1])
            TT(ctx, "vector", m2[:], mrow_sb[0:1, t0:t0 + TQ], rrow_sb[0:1, t0:t0 + TQ],
               ALU.mult, reads=[mrow_sb, rrow_sb], writes=[m2])
            MM(ctx, ps_bc[:], ones_row_f[:], rp1[:], start=True, stop=True,
               reads=[rp1, ones_row_f], writes=[ps_bc])
            COPY(ctx, "vector", rp1B[:], ps_bc[:], reads=[ps_bc], writes=[rp1B])
            MM(ctx, ps_bc[:], ones_row_f[:], m2[:], start=True, stop=True,
               reads=[m2, ones_row_f], writes=[ps_bc])
            COPY(ctx, "vector", m2B[:], ps_bc[:], reads=[ps_bc], writes=[m2B])
            DMA(ctx, "sync", y_[:].rearrange("p (k t) -> p k t", k=16),
                y_in[b, :, t0:t0 + TQ].rearrange("(k p) t -> p k t", p=128), writes=[y_])
            DMA(ctx, "sync", res_[:].rearrange("p (k t) -> p k t", k=16),
                res_in[b, :, t0:t0 + TQ].rearrange("(k p) t -> p k t", p=128), writes=[res_])
            DMA(ctx, "sync", xf_[:].rearrange("p (k t) -> p k t", k=PD),
                x_in[b, :, t0:t0 + TQ].rearrange("(k p) t -> p k t", p=128), writes=[xf_])
            ACT(ctx, g_sb[:], res_[:], AF.Silu, reads=[res_], writes=[g_sb])
            TT(ctx, "vector", yg_sb[:], y_[:], g_sb[:], ALU.mult,
               reads=[y_, g_sb], writes=[yg_sb])
            for m in range(PD):
                pm = ps_main[m % 2]
                for k in range(16):
                    MM(ctx, pm[:], wop_b[k][:, m * 128:(m + 1) * 128],
                       yg_sb[:, k * TQ:(k + 1) * TQ], start=(k == 0), stop=(k == 15),
                       reads=[wop_b[k], yg_sb], writes=[pm])
                TT(ctx, "vector", v1[:], xf_[:, m * TQ:(m + 1) * TQ], rp1B[:], ALU.mult,
                   reads=[xf_, rp1B], writes=[v1])
                TT(ctx, "vector", v1[:], v1[:], m2B[:], ALU.subtract,
                   reads=[v1, m2B], writes=[v1])
                TT(ctx, "vector", out_f[:, (m % 4) * TQ:(m % 4 + 1) * TQ], v1[:], pm[:],
                   ALU.add, reads=[v1, pm], writes=[outb[m % 4]])
                if m % 4 == 3:
                    DMA(ctx, "gpsimd",
                        x_out[b, (m - 3) * 128:(m + 1) * 128, t0:t0 + TQ].rearrange("(k p) t -> p k t", p=128),
                        out_f[:].rearrange("p (k t) -> p k t", k=4), reads=list(outb))



def mamba_weight_arrays(inp):
    ipw = np.asarray(inp["L0_attn_in_proj_w"])
    return {
        "wipT": np.ascontiguousarray(ipw.T),
        "normw_cols": np.ascontiguousarray(np.asarray(inp["L0_attn_norm_w"]).reshape(PD, 128).T),
        "dtwT": np.ascontiguousarray(np.asarray(inp["L0_attn_dt_w"]).T),
        "xpwT": np.ascontiguousarray(np.asarray(inp["L0_attn_x_proj_w"])[64:128].T),
        "conv_cols": np.ascontiguousarray(
            np.transpose(np.asarray(inp["L0_attn_conv_w"])[:, 0, :].reshape(16, 128, 4), (1, 0, 2)).reshape(128, 64)),
        "convb_cols": np.ascontiguousarray(np.asarray(inp["L0_attn_conv_b"]).reshape(16, 128).T),
        "dtb_cols": np.ascontiguousarray(np.asarray(inp["L0_attn_dt_b"]).reshape(16, 128).T),
        "A_rs": np.ascontiguousarray(
            np.transpose(np.asarray(inp["L0_attn_A_log"]).reshape(16, 128, 64), (1, 0, 2)).reshape(128, 1024)),
        "D_cols": np.ascontiguousarray(np.asarray(inp["L0_attn_D_param"]).reshape(16, 128).T),
        "wopT": np.ascontiguousarray(np.asarray(inp["L0_attn_out_proj_w"]).T),
    }


def build_mba_program():
    nc = bass.Bass("TRN2", target_bir_lowering=False)
    ctx = Ctx(nc)
    x_in = nc.dram_tensor("x_fm", [BPC, D, S], F32, kind="ExternalInput").ap()
    xi_out = nc.dram_tensor("xi_fm", [BPC, 2048, S], F32, kind="ExternalOutput").ap()
    res_out = nc.dram_tensor("res_fm", [BPC, 2048, S], BF16, kind="ExternalOutput").ap()
    mrow_out = nc.dram_tensor("mrow", [BPC, S], F32, kind="ExternalOutput").ap()
    rrow_out = nc.dram_tensor("rrow", [BPC, S], F32, kind="ExternalOutput").ap()
    ctx.bufmap["mrow"] = ctx.mkbuf("mrow"); ctx.bufmap["rrow"] = ctx.mkbuf("rrow")
    ctx.w = {
        "wipT": nc.dram_tensor("wipT", [D, 4096], F32, kind="ExternalInput").ap(),
        "normw_cols": nc.dram_tensor("normw_cols", [128, PD], F32, kind="ExternalInput").ap(),
    }
    emit_mamba_A(ctx, x_in, xi_out, res_out, mrow_out, rrow_out)
    ctx.finalize()
    return nc


def build_mbb_program():
    nc = bass.Bass("TRN2", target_bir_lowering=False)
    ctx = Ctx(nc)
    xi_in = nc.dram_tensor("xi_fm", [BPC, 2048, S], F32, kind="ExternalInput").ap()
    xc_out = nc.dram_tensor("xc_fm", [BPC, 2048, S], BF16, kind="ExternalOutput").ap()
    dt_out = nc.dram_tensor("dt_fm", [BPC, 2048, S], BF16, kind="ExternalOutput").ap()
    B_out = nc.dram_tensor("B_fm", [BPC, 64, S], BF16, kind="ExternalOutput").ap()
    ctx.w = {
        "dtwT": nc.dram_tensor("dtwT", [2048, 2048], F32, kind="ExternalInput").ap(),
        "xpwT": nc.dram_tensor("xpwT", [2048, 64], F32, kind="ExternalInput").ap(),
        "conv_cols": nc.dram_tensor("conv_cols", [128, 64], F32, kind="ExternalInput").ap(),
        "convb_cols": nc.dram_tensor("convb_cols", [128, 16], F32, kind="ExternalInput").ap(),
        "dtb_cols": nc.dram_tensor("dtb_cols", [128, 16], F32, kind="ExternalInput").ap(),
    }
    emit_mamba_B(ctx, xi_in, xc_out, dt_out, B_out)
    ctx.finalize()
    return nc


def build_mbc_program():
    nc = bass.Bass("TRN2", target_bir_lowering=False)
    ctx = Ctx(nc)
    xc_in = nc.dram_tensor("xc_fm", [BPC, 2048, S], BF16, kind="ExternalInput").ap()
    dt_in = nc.dram_tensor("dt_fm", [BPC, 2048, S], BF16, kind="ExternalInput").ap()
    B_in = nc.dram_tensor("B_fm", [BPC, 64, S], BF16, kind="ExternalInput").ap()
    y_out = nc.dram_tensor("y_fm", [BPC, 2048, S], BF16, kind="ExternalOutput").ap()
    ctx.w = {
        "A_rs": nc.dram_tensor("A_rs", [128, 1024], F32, kind="ExternalInput").ap(),
        "D_cols": nc.dram_tensor("D_cols", [128, 16], F32, kind="ExternalInput").ap(),
    }
    emit_mamba_C(ctx, xc_in, dt_in, B_in, y_out)
    ctx.finalize()
    return nc


def build_mbd_program():
    nc = bass.Bass("TRN2", target_bir_lowering=False)
    ctx = Ctx(nc)
    y_in = nc.dram_tensor("y_fm", [BPC, 2048, S], BF16, kind="ExternalInput").ap()
    res_in = nc.dram_tensor("res_fm", [BPC, 2048, S], BF16, kind="ExternalInput").ap()
    x_in = nc.dram_tensor("x_fm", [BPC, D, S], F32, kind="ExternalInput").ap()
    x_out = nc.dram_tensor("xo_fm", [BPC, D, S], F32, kind="ExternalOutput").ap()
    mrow_in = nc.dram_tensor("mrow", [BPC, S], F32, kind="ExternalInput").ap()
    rrow_in = nc.dram_tensor("rrow", [BPC, S], F32, kind="ExternalInput").ap()
    ctx.bufmap["mrow"] = ctx.mkbuf("mrow"); ctx.bufmap["rrow"] = ctx.mkbuf("rrow")
    ctx.w = {
        "wopT": nc.dram_tensor("wopT", [2048, D], F32, kind="ExternalInput").ap(),
    }
    emit_mamba_D(ctx, y_in, res_in, x_in, x_out, mrow_in, rrow_in)
    ctx.finalize()
    return nc


def run_mamba(x_fm_percore, inp, debug=False):
    warr = mamba_weight_arrays(inp)
    nc = build_mba_program()
    in_maps = [dict(x_fm=x_fm_percore[c], wipT=warr["wipT"], normw_cols=warr["normw_cols"])
               for c in range(NCORES)]
    resA = run_bass_kernel_spmd(nc, in_maps, core_ids=list(range(NCORES))).results
    nc = build_mbb_program()
    in_maps = [dict(xi_fm=resA[c]["xi_fm"], dtwT=warr["dtwT"], xpwT=warr["xpwT"],
                    conv_cols=warr["conv_cols"], convb_cols=warr["convb_cols"],
                    dtb_cols=warr["dtb_cols"]) for c in range(NCORES)]
    resB = run_bass_kernel_spmd(nc, in_maps, core_ids=list(range(NCORES))).results
    nc = build_mbc_program()
    in_maps = [dict(xc_fm=resB[c]["xc_fm"], dt_fm=resB[c]["dt_fm"], B_fm=resB[c]["B_fm"],
                    A_rs=warr["A_rs"], D_cols=warr["D_cols"]) for c in range(NCORES)]
    resC = run_bass_kernel_spmd(nc, in_maps, core_ids=list(range(NCORES))).results
    nc = build_mbd_program()
    in_maps = [dict(y_fm=resC[c]["y_fm"], res_fm=resA[c]["res_fm"], x_fm=x_fm_percore[c],
                    mrow=resA[c]["mrow"], rrow=resA[c]["rrow"],
                    wopT=warr["wopT"]) for c in range(NCORES)]
    resD = run_bass_kernel_spmd(nc, in_maps, core_ids=list(range(NCORES))).results
    out = [resD[c]["xo_fm"] for c in range(NCORES)]
    if debug:
        return out, dict(A=resA, B=resB, C=resC)
    return out



def emit_mla(ctx, x_in, xt1_out, dbg=None):
    TQ = 512
    NT = S // TQ
    w = ctx.w

    ones_col_b = ctx.sb("mla_ones_col", [128, 1], BF16)
    ones_row_f = ctx.sb("mla_ones_rowf", [1, 128], F32)
    MEMSET(ctx, "vector", ones_col_b[:], 1.0, writes=[ones_col_b])
    MEMSET(ctx, "vector", ones_row_f[:], 1.0, writes=[ones_row_f])

    lnw_cols = ctx.sb("mla_lnw", [128, PD], F32)
    DMA(ctx, "sync", lnw_cols[:], w["lnw1_cols"], writes=[lnw_cols])
    stage0 = ctx.sb("mla_stg0", [128, 512], F32)
    stage = [stage0, stage0]
    wkvd_b = [load_weight_staged(ctx, f"mla_wkvd{k}", w["wkvd"][k * 128:(k + 1) * 128, :],
                                 128, 256, stage[k % 2], scale_col=lnw_cols[:, k:k + 1],
                                 scale_col_reads=[lnw_cols]) for k in range(PD)]
    wqd_b = [load_weight_staged(ctx, f"mla_wqd{k}", w["wqd"][k * 128:(k + 1) * 128, :],
                                128, 256, stage[k % 2], scale_col=lnw_cols[:, k:k + 1],
                                scale_col_reads=[lnw_cols]) for k in range(PD)]
    w16 = ctx.sb("mla_w16", [128, 16384], BF16)
    negln = ctx.sb("mla_negln", [128, PD], F32)
    TS(ctx, "vector", negln[:], lnw_cols[:], -1.0, ALU.mult, reads=[lnw_cols], writes=[negln])
    wropek_b = [w16[:, k * 512:(k + 1) * 512] for k in range(PD)]
    wropekr_b = [w16[:, 4096 + k * 512:4096 + (k + 1) * 512] for k in range(PD)]
    wo_b = [w16[:, k * 1024:(k + 1) * 1024] for k in range(PD)]
    w16buf = ctx.bufmap[ctx.prefix + "mla_w16"]

    def load_ropek(ctx=ctx):
        for k in range(PD):
            convert_into(ctx, wropek_b[k], w16buf, w["wropek"][k * 128:(k + 1) * 128, :],
                         128, 512, stage[0], scale_col=lnw_cols[:, k:k + 1],
                         scale_col_reads=[lnw_cols])
            DMA(ctx, "sync", stage[0][:, 0:512], w["wropek_perm"][k * 128:(k + 1) * 128, :],
                writes=[stage[0]])
            v_in = stage[0][:].rearrange("p (h c) -> p h c", c=32)
            v_out2 = wropekr_b[k].rearrange("p (h c) -> p h c", c=32)
            TS(ctx, "vector", v_out2[:, :, 0:16], v_in[:, 0:16, 0:16] if False else v_in[:, :, 0:16],
               negln[:, k:k + 1], ALU.mult, reads=[stage[0], negln], writes=[w16buf])
            TS(ctx, "vector", v_out2[:, :, 16:32], v_in[:, :, 16:32], lnw_cols[:, k:k + 1],
               ALU.mult, reads=[stage[0], lnw_cols], writes=[w16buf])

    def load_wo(ctx=ctx):
        for k in range(PD):
            for cc in range(2):
                convert_into(ctx, wo_b[k][:, cc * 512:(cc + 1) * 512], w16buf,
                             w["wo"][k * 128:(k + 1) * 128, cc * 512:(cc + 1) * 512],
                             128, 512, stage[0])
    wku_b = [load_weight_staged(ctx, f"mla_wku{k}", w["wku"][k * 128:(k + 1) * 128, :],
                                128, 512, stage[k % 2]) for k in range(2)]
    wqu_b = [load_weight_staged(ctx, f"mla_wqu{k}", w["wqu"][k * 128:(k + 1) * 128, :],
                                128, 512, stage[k % 2], scale=0.125) for k in range(2)]
    wvu_b = [load_weight_staged(ctx, f"mla_wvu{k}", w["wvu"][k * 128:(k + 1) * 128, :],
                                128, 1024, stage[k % 2]) for k in range(2)]
    wropeq_b = [load_weight_staged(ctx, f"mla_wrq{k}", w["wropeq"][k * 128:(k + 1) * 128, :],
                                   128, 512, stage[k % 2], scale=0.125) for k in range(2)]
    wropeqr_b = []
    for k in range(2):
        t = ctx.sb(f"mla_wrqr{k}", [128, 512], BF16)
        DMA(ctx, "sync", stage[k % 2][:, 0:512], w["wropeq_perm"][k * 128:(k + 1) * 128, :],
            writes=[stage[k % 2]])
        v_in = stage[k % 2][:].rearrange("p (h c) -> p h c", c=32)
        v_out = t[:].rearrange("p (h c) -> p h c", c=32)
        TS(ctx, "vector", v_out[:, :, 0:16], v_in[:, :, 0:16], -0.125, ALU.mult,
           reads=[stage[k % 2]], writes=[t])
        TS(ctx, "vector", v_out[:, :, 16:32], v_in[:, :, 16:32], 0.125, ALU.mult,
           reads=[stage[k % 2]], writes=[t])
        wropeqr_b.append(t)


    cosT = ctx.sb("mla_cosT", [32, S], BF16)
    sinT = ctx.sb("mla_sinT", [32, S], BF16)
    for (dst, src) in ((cosT, "cosT"), (sinT, "sinT")):
        for cc in range(4):
            DMA(ctx, "sync", stage0[0:32, 0:512], w[src][:, cc * 512:(cc + 1) * 512],
                writes=[stage0])
            COPY(ctx, "vector", dst[:, cc * 512:(cc + 1) * 512], stage0[0:32, 0:512],
                 reads=[stage0], writes=[dst])
    idf_mla, idb_mla = make_ident(ctx, "mla_id")
    ctx.identb = idb_mla
    ident64 = ctx.sb("mla_ident64", [128, 64], BF16)
    COPY(ctx, "vector", ident64[0:64, :], idb_mla[0:64, 0:64], reads=[idb_mla], writes=[ident64])
    COPY(ctx, "vector", ident64[64:128, :], idb_mla[0:64, 0:64], reads=[idb_mla], writes=[ident64])
    ctx.ident64 = ident64
    masks = ctx.sb("mla_masks", [128, 4 * TQ], BF16)
    mask_stage = ctx.sb("mla_mask_stage", [128, TQ], F32)
    for i in range(4):
        DMA(ctx, "sync", mask_stage[:], w["maskneg"][i], writes=[mask_stage])
        COPY(ctx, "vector", masks[:, i * TQ:(i + 1) * TQ], mask_stage[:],
             reads=[mask_stage], writes=[masks])

    ps_mean = ctx.ps("mla_ps_mean", [1, TQ], F32)
    ps_sq = ctx.ps("mla_ps_sq", [1, TQ], F32)
    wsum_kvd = emit_wsum_neg(ctx, "mla_ws_kvd", wkvd_b, 256, ones_col_b, ps_mean)
    wsum_qd = emit_wsum_neg(ctx, "mla_ws_qd", wqd_b, 256, ones_col_b, ps_mean)
    wsum_rk = ctx.sb("mla_ws_rk", [1, 512], BF16)
    wsum_rkr = ctx.sb("mla_ws_rkr", [1, 512], BF16)

    def calc_rope_wsums(ctx=ctx):
        for (wchunks, dst) in ((wropek_b, wsum_rk), (wropekr_b, wsum_rkr)):
            for k in range(PD):
                MM(ctx, ps_mean[0:1, 0:512], ones_col_b[:], wchunks[k],
                   start=(k == 0), stop=(k == PD - 1), reads=[w16buf, ones_col_b],
                   writes=[ps_mean])
            TS(ctx, "vector", dst[:], ps_mean[0:1, 0:512], -1.0, ALU.mult,
               reads=[ps_mean], writes=[dst])
    ps_util = ctx.ps("mla_ps_util", [128, TQ], F32)
    ps_main = ctx.ps("mla_ps_main", [128, TQ], F32)
    ps_rot = ctx.ps("mla_ps_rot", [64, TQ], F32)
    ps_tr = ctx.ps("mla_ps_tr", [128, 128], BF16)
    ps_St = ctx.ps("mla_ps_St", [128, TQ], F32)
    ps_o = ctx.ps("mla_ps_o", [64, TQ], F32)

    mean_f = ctx.sb("mla_mean_f", [1, TQ], F32)
    sq_f = ctx.sb("mla_sq_f", [1, TQ], F32)
    var_f = ctx.sb("mla_var_f", [1, TQ], F32)
    rstd2 = ctx.sb("mla_rstd2", [1, TQ], F32)
    meansave_b = ctx.sb("mla_meansave_b", [1, S], BF16)
    rstdsave = ctx.sb("mla_rstdsave", [1, S], BF16)
    s1row = ctx.sb("mla_s1row", [1, 512], F32)
    sB1 = ctx.sb("mla_sB1", [128, TQ], F32)
    csK = ctx.sb("mla_csK", [32, 512], BF16)
    ssK = ctx.sb("mla_ssK", [32, 512], BF16)
    zrecB = ctx.sb("mla_zrecB", [64, TQ], F32)

    xf = [ctx.sb("mla_xf0", [128, 4 * TQ], F32)]
    xfb = [[ctx.mkbuf(f"mla_xfb0_{k}") for k in range(4)]]
    xb = ctx.sb("mla_xb", [128, PD * TQ], BF16)
    x2s = ctx.sb("mla_x2s", [128, TQ], BF16)
    kvlat = ctx.sb("mla_kvlat", [128, 2 * TQ], BF16)
    kvb = [ctx.mkbuf(f"mla_kvb{i}") for i in range(2)]
    qlat = ctx.sb("mla_qlat", [128, 2 * TQ], BF16)
    qlb = [ctx.mkbuf(f"mla_qlb{i}") for i in range(2)]
    qlat_hbm = ctx.dram_i("mla_qlat_hbm", [BPC, 256, S], BF16)
    qlhb = ctx.bufmap[ctx.prefix + "mla_qlat_hbm"]
    kk = ctx.sb("mla_kk", [128, 8 * S], BF16)
    kkb = [ctx.mkbuf(f"mla_kkb{g}") for g in range(8)]
    vT = ctx.sb("mla_vT", [128, 256 * 64], BF16)
    vTb = ctx.mkbuf("mla_vTb")
    vtmp = ctx.sb("mla_vtmp", [128, TQ], BF16)
    ta = ctx.sb("mla_ta", [32, TQ], F32)
    tb = ctx.sb("mla_tb", [32, TQ], F32)
    qq = ctx.sb("mla_qq", [128, TQ], BF16)
    qqb = [ctx.mkbuf("mla_qqb")]
    probs = [ctx.sb(f"mla_probs{i}", [128, TQ], BF16) for i in range(2)]
    stmask = ctx.sb("mla_stmask", [128, TQ], F32)
    o_qc = ctx.sb("mla_oqc", [128, 8 * TQ], BF16)
    oqb = [ctx.mkbuf(f"mla_oqb{g}") for g in range(8)]
    v1t = ctx.sb("mla_v1t", [128, TQ], F32)
    out_f = ctx.sb("mla_outf", [128, TQ], F32)
    outb = [ctx.mkbuf("mla_outb0")]

    for b in range(BPC):
        load_ropek()
        calc_rope_wsums()
        for tci in range(NT):
            t0 = tci * TQ
            xf_, xfb_ = xf[0], xfb[0]
            for hh in range(2):
                DMA(ctx, "sync", xf_[:].rearrange("p (k t) -> p k t", k=4),
                    x_in[b, hh * 512:(hh + 1) * 512, t0:t0 + TQ].rearrange("(k p) t -> p k t", p=128),
                    writes=list(xfb_))
                for k in range(4):
                    kk4 = hh * 4 + k
                    COPY(ctx, "vector", xb[:, kk4 * TQ:(kk4 + 1) * TQ],
                         xf_[:, k * TQ:(k + 1) * TQ], reads=[xfb_[k]], writes=[xb])
                    ACT(ctx, x2s[:], xf_[:, k * TQ:(k + 1) * TQ], AF.Square,
                        reads=[xfb_[k]], writes=[x2s])
                    MM(ctx, ps_sq[0:1, :], ones_col_b[:], x2s[:],
                       start=(kk4 == 0), stop=(kk4 == PD - 1), reads=[x2s, ones_col_b],
                       writes=[ps_sq])
            for k in range(PD):
                MM(ctx, ps_mean[0:1, :], ones_col_b[:], xb[:, k * TQ:(k + 1) * TQ],
                   start=(k == 0), stop=(k == PD - 1), reads=[xb, ones_col_b], writes=[ps_mean])
            ACT(ctx, mean_f[:], ps_mean[0:1, :], AF.Copy, scale=1.0 / D,
                reads=[ps_mean], writes=[mean_f])
            COPY(ctx, "vector", meansave_b[:, t0:t0 + TQ], mean_f[:], reads=[mean_f], writes=[meansave_b])
            ACT(ctx, sq_f[:], mean_f[:], AF.Square, reads=[mean_f], writes=[sq_f])
            STT(ctx, "vector", var_f[:], ps_sq[0:1, :], 1.0 / D, sq_f[:],
                ALU.mult, ALU.subtract, reads=[ps_sq, sq_f], writes=[var_f])
            ACT(ctx, sq_f[:], var_f[:], AF.Sqrt, bias=1e-5, reads=[var_f], writes=[sq_f])
            RECIP(ctx, mean_f[:], sq_f[:], reads=[sq_f], writes=[mean_f])
            COPY(ctx, "vector", rstdsave[:, t0:t0 + TQ], mean_f[:], reads=[mean_f], writes=[rstdsave])
            ACT(ctx, rstd2[:], mean_f[:], AF.Square, reads=[mean_f], writes=[rstd2])
            TT(ctx, "vector", rstd2[:], var_f[:], rstd2[:], ALU.mult, reads=[var_f, rstd2], writes=[rstd2])
            ACT(ctx, rstd2[:], rstd2[:], AF.Sqrt, reads=[rstd2], writes=[rstd2])
            TS(ctx, "vector", rstd2[:], rstd2[:], 1e-5, ALU.add, reads=[rstd2], writes=[rstd2])
            RECIP(ctx, rstd2[:], rstd2[:], reads=[rstd2], writes=[rstd2])
            TT(ctx, "vector", s1row[:], mean_f[:], rstd2[:],
               ALU.mult, reads=[mean_f, rstd2], writes=[s1row])
            MM(ctx, ps_util[:], ones_row_f[:], s1row[:], start=True, stop=True,
               reads=[s1row, ones_row_f], writes=[ps_util])
            COPY(ctx, "vector", sB1[:], ps_util[:], reads=[ps_util], writes=[sB1])
            TT(ctx, "vector", csK[:], cosT[:, t0:t0 + TQ], sB1[0:32, :],
               ALU.mult, reads=[cosT, sB1], writes=[csK])
            TT(ctx, "vector", ssK[:], sinT[:, t0:t0 + TQ], sB1[0:32, :],
               ALU.mult, reads=[sinT, sB1], writes=[ssK])
            for m in range(2):
                for k in range(PD):
                    MM(ctx, ps_main[:], wkvd_b[k][:, m * 128:(m + 1) * 128],
                       xb[:, k * TQ:(k + 1) * TQ], start=(k == 0), stop=False,
                       reads=[wkvd_b[k], xb], writes=[ps_main])
                MM(ctx, ps_main[:], wsum_kvd[:, m * 128:(m + 1) * 128],
                   meansave_b[:, t0:t0 + TQ], start=False, stop=True,
                   reads=[wsum_kvd, meansave_b], writes=[ps_main])
                TT(ctx, "vector", kvlat[:, m * TQ:(m + 1) * TQ], ps_main[:], sB1[:],
                   ALU.mult, reads=[ps_main, sB1], writes=[kvb[m]])
                for k in range(PD):
                    MM(ctx, ps_main[:], wqd_b[k][:, m * 128:(m + 1) * 128],
                       xb[:, k * TQ:(k + 1) * TQ], start=(k == 0), stop=False,
                       reads=[wqd_b[k], xb], writes=[ps_main])
                MM(ctx, ps_main[:], wsum_qd[:, m * 128:(m + 1) * 128],
                   meansave_b[:, t0:t0 + TQ], start=False, stop=True,
                   reads=[wsum_qd, meansave_b], writes=[ps_main])
                TT(ctx, "vector", qlat[:, m * TQ:(m + 1) * TQ], ps_main[:], sB1[:],
                   ALU.mult, reads=[ps_main, sB1], writes=[qlb[m]])
                DMA(ctx, "gpsimd", qlat_hbm[b, m * 128:(m + 1) * 128, t0:t0 + TQ],
                    qlat[:, m * TQ:(m + 1) * TQ], reads=[qlb[m]])
            for g in range(8):
                h0, h1 = 2 * g, 2 * g + 1
                for (base, hh) in ((0, h0), (64, h1)):
                    for kc in range(2):
                        MM(ctx, ps_main[base:base + 32, :],
                           wku_b[kc][:, hh * 32:(hh + 1) * 32],
                           kvlat[:, kc * TQ:(kc + 1) * TQ], start=(kc == 0),
                           stop=False, reads=[wku_b[kc], kvb[kc]], writes=[ps_main],
                           tp=(0, base))
                for (base, hh) in ((32, h0), (96, h1)):
                    for k in range(PD):
                        MM(ctx, ps_main[base:base + 32, :],
                           wropek_b[k][:, hh * 32:(hh + 1) * 32],
                           xb[:, k * TQ:(k + 1) * TQ], start=(k == 0), stop=False,
                           reads=[w16buf, xb], writes=[ps_main], tp=(0, base))
                    MM(ctx, ps_main[base:base + 32, :],
                       wsum_rk[:, hh * 32:(hh + 1) * 32], meansave_b[:, t0:t0 + TQ],
                       start=False, stop=(base == 96), reads=[wsum_rk, meansave_b],
                       writes=[ps_main], tp=(0, base))
                for (base, hh) in ((0, h0), (32, h1)):
                    for k in range(PD):
                        MM(ctx, ps_rot[base:base + 32, :],
                           wropekr_b[k][:, hh * 32:(hh + 1) * 32],
                           xb[:, k * TQ:(k + 1) * TQ], start=(k == 0),
                           stop=False, reads=[w16buf, xb], writes=[ps_rot], tp=(0, base))
                    MM(ctx, ps_rot[base:base + 32, :],
                       wsum_rkr[:, hh * 32:(hh + 1) * 32], meansave_b[:, t0:t0 + TQ],
                       start=False, stop=(base == 32), reads=[wsum_rkr, meansave_b],
                       writes=[ps_rot], tp=(0, base))
                col = g * S + t0
                COPY(ctx, "vector", kk[0:32, col:col + TQ], ps_main[0:32, :],
                     reads=[ps_main], writes=[kkb[g]])
                COPY(ctx, "vector", kk[64:96, col:col + TQ], ps_main[64:96, :],
                     reads=[ps_main], writes=[kkb[g]])
                for (pbase, rbase, kbase) in ((32, 0, 32), (96, 32, 96)):
                    TT(ctx, "vector", ta[:], ps_main[pbase:pbase + 32, :], csK[:],
                       ALU.mult, reads=[ps_main, csK], writes=[ta])
                    TT(ctx, "vector", tb[:], ps_rot[rbase:rbase + 32, :], ssK[:],
                       ALU.mult, reads=[ps_rot, ssK], writes=[tb])
                    TT(ctx, "vector", kk[kbase:kbase + 32, col:col + TQ], ta[:], tb[:],
                       ALU.add, reads=[ta, tb], writes=[kkb[g]])
                for kc in range(2):
                    MM(ctx, ps_main[:], wvu_b[kc][:, g * 128:(g + 1) * 128],
                       kvlat[:, kc * TQ:(kc + 1) * TQ], start=(kc == 0), stop=(kc == 1),
                       reads=[wvu_b[kc], kvb[kc]], writes=[ps_main])
                COPY(ctx, "vector", vtmp[:], ps_main[:], reads=[ps_main], writes=[vtmp])
                for j in range(4):
                    kb = tci * 4 + j
                    for (vb, hh) in ((0, h0), (64, h1)):
                        TRANS(ctx, ps_tr[:, 0:64], vtmp[vb:vb + 64, j * 128:(j + 1) * 128],
                              ctx.ident64[vb:vb + 64, :], reads=[vtmp, ctx.ident64], writes=[ps_tr])
                        COPY(ctx, "vector", vT[:, (kb * 16 + hh) * 64:(kb * 16 + hh) * 64 + 64],
                             ps_tr[:, 0:64], reads=[ps_tr], writes=[vTb])
        if dbg is not None:
            for g in range(8):
                DMA(ctx, "gpsimd", dbg["kk"][b, g * 128:(g + 1) * 128, :],
                    kk[:, g * S:(g + 1) * S], reads=[kkb[g]])
            DMA(ctx, "gpsimd", dbg["vT"][b], vT[:], reads=[vTb])
        load_wo()
        for qci in range(NT):
            q0 = qci * TQ
            for g in range(8):
                h0, h1 = 2 * g, 2 * g + 1
                for (base, hh) in ((0, h0), (64, h1)):
                    for kc in range(2):
                        MM(ctx, ps_main[base:base + 32, :],
                           wqu_b[kc][:, hh * 32:(hh + 1) * 32],
                           qlat[:, kc * TQ:(kc + 1) * TQ],
                           start=(kc == 0), stop=False,
                           reads=[wqu_b[kc], qlb[kc]], writes=[ps_main], tp=(0, base))
                for (base, hh) in ((32, h0), (96, h1)):
                    for kc in range(2):
                        MM(ctx, ps_main[base:base + 32, :],
                           wropeq_b[kc][:, hh * 32:(hh + 1) * 32],
                           qlat[:, kc * TQ:(kc + 1) * TQ],
                           start=(kc == 0), stop=(base == 96 and kc == 1),
                           reads=[wropeq_b[kc], qlb[kc]], writes=[ps_main], tp=(0, base))
                for (base, hh) in ((0, h0), (32, h1)):
                    for kc in range(2):
                        MM(ctx, ps_rot[base:base + 32, :],
                           wropeqr_b[kc][:, hh * 32:(hh + 1) * 32],
                           qlat[:, kc * TQ:(kc + 1) * TQ],
                           start=(kc == 0), stop=(base == 32 and kc == 1),
                           reads=[wropeqr_b[kc], qlb[kc]], writes=[ps_rot], tp=(0, base))
                COPY(ctx, "vector", qq[0:32, :], ps_main[0:32, :],
                     reads=[ps_main], writes=[qqb[0]])
                COPY(ctx, "vector", qq[64:96, :], ps_main[64:96, :],
                     reads=[ps_main], writes=[qqb[0]])
                for (pbase, rbase, kbase) in ((32, 0, 32), (96, 32, 96)):
                    TT(ctx, "vector", ta[:], ps_main[pbase:pbase + 32, :], cosT[:, q0:q0 + TQ],
                       ALU.mult, reads=[ps_main, cosT], writes=[ta])
                    TT(ctx, "vector", tb[:], ps_rot[rbase:rbase + 32, :], sinT[:, q0:q0 + TQ],
                       ALU.mult, reads=[ps_rot, sinT], writes=[tb])
                    TT(ctx, "vector", qq[kbase:kbase + 32, :], ta[:], tb[:],
                       ALU.add, reads=[ta, tb], writes=[qqb[0]])
                for h in (2 * g, 2 * g + 1):
                    vb = (h % 2) * 64
                    nkb = qci * 4 + 4
                    for kb in range(nkb):
                        MM(ctx, ps_St[:], kk[vb:vb + 64, g * S + kb * 128:g * S + (kb + 1) * 128],
                           qq[vb:vb + 64, :], start=True, stop=True,
                           reads=[kkb[g], qqb[0]], writes=[ps_St])
                        pslot = probs[kb % 2]
                        if kb >= qci * 4:
                            ii = kb - qci * 4
                            TT(ctx, "vector", stmask[:], ps_St[:], masks[:, ii * TQ:(ii + 1) * TQ],
                               ALU.add, reads=[ps_St, masks], writes=[stmask])
                            ACT(ctx, pslot[:], stmask[:], AF.Exp, reads=[stmask], writes=[pslot])
                        else:
                            ACT(ctx, pslot[:], ps_St[:], AF.Exp, reads=[ps_St], writes=[pslot])
                        MM(ctx, ps_util[0:1, :], ones_col_b[:], pslot[:],
                           start=(kb == 0), stop=(kb == nkb - 1), reads=[pslot, ones_col_b],
                           writes=[ps_util])
                        MM(ctx, ps_o[:], vT[:, (kb * 16 + h) * 64:(kb * 16 + h) * 64 + 64],
                           pslot[:], start=(kb == 0), stop=(kb == nkb - 1),
                           reads=[vTb, pslot], writes=[ps_o])
                    COPY(ctx, "vector", mean_f[:], ps_util[0:1, :], reads=[ps_util], writes=[mean_f])
                    RECIP(ctx, var_f[:], mean_f[:], reads=[mean_f], writes=[var_f])
                    MM(ctx, ps_util[0:64, :], ones_row_f[0:1, 0:64], var_f[:], start=True, stop=True,
                       reads=[var_f, ones_row_f], writes=[ps_util])
                    COPY(ctx, "vector", zrecB[:], ps_util[0:64, :], reads=[ps_util], writes=[zrecB])
                    TT(ctx, "vector", o_qc[vb:vb + 64, g * TQ:(g + 1) * TQ], ps_o[:], zrecB[:],
                       ALU.mult, reads=[ps_o, zrecB], writes=[oqb[g]])
            xf_, xfb_ = xf[0], xfb[0]
            COPY(ctx, "vector", var_f[:], rstdsave[:, q0:q0 + TQ], reads=[rstdsave], writes=[var_f])
            MM(ctx, ps_util[:], ones_row_f[:], var_f[:], start=True, stop=True,
               reads=[var_f, ones_row_f], writes=[ps_util])
            COPY(ctx, "vector", sB1[:], ps_util[:], reads=[ps_util], writes=[sB1])
            TT(ctx, "vector", sq_f[:], meansave_b[:, q0:q0 + TQ], rstdsave[:, q0:q0 + TQ],
               ALU.mult, reads=[meansave_b, rstdsave], writes=[sq_f])
            MM(ctx, ps_util[:], ones_row_f[:], sq_f[:], start=True, stop=True,
               reads=[sq_f, ones_row_f], writes=[ps_util])
            COPY(ctx, "vector", stmask[:], ps_util[:], reads=[ps_util], writes=[stmask])
            for m in range(PD):
                if m % 4 == 0:
                    DMA(ctx, "sync", xf_[:].rearrange("p (k t) -> p k t", k=4),
                        x_in[b, m * 128:(m + 4) * 128, q0:q0 + TQ].rearrange("(k p) t -> p k t", p=128),
                        writes=list(xfb_))
                for k in range(PD):
                    MM(ctx, ps_main[:], wo_b[k][:, m * 128:(m + 1) * 128],
                       o_qc[:, k * TQ:(k + 1) * TQ], start=(k == 0), stop=(k == PD - 1),
                       reads=[w16buf, oqb[k]], writes=[ps_main])
                TT(ctx, "vector", v1t[:], xf_[:, (m % 4) * TQ:(m % 4 + 1) * TQ], sB1[:], ALU.mult,
                   reads=[xfb_[m % 4], sB1], writes=[v1t])
                TT(ctx, "vector", v1t[:], v1t[:], stmask[:], ALU.subtract,
                   reads=[v1t, stmask], writes=[v1t])
                TT(ctx, "vector", out_f[:], v1t[:], ps_main[:],
                   ALU.add, reads=[v1t, ps_main], writes=[outb[0]])
                DMA(ctx, "gpsimd",
                    xt1_out[b, m * 128:(m + 1) * 128, q0:q0 + TQ],
                    out_f[:], reads=[outb[0]])


def mla_weight_arrays(inp):
    HH = 32
    inv_freq = 1.0 / (10000.0 ** (np.arange(0, HH, 2, dtype=np.float64) / HH))
    t = np.arange(S, dtype=np.float64)
    freqs = np.outer(t, inv_freq)
    emb = np.concatenate([freqs, freqs], axis=1)
    cosT = np.ascontiguousarray(np.cos(emb).T.astype(np.float32))
    sinT = np.ascontiguousarray(np.sin(emb).T.astype(np.float32))
    kkk = np.arange(128)[:, None]
    qqq = np.arange(512)[None, :]
    maskneg = np.stack([np.where(128 * i + kkk <= qqq, 0.0, -1e9).astype(np.float32)
                        for i in range(4)])
    perm = np.concatenate([np.arange(16, 32), np.arange(0, 16)])
    full_perm = np.concatenate([h * 32 + perm for h in range(16)])
    return {
        "lnw1_cols": np.ascontiguousarray(np.asarray(inp["L1_ffn_w_ln1"]).reshape(PD, 128).T),
        "wkvd": np.ascontiguousarray(inp["L1_ffn_W_kvd"]),
        "wqd": np.ascontiguousarray(inp["L1_ffn_W_qd"]),
        "wku": np.ascontiguousarray(inp["L1_ffn_W_ku"]),
        "wqu": np.ascontiguousarray(inp["L1_ffn_W_qu"]),
        "wvu": np.ascontiguousarray(inp["L1_ffn_W_vu"]),
        "wropeq": np.ascontiguousarray(inp["L1_ffn_W_ropeq"]),
        "wropeq_perm": np.ascontiguousarray(np.asarray(inp["L1_ffn_W_ropeq"])[:, full_perm[:512]]),
        "wropek": np.ascontiguousarray(inp["L1_ffn_W_ropek"]),
        "wropek_perm": np.ascontiguousarray(np.asarray(inp["L1_ffn_W_ropek"])[:, full_perm[:512]]),
        "wo": np.ascontiguousarray(inp["L1_ffn_W_o"]),
        "cosT": cosT, "sinT": sinT, "maskneg": maskneg,
    }


def build_mla_program(dbg=False):
    nc = bass.Bass("TRN2", target_bir_lowering=False)
    ctx = Ctx(nc)
    x_in = nc.dram_tensor("x_fm", [BPC, D, S], F32, kind="ExternalInput").ap()
    xt1_out = nc.dram_tensor("xt1_fm", [BPC, D, S], F32, kind="ExternalOutput").ap()
    names = {"wkvd": [D, 256], "wqd": [D, 256], "wku": [256, 512], "wqu": [256, 512],
             "wvu": [256, 1024], "wropeq": [256, 512], "wropeq_perm": [256, 512],
             "wropek": [D, 512], "wropek_perm": [D, 512], "wo": [D, D],
             "lnw1_cols": [128, PD], "cosT": [32, S], "sinT": [32, S]}
    ctx.w = {k: nc.dram_tensor(k, v, F32, kind="ExternalInput").ap() for k, v in names.items()}
    ctx.w["maskneg"] = nc.dram_tensor("maskneg", [4, 128, 512], F32, kind="ExternalInput").ap()
    dbgd = None
    if dbg:
        dbgd = {"kk": nc.dram_tensor("kk_dbg", [BPC, 1024, S], BF16, kind="ExternalOutput").ap(),
                "vT": nc.dram_tensor("vT_dbg", [BPC, 128, 256 * 64], BF16, kind="ExternalOutput").ap()}
        for k in dbgd:
            ctx.bufmap[dbgd[k].tensor.name] = Buf(k)
    emit_mla(ctx, x_in, xt1_out, dbg=dbgd)
    ctx.finalize()
    return nc


def run_mla(x_fm_percore, inp, dbg=False):
    warr = mla_weight_arrays(inp)
    nc = build_mla_program(dbg=dbg)
    in_maps = [dict(x_fm=x_fm_percore[c], **warr) for c in range(NCORES)]
    res = run_bass_kernel_spmd(nc, in_maps, core_ids=list(range(NCORES)))
    if dbg:
        return [r["xt1_fm"] for r in res.results], res.results
    return [r["xt1_fm"] for r in res.results]



def emit_moe(ctx, xt1_in, x_in, x_out):
    TQ = 512
    NT = S // TQ
    w = ctx.w
    nc = ctx.nc

    ones_col_b = ctx.sb("moe_ones_col", [128, 1], BF16)
    ones_row_b = ctx.sb("moe_ones_rowb", [1, 128], BF16)
    ones_row_f = ctx.sb("moe_ones_rowf", [1, 128], F32)
    MEMSET(ctx, "vector", ones_col_b[:], 1.0, writes=[ones_col_b])
    MEMSET(ctx, "vector", ones_row_b[:], 1.0, writes=[ones_row_b])
    MEMSET(ctx, "vector", ones_row_f[:], 1.0, writes=[ones_row_f])
    idf, idb = make_ident(ctx, "moe_id")

    lnw_cols = ctx.sb("moe_lnw", [128, PD], F32)
    DMA(ctx, "sync", lnw_cols[:], w["lnw2_cols"], writes=[lnw_cols])
    stage0 = ctx.sb("moe_stg0", [128, 1024], F32)
    stg_buf = ctx.bufmap[ctx.prefix + "moe_stg0"]
    cvt = ctx.sb("moe_cvt", [128, 1024], BF16)
    cvt_buf = ctx.bufmap[ctx.prefix + "moe_cvt"]

    wgu_bf = ctx.dram_i("moe_wgu_bf", [8, 2, PD, 128, 1024], BF16)
    wgu_buf = ctx.bufmap[ctx.prefix + "moe_wgu_bf"]
    wd_bf = ctx.dram_i("moe_wd_bf", [PD, 64, 128, 128], BF16)
    wd_buf = ctx.bufmap[ctx.prefix + "moe_wd_bf"]
    hmid_hbm = ctx.dram_i("moe_hmid", [BPC, NT, 64, 128, TQ], BF16)
    hmid_buf = ctx.bufmap[ctx.prefix + "moe_hmid"]

    for e in range(8):
        for gi in range(2):
            src = w["wg_r"] if gi == 0 else w["wu_r"]
            for kc in range(PD):
                if e < 7:
                    sap = src[e, kc * 128:(kc + 1) * 128, :]
                else:
                    sap = (w["wg_s"] if gi == 0 else w["wu_s"])[kc * 128:(kc + 1) * 128, :]
                DMA(ctx, "sync", stage0[:], sap, writes=[stg_buf])
                TS(ctx, "vector", cvt[:], stage0[:], lnw_cols[:, kc:kc + 1], ALU.mult,
                   reads=[stg_buf, lnw_cols], writes=[cvt_buf])
                DMA(ctx, "gpsimd", wgu_bf[e, gi, kc], cvt[:], reads=[cvt_buf], writes=[wgu_buf])
    for e in range(8):
        for kc in range(PD):
            if e < 7:
                sap = w["wd_r"][e, kc * 128:(kc + 1) * 128, :]
            else:
                sap = w["wd_s"][kc * 128:(kc + 1) * 128, :]
            DMA(ctx, "sync", stage0[:], sap, writes=[stg_buf])
            COPY(ctx, "vector", cvt[:], stage0[:], reads=[stg_buf], writes=[cvt_buf])
            DMA(ctx, "gpsimd", wd_bf[:, e * PD + kc].rearrange("m p c -> p m c"),
                cvt[:].rearrange("p (m c) -> p m c", m=PD), reads=[cvt_buf], writes=[wd_buf])

    wr_b = [load_weight_staged(ctx, f"moe_wr{k}", w["wrouter"][k * 128:(k + 1) * 128, :],
                               128, 7, stage0, scale_col=lnw_cols[:, k:k + 1],
                               scale_col_reads=[lnw_cols]) for k in range(PD)]
    biasB = ctx.sb("moe_biasB", [128, 7], F32)
    DMA(ctx, "sync", biasB[:], bass.AP(w["rbias"].tensor, 0, [[0, 128], [1, 7]]), writes=[biasB])

    ps_sq = ctx.ps("moe_ps_sq", [1, TQ], F32)
    ps_util = ctx.ps("moe_ps_util", [128, TQ], F32)
    ps_g = ctx.ps("moe_ps_g", [128, TQ], F32)
    ps_u = ctx.ps("moe_ps_u", [128, TQ], F32)
    ps_w = ctx.ps("moe_ps_w", [128, TQ], F32)
    ps_tr = ctx.ps("moe_ps_tr", [128, 128], F32)
    ps_comb = [ctx.ps(f"moe_ps_comb{i}", [128, TQ], F32) for i in range(2)]

    xf = ctx.sb("moe_xf", [128, 4 * TQ], F32)
    xfb = [ctx.mkbuf(f"moe_xfb{k}") for k in range(4)]
    x2s = ctx.sb("moe_x2s", [128, TQ], BF16)
    sqrow = ctx.sb("moe_sqrow", [1, TQ], F32)
    s2row = ctx.sb("moe_s2row", [1, TQ], F32)
    s2B = ctx.sb("moe_s2B", [128, TQ], F32)
    xn_b = ctx.sb("moe_xn", [128, PD * S], BF16)
    xnb = [ctx.mkbuf(f"moe_xnb{i}") for i in range(NT)]
    logit = ctx.sb("moe_logit", [128, 4 * 8], F32)
    p8 = ctx.sb("moe_p8", [128, 4 * 8], F32)
    t8 = ctx.sb("moe_t8", [128, 4 * 8], F32)
    denr = ctx.sb("moe_denr", [128, 4], F32)
    gmask = ctx.sb("moe_gmask", [128, 4 * 8], F32)
    wts = ctx.sb("moe_wts", [128, 4 * 8], F32)
    rstd_cols = ctx.sb("moe_rstdcols", [128, 4], F32)
    wts_all = ctx.sb("moe_wts_all", [128, NT * 32], F32)
    wts_all_buf = ctx.bufmap[ctx.prefix + "moe_wts_all"]
    wrow_t = ctx.sb("moe_wrow_t", [1, TQ], BF16)
    wrow_buf = ctx.bufmap[ctx.prefix + "moe_wrow_t"]
    wgu_sb = ctx.sb("moe_wgu_sb", [128, 2 * PD * 1024], BF16)
    wgu_sb_buf = ctx.bufmap[ctx.prefix + "moe_wgu_sb"]
    g_sb = ctx.sb("moe_g", [128, TQ], BF16)
    hm_sb = ctx.sb("moe_hm", [128, TQ], BF16)
    hmw_sb = ctx.sb("moe_hmw", [128, PD * TQ], BF16)
    hmwb = [ctx.mkbuf(f"moe_hmwb{m}") for m in range(PD)]
    hmid_sb = ctx.sb("moe_hmid_sb", [128, 64 * TQ], BF16)
    hmid_sb_buf = ctx.bufmap[ctx.prefix + "moe_hmid_sb"]
    wd_sb = [ctx.sb("moe_wd_sb0", [128, 64 * 128], BF16)]
    xb2 = ctx.sb("moe_xb2", [128, PD * TQ], F32)
    xb2b = [ctx.mkbuf(f"moe_xb2b{m}") for m in range(PD)]
    out_t = ctx.sb("moe_out", [128, TQ], F32)
    outb = ctx.mkbuf("moe_outb")

    for b in range(BPC):
        for tci in range(NT):
            t0 = tci * TQ
            for hh in range(2):
                DMA(ctx, "sync", xf[:].rearrange("p (k t) -> p k t", k=4),
                    xt1_in[b, hh * 512:(hh + 1) * 512, t0:t0 + TQ].rearrange("(k p) t -> p k t", p=128),
                    writes=list(xfb))
                for k in range(4):
                    kk4 = hh * 4 + k
                    ACT(ctx, x2s[:], xf[:, k * TQ:(k + 1) * TQ], AF.Square,
                        reads=[xfb[k]], writes=[x2s])
                    MM(ctx, ps_sq[0:1, :], ones_col_b[:], x2s[:],
                       start=(kk4 == 0), stop=(kk4 == PD - 1), reads=[x2s, ones_col_b],
                       writes=[ps_sq])
            ACT(ctx, sqrow[:], ps_sq[0:1, :], AF.Sqrt, scale=1.0 / D, reads=[ps_sq], writes=[sqrow])
            TS(ctx, "vector", sqrow[:], sqrow[:], 1e-5, ALU.add, reads=[sqrow], writes=[sqrow])
            RECIP(ctx, s2row[:], sqrow[:], reads=[sqrow], writes=[s2row])
            MM(ctx, ps_util[:], ones_row_f[:], s2row[:], start=True, stop=True,
               reads=[s2row, ones_row_f], writes=[ps_util])
            COPY(ctx, "vector", s2B[:], ps_util[:], reads=[ps_util], writes=[s2B])
            for hh in range(2):
                DMA(ctx, "sync", xf[:].rearrange("p (k t) -> p k t", k=4),
                    xt1_in[b, hh * 512:(hh + 1) * 512, t0:t0 + TQ].rearrange("(k p) t -> p k t", p=128),
                    writes=list(xfb))
                for k in range(4):
                    kk4 = hh * 4 + k
                    TT(ctx, "vector", xn_b[:, kk4 * S + t0:kk4 * S + t0 + TQ],
                       xf[:, k * TQ:(k + 1) * TQ], s2B[:], ALU.mult,
                       reads=[xfb[k], s2B], writes=[xnb[tci]])
            for j in range(4):
                sl = slice(j * 8, j * 8 + 7)
                for k in range(PD):
                    MM(ctx, ps_util[0:128, j * 8:j * 8 + 7],
                       xn_b[:, k * S + t0 + j * 128:k * S + t0 + (j + 1) * 128], wr_b[k][:],
                       start=(k == 0), stop=(k == PD - 1), reads=[xnb[tci], wr_b[k]],
                       writes=[ps_util])
                TT(ctx, "vector", logit[:, sl], ps_util[0:128, j * 8:j * 8 + 7], biasB[0:128, 0:7],
                   ALU.add, reads=[ps_util, biasB], writes=[logit])
                ACT(ctx, p8[:, sl], logit[:, sl], AF.Sigmoid, reads=[logit], writes=[p8])
                ctx.op("vector", lambda e, _j=j: e.memset(p8[:, _j * 8 + 7:_j * 8 + 8], -1.0),
                       writes=[p8])
                MAX8(ctx, t8[:, j * 8:(j + 1) * 8], p8[:, j * 8:(j + 1) * 8],
                     reads=[p8], writes=[t8])
                TT(ctx, "vector", denr[:, j:j + 1], t8[:, j * 8:j * 8 + 1],
                   t8[:, j * 8 + 1:j * 8 + 2], ALU.add, reads=[t8], writes=[denr])
                RECIP(ctx, denr[:, j:j + 1], denr[:, j:j + 1], reads=[denr], writes=[denr])
                TS(ctx, "vector", gmask[:, sl], p8[:, sl], t8[:, j * 8 + 1:j * 8 + 2],
                   ALU.is_ge, reads=[p8, t8], writes=[gmask])
                TT(ctx, "vector", wts[:, sl], p8[:, sl], gmask[:, sl], ALU.mult,
                   reads=[p8, gmask], writes=[wts])
                TS(ctx, "vector", wts_all[:, tci * 32 + j * 8:tci * 32 + j * 8 + 7],
                   wts[:, sl], denr[:, j:j + 1], ALU.mult,
                   reads=[wts, denr], writes=[wts_all_buf])
        for e in range(8):
            DMA(ctx, "sync", wgu_sb[:].rearrange("p (i c) -> p i c", i=2 * PD),
                wgu_bf[e].rearrange("g k p c -> p (g k) c"), reads=[wgu_buf],
                writes=[wgu_sb_buf])
            for tci in range(NT):
                t0 = tci * TQ
                if e < 7:
                    for j in range(4):
                        TRANS(ctx, ps_tr[0:1, 0:128],
                              wts_all[:, tci * 32 + j * 8 + e:tci * 32 + j * 8 + e + 1], idf[:],
                              reads=[wts_all_buf, idf], writes=[ps_tr])
                        COPY(ctx, "vector", wrow_t[0:1, j * 128:(j + 1) * 128],
                             ps_tr[0:1, 0:128], reads=[ps_tr], writes=[wrow_buf])
                    MM(ctx, ps_w[:], ones_row_b[:], wrow_t[:],
                       start=True, stop=True, reads=[ones_row_b, wrow_buf], writes=[ps_w])
                for m in range(PD):
                    for kc in range(PD):
                        MM(ctx, ps_g[:], wgu_sb[:, kc * 1024 + m * 128:kc * 1024 + (m + 1) * 128],
                           xn_b[:, kc * S + t0:kc * S + t0 + TQ], start=(kc == 0), stop=(kc == PD - 1),
                           reads=[wgu_sb_buf, xnb[tci]], writes=[ps_g])
                    for kc in range(PD):
                        MM(ctx, ps_u[:], wgu_sb[:, (PD + kc) * 1024 + m * 128:(PD + kc) * 1024 + (m + 1) * 128],
                           xn_b[:, kc * S + t0:kc * S + t0 + TQ], start=(kc == 0), stop=(kc == PD - 1),
                           reads=[wgu_sb_buf, xnb[tci]], writes=[ps_u])
                    ACT(ctx, g_sb[:], ps_g[:], AF.Silu, reads=[ps_g], writes=[g_sb])
                    TT(ctx, "vector", hm_sb[:], g_sb[:], ps_u[:], ALU.mult,
                       reads=[g_sb, ps_u], writes=[hm_sb])
                    if e < 7:
                        TT(ctx, "vector", hmw_sb[:, m * TQ:(m + 1) * TQ], hm_sb[:], ps_w[:],
                           ALU.mult, reads=[hm_sb, ps_w], writes=[hmwb[m]])
                    else:
                        COPY(ctx, "vector", hmw_sb[:, m * TQ:(m + 1) * TQ], hm_sb[:],
                             reads=[hm_sb], writes=[hmwb[m]])
                DMA(ctx, "gpsimd", hmid_hbm[b, tci, e * PD:(e + 1) * PD].rearrange("m p t -> p m t"),
                    hmw_sb[:].rearrange("p (m t) -> p m t", m=PD), reads=list(hmwb),
                    writes=[hmid_buf])
        for tci in range(NT):
            t0 = tci * TQ
            DMA(ctx, "sync", hmid_sb[:].rearrange("p (i t) -> p i t", i=64),
                hmid_hbm[b, tci].rearrange("i p t -> p i t"), reads=[hmid_buf],
                writes=[hmid_sb_buf])
            for hh in range(2):
                DMA(ctx, "sync", xf[:].rearrange("p (k t) -> p k t", k=4),
                    xt1_in[b, hh * 512:(hh + 1) * 512, t0:t0 + TQ].rearrange("(k p) t -> p k t", p=128),
                    writes=list(xfb))
                for k in range(4):
                    TS(ctx, "vector", xb2[:, (hh * 4 + k) * TQ:(hh * 4 + k + 1) * TQ],
                       xf[:, k * TQ:(k + 1) * TQ], 0.5, ALU.mult,
                       reads=[xfb[k]], writes=[xb2b[hh * 4 + k]])
                DMA(ctx, "sync", xf[:].rearrange("p (k t) -> p k t", k=4),
                    x_in[b, hh * 512:(hh + 1) * 512, t0:t0 + TQ].rearrange("(k p) t -> p k t", p=128),
                    writes=list(xfb))
                for k in range(4):
                    m = hh * 4 + k
                    TT(ctx, "vector", xb2[:, m * TQ:(m + 1) * TQ], xb2[:, m * TQ:(m + 1) * TQ],
                       xf[:, k * TQ:(k + 1) * TQ], ALU.add, reads=[xfb[k], xb2b[m]],
                       writes=[xb2b[m]])
            for m in range(PD):
                wd_ = wd_sb[0]
                DMA(ctx, "sync", wd_[:].rearrange("p (i c) -> p i c", i=64),
                    wd_bf[m].rearrange("i p c -> p i c"), reads=[wd_buf], writes=[wd_])
                pc = ps_comb[m % 2]
                for kc in range(64):
                    MM(ctx, pc[:], wd_[:, kc * 128:(kc + 1) * 128],
                       hmid_sb[:, kc * TQ:(kc + 1) * TQ], start=(kc == 0), stop=(kc == 63),
                       reads=[wd_, hmid_sb_buf], writes=[pc])
                STT(ctx, "vector", out_t[:], pc[:], 0.5, xb2[:, m * TQ:(m + 1) * TQ],
                    ALU.mult, ALU.add, reads=[pc, xb2b[m]], writes=[outb])
                DMA(ctx, "gpsimd", x_out[b, m * 128:(m + 1) * 128, t0:t0 + TQ], out_t[:],
                    reads=[outb])


def moe_weight_arrays(inp):
    return {
        "lnw2_cols": np.ascontiguousarray(np.asarray(inp["L1_ffn_w_ln2"]).reshape(PD, 128).T),
        "wg_r": np.ascontiguousarray(inp["L1_ffn_Wg_r"]),
        "wu_r": np.ascontiguousarray(inp["L1_ffn_Wu_r"]),
        "wd_r": np.ascontiguousarray(inp["L1_ffn_Wd_r"]),
        "wg_s": np.ascontiguousarray(inp["L1_ffn_Wg_s"]),
        "wu_s": np.ascontiguousarray(inp["L1_ffn_Wu_s"]),
        "wd_s": np.ascontiguousarray(inp["L1_ffn_Wd_s"]),
        "wrouter": np.ascontiguousarray(inp["L1_ffn_W_router"]),
        "rbias": np.ascontiguousarray(np.asarray(inp["L1_ffn_routing_bias"]).reshape(1, 7)),
    }


def build_moe_program():
    nc = bass.Bass("TRN2", target_bir_lowering=False)
    ctx = Ctx(nc)
    xt1_in = nc.dram_tensor("xt1_fm", [BPC, D, S], F32, kind="ExternalInput").ap()
    x_in = nc.dram_tensor("x_fm", [BPC, D, S], F32, kind="ExternalInput").ap()
    x_out = nc.dram_tensor("xo_fm", [BPC, D, S], F32, kind="ExternalOutput").ap()
    names = {"lnw2_cols": [128, PD], "wg_r": [7, D, D], "wu_r": [7, D, D], "wd_r": [7, D, D],
             "wg_s": [D, D], "wu_s": [D, D], "wd_s": [D, D], "wrouter": [D, 7], "rbias": [1, 7]}
    ctx.w = {k: nc.dram_tensor(k, v, F32, kind="ExternalInput").ap() for k, v in names.items()}
    emit_moe(ctx, xt1_in, x_in, x_out)
    ctx.finalize()
    return nc


def run_moe(xt1_percore, x_percore, inp):
    warr = moe_weight_arrays(inp)
    nc = build_moe_program()
    in_maps = [dict(xt1_fm=xt1_percore[c], x_fm=x_percore[c], **warr) for c in range(NCORES)]
    res = run_bass_kernel_spmd(nc, in_maps, core_ids=list(range(NCORES)))
    return [r["xo_fm"] for r in res.results]



def emit_qatt(ctx, x_in, x_out):
    TQ = 512
    NT = S // TQ
    w = ctx.w
    RT2 = 0.7071067811865476

    ones_col_b = ctx.sb("qa_ones_col", [128, 1], BF16)
    ones_row_f = ctx.sb("qa_ones_rowf", [1, 128], F32)
    MEMSET(ctx, "vector", ones_col_b[:], 1.0, writes=[ones_col_b])
    MEMSET(ctx, "vector", ones_row_f[:], 1.0, writes=[ones_row_f])
    idf, idb = make_ident(ctx, "qa_id")
    ident64 = ctx.sb("qa_ident64", [128, 64], BF16)
    COPY(ctx, "vector", ident64[0:64, :], idb[0:64, 0:64], reads=[idb], writes=[ident64])
    COPY(ctx, "vector", ident64[64:128, :], idb[0:64, 0:64], reads=[idb], writes=[ident64])

    stage0 = ctx.sb("qa_stg0", [128, 512], F32)
    stg_buf = ctx.bufmap[ctx.prefix + "qa_stg0"]
    wbuf = ctx.sb("qa_wbuf", [128, PD * 1024], BF16)
    wbuf_b = ctx.bufmap[ctx.prefix + "qa_wbuf"]

    def load_w(which, scale=None):
        for k in range(PD):
            for cc in range(2):
                convert_into(ctx, wbuf[:, k * 1024 + cc * 512:k * 1024 + (cc + 1) * 512],
                             wbuf_b, w[which][k * 128:(k + 1) * 128, cc * 512:(cc + 1) * 512],
                             128, 512, stage0, scale=scale)

    wsum = ctx.sb("qa_wsum", [1, 1024], BF16)
    wsum_b = ctx.bufmap[ctx.prefix + "qa_wsum"]
    ps_mean = ctx.ps("qa_ps_mean", [1, TQ], F32)
    ps_sq = ctx.ps("qa_ps_sq", [1, TQ], F32)
    ps_util = ctx.ps("qa_ps_util", [128, TQ], F32)
    ps_main = ctx.ps("qa_ps_main", [128, TQ], F32)
    ps_St = ctx.ps("qa_ps_St", [128, TQ], F32)
    ps_o = ctx.ps("qa_ps_o", [64, TQ], F32)
    ps_tr = ctx.ps("qa_ps_tr", [128, 128], BF16)

    def calc_wsum(ctx=ctx):
        for blk in range(0, 1024, 512):
            for k in range(PD):
                MM(ctx, ps_mean[0:1, 0:512], ones_col_b[:],
                   wbuf[:, k * 1024 + blk:k * 1024 + blk + 512],
                   start=(k == 0), stop=(k == PD - 1), reads=[wbuf_b, ones_col_b],
                   writes=[ps_mean])
            TS(ctx, "vector", wsum[:, blk:blk + 512], ps_mean[0:1, 0:512], -1.0, ALU.mult,
               reads=[ps_mean], writes=[wsum_b])

    masks01 = ctx.sb("qa_masks", [128, 4 * TQ], BF16)
    mstage = ctx.sb("qa_mstage", [128, TQ], F32)
    for i in range(4):
        DMA(ctx, "sync", mstage[:], w["mask01"][i], writes=[mstage])
        COPY(ctx, "vector", masks01[:, i * TQ:(i + 1) * TQ], mstage[:],
             reads=[mstage], writes=[masks01])

    mean_f = ctx.sb("qa_mean_f", [1, TQ], F32)
    sq_f = ctx.sb("qa_sq_f", [1, TQ], F32)
    var_f = ctx.sb("qa_var_f", [1, TQ], F32)
    meansave_b = ctx.sb("qa_meansave", [1, S], BF16)
    rstdsave = ctx.sb("qa_rstdsave", [1, S], F32)
    rB = ctx.sb("qa_rB", [128, TQ], F32)
    xf = ctx.sb("qa_xf", [128, 4 * TQ], F32)
    xfb = [ctx.mkbuf(f"qa_xfb{k}") for k in range(4)]
    xb = ctx.sb("qa_xb", [128, PD * TQ], BF16)
    x2s = ctx.sb("qa_x2s", [128, TQ], BF16)
    qq = ctx.sb("qa_qq", [128, PD * S], BF16)
    qqb = [ctx.mkbuf(f"qa_qqb{g}") for g in range(PD)]
    kk = ctx.sb("qa_kk", [128, PD * S], BF16)
    kkb = [ctx.mkbuf(f"qa_kkb{g}") for g in range(PD)]
    vT = ctx.sb("qa_vT", [128, 256 * 64], BF16)
    vTb = ctx.mkbuf("qa_vTb")
    vtmp = ctx.sb("qa_vtmp", [128, TQ], BF16)
    attn_f = ctx.sb("qa_attnf", [128, TQ], F32)
    probs = [ctx.sb(f"qa_probs{i}", [128, TQ], BF16) for i in range(2)]
    zrecB = ctx.sb("qa_zrecB", [64, TQ], F32)
    o_qc = ctx.sb("qa_oqc", [128, PD * TQ], BF16)
    oqb = [ctx.mkbuf(f"qa_oqb{g}") for g in range(PD)]
    out_t = ctx.sb("qa_out", [128, TQ], F32)
    outb = ctx.mkbuf("qa_outb")

    for b in range(BPC):
        for (wi, (wname, wscale, dest, destb)) in enumerate(
                (("wq", 0.125, qq, qqb), ("wk", None, kk, kkb), ("wv", None, None, None))):
            load_w(wname, scale=wscale)
            calc_wsum()
            for tci in range(NT):
                t0 = tci * TQ
                for hh in range(2):
                    DMA(ctx, "sync", xf[:].rearrange("p (k t) -> p k t", k=4),
                        x_in[b, hh * 512:(hh + 1) * 512, t0:t0 + TQ].rearrange("(k p) t -> p k t", p=128),
                        writes=list(xfb))
                    for k in range(4):
                        kk4 = hh * 4 + k
                        COPY(ctx, "vector", xb[:, kk4 * TQ:(kk4 + 1) * TQ],
                             xf[:, k * TQ:(k + 1) * TQ], reads=[xfb[k]], writes=[xb])
                        if wi == 0:
                            ACT(ctx, x2s[:], xf[:, k * TQ:(k + 1) * TQ], AF.Square,
                                reads=[xfb[k]], writes=[x2s])
                            MM(ctx, ps_sq[0:1, :], ones_col_b[:], x2s[:],
                               start=(kk4 == 0), stop=(kk4 == PD - 1),
                               reads=[x2s, ones_col_b], writes=[ps_sq])
                if wi == 0:
                    for k in range(PD):
                        MM(ctx, ps_mean[0:1, :], ones_col_b[:], xb[:, k * TQ:(k + 1) * TQ],
                           start=(k == 0), stop=(k == PD - 1), reads=[xb, ones_col_b],
                           writes=[ps_mean])
                    ACT(ctx, mean_f[:], ps_mean[0:1, :], AF.Copy, scale=1.0 / D,
                        reads=[ps_mean], writes=[mean_f])
                    COPY(ctx, "vector", meansave_b[:, t0:t0 + TQ], mean_f[:],
                         reads=[mean_f], writes=[meansave_b])
                    ACT(ctx, sq_f[:], mean_f[:], AF.Square, reads=[mean_f], writes=[sq_f])
                    STT(ctx, "vector", var_f[:], ps_sq[0:1, :], 1.0 / D, sq_f[:],
                        ALU.mult, ALU.subtract, reads=[ps_sq, sq_f], writes=[var_f])
                    ACT(ctx, sq_f[:], var_f[:], AF.Sqrt, bias=1e-5, reads=[var_f], writes=[sq_f])
                    RECIP(ctx, var_f[:], sq_f[:], reads=[sq_f], writes=[var_f])
                    COPY(ctx, "vector", rstdsave[:, t0:t0 + TQ], var_f[:],
                         reads=[var_f], writes=[rstdsave])
                else:
                    COPY(ctx, "vector", var_f[:], rstdsave[:, t0:t0 + TQ],
                         reads=[rstdsave], writes=[var_f])
                MM(ctx, ps_util[:], ones_row_f[:], var_f[:], start=True, stop=True,
                   reads=[var_f, ones_row_f], writes=[ps_util])
                COPY(ctx, "vector", rB[:], ps_util[:], reads=[ps_util], writes=[rB])
                for g in range(PD):
                    for k in range(PD):
                        MM(ctx, ps_main[:], wbuf[:, k * 1024 + g * 128:k * 1024 + (g + 1) * 128],
                           xb[:, k * TQ:(k + 1) * TQ], start=(k == 0), stop=False,
                           reads=[wbuf_b, xb], writes=[ps_main])
                    MM(ctx, ps_main[:], wsum[:, g * 128:(g + 1) * 128],
                       meansave_b[:, t0:t0 + TQ], start=False, stop=True,
                       reads=[wsum_b, meansave_b], writes=[ps_main])
                    if wi < 2:
                        TT(ctx, "vector", dest[:, g * S + t0:g * S + t0 + TQ], ps_main[:], rB[:],
                           ALU.mult, reads=[ps_main, rB], writes=[destb[g]])
                    else:
                        TT(ctx, "vector", vtmp[:], ps_main[:], rB[:], ALU.mult,
                           reads=[ps_main, rB], writes=[vtmp])
                        for j in range(4):
                            kb = tci * 4 + j
                            for (vb, hh2) in ((0, 2 * g), (64, 2 * g + 1)):
                                TRANS(ctx, ps_tr[:, 0:64], vtmp[vb:vb + 64, j * 128:(j + 1) * 128],
                                      ident64[vb:vb + 64, :], reads=[vtmp, ident64], writes=[ps_tr])
                                COPY(ctx, "vector", vT[:, (kb * 16 + hh2) * 64:(kb * 16 + hh2) * 64 + 64],
                                     ps_tr[:, 0:64], reads=[ps_tr], writes=[vTb])
        load_w("wo")
        for qci in range(NT):
            q0 = qci * TQ
            for h in range(16):
                g = h // 2
                vb = (h % 2) * 64
                nkb = qci * 4 + 4
                for kb in range(nkb):
                    MM(ctx, ps_St[:], kk[vb:vb + 64, g * S + kb * 128:g * S + (kb + 1) * 128],
                       qq[vb:vb + 64, g * S + q0:g * S + q0 + TQ], start=True, stop=True,
                       reads=[kkb[g], qqb[g]], writes=[ps_St])
                    pslot = probs[kb % 2]
                    ACT(ctx, attn_f[:], ps_St[:], AF.Square, bias=RT2, scale=RT2,
                        reads=[ps_St], writes=[attn_f])
                    if kb >= qci * 4:
                        ii = kb - qci * 4
                        STT(ctx, "vector", pslot[:], attn_f[:], 0.5,
                            masks01[:, ii * TQ:(ii + 1) * TQ], ALU.add, ALU.mult,
                            reads=[attn_f, masks01], writes=[pslot])
                    else:
                        ctx.op("scalar", lambda e, _o=pslot[:], _i=attn_f[:]:
                               e.add(out=_o, in_=_i, add=0.5), reads=[attn_f, ctx._cpool],
                               writes=[pslot]) if False else None
                        TS(ctx, "vector", pslot[:], attn_f[:], 0.5, ALU.add,
                           reads=[attn_f], writes=[pslot])
                    MM(ctx, ps_util[0:1, :], ones_col_b[:], pslot[:],
                       start=(kb == 0), stop=(kb == nkb - 1), reads=[pslot, ones_col_b],
                       writes=[ps_util])
                    MM(ctx, ps_o[:], vT[:, (kb * 16 + h) * 64:(kb * 16 + h) * 64 + 64],
                       pslot[:], start=(kb == 0), stop=(kb == nkb - 1),
                       reads=[vTb, pslot], writes=[ps_o])
                ACT(ctx, mean_f[:], ps_util[0:1, :], AF.Copy, bias=1e-6,
                    reads=[ps_util], writes=[mean_f])
                RECIP(ctx, var_f[:], mean_f[:], reads=[mean_f], writes=[var_f])
                MM(ctx, ps_util[0:64, :], ones_row_f[0:1, 0:64], var_f[:], start=True, stop=True,
                   reads=[var_f, ones_row_f], writes=[ps_util])
                COPY(ctx, "vector", zrecB[:], ps_util[0:64, :], reads=[ps_util], writes=[zrecB])
                TT(ctx, "vector", o_qc[vb:vb + 64, g * TQ:(g + 1) * TQ], ps_o[:], zrecB[:],
                   ALU.mult, reads=[ps_o, zrecB], writes=[oqb[g]])
            for m in range(PD):
                if m % 4 == 0:
                    DMA(ctx, "sync", xf[:].rearrange("p (k t) -> p k t", k=4),
                        x_in[b, m * 128:(m + 4) * 128, q0:q0 + TQ].rearrange("(k p) t -> p k t", p=128),
                        writes=list(xfb))
                for k in range(PD):
                    MM(ctx, ps_main[:], wbuf[:, k * 1024 + m * 128:k * 1024 + (m + 1) * 128],
                       o_qc[:, k * TQ:(k + 1) * TQ], start=(k == 0), stop=(k == PD - 1),
                       reads=[wbuf_b, oqb[k]], writes=[ps_main])
                TT(ctx, "vector", out_t[:], ps_main[:], xf[:, (m % 4) * TQ:(m % 4 + 1) * TQ],
                   ALU.add, reads=[ps_main, xfb[m % 4]], writes=[outb])
                DMA(ctx, "gpsimd", x_out[b, m * 128:(m + 1) * 128, q0:q0 + TQ], out_t[:],
                    reads=[outb])


def qatt_weight_arrays(inp):
    kkk = np.arange(128)[:, None]
    qqq = np.arange(512)[None, :]
    mask01 = np.stack([np.where(128 * i + kkk <= qqq, 1.0, 0.0).astype(np.float32)
                       for i in range(4)])
    return {
        "wq": np.ascontiguousarray(np.asarray(inp["L1_attn_Wq"]).T),
        "wk": np.ascontiguousarray(np.asarray(inp["L1_attn_Wk"]).T),
        "wv": np.ascontiguousarray(np.asarray(inp["L1_attn_Wv"]).T),
        "wo": np.ascontiguousarray(np.asarray(inp["L1_attn_Wo"]).T),
        "mask01": mask01,
    }


def build_qatt_program():
    nc = bass.Bass("TRN2", target_bir_lowering=False)
    ctx = Ctx(nc)
    x_in = nc.dram_tensor("x_fm", [BPC, D, S], F32, kind="ExternalInput").ap()
    x_out = nc.dram_tensor("xo_fm", [BPC, D, S], F32, kind="ExternalOutput").ap()
    names = {"wq": [D, D], "wk": [D, D], "wv": [D, D], "wo": [D, D]}
    ctx.w = {k: nc.dram_tensor(k, v, F32, kind="ExternalInput").ap() for k, v in names.items()}
    ctx.w["mask01"] = nc.dram_tensor("mask01", [4, 128, 512], F32, kind="ExternalInput").ap()
    emit_qatt(ctx, x_in, x_out)
    ctx.finalize()
    return nc


def run_qatt(x_fm_percore, inp):
    warr = qatt_weight_arrays(inp)
    nc = build_qatt_program()
    in_maps = [dict(x_fm=x_fm_percore[c], **warr) for c in range(NCORES)]
    res = run_bass_kernel_spmd(nc, in_maps, core_ids=list(range(NCORES)))
    return [r["xo_fm"] for r in res.results]



_PROGRAMS = {}

def _get_prog(name):
    if name not in _PROGRAMS:
        _PROGRAMS[name] = {
            "ffn0": build_ffn0_program, "mba": build_mba_program,
            "mbb": build_mbb_program, "mbc": build_mbc_program,
            "mbd": build_mbd_program, "mla": build_mla_program,
            "moe": build_moe_program, "qatt": build_qatt_program,
        }[name]()
    return _PROGRAMS[name]


def _launch(name, in_maps):
    nc = _get_prog(name)
    return run_bass_kernel_spmd(nc, in_maps, core_ids=list(range(NCORES))).results


def kernel_unfused(**inputs):
    inp = {k: np.asarray(v) for k, v in inputs.items()}
    x = inp["L0_attn_x"]
    x_fm = _fm(x).reshape(NCORES, BPC, D, S)
    xs = [np.ascontiguousarray(x_fm[c]) for c in range(NCORES)]

    w_ffn = ffn0_weight_arrays(inp)
    w_mb = mamba_weight_arrays(inp)
    w_mla = mla_weight_arrays(inp)
    w_moe = moe_weight_arrays(inp)
    w_qa = qatt_weight_arrays(inp)

    r = _launch("ffn0", [dict(x_fm=xs[c], **w_ffn) for c in range(NCORES)])
    x1 = [r[c]["y_fm"] for c in range(NCORES)]
    r = _launch("mba", [dict(x_fm=x1[c], wipT=w_mb["wipT"], normw_cols=w_mb["normw_cols"])
                        for c in range(NCORES)])
    xi = [r[c]["xi_fm"] for c in range(NCORES)]
    res_ = [r[c]["res_fm"] for c in range(NCORES)]
    mrow = [r[c]["mrow"] for c in range(NCORES)]
    rrow = [r[c]["rrow"] for c in range(NCORES)]
    r = _launch("mbb", [dict(xi_fm=xi[c], dtwT=w_mb["dtwT"], xpwT=w_mb["xpwT"],
                             conv_cols=w_mb["conv_cols"], convb_cols=w_mb["convb_cols"],
                             dtb_cols=w_mb["dtb_cols"]) for c in range(NCORES)])
    xc = [r[c]["xc_fm"] for c in range(NCORES)]
    dt = [r[c]["dt_fm"] for c in range(NCORES)]
    Bs = [r[c]["B_fm"] for c in range(NCORES)]
    r = _launch("mbc", [dict(xc_fm=xc[c], dt_fm=dt[c], B_fm=Bs[c],
                             A_rs=w_mb["A_rs"], D_cols=w_mb["D_cols"]) for c in range(NCORES)])
    ys = [r[c]["y_fm"] for c in range(NCORES)]
    r = _launch("mbd", [dict(y_fm=ys[c], res_fm=res_[c], x_fm=x1[c], wopT=w_mb["wopT"],
                             mrow=mrow[c], rrow=rrow[c]) for c in range(NCORES)])
    x2 = [r[c]["xo_fm"] for c in range(NCORES)]
    r = _launch("ffn0", [dict(x_fm=x2[c], **w_ffn) for c in range(NCORES)])
    x3 = [r[c]["y_fm"] for c in range(NCORES)]
    r = _launch("mla", [dict(x_fm=x3[c], **w_mla) for c in range(NCORES)])
    xt1 = [r[c]["xt1_fm"] for c in range(NCORES)]
    r = _launch("moe", [dict(xt1_fm=xt1[c], x_fm=x3[c], **w_moe) for c in range(NCORES)])
    x4 = [r[c]["xo_fm"] for c in range(NCORES)]
    r = _launch("qatt", [dict(x_fm=x4[c], **w_qa) for c in range(NCORES)])
    x5 = [r[c]["xo_fm"] for c in range(NCORES)]
    r = _launch("mla", [dict(x_fm=x5[c], **w_mla) for c in range(NCORES)])
    xt1b = [r[c]["xt1_fm"] for c in range(NCORES)]
    r = _launch("moe", [dict(xt1_fm=xt1b[c], x_fm=x5[c], **w_moe) for c in range(NCORES)])
    out = [r[c]["xo_fm"] for c in range(NCORES)]

    y = np.stack(out).reshape(32, D, S)
    return np.ascontiguousarray(np.swapaxes(y, -1, -2)).astype(np.float32)



def build_fused_program():
    nc = bass.Bass("TRN2", target_bir_lowering=False)
    ctx = Ctx(nc)
    x_in = nc.dram_tensor("x_fm", [BPC, D, S], F32, kind="ExternalInput").ap()
    out_fm = nc.dram_tensor("out_fm", [BPC, D, S], F32, kind="ExternalOutput").ap()
    wnames = {
        "wlinT": [D, D], "lcat": [D, 32], "rcat": [32, D], "gateT": [D, 8], "blin": [128, 8],
        "wipT": [D, 4096], "normw_cols": [128, PD], "dtwT": [2048, 2048], "xpwT": [2048, 64],
        "conv_cols": [128, 64], "convb_cols": [128, 16], "dtb_cols": [128, 16],
        "A_rs": [128, 1024], "D_cols": [128, 16], "wopT": [2048, D],
        "wkvd": [D, 256], "wqd": [D, 256], "wku": [256, 512], "wqu": [256, 512],
        "wvu": [256, 1024], "wropeq": [256, 512], "wropeq_perm": [256, 512],
        "wropek": [D, 512], "wropek_perm": [D, 512], "wo": [D, D],
        "lnw1_cols": [128, PD], "cosT": [32, S], "sinT": [32, S],
        "lnw2_cols": [128, PD], "wg_r": [7, D, D], "wu_r": [7, D, D], "wd_r": [7, D, D],
        "wg_s": [D, D], "wu_s": [D, D], "wd_s": [D, D], "wrouter": [D, 7], "rbias": [1, 7],
        "wq": [D, D], "wk": [D, D], "wv": [D, D], "wo_qa": [D, D],
    }
    ctx.w = {k: nc.dram_tensor(k, v, F32, kind="ExternalInput").ap() for k, v in wnames.items()}
    ctx.w["maskneg"] = nc.dram_tensor("maskneg", [4, 128, 512], F32, kind="ExternalInput").ap()
    ctx.w["mask01"] = nc.dram_tensor("mask01", [4, 128, 512], F32, kind="ExternalInput").ap()
    for v in (1e-5, 1e-6, 1.0, 0.5, 0.7071067811865476):
        ctx.constf(v)

    x1 = nc.dram_tensor("fz_x1", [BPC, D, S], F32, kind="Internal").ap()
    x2 = nc.dram_tensor("fz_x2", [BPC, D, S], F32, kind="Internal").ap()
    x3 = nc.dram_tensor("fz_x3", [BPC, D, S], F32, kind="Internal").ap()
    x4 = nc.dram_tensor("fz_x4", [BPC, D, S], F32, kind="Internal").ap()
    x5 = nc.dram_tensor("fz_x5", [BPC, D, S], F32, kind="Internal").ap()
    xt1 = nc.dram_tensor("fz_xt1", [BPC, D, S], F32, kind="Internal").ap()
    xi = nc.dram_tensor("fz_xi", [BPC, 2048, S], F32, kind="Internal").ap()
    res = nc.dram_tensor("fz_res", [BPC, 2048, S], BF16, kind="Internal").ap()
    xc = nc.dram_tensor("fz_xc", [BPC, 2048, S], BF16, kind="Internal").ap()
    dtt = nc.dram_tensor("fz_dt", [BPC, 2048, S], BF16, kind="Internal").ap()
    Bss = nc.dram_tensor("fz_B", [BPC, 64, S], BF16, kind="Internal").ap()
    yy = nc.dram_tensor("fz_y", [BPC, 2048, S], BF16, kind="Internal").ap()
    mrow = nc.dram_tensor("fz_mrow", [BPC, S], F32, kind="Internal").ap()
    rrow = nc.dram_tensor("fz_rrow", [BPC, S], F32, kind="Internal").ap()
    for nm in ("fz_mrow", "fz_rrow"):
        ctx.bufmap[nm] = Buf(nm)
    ctx.bufmap["mrow"] = ctx.bufmap["fz_mrow"]
    ctx.bufmap["rrow"] = ctx.bufmap["fz_rrow"]

    base_w = ctx.w

    ctx.begin_stage("s0_"); emit_ffn0(ctx, x_in, x1); ctx.end_stage()
    ctx.begin_stage("s1_"); emit_mamba_A(ctx, x1, xi, res, mrow, rrow); ctx.end_stage()
    ctx.begin_stage("s2_"); emit_mamba_B(ctx, xi, xc, dtt, Bss); ctx.end_stage()
    ctx.begin_stage("s3_"); emit_mamba_C(ctx, xc, dtt, Bss, yy); ctx.end_stage()
    ctx.begin_stage("s4_"); emit_mamba_D(ctx, yy, res, x1, x2, mrow, rrow); ctx.end_stage()
    ctx.begin_stage("s5_"); emit_ffn0(ctx, x2, x3); ctx.end_stage()
    ctx.begin_stage("s6_"); emit_mla(ctx, x3, xt1); ctx.end_stage()
    ctx.begin_stage("s7_"); emit_moe(ctx, xt1, x3, x4); ctx.end_stage()
    ctx.w = dict(base_w); ctx.w["wo"] = base_w["wo_qa"]
    ctx.begin_stage("s8_"); emit_qatt(ctx, x4, x5); ctx.end_stage()
    ctx.w = base_w
    ctx.begin_stage("s9_"); emit_mla(ctx, x5, xt1); ctx.end_stage()
    ctx.begin_stage("s10_"); emit_moe(ctx, xt1, x5, out_fm); ctx.end_stage()
    ctx.finalize()
    return nc


def kernel(**inputs):
    inp = {k: np.asarray(v) for k, v in inputs.items()}
    x = inp["L0_attn_x"]
    x_fm = _fm(x).reshape(NCORES, BPC, D, S)
    xs = [np.ascontiguousarray(x_fm[c]) for c in range(NCORES)]
    warr = {}
    warr.update(ffn0_weight_arrays(inp))
    warr.update(mamba_weight_arrays(inp))
    warr.update(mla_weight_arrays(inp))
    warr.update(moe_weight_arrays(inp))
    qa = qatt_weight_arrays(inp)
    warr["wq"] = qa["wq"]; warr["wk"] = qa["wk"]; warr["wv"] = qa["wv"]
    warr["wo_qa"] = qa["wo"]; warr["mask01"] = qa["mask01"]
    if "fused" not in _PROGRAMS:
        _PROGRAMS["fused"] = build_fused_program()
    nc = _PROGRAMS["fused"]
    in_maps = [dict(x_fm=xs[c], **warr) for c in range(NCORES)]
    res = run_bass_kernel_spmd(nc, in_maps, core_ids=list(range(NCORES)))
    out = [res.results[c]["out_fm"] for c in range(NCORES)]
    y = np.stack(out).reshape(32, D, S)
    return np.ascontiguousarray(np.swapaxes(y, -1, -2)).astype(np.float32)

